# version 02
# baseline (speedup 1.0000x reference)
import math
import contextlib
import numpy as np
import concourse.bass as bass
import concourse.mybir as mybir
from concourse.bass_utils import run_bass_kernel_spmd

F32 = mybir.dt.float32
BF16 = mybir.dt.bfloat16
I32 = mybir.dt.int32
AF = mybir.ActivationFunctionType
ALU = mybir.AluOpType
AX = mybir.AxisListType

B, T, D = 2, 8192, 1024
_LN_UID = 0
NCORE = 8
TPC = T // 4
NT = 512
H, HK, HV = 16, 64, 64
HPC = 4


class Ctx:
    def __init__(self, nc, stack):
        self.nc = nc
        self.stack = stack
        self.root_stack = stack
        self.dram = {}
        self.sb = {}
        self.sems = {}
        self.semct = {}

    def dram_in(self, name, shape, dtype=F32):
        t = self.nc.dram_tensor(name, list(shape), dtype, kind="ExternalInput")
        self.dram[name] = t.ap()
        return self.dram[name]

    def dram_out(self, name, shape, dtype=F32):
        t = self.nc.dram_tensor(name, list(shape), dtype, kind="ExternalOutput")
        self.dram[name] = t.ap()
        return self.dram[name]

    def dram_tmp(self, name, shape, dtype=F32):
        t = self.nc.dram_tensor(name, list(shape), dtype)
        self.dram[name] = t.ap()
        return self.dram[name]

    def sbuf(self, name, shape, dtype=F32, side=None):
        h = self.stack.enter_context(self.nc.sbuf_tensor(name, list(shape), dtype, side=side))
        self.sb[name] = h
        return h

    def psum(self, name, shape, dtype=F32):
        h = self.stack.enter_context(self.nc.psum_tensor(name, list(shape), dtype))
        self.sb[name] = h
        return h

    def sem(self, name):
        if name not in self.sems:
            self.sems[name] = self.root_stack.enter_context(self.nc.semaphore(name))
            self.semct[name] = 0
        return self.sems[name]

    def sub(self, stack):
        c = Ctx.__new__(Ctx)
        c.__dict__.update(self.__dict__)
        c.stack = stack
        c.sb = {k: v for k, v in self.sb.items() if not k.startswith("ln_")}
        return c

    def bump(self, name, n):
        self.semct[name] += n
        return self.semct[name]


def emit_ln_fm(ctx, blkfac, xtiles, outtiles, ncols, gain_col=None, bias_col=None,
               scratch_pref="ln", eps=1e-5, add_from=None):
    nc = ctx.nc
    global _LN_UID
    if "ln_sq" not in ctx.sb:
        _LN_UID += 1
        u = _LN_UID
        ctx.sb["ln_sq"] = ctx.stack.enter_context(nc.sbuf_tensor(f"ln_sq{u}", [128, NT], F32))
        ctx.sb["ln_rows"] = ctx.stack.enter_context(nc.sbuf_tensor(f"ln_rows{u}", [1, 4 * NT], F32))
        ctx.sb["ln_bc"] = ctx.stack.enter_context(nc.sbuf_tensor(f"ln_bc{u}", [128, 2 * NT], F32))
        ctx.sb["ln_pstat"] = ctx.stack.enter_context(nc.psum_tensor(f"ln_pstat{u}", [1, 1024], F32))
        ctx.sb["ln_pbc"] = ctx.stack.enter_context(nc.psum_tensor(f"ln_pbc{u}", [128, 1024], F32))
    sq = ctx.sb["ln_sq"]
    rows = ctx.sb["ln_rows"]
    bc = ctx.sb["ln_bc"]
    pstat = ctx.sb["ln_pstat"]
    assert ncols <= NT

    if add_from is not None:
        with nc.Block() as blk:
            @blk.vector
            def _(v):
                for i in range(8):
                    v.tensor_add(xtiles[i], xtiles[i], add_from[i])

    with nc.Block() as blk:
        @blk.tensor
        def _(t):
            for i in range(8):
                t.matmul(pstat[0:1, 0:ncols], ctx.sb["c_ones128"][:, 0:1], xtiles[i],
                         start=(i == 0), stop=(i == 7))
    for i in range(8):
        with nc.Block() as blk:
            @blk.scalar
            def _(s):
                s.activation(sq[:, 0:ncols], xtiles[i], AF.Square)
        with nc.Block() as blk:
            @blk.tensor
            def _(t):
                t.matmul(pstat[0:1, 512:512 + ncols], ctx.sb["c_ones128"][:, 0:1],
                         sq[:, 0:ncols], start=(i == 0), stop=(i == 7))
    with nc.Block() as blk:
        @blk.vector
        def _(v):
            v.tensor_scalar_mul(rows[0:1, 0:ncols], pstat[0:1, 0:ncols], 1.0 / D)
            v.tensor_scalar_mul(rows[0:1, NT:NT + ncols], pstat[0:1, 512:512 + ncols], 1.0 / D)
        @blk.scalar
        def _(s):
            pass
    with nc.Block() as blk:
        @blk.scalar
        def _(s):
            s.activation(rows[0:1, 2 * NT:2 * NT + ncols], rows[0:1, 0:ncols], AF.Square)
    with nc.Block() as blk:
        @blk.vector
        def _(v):
            v.tensor_sub(rows[0:1, NT:NT + ncols], rows[0:1, NT:NT + ncols],
                         rows[0:1, 2 * NT:2 * NT + ncols])
            v.tensor_scalar_add(rows[0:1, NT:NT + ncols], rows[0:1, NT:NT + ncols], eps)
    with nc.Block() as blk:
        @blk.scalar
        def _(s):
            s.activation(rows[0:1, 2 * NT:2 * NT + ncols], rows[0:1, NT:NT + ncols], AF.Sqrt)
    with nc.Block() as blk:
        @blk.vector
        def _(v):
            v.reciprocal(rows[0:1, 3 * NT:3 * NT + ncols], rows[0:1, 2 * NT:2 * NT + ncols])
    pbc = ctx.sb["ln_pbc"]
    with nc.Block() as blk:
        @blk.tensor
        def _(t):
            t.matmul(pbc[:, 0:ncols], ctx.sb["c_ones1"][0:1, :], rows[0:1, 0:ncols], start=True, stop=True)
            t.matmul(pbc[:, 512:512 + ncols], ctx.sb["c_ones1"][0:1, :], rows[0:1, 3 * NT:3 * NT + ncols],
                     start=True, stop=True)
    with nc.Block() as blk:
        @blk.vector
        def _(v):
            v.tensor_copy(bc[:, 0:ncols], pbc[:, 0:ncols])
            v.tensor_copy(bc[:, NT:NT + ncols], pbc[:, 512:512 + ncols])
    with nc.Block() as blk:
        @blk.vector
        def _(v):
            for i in range(8):
                v.tensor_sub(outtiles[i], xtiles[i], bc[:, 0:ncols])
        @blk.gpsimd
        def _(g):
            pass
    with nc.Block() as blk:
        @blk.vector
        def _(v):
            for i in range(8):
                v.tensor_mul(outtiles[i], outtiles[i], bc[:, NT:NT + ncols])
    if gain_col is not None:
        with nc.Block() as blk:
            @blk.vector
            def _(v):
                for i in range(8):
                    v.tensor_scalar(outtiles[i], outtiles[i], gain_col[i], None, ALU.mult)
    if bias_col is not None:
        with nc.Block() as blk:
            @blk.vector
            def _(v):
                for i in range(8):
                    v.tensor_scalar(outtiles[i], outtiles[i], bias_col[i], None, ALU.add)


def emit_stageB(ctx, src="x0", dst="x1"):
    nc = ctx.nc
    HID = 3072
    NOC = HID // 128
    TG = 1024
    xs = [ctx.sbuf(f"B_x{i}", [128, TG], F32) for i in range(8)]
    xb = [ctx.sbuf(f"B_xb{i}", [128, TG], BF16) for i in range(8)]
    hh = [ctx.sbuf(f"B_h{i}", [128, TG], BF16) for i in range(NOC)]
    yy = [ctx.sbuf(f"B_y{i}", [128, TG], F32) for i in range(8)]
    wbuf = ctx.sbuf("B_w", [128, 2048], BF16)
    dbuf = ctx.sbuf("B_wd", [128, 24 * 128], BF16)
    pg = ctx.psum("B_pg", [128, 512], F32)
    pu = ctx.psum("B_pu", [128, 512], F32)
    pd = ctx.psum("B_pd", [128, 512], F32)
    sg = ctx.sbuf("B_sg", [128, 512], BF16)
    ld = ctx.sem("B_ld")
    st = ctx.sem("B_st")

    wg = ctx.dram["wgT"]
    wu = ctx.dram["wuT"]
    wd = ctx.dram["wdT"]

    for g0 in range(TPC // TG):
        gcols = slice(TG * g0, TG * (g0 + 1))
        with nc.Block() as blk:
            @blk.sync
            def _(sy):
                for i in range(8):
                    sy.dma_start(out=xs[i][:, :], in_=ctx.dram[src][128 * i:128 * (i + 1), gcols]).then_inc(ld, 16)
                sy.wait_ge(ld, ctx.bump("B_ld", 16 * 8))
        with nc.Block() as blk:
            @blk.vector
            def _(v):
                for i in range(8):
                    v.tensor_copy(xb[i][:, :], xs[i][:, :])

        ntile = TG // 512
        for oc in range(NOC):
            with nc.Block() as blk:
                @blk.gpsimd
                def _(g):
                    g.dma_start(out=wbuf[:, 0:1024].rearrange("p (k c) -> p k c", k=8),
                                in_=wg[:, 128 * oc:128 * (oc + 1)].rearrange("(k p) c -> p k c", p=128)).then_inc(ld, 16)
                    g.dma_start(out=wbuf[:, 1024:2048].rearrange("p (k c) -> p k c", k=8),
                                in_=wu[:, 128 * oc:128 * (oc + 1)].rearrange("(k p) c -> p k c", p=128)).then_inc(ld, 16)
                    g.wait_ge(ld, ctx.bump("B_ld", 32))
            for n in range(ntile):
                cols = slice(512 * n, 512 * (n + 1))
                with nc.Block() as blk:
                    @blk.tensor
                    def _(t):
                        for kc in range(8):
                            t.matmul(pg[:, :], wbuf[:, 128 * kc:128 * (kc + 1)], xb[kc][:, cols],
                                     start=(kc == 0), stop=(kc == 7))
                        for kc in range(8):
                            t.matmul(pu[:, :], wbuf[:, 1024 + 128 * kc:1024 + 128 * (kc + 1)], xb[kc][:, cols],
                                     start=(kc == 0), stop=(kc == 7))
                with nc.Block() as blk:
                    @blk.scalar
                    def _(s):
                        s.activation(sg[:, :], pg[:, :], AF.Silu, scale=1.0 / 127.0)
                    @blk.vector
                    def _(v):
                        v.tensor_copy(hh[oc][:, cols], pu[:, :])
                with nc.Block() as blk:
                    @blk.vector
                    def _(v):
                        v.tensor_mul(hh[oc][:, cols], hh[oc][:, cols], sg[:, :])

        for oc in range(8):
            with nc.Block() as blk:
                @blk.gpsimd
                def _(g):
                    g.dma_start(out=dbuf[:, 0:24 * 128].rearrange("p (k c) -> p k c", k=24),
                                in_=wd[:, 128 * oc:128 * (oc + 1)].rearrange("(k p) c -> p k c", p=128)).then_inc(ld, 16)
                    g.wait_ge(ld, ctx.bump("B_ld", 16))
            for n in range(ntile):
                cols = slice(512 * n, 512 * (n + 1))
                with nc.Block() as blk:
                    @blk.tensor
                    def _(t):
                        for kc in range(24):
                            t.matmul(pd[:, :], dbuf[:, 128 * kc:128 * (kc + 1)], hh[kc][:, cols],
                                     start=(kc == 0), stop=(kc == 23))
                with nc.Block() as blk:
                    @blk.vector
                    def _(v):
                        v.scalar_tensor_tensor(yy[oc][:, cols], pd[:, :], 1.0 / (127.0 * 127.0),
                                               xs[oc][:, cols], ALU.mult, ALU.add)

        for n in range(ntile):
            cols = slice(512 * n, 512 * (n + 1))
            emit_ln_fm(ctx, None, [yy[i][:, cols] for i in range(8)],
                       [yy[i][:, cols] for i in range(8)], 512)
        with nc.Block() as blk:
            @blk.sync
            def _(sy):
                for i in range(8):
                    sy.dma_start(out=ctx.dram[dst][128 * i:128 * (i + 1), gcols], in_=yy[i][:, :]).then_inc(st, 16)
                sy.wait_ge(st, ctx.bump("B_st", 16 * 8))


def emit_stageC(ctx, src="x1", dst="x2", upto=99):
    nc = ctx.nc
    SEG = 512
    xs = [ctx.sbuf(f"C_x{i}", [128, SEG], F32) for i in range(8)]
    xb = [ctx.sbuf(f"C_xb{i}", [128, SEG], BF16) for i in range(8)]
    qkv = [ctx.sbuf(f"C_qkv{i}", [128, SEG], BF16) for i in range(24)]
    ob = [ctx.sbuf(f"C_o{i}", [128, SEG], BF16) for i in range(8)]
    yy = [ctx.sbuf(f"C_y{i}", [128, SEG], F32) for i in range(8)]
    wc = ctx.sbuf("C_w", [128, 8 * 3072], BF16)
    wo = ctx.sbuf("C_wo", [128, 8 * 1024], BF16)
    bc = ctx.sbuf("C_b", [128, 24], F32)
    bo = ctx.sbuf("C_bo", [128, 8], F32)
    qk = ctx.sbuf("C_qk", [128, SEG], BF16)
    sc = ctx.sbuf("C_sc", [16, SEG], F32)
    eb = ctx.sbuf("C_eb", [16, SEG], F32)
    srow = ctx.sbuf("C_srow", [16, 16], F32)
    pbc16 = ctx.sbuf("C_pb", [128, SEG], BF16)
    inds = ctx.sbuf("C_inds", [128, 128], BF16)
    bc16 = ctx.sbuf("C_bc16", [16, 1024], F32)
    hsel = ctx.sbuf("C_hsel", [16, 4], F32)
    phm = ctx.sbuf("C_phm", [16, SEG], F32)
    pqkv = ctx.psum("C_pqkv", [128, 512], F32)
    psc = ctx.psum("C_psc", [16, 512], F32)
    ppb = ctx.psum("C_ppb", [128, 512], F32)
    pwo = ctx.psum("C_pwo", [128, 512], F32)
    ld = ctx.sem("C_ld")
    st = ctx.sem("C_st")

    with nc.Block() as blk:
        @blk.gpsimd
        def _(g):
            g.dma_start(out=wc[:, :].rearrange("p (k c) -> p k c", k=8),
                        in_=ctx.dram["wC"].rearrange("(k p) c -> p k c", p=128)).then_inc(ld, 16)
            g.dma_start(out=wo[:, :].rearrange("p (k c) -> p k c", k=8),
                        in_=ctx.dram["woC"].rearrange("(k p) c -> p k c", p=128)).then_inc(ld, 16)
            g.dma_start(out=inds[:, :], in_=ctx.dram["cd_indqk"]).then_inc(ld, 16)
        @blk.sync
        def _(sy):
            sy.dma_start(out=bc[:, :], in_=ctx.dram["bC"]).then_inc(ld, 16)
            sy.dma_start(out=bo[:, :], in_=ctx.dram["boC"]).then_inc(ld, 16)
            sy.dma_start(out=bc16[:, :], in_=ctx.dram["cd_bc16"]).then_inc(ld, 16)
            sy.dma_start(out=hsel[:, :], in_=ctx.dram["cd_hsel"]).then_inc(ld, 16)
            sy.dma_start(out=phm[:, :], in_=ctx.dram["cd_phmask"]).then_inc(ld, 16)
            sy.wait_ge(ld, ctx.bump("C_ld", 16 * 8))

    if upto < 0:
        return
    for seg in range(TPC // SEG):
        if upto == 0 and seg > 0:
            break
        cols = slice(SEG * seg, SEG * (seg + 1))
        with nc.Block() as blk:
            @blk.sync
            def _(sy):
                for i in range(8):
                    sy.dma_start(out=xs[i][:, :], in_=ctx.dram[src][128 * i:128 * (i + 1), cols]).then_inc(ld, 16)
                sy.wait_ge(ld, ctx.bump("C_ld", 16 * 8))
        with nc.Block() as blk:
            @blk.vector
            def _(v):
                for i in range(8):
                    v.tensor_copy(xb[i][:, :], xs[i][:, :])
        if upto < 0.8:
            continue
        for oc in range(24 if upto >= 0.95 else 1):
            with nc.Block() as blk:
                @blk.tensor
                def _(t):
                    for kc in range(8):
                        t.matmul(pqkv[:, :], wc[:, 3072 * kc + 128 * oc:3072 * kc + 128 * (oc + 1)],
                                 xb[kc][:, :], start=(kc == 0), stop=(kc == 7))
            with nc.Block() as blk:
                @blk.scalar
                def _(s):
                    s.activation(qkv[oc][:, :], pqkv[:, :], AF.Identity, bias=bc[:, oc:oc + 1])
        if upto < 1:
            continue
        with nc.Block() as blk:
            @blk.vector
            def _(v):
                v.tensor_mul(qk[:, :], qkv[0][:, :], qkv[8][:, :])
            @blk.tensor
            def _(t):
                pass
        for i in range(8):
            with nc.Block() as blk:
                @blk.tensor
                def _(t):
                    t.matmul(psc[:, :], inds[:, 16 * i:16 * (i + 1)], qk[:, :],
                             start=(i == 0), stop=(i == 7))
            if i < 7:
                with nc.Block() as blk:
                    @blk.vector
                    def _(v):
                        v.tensor_mul(qk[:, :], qkv[i + 1][:, :], qkv[9 + i][:, :])
        with nc.Block() as blk:
            @blk.vector
            def _(v):
                v.tensor_copy(sc[:, :], psc[:, :])
        if upto < 2:
            continue
        with nc.Block() as blk:
            @blk.vector
            def _(v):
                for j in range(4):
                    v.reduce_max(srow[:, j:j + 1], sc[:, j::4], axis=AX.X)
        with nc.Block() as blk:
            @blk.vector
            def _(v):
                v.tensor_scalar_mul(srow[:, 4:8], srow[:, 0:4], -0.125)
        with nc.Block() as blk:
            @blk.scalar
            def _(s):
                for j in range(4):
                    s.activation(eb[:, j::4], sc[:, j::4], AF.Exp, bias=srow[:, 4 + j:5 + j],
                                 scale=0.125, accum_out=srow[:, 8 + j:9 + j])
        with nc.Block() as blk:
            @blk.vector
            def _(v):
                v.reciprocal(srow[:, 12:16], srow[:, 8:12])
        with nc.Block() as blk:
            @blk.vector
            def _(v):
                v.tensor_mul(srow[:, 0:4], srow[:, 12:16], hsel[:, :])
        with nc.Block() as blk:
            @blk.vector
            def _(v):
                v.reduce_sum(srow[:, 4:5], srow[:, 0:4], axis=AX.X)
        with nc.Block() as blk:
            @blk.vector
            def _(v):
                v.tensor_scalar(eb[:, :], eb[:, :], srow[:, 4:5], None, ALU.mult)
        with nc.Block() as blk:
            @blk.vector
            def _(v):
                v.tensor_mul(eb[:, :], eb[:, :], phm[:, :])
        if upto < 3:
            continue
        for i in range(8):
            with nc.Block() as blk:
                @blk.tensor
                def _(t):
                    t.matmul(ppb[:, :], bc16[:, 128 * i:128 * (i + 1)], eb[:, :], start=True, stop=True)
            with nc.Block() as blk:
                @blk.vector
                def _(v):
                    v.tensor_copy(pbc16[:, :], ppb[:, :])
            with nc.Block() as blk:
                @blk.vector
                def _(v):
                    v.tensor_mul(ob[i][:, :], qkv[16 + i][:, :], pbc16[:, :])
        if upto < 4:
            continue
        for oc in range(8):
            with nc.Block() as blk:
                @blk.tensor
                def _(t):
                    for kc in range(8):
                        t.matmul(pwo[:, :], wo[:, 1024 * kc + 128 * oc:1024 * kc + 128 * (oc + 1)],
                                 ob[kc][:, :], start=(kc == 0), stop=(kc == 7))
            with nc.Block() as blk:
                @blk.vector
                def _(v):
                    v.scalar_tensor_tensor(yy[oc][:, :], pwo[:, :], bo[:, oc:oc + 1], xs[oc][:, :],
                                           ALU.add, ALU.add)
        if upto >= 5:
            emit_ln_fm(ctx, None, [yy[i][:, :] for i in range(8)],
                       [yy[i][:, :] for i in range(8)], 512)
        if upto >= 4:
            with nc.Block() as blk:
                @blk.sync
                def _(sy):
                    for i in range(8):
                        sy.dma_start(out=ctx.dram[dst][128 * i:128 * (i + 1), cols], in_=yy[i][:, :]).then_inc(st, 16)
                    sy.wait_ge(st, ctx.bump("C_st", 16 * 8))


def build_stageC_program(upto=99):
    nc = bass.Bass("TRN2", target_bir_lowering=False)
    stack = contextlib.ExitStack()
    ctx = Ctx(nc, stack)
    with stack:
        ctx.dram_in("x1", [1024, TPC], F32)
        ctx.dram_in("wC", [1024, 3072], F32)
        ctx.dram_in("bC", [128, 24], F32)
        ctx.dram_in("woC", [1024, 1024], F32)
        ctx.dram_in("boC", [128, 8], F32)
        ctx.dram_out("x2", [1024, TPC], F32)
        for name, arr in _const_arrays().items():
            ctx.dram_in(name, arr.shape, I32 if arr.dtype == np.int32 else F32)
        _preload_consts(ctx)
        emit_stageC(ctx, upto=upto)
    return nc


HB = 513
NTOK = HB + TPC
NQK = 2560


def emit_stageD_layer(ctx, l, src, halo, xmprev, xm_dram, dst):
    emit_stageD_attn(ctx, l, src, halo, xm_dram)
    emit_stageD_ffn(ctx, l, xmprev, xm_dram, dst)


def emit_stageD_attn(ctx, l, src, halo, xm_dram, halo_ag=None, xmpub=None):
    nc = ctx.nc
    ld = ctx.sem("D_ld")
    st = ctx.sem("D_st")
    sb = ctx.sb
    wmA, pmA = sb["D_wmA"], sb["D_pmA"]
    wmBlo, pmBlo = sb["D_wmBlo"], sb["D_pmBlo"]
    P0, P1, P2, P3 = sb["D_P0"], sb["D_P1"], sb["D_P2"], sb["D_P3"]
    i128b, i128f = sb["c_i128b"], sb["c_i128f"]
    nones2 = sb["c_nones2"]
    onesb = sb["c_onesb"]

    phA = contextlib.ExitStack()
    cA = Ctx.__new__(Ctx)
    cA.__dict__.update(ctx.__dict__)
    cA.stack = phA
    qf = [cA.sbuf(f"D{l}_qf{i}", [128, NQK], BF16, side="right") for i in range(4)]
    kf = [cA.sbuf(f"D{l}_kf{i}", [128, NQK], BF16, side="right") for i in range(4)]
    vtm = [cA.sbuf(f"D{l}_vtm{i}", [128, 512], BF16, side="right") for i in range(20)]

    with contextlib.ExitStack() as ph1:
        c1 = Ctx.__new__(Ctx)
        c1.__dict__.update(ctx.__dict__)
        c1.stack = ph1
        xb = [c1.sbuf(f"D{l}_xb{i}", [128, NTOK], BF16, side="right") for i in range(8)]
        wqb = c1.sbuf(f"D{l}_wqb", [128, 2 * 1024], BF16, side="right")

        if halo_ag is None:
            with nc.Block() as blk:
                @blk.gpsimd
                def _(g):
                    for i in range(8):
                        g.dma_start(out=xb[i][:, 0:HB], in_=ctx.dram[halo][128 * i:128 * (i + 1), :]).then_inc(ld, 16)
                        g.dma_start(out=xb[i][:, HB:NTOK], in_=ctx.dram[src][128 * i:128 * (i + 1), :]).then_inc(ld, 16)
                    g.wait_ge(ld, ctx.bump("D_ld", 16 * 16))
            with nc.Block() as blk:
                @blk.vector
                def _(v):
                    for i in range(8):
                        v.tensor_scalar(xb[i][:, 0:HB], xb[i][:, 0:HB], sb["D_hmask"][:, 0:1], None, ALU.mult)
        else:
            hcand = c1.sbuf(f"D{l}_hcand", [128, 4 * HB], BF16, side="right")
            for i in range(8):
                with nc.Block() as blk:
                    @blk.gpsimd
                    def _(g):
                        for q in range(4):
                            g.dma_start(out=hcand[:, HB * q:HB * (q + 1)],
                                        in_=ctx.dram[halo_ag][1024 * q + 128 * i:1024 * q + 128 * (i + 1), :]).then_inc(ld, 16)
                        g.dma_start(out=xb[i][:, HB:NTOK], in_=ctx.dram[src][128 * i:128 * (i + 1), :]).then_inc(ld, 16)
                        g.wait_ge(ld, ctx.bump("D_ld", 16 * 5))
                with nc.Block() as blk:
                    @blk.vector
                    def _(v):
                        v.tensor_scalar(xb[i][:, 0:HB], hcand[:, 0:HB], sb["D_hsel"][:, 0:1], None, ALU.mult)
                for q in range(1, 4):
                    with nc.Block() as blk:
                        @blk.vector
                        def _(v):
                            v.scalar_tensor_tensor(xb[i][:, 0:HB], hcand[:, HB * q:HB * (q + 1)],
                                                   sb["D_hsel"][:, q:q + 1], xb[i][:, 0:HB],
                                                   ALU.mult, ALU.add)
        hbv = [xb[i][:, 1:NTOK] if i < 4 else xb[i][:, 0:NTOK - 1] for i in range(8)]

        wqkv = ctx.dram[f"wqkvD{l}"]
        for oc in range(8):
            dstt = qf[oc] if oc < 4 else kf[oc - 4]
            with nc.Block() as blk:
                @blk.gpsimd
                def _(g):
                    g.dma_start(out=wqb[:, 0:1024].rearrange("p (k c) -> p k c", k=8),
                                in_=wqkv[:, 128 * oc:128 * (oc + 1)].rearrange("(k p) c -> p k c", p=128)).then_inc(ld, 16)
                    g.wait_ge(ld, ctx.bump("D_ld", 16))
            for n in range(5):
                cols = slice(512 * n, 512 * (n + 1))
                with nc.Block() as blk:
                    @blk.tensor
                    def _(t):
                        for kc in range(8):
                            t.matmul(P0[:, :], wqb[:, 128 * kc:128 * (kc + 1)], hbv[kc][:, cols],
                                     start=(kc == 0), stop=(kc == 7))
                with nc.Block() as blk:
                    @blk.vector
                    def _(v):
                        v.tensor_copy(dstt[:, cols], P0[:, :])
        with nc.Block() as blk:
            @blk.gpsimd
            def _(g):
                g.dma_start(out=wqb[:, 0:1024].rearrange("p (k c) -> p k c", k=8),
                            in_=wqkv[:, 1024:1152].rearrange("(k p) c -> p k c", p=128)).then_inc(ld, 16)
                g.dma_start(out=wqb[:, 1024:2048].rearrange("p (k c) -> p k c", k=8),
                            in_=wqkv[:, 1152:1280].rearrange("(k p) c -> p k c", p=128)).then_inc(ld, 16)
                g.wait_ge(ld, ctx.bump("D_ld", 32))
        for half in range(2):
            vc = slice(1024 + 256 * half, 1024 + 256 * (half + 1))
            with nc.Block() as blk:
                @blk.gpsimd
                def _(g):
                    g.dma_start(out=wqb[:, 0:2048].rearrange("p (k c) -> p k c", k=8),
                                in_=wqkv[:, vc].rearrange("(k p) c -> p k c", p=128)).then_inc(ld, 16)
                    g.wait_ge(ld, ctx.bump("D_ld", 16))
            for tt in range(20):
                tc = slice(128 * tt, 128 * (tt + 1))
                with nc.Block() as blk:
                    @blk.tensor
                    def _(t):
                        for kc in range(8):
                            t.matmul(P0[:, 0:256], hbv[kc][:, tc], wqb[:, 256 * kc:256 * (kc + 1)],
                                     start=(kc == 0), stop=(kc == 7))
                with nc.Block() as blk:
                    @blk.vector
                    def _(v):
                        v.tensor_copy(vtm[tt][:, 256 * half:256 * (half + 1)], P0[:, 0:256])

    with contextlib.ExitStack() as ph2:
        c2 = Ctx.__new__(Ctx)
        c2.__dict__.update(ctx.__dict__)
        c2.stack = ph2
        cs = c2.sbuf(f"D{l}_cs", [128, NQK], BF16, side="right")
        ss = c2.sbuf(f"D{l}_ss", [128, NQK], BF16, side="right")
        scr = c2.sbuf(f"D{l}_scr", [128, 512], F32, side="right")
        qhb = c2.sbuf(f"D{l}_qhb", [128, 512], BF16, side="right")
        rot = c2.sbuf(f"D{l}_rot", [128, 256], BF16, side="right")
        t2b = c2.sbuf(f"D{l}_t2b", [128, 512], BF16, side="right")
        with nc.Block() as blk:
            @blk.gpsimd
            def _(g):
                g.dma_start(out=rot[:, 0:128], in_=ctx.dram[f"rotq{l}"]).then_inc(ld, 16)
                g.dma_start(out=rot[:, 128:256], in_=ctx.dram[f"rotk{l}"]).then_inc(ld, 16)
                g.wait_ge(ld, ctx.bump("D_ld", 32))
        for side in range(2):
            tiles = qf if side == 0 else kf
            rsl = rot[:, 0:128] if side == 0 else rot[:, 128:256]
            with nc.Block() as blk:
                @blk.gpsimd
                def _(g):
                    g.dma_start(out=cs[:, :], in_=ctx.dram[("csq" if side == 0 else "csk") + str(l)]).then_inc(ld, 16)
                    g.dma_start(out=ss[:, :], in_=ctx.dram[("ssq" if side == 0 else "ssk") + str(l)]).then_inc(ld, 16)
                    g.wait_ge(ld, ctx.bump("D_ld", 32))
            for i in range(4):
                for n in range(5):
                    cols = slice(512 * n, 512 * (n + 1))
                    with nc.Block() as blk:
                        @blk.scalar
                        def _(sc_):
                            sc_.activation(scr[:, :], tiles[i][:, cols], AF.Square)
                    with nc.Block() as blk:
                        @blk.tensor
                        def _(t):
                            t.matmul(P0[0:2, 0:512], sb["c_indh2"][:, 0:2], scr[:, :],
                                     start=True, stop=True)
                    with nc.Block() as blk:
                        @blk.scalar
                        def _(sc_):
                            sc_.activation(scr[0:2, :], P0[0:2, 0:512], AF.Sqrt)
                    with nc.Block() as blk:
                        @blk.vector
                        def _(v):
                            v.tensor_scalar_max(scr[0:2, :], scr[0:2, :], 1e-12)
                    with nc.Block() as blk:
                        @blk.vector
                        def _(v):
                            v.reciprocal(scr[0:2, :], scr[0:2, :])
                    with nc.Block() as blk:
                        @blk.tensor
                        def _(t):
                            t.matmul(P0[:, 0:512], sb["c_indh2T"][0:2, :], scr[0:2, :],
                                     start=True, stop=True)
                    with nc.Block() as blk:
                        @blk.vector
                        def _(v):
                            v.tensor_tensor(qhb[:, :], tiles[i][:, cols], P0[:, 0:512], ALU.mult)
                    with nc.Block() as blk:
                        @blk.tensor
                        def _(t):
                            t.matmul(P0[:, 0:512], rsl, qhb[:, :], start=True, stop=True)
                    with nc.Block() as blk:
                        @blk.vector
                        def _(v):
                            v.tensor_tensor(t2b[:, :], P0[:, 0:512], ss[:, cols], ALU.mult)
                    with nc.Block() as blk:
                        @blk.vector
                        def _(v):
                            v.tensor_tensor(tiles[i][:, cols], qhb[:, :], cs[:, cols], ALU.mult)
                    with nc.Block() as blk:
                        @blk.vector
                        def _(v):
                            v.tensor_add(tiles[i][:, cols], tiles[i][:, cols], t2b[:, :])

    if l == 0 and "dbgq" in ctx.dram:
        with nc.Block() as blk:
            @blk.sync
            def _(sy):
                for i in range(4):
                    sy.dma_start(out=ctx.dram["dbgq"][128 * i:128 * (i + 1), :], in_=qf[i][:, :]).then_inc(st, 16)
                    sy.dma_start(out=ctx.dram["dbgk"][128 * i:128 * (i + 1), :], in_=kf[i][:, :]).then_inc(st, 16)
                for i in range(20):
                    sy.dma_start(out=ctx.dram["dbgv"][128 * i:128 * (i + 1), :], in_=vtm[i][:, :]).then_inc(st, 16)
                sy.wait_ge(st, ctx.bump("D_st", 16 * 28))

    ph3 = contextlib.ExitStack()
    c3a = Ctx.__new__(Ctx)
    c3a.__dict__.update(ctx.__dict__)
    c3a.stack = ph3
    wo = c3a.sbuf(f"D{l}_wo", [128, 4 * 1024], BF16, side="right")
    p2 = c3a.sbuf(f"D{l}_p2b", [128, 8 * 512], BF16, side="right")
    ofm = [c3a.sbuf(f"D{l}_ofm{i}", [128, 512], BF16, side="right") for i in range(4)]
    mrow = c3a.sbuf(f"D{l}_mrow", [128, 512], BF16, side="right")
    zrow = c3a.sbuf(f"D{l}_zrow", [1, 512], F32, side="right")
    rzbc = c3a.sbuf(f"D{l}_rzbc", [64, 512], F32, side="right")
    ctx.sb["D_mcol"] = c3a.sbuf(f"D{l}_mcol", [128, 2], F32, side="right")
    res1 = c3a.sbuf(f"D{l}_res1", [128, 512], F32, side="right")
    with nc.Block() as blk:
        @blk.gpsimd
        def _(g):
            g.dma_start(out=wo[:, :].rearrange("p (k c) -> p k c", k=4),
                        in_=ctx.dram[f"woD{l}"].rearrange("(k p) c -> p k c", p=128)).then_inc(ld, 16)
            g.wait_ge(ld, ctx.bump("D_ld", 16))
    for w in range(4):
        qoff = 512 * (w + 1)
        koff = 512 * w
        qcols = slice(qoff, qoff + 512)
        for h in range(8):
            til, hb0 = h // 2, 64 * (h % 2)
            hsl = slice(hb0, hb0 + 64)
            for qt in range(4):
                qc = slice(qoff + 128 * qt, qoff + 128 * (qt + 1))
                with nc.Block() as blk:
                    @blk.tensor
                    def _(t):
                        t.matmul(P1[:, :], qf[til][hsl, qc], kf[til][hsl, koff:koff + 512],
                                 start=True, stop=True)
                        t.matmul(P2[:, :], qf[til][hsl, qc], kf[til][hsl, koff + 512:koff + 1024],
                                 start=True, stop=True)
                with nc.Block() as blk:
                    @blk.vector
                    def _(v):
                        v.reduce_max(scrM(ctx)[:, 0:1], P1[:, :], axis=AX.X)
                        v.reduce_max(scrM(ctx)[:, 1:2], P2[:, :], axis=AX.X)
                with nc.Block() as blk:
                    @blk.vector
                    def _(v):
                        v.tensor_max(scrM(ctx)[:, 0:1], scrM(ctx)[:, 0:1], scrM(ctx)[:, 1:2])
                with nc.Block() as blk:
                    @blk.tensor
                    def _(t):
                        t.matmul(P3[0:1, 0:128], scrM(ctx)[:, 0:1], i128f[:, :], start=True, stop=True)
                with nc.Block() as blk:
                    @blk.vector
                    def _(v):
                        v.tensor_copy(mrow[0:1, 128 * qt:128 * (qt + 1)], P3[0:1, 0:128])
                        v.tensor_copy(mrow[64:65, 128 * qt:128 * (qt + 1)], P3[0:1, 0:128])
            for kt in range(8):
                kc = slice(koff + 128 * kt, koff + 128 * (kt + 1))
                pad = (w == 0 and kt < 4)
                with nc.Block() as blk:
                    @blk.tensor
                    def _(t):
                        if hb0 == 0:
                            t.matmul(P1[:, :], kf[til][hsl, kc], qf[til][hsl, qcols], start=True, stop=False)
                            t.matmul(P1[:, :], nones2[0:1, :], mrow[0:1, :], start=False, stop=False)
                            t.matmul(P1[:, :], i128b[:, :], wmA[:, 512 * kt:512 * (kt + 1)],
                                     start=False, stop=(not pad))
                            if pad:
                                t.matmul(P1[:, :], i128b[:, :], pmA[:, 512 * kt:512 * (kt + 1)],
                                         start=False, stop=True)
                        else:
                            t.matmul(P1[:, :], kf[til][hsl, kc], qf[til][hsl, qcols], start=True, stop=False)
                            t.matmul(P1[:, :], nones2[64:65, :], mrow[64:65, :], start=False, stop=False)
                            t.matmul(P1[0:64, :], i128b[64:128, 64:128], wmBlo[64:128, 512 * kt:512 * (kt + 1)],
                                     start=False, stop=False, tile_position=(64, 0))
                            t.matmul(P1[64:128, :], i128b[64:128, 64:128], wmA[64:128, 512 * kt:512 * (kt + 1)],
                                     start=False, stop=(not pad), tile_position=(64, 64))
                            if pad:
                                t.matmul(P1[0:64, :], i128b[64:128, 64:128], pmBlo[64:128, 512 * kt:512 * (kt + 1)],
                                         start=False, stop=False, tile_position=(64, 0))
                                t.matmul(P1[64:128, :], i128b[64:128, 64:128], pmA[64:128, 512 * kt:512 * (kt + 1)],
                                         start=False, stop=True, tile_position=(64, 64))
                with nc.Block() as blk:
                    @blk.scalar
                    def _(sc_):
                        sc_.activation(p2[:, 512 * kt:512 * (kt + 1)], P1[:, :], AF.Exp)
            with nc.Block() as blk:
                @blk.tensor
                def _(t):
                    for kt in range(8):
                        t.matmul(P3[0:1, 0:512], onesb[:, 0:1], p2[:, 512 * kt:512 * (kt + 1)],
                                 start=(kt == 0), stop=(kt == 7))
            with nc.Block() as blk:
                @blk.tensor
                def _(t):
                    for kt in range(8):
                        t.matmul(P2[0:64, :], vtm[4 * w + kt][:, 64 * h:64 * (h + 1)],
                                 p2[:, 512 * kt:512 * (kt + 1)], start=(kt == 0), stop=(kt == 7))
                @blk.vector
                def _(v):
                    v.tensor_copy(zrow[0:1, 0:512], P3[0:1, 0:512])
            with nc.Block() as blk:
                @blk.vector
                def _(v):
                    v.reciprocal(zrow[0:1, 0:512], zrow[0:1, 0:512])
            with nc.Block() as blk:
                @blk.tensor
                def _(t):
                    t.matmul(P0[0:64, :], sb["c_ones1"][0:1, 0:64], zrow[0:1, 0:512], start=True, stop=True)
            with nc.Block() as blk:
                @blk.vector
                def _(v):
                    v.tensor_copy(rzbc[0:64, :], P0[0:64, :])
            with nc.Block() as blk:
                @blk.vector
                def _(v):
                    v.tensor_tensor(ofm[til][hsl, :], P2[0:64, :], rzbc[0:64, :], ALU.mult)
        if l == 0 and "dbgo" in ctx.dram:
            with nc.Block() as blk:
                @blk.sync
                def _(sy):
                    for i in range(4):
                        sy.dma_start(out=ctx.dram["dbgo"][128 * i:128 * (i + 1), 512 * w:512 * (w + 1)],
                                     in_=ofm[i][:, :]).then_inc(st, 16)
                    sy.wait_ge(st, ctx.bump("D_st", 16 * 4))
        for oc in range(8):
            with nc.Block() as blk:
                @blk.sync
                def _(sy):
                    sy.dma_start(out=res1[:, :], in_=ctx.dram[src][128 * oc:128 * (oc + 1),
                                                                  512 * w:512 * (w + 1)]).then_inc(ld, 16)
                    sy.wait_ge(ld, ctx.bump("D_ld", 16))
                @blk.tensor
                def _(t):
                    for kc in range(4):
                        t.matmul(P0[:, :], wo[:, 1024 * kc + 128 * oc:1024 * kc + 128 * (oc + 1)],
                                 ofm[kc][:, :], start=(kc == 0), stop=(kc == 3))
            with nc.Block() as blk:
                @blk.vector
                def _(v):
                    v.tensor_add(res1[:, :], res1[:, :], P0[:, :])
                    if xmpub is not None and w == 3:
                        v.tensor_copy(sb["D_xlast"][:, oc:oc + 1], res1[:, 511:512])
            with nc.Block() as blk:
                @blk.sync
                def _(sy):
                    sy.dma_start(out=ctx.dram[xm_dram][128 * oc:128 * (oc + 1), 512 * w:512 * (w + 1)],
                                 in_=res1[:, :]).then_inc(st, 16)
                    sy.wait_ge(st, ctx.bump("D_st", 16))
        if xmpub is not None and w == 3:
            with nc.Block() as blk:
                @blk.sync
                def _(sy):
                    sy.dma_start(out=ctx.dram[xmpub], in_=sb["D_xlast"][:, :]).then_inc(st, 16)
                    sy.wait_ge(st, ctx.bump("D_st", 16))

    ph3.close()
    phA.close()


def emit_stageD_ffn(ctx, l, xmprev, xm_dram, dst, xmprev_sb=None):
    nc = ctx.nc
    sb = ctx.sb
    ld = ctx.sem("D_ld")
    st = ctx.sem("D_st")
    P0 = sb["D_P0"]
    with contextlib.ExitStack() as phF:
        c3 = Ctx.__new__(Ctx)
        c3.__dict__.update(ctx.__dict__)
        c3.stack = phF
        TG = 512
        xm = [c3.sbuf(f"D{l}_fxm{i}", [128, TG + 1], F32, side="right") for i in range(8)]
        hblk = [c3.sbuf(f"D{l}_fh{i}", [128, TG], BF16, side="right") for i in range(44)]
        lnt = [c3.sbuf(f"D{l}_flt{i}", [128, TG], BF16, side="right") for i in range(8)]
        w1b = c3.sbuf(f"D{l}_fw1", [128, 1024], BF16, side="right")
        w2b = c3.sbuf(f"D{l}_fw2", [128, 22 * 128], BF16, side="right")
        bcol = c3.sbuf(f"D{l}_fbc", [128, 76], F32, side="right")
        x3t = [c3.sbuf(f"D{l}_fx3{i}", [128, 512], F32, side="right") for i in range(8)]
        with nc.Block() as blk:
            @blk.sync
            def _(sy):
                sy.dma_start(out=bcol[:, 0:44], in_=ctx.dram[f"b1D{l}"]).then_inc(ld, 16)
                sy.dma_start(out=bcol[:, 44:52], in_=ctx.dram[f"b2D{l}"]).then_inc(ld, 16)
                sy.dma_start(out=bcol[:, 52:60], in_=ctx.dram[f"lngD{l}"]).then_inc(ld, 16)
                sy.dma_start(out=bcol[:, 60:68], in_=ctx.dram[f"lnbD{l}"]).then_inc(ld, 16)
                if xmprev_sb is None:
                    sy.dma_start(out=bcol[:, 68:76], in_=ctx.dram[xmprev]).then_inc(ld, 16)
                    sy.wait_ge(ld, ctx.bump("D_ld", 16 * 5))
                else:
                    sy.wait_ge(ld, ctx.bump("D_ld", 16 * 4))
        if xmprev_sb is not None:
            with nc.Block() as blk:
                @blk.vector
                def _(v):
                    v.tensor_copy(bcol[:, 68:76], xmprev_sb[:, :])
        for g0 in range(TPC // TG):
            gcols = slice(TG * g0, TG * (g0 + 1))
            with nc.Block() as blk:
                @blk.sync
                def _(sy):
                    for i in range(8):
                        sy.dma_start(out=xm[i][:, 1:TG + 1],
                                     in_=ctx.dram[xm_dram][128 * i:128 * (i + 1), gcols]).then_inc(ld, 16)
                    sy.wait_ge(ld, ctx.bump("D_ld", 16 * 8))
            if g0 == 0:
                with nc.Block() as blk:
                    @blk.vector
                    def _(v):
                        for i in range(8):
                            v.tensor_scalar(xm[i][:, 0:1], bcol[:, 68 + i:69 + i],
                                            sb["D_hmask"][:, 0:1], None, ALU.mult)
            else:
                with nc.Block() as blk:
                    @blk.sync
                    def _(sy):
                        for i in range(8):
                            sy.dma_start(out=xm[i][:, 0:1],
                                         in_=ctx.dram[xm_dram][128 * i:128 * (i + 1),
                                                               TG * g0 - 1:TG * g0],
                                         allow_slow_non_contiguous=True).then_inc(ld, 16)
                        sy.wait_ge(ld, ctx.bump("D_ld", 16 * 8))
            hsv = [xm[i][:, 1:TG + 1] if i < 4 else xm[i][:, 0:TG] for i in range(8)]
            for n in range(TG // 512):
                cols = slice(512 * n, 512 * (n + 1))
                emit_ln_fm(ctx, None, [hsv[i][:, cols] for i in range(8)],
                           [lnt[i][:, cols] for i in range(8)], 512,
                           gain_col=[bcol[:, 52 + i:53 + i] for i in range(8)],
                           bias_col=[bcol[:, 60 + i:61 + i] for i in range(8)])
            w1 = ctx.dram[f"w1D{l}"]
            for oc in range(44):
                with nc.Block() as blk:
                    @blk.gpsimd
                    def _(g):
                        g.dma_start(out=w1b[:, :].rearrange("p (k c) -> p k c", k=8),
                                    in_=w1[:, 128 * oc:128 * (oc + 1)].rearrange("(k p) c -> p k c", p=128)).then_inc(ld, 16)
                        g.wait_ge(ld, ctx.bump("D_ld", 16))
                for n in range(TG // 512):
                    cols = slice(512 * n, 512 * (n + 1))
                    with nc.Block() as blk:
                        @blk.tensor
                        def _(t):
                            for kc in range(8):
                                t.matmul(P0[:, :], w1b[:, 128 * kc:128 * (kc + 1)], lnt[kc][:, cols],
                                         start=(kc == 0), stop=(kc == 7))
                    with nc.Block() as blk:
                        @blk.scalar
                        def _(sc_):
                            fn = AF.Identity if oc % 2 == 0 else AF.Gelu
                            sc_.activation(hblk[oc][:, cols], P0[:, :], fn, bias=bcol[:, oc:oc + 1])
            with nc.Block() as blk:
                @blk.vector
                def _(v):
                    for i in range(22):
                        v.tensor_mul(hblk[2 * i][:, :], hblk[2 * i][:, :], hblk[2 * i + 1][:, :])
            w2 = ctx.dram[f"w2D{l}"]
            for oc in range(8):
                with nc.Block() as blk:
                    @blk.gpsimd
                    def _(g):
                        g.dma_start(out=w2b[:, :].rearrange("p (k c) -> p k c", k=22),
                                    in_=w2[:, 128 * oc:128 * (oc + 1)].rearrange("(k p) c -> p k c", p=128)).then_inc(ld, 16)
                        g.wait_ge(ld, ctx.bump("D_ld", 16))
                for n in range(TG // 512):
                    cols = slice(512 * n, 512 * (n + 1))
                    with nc.Block() as blk:
                        @blk.tensor
                        def _(t):
                            for kc in range(22):
                                t.matmul(P0[:, :], w2b[:, 128 * kc:128 * (kc + 1)], hblk[2 * kc][:, cols],
                                         start=(kc == 0), stop=(kc == 21))
                    with nc.Block() as blk:
                        @blk.vector
                        def _(v):
                            v.scalar_tensor_tensor(x3t[oc][:, :], P0[:, :], bcol[:, 44 + oc:45 + oc],
                                                   xm[oc][:, 1 + 512 * n:1 + 512 * (n + 1)], ALU.add, ALU.add)
                    with nc.Block() as blk:
                        @blk.sync
                        def _(sy):
                            sy.dma_start(out=ctx.dram[dst][128 * oc:128 * (oc + 1),
                                                           TG * g0 + 512 * n:TG * g0 + 512 * (n + 1)],
                                         in_=x3t[oc][:, :]).then_inc(st, 16)
                            sy.wait_ge(st, ctx.bump("D_st", 16))


def scrM(ctx):
    return ctx.sb["D_mcol"]


def emit_stageD(ctx, src="x2", dst="xd", nlayers=2):
    nc = ctx.nc
    ctx.sbuf("D_wmA", [128, 8 * 512], BF16)
    ctx.sbuf("D_wmBlo", [128, 8 * 512], BF16)
    ctx.sbuf("D_pmA", [128, 4 * 512], BF16)
    ctx.sbuf("D_pmBlo", [128, 4 * 512], BF16)
    ctx.sbuf("D_hmask", [128, 1], F32)
    ctx.sb["D_P0"] = ctx.psum("D_P0", [128, 512], F32)
    ctx.sb["D_P1"] = ctx.psum("D_P1", [128, 512], F32)
    ctx.sb["D_P2"] = ctx.psum("D_P2", [128, 512], F32)
    ctx.sb["D_P3"] = ctx.psum("D_P3", [128, 512], F32)
    ld = ctx.sem("D_ld")
    with nc.Block() as blk:
        @blk.gpsimd
        def _(g):
            for nm in ("wmA", "wmBlo"):
                g.dma_start(out=ctx.sb[f"D_{nm}"][:, :], in_=ctx.dram[f"cd_{nm}"]).then_inc(ld, 16)
            for nm in ("pmA", "pmBlo"):
                g.dma_start(out=ctx.sb[f"D_{nm}"][:, :], in_=ctx.dram[nm]).then_inc(ld, 16)
        @blk.sync
        def _(sy):
            sy.dma_start(out=ctx.sb["D_hmask"][:, :], in_=ctx.dram["hmask"]).then_inc(ld, 16)
            sy.wait_ge(ld, ctx.bump("D_ld", 16 * 5))
    emit_stageD_layer(ctx, 0, src, "h2", "xmprev0", "xmscr", "x3scr")
    if nlayers > 1:
        emit_stageD_layer(ctx, 1, "x3scr", "h3", "xmprev1", "xmscr", "x4scr")
    emit_final_ln(ctx, src, "x4scr", dst)


def emit_final_ln(ctx, src1, src2, dst):
    nc = ctx.nc
    ld = ctx.sem("D_ld")
    st = ctx.sem("D_st")
    yy = [ctx.sbuf(f"D_fin{i}", [128, 512], F32) for i in range(8)]
    y2 = [ctx.sbuf(f"D_fin2{i}", [128, 512], F32) for i in range(8)]
    for n in range(TPC // 512):
        cols = slice(512 * n, 512 * (n + 1))
        with nc.Block() as blk:
            @blk.sync
            def _(sy):
                for i in range(8):
                    sy.dma_start(out=yy[i][:, :], in_=ctx.dram[src1][128 * i:128 * (i + 1), cols]).then_inc(ld, 16)
                    sy.dma_start(out=y2[i][:, :], in_=ctx.dram[src2][128 * i:128 * (i + 1), cols]).then_inc(ld, 16)
                sy.wait_ge(ld, ctx.bump("D_ld", 16 * 16))
        with nc.Block() as blk:
            @blk.vector
            def _(v):
                for i in range(8):
                    v.tensor_add(yy[i][:, :], yy[i][:, :], y2[i][:, :])
        emit_ln_fm(ctx, None, [yy[i][:, :] for i in range(8)],
                   [yy[i][:, :] for i in range(8)], 512)
        with nc.Block() as blk:
            @blk.sync
            def _(sy):
                for i in range(8):
                    sy.dma_start(out=ctx.dram[dst][128 * i:128 * (i + 1), cols], in_=yy[i][:, :]).then_inc(st, 16)
                sy.wait_ge(st, ctx.bump("D_st", 16 * 8))


def build_stageD_program(debug=False, nlayers=2):
    nc = bass.Bass("TRN2", target_bir_lowering=False)
    stack = contextlib.ExitStack()
    ctx = Ctx(nc, stack)
    with stack:
        ctx.dram_in("x2", [1024, TPC], F32)
        ctx.dram_in("h2", [1024, HB], F32)
        ctx.dram_in("h3", [1024, HB], F32)
        ctx.dram_in("xmprev0", [128, 8], F32)
        ctx.dram_in("xmprev1", [128, 8], F32)
        ctx.dram_in("hmask", [128, 1], F32)
        ctx.dram_in("pmA", [128, 4 * 512], F32)
        ctx.dram_in("pmBlo", [128, 4 * 512], F32)
        for l in range(2):
            ctx.dram_in(f"wqkvD{l}", [1024, 1536], F32)
            ctx.dram_in(f"woD{l}", [512, 1024], F32)
            ctx.dram_in(f"w1D{l}", [1024, 5632], F32)
            ctx.dram_in(f"w2D{l}", [2816, 1024], F32)
            ctx.dram_in(f"b1D{l}", [128, 44], F32)
            ctx.dram_in(f"b2D{l}", [128, 8], F32)
            ctx.dram_in(f"lngD{l}", [128, 8], F32)
            ctx.dram_in(f"lnbD{l}", [128, 8], F32)
            ctx.dram_in(f"rotq{l}", [128, 128], F32)
            ctx.dram_in(f"rotk{l}", [128, 128], F32)
            for nm in ("csq", "ssq", "csk", "ssk"):
                ctx.dram_in(f"{nm}{l}", [128, NQK], F32)
        mk = ctx.dram_out if debug else ctx.dram_tmp
        if debug:
            import ml_dtypes
            ctx.dram_out("dbgq", [512, NQK], BF16)
            ctx.dram_out("dbgk", [512, NQK], BF16)
            ctx.dram_out("dbgv", [2560, 512], BF16)
            ctx.dram_out("dbgo", [512, TPC], BF16)
        mk("xmscr", [1024, TPC], F32)
        mk("x3scr", [1024, TPC], F32)
        mk("x4scr", [1024, TPC], F32)
        ctx.dram_out("xd", [1024, TPC], F32)
        for name, arr in _const_arrays().items():
            ctx.dram_in(name, arr.shape, I32 if arr.dtype == np.int32 else F32)
        _preload_consts(ctx)
        emit_stageD(ctx, nlayers=nlayers)
    return nc


CC = 32
CAPA = 80.0


def _r3(ap, w):
    return ap.rearrange("p (a b) -> p a b", b=w)


def emit_stageA(ctx, dst="x0", TT=T, do_collective=True):
    nc = ctx.nc
    sb = ctx.sb
    ld = ctx.sem("A_ld")
    st = ctx.sem("A_st")
    ctx.sem("A_sm")
    ctx.sem("A_se")
    P0 = ctx.psum("A_P0", [128, 512], F32)
    _ps_stack = contextlib.ExitStack()
    _cps = Ctx.__new__(Ctx)
    _cps.__dict__.update(ctx.__dict__)
    _cps.stack = _ps_stack
    PA = _cps.psum("A_PA", [128, 512], F32)
    PB = _cps.psum("A_PB", [128, 512], F32)
    PS = _cps.psum("A_PS", [128, 512], F32)
    PC = [_cps.psum(f"A_PC{j}", [128, 384], F32) for j in range(4)]
    ctx.sb["A_PSUMS"] = (P0, PA, PB, PS, PC)

    wab = ctx.sbuf("A_wa", [128, 8 * 1024], BF16)
    wfg = ctx.sbuf("A_wfg", [128, 256], BF16)
    convw = ctx.sbuf("A_cw", [128, 24], F32)
    convb = ctx.sbuf("A_cb", [128, 6], F32)
    dtb = ctx.sbuf("A_dtb", [128, 2], F32)
    nacol = ctx.sbuf("A_na", [128, 2], F32)
    acol = ctx.sbuf("A_ac", [4, 2], F32)
    bg2bc = ctx.sbuf("A_bg2", [128, 256], F32)
    onwbc = ctx.sbuf("A_onw", [128, 256], F32)
    rbg2 = ctx.sbuf("A_r_bg2", [1, 256], F32)
    ronw = ctx.sbuf("A_r_onw", [1, 256], F32)
    i32g = ctx.sbuf("A_i32g", [128, 64], BF16)
    i32st = ctx.sbuf("A_i32s", [128, 32], BF16)
    i64s = ctx.sbuf("A_i64s", [128, 64], BF16)
    ni64s = ctx.sbuf("A_ni64s", [128, 64], BF16)
    msel = ctx.sbuf("A_msel", [128, 5 * 64], I32)
    msel2 = ctx.sbuf("A_msel2", [128, 5 * 64], I32)
    zero64 = ctx.sbuf("A_z64", [128, 64], F32)
    resetm = ctx.sbuf("A_rst", [128, 512], F32)
    bc4 = ctx.sbuf("A_bc4", [4, 256], F32)
    nbc4 = ctx.sbuf("A_nbc4", [4, 256], F32)

    Sst = ctx.sbuf("A_S", [128, 2 * 128], BF16)
    Shist = ctx.sbuf("A_Sh", [128, 16 * 128], BF16)

    hsbf = [ctx.sbuf(f"A_hs{i}", [128, 512], BF16) for i in range(8)]
    qraw = [ctx.sbuf(f"A_qr{i}", [128, 515], BF16) for i in range(2)]
    kraw = [ctx.sbuf(f"A_kr{i}", [128, 515], BF16) for i in range(2)]
    vraw = [ctx.sbuf(f"A_vr{i}", [128, 515], BF16) for i in range(2)]
    fgout = ctx.sbuf("A_fg", [128, 512], BF16)
    qh = [ctx.sbuf(f"A_qh{i}", [128, 512], BF16) for i in range(2)]
    kh = [ctx.sbuf(f"A_kh{i}", [128, 512], BF16) for i in range(2)]
    kb = [ctx.sbuf(f"A_kb{i}", [128, 512], BF16) for i in range(2)]
    vb = [ctx.sbuf(f"A_vb{i}", [128, 512], BF16) for i in range(2)]
    gcf = [ctx.sbuf(f"A_gc{i}", [128, 512], F32) for i in range(2)]
    gprev = [ctx.sbuf(f"A_gp{i}", [128, 512], F32) for i in range(2)]
    REB = ["aK4", "aQ4", "aK8", "aQ8", "aK16", "aQ16", "aK32", "aQ32",
           "b4s", "b4n", "b8n", "b16n", "btl"]
    reb = {nm: [ctx.sbuf(f"A_{nm}{i}", [128, 512], BF16) for i in range(2)] for nm in REB}
    scr = ctx.sbuf("A_scr", [128, 512], F32)
    scr2 = ctx.sbuf("A_scr2", [128, 512], F32)
    scrb = ctx.sbuf("A_scrb", [128, 512], BF16)
    brow = ctx.sbuf("A_brow", [4, 512], F32)
    betabc = [ctx.sbuf(f"A_bbc{i}", [128, 512], BF16) for i in range(2)]

    LtX = {0: ctx.sbuf("A_LtA", [128, 64], BF16), 1: ctx.sbuf("A_LtB", [128, 64], BF16)}
    LpX = {0: ctx.sbuf("A_LpA", [128, 64], BF16), 1: ctx.sbuf("A_LpB", [128, 64], BF16)}
    ScX = {0: ctx.sbuf("A_ScA", [128, 64], BF16), 1: ctx.sbuf("A_ScB", [128, 64], BF16)}
    PtX = {0: ctx.sbuf("A_PtA", [128, 64], BF16), 1: ctx.sbuf("A_PtB", [128, 64], BF16)}
    PpX = {0: ctx.sbuf("A_PpA", [128, 64], BF16), 1: ctx.sbuf("A_PpB", [128, 64], BF16)}
    GtX = {0: ctx.sbuf("A_GtA", [128, 64], BF16), 1: ctx.sbuf("A_GtB", [128, 64], BF16)}
    rhsb = ctx.sbuf("A_rhs", [128, 4 * 512], BF16)
    wsol = ctx.sbuf("A_wsol", [128, 4 * 512], BF16)
    btm = ctx.sbuf("A_btm", [128, 4 * 256], BF16)
    oloc = ctx.sbuf("A_oloc", [128, 4 * 256], F32)
    omt = ctx.sbuf("A_omt", [128, 16 * 64], BF16)
    phir = ctx.sbuf("A_phr", [128, 16 * 128], BF16)
    psig = ctx.sbuf("A_psi", [128, 16 * 128], BF16)
    gamc = ctx.sbuf("A_gam", [128, 32], F32)
    oarr = ctx.sbuf("A_oar", [128, 256], F32)
    gatet = ctx.sbuf("A_gate", [128, 4 * 256], F32)
    ofmt = ctx.sbuf("A_ofmt", [128, 2 * 512], BF16)

    nt = TT // 512
    hsT = ctx.dram["hsT"]
    oscr = ctx.dram["oscr"]

    with nc.Block() as blk:
        @blk.gpsimd
        def _(g):
            g.dma_start(out=wab[:, :].rearrange("p (k c) -> p k c", k=8),
                        in_=ctx.dram["wA"].rearrange("(k p) c -> p k c", p=128)).then_inc(ld, 16)
            g.dma_start(out=wfg[:, :], in_=ctx.dram["wfg2"]).then_inc(ld, 16)
            g.dma_start(out=i32g[:, :], in_=ctx.dram["cd_i32g"]).then_inc(ld, 16)
            g.dma_start(out=i32st[:, :], in_=ctx.dram["cd_i32st"]).then_inc(ld, 16)
            g.dma_start(out=i64s[:, :], in_=ctx.dram["cd_i64st"]).then_inc(ld, 16)
            g.dma_start(out=ni64s[:, :], in_=ctx.dram["cd_ni64st"]).then_inc(ld, 16)
        @blk.sync
        def _(sy):
            sy.dma_start(out=msel[:, :], in_=ctx.dram["cd_msel"]).then_inc(ld, 16)
            sy.dma_start(out=msel2[:, :], in_=ctx.dram["cd_msel2"]).then_inc(ld, 16)
            sy.dma_start(out=convw[:, :], in_=ctx.dram["convw"]).then_inc(ld, 16)
            sy.dma_start(out=convb[:, :], in_=ctx.dram["convb"]).then_inc(ld, 16)
            sy.dma_start(out=dtb[:, :], in_=ctx.dram["dtb2"]).then_inc(ld, 16)
            sy.dma_start(out=acol[0:4, 0:1], in_=ctx.dram["alogc"]).then_inc(ld, 16)
            sy.dma_start(out=resetm[:, :], in_=ctx.dram["cd_reset"]).then_inc(ld, 16)
            sy.dma_start(out=bc4[:, :], in_=ctx.dram["cd_bc4"]).then_inc(ld, 16)
            sy.dma_start(out=nbc4[:, :], in_=ctx.dram["cd_nbc4"]).then_inc(ld, 16)
            sy.dma_start(out=rbg2[:, :], in_=ctx.dram["bg2r"]).then_inc(ld, 16)
            sy.dma_start(out=ronw[:, :], in_=ctx.dram["onw4"]).then_inc(ld, 16)
            sy.wait_ge(ld, ctx.bump("A_ld", 16 * 17))
    with nc.Block() as blk:
        @blk.scalar
        def _(s):
            s.activation(acol[0:4, 1:2], acol[0:4, 0:1], AF.Exp)
    with nc.Block() as blk:
        @blk.tensor
        def _(t):
            t.matmul(P0[:, 0:1], nbc4[0:4, 0:128], acol[0:4, 1:2], start=True, stop=True)
            t.matmul(P0[:, 1:2], nbc4[0:4, 128:256], acol[0:4, 1:2], start=True, stop=True)
            t.matmul(PA[:, 0:256], sb["c_ones1"][0:1, :], rbg2[0:1, :], start=True, stop=True)
            t.matmul(PB[:, 0:256], sb["c_ones1"][0:1, :], ronw[0:1, :], start=True, stop=True)
    with nc.Block() as blk:
        @blk.vector
        def _(v):
            v.tensor_copy(nacol[:, :], P0[:, 0:2])
            v.tensor_copy(bg2bc[:, :], PA[:, 0:256])
            v.tensor_copy(onwbc[:, :], PB[:, 0:256])
        @blk.gpsimd
        def _(g):
            for rr in (qraw, kraw, vraw):
                for i in range(2):
                    g.memset(rr[i][:, 0:3], 0.0)
            g.memset(Sst[:, :], 0.0)
            g.memset(zero64[:, :], 0.0)

    for tj in range(nt):
        tc = slice(512 * tj, 512 * (tj + 1))
        with nc.Block() as blk:
            @blk.gpsimd
            def _(g):
                for kc in range(8):
                    g.dma_start(out=hsbf[kc][:, :], in_=hsT[128 * kc:128 * (kc + 1), tc]).then_inc(ld, 16)
                g.wait_ge(ld, ctx.bump("A_ld", 16 * 8))
        for oc in range(8):
            with nc.Block() as blk:
                @blk.tensor
                def _(t):
                    for kc in range(8):
                        t.matmul(P0[:, :], wab[:, 1024 * kc + 128 * oc:1024 * kc + 128 * (oc + 1)],
                                 hsbf[kc][:, :], start=(kc == 0), stop=(kc == 7))
            with nc.Block() as blk:
                if oc < 6:
                    dstt = [qraw, qraw, kraw, kraw, vraw, vraw][oc][oc % 2]
                    @blk.vector
                    def _(v):
                        v.tensor_copy(dstt[:, 3:515], P0[:, :])
                elif oc == 6:
                    @blk.vector
                    def _(v):
                        v.tensor_copy(fgout[:, :], P0[:, :])
                else:
                    @blk.scalar
                    def _(s):
                        s.activation(brow[0:4, :], P0[0:4, :], AF.Sigmoid)
        for i in range(2):
            for raw, dstt, cwof, cbof in ((qraw[i], qh[i], 12 * i + 0, 3 * i + 0),
                                          (kraw[i], kh[i], 12 * i + 4, 3 * i + 1),
                                          (vraw[i], vb[i], 12 * i + 8, 3 * i + 2)):
                with nc.Block() as blk:
                    @blk.vector
                    def _(v):
                        v.tensor_scalar(scrb[:, :], raw[:, 0:512], convw[:, cwof + 0:cwof + 1], None, ALU.mult)
                for jj in range(1, 4):
                    with nc.Block() as blk:
                        @blk.vector
                        def _(v):
                            v.scalar_tensor_tensor(scrb[:, :], raw[:, jj:512 + jj],
                                                   convw[:, cwof + jj:cwof + jj + 1],
                                                   scrb[:, :], ALU.mult, ALU.add)
                with nc.Block() as blk:
                    @blk.scalar
                    def _(s):
                        s.activation(dstt[:, :], scrb[:, :], AF.Silu, bias=convb[:, cbof:cbof + 1])
                    @blk.vector
                    def _(v):
                        v.tensor_copy(raw[:, 0:3], raw[:, 512:515])
        with nc.Block() as blk:
            @blk.tensor
            def _(t):
                t.matmul(PA[:, 0:512], bc4[0:4, 0:128], brow[0:4, :], start=True, stop=True)
                t.matmul(PB[:, 0:512], bc4[0:4, 128:256], brow[0:4, :], start=True, stop=True)
        with nc.Block() as blk:
            @blk.vector
            def _(v):
                v.tensor_copy(betabc[0][:, :], PA[:, 0:512])
                v.tensor_copy(betabc[1][:, :], PB[:, 0:512])
        for i in range(2):
            with nc.Block() as blk:
                @blk.vector
                def _(v):
                    v.tensor_mul(vb[i][:, :], vb[i][:, :], betabc[i][:, :])
            for outt, qscale in ((qh[i], True), (kh[i], False)):
                with nc.Block() as blk:
                    @blk.scalar
                    def _(s):
                        s.activation(scr[:, :], outt[:, :], AF.Square)
                with nc.Block() as blk:
                    @blk.tensor
                    def _(t):
                        t.matmul(P0[0:2, :], sb["c_indh2"][:, 0:2], scr[:, :], start=True, stop=True)
                with nc.Block() as blk:
                    @blk.scalar
                    def _(s):
                        s.activation(scr[0:2, :], P0[0:2, :], AF.Sqrt)
                with nc.Block() as blk:
                    @blk.vector
                    def _(v):
                        if qscale:
                            v.tensor_scalar(scr[0:2, :], scr[0:2, :], 8.0, 8e-6, ALU.mult, ALU.add)
                        else:
                            v.tensor_scalar_add(scr[0:2, :], scr[0:2, :], 1e-6)
                with nc.Block() as blk:
                    @blk.vector
                    def _(v):
                        v.reciprocal(scr[0:2, :], scr[0:2, :])
                with nc.Block() as blk:
                    @blk.tensor
                    def _(t):
                        t.matmul(P0[:, :], sb["c_indh2T"][0:2, :], scr[0:2, :], start=True, stop=True)
                with nc.Block() as blk:
                    @blk.vector
                    def _(v):
                        v.tensor_tensor(outt[:, :], outt[:, :], P0[:, :], ALU.mult)
            with nc.Block() as blk:
                @blk.vector
                def _(v):
                    v.tensor_mul(kb[i][:, :], kh[i][:, :], betabc[i][:, :])
        with nc.Block() as blk:
            @blk.tensor
            def _(t):
                t.matmul(PA[:, 0:512], wfg[0:64, 0:128], fgout[0:64, :], start=True, stop=True)
                t.matmul(PB[:, 0:512], wfg[0:64, 128:256], fgout[0:64, :], start=True, stop=True)
        for i, PX in ((0, PA), (1, PB)):
            with nc.Block() as blk:
                @blk.scalar
                def _(s):
                    s.activation(scr[:, :], PX[:, 0:512], AF.Abs, bias=dtb[:, i:i + 1])
            with nc.Block() as blk:
                @blk.scalar
                def _(s):
                    s.activation(scr2[:, :], PX[:, 0:512], AF.Relu, bias=dtb[:, i:i + 1])
            with nc.Block() as blk:
                @blk.scalar
                def _(s):
                    s.activation(scr[:, :], scr[:, :], AF.Exp, scale=-1.0)
            with nc.Block() as blk:
                @blk.scalar
                def _(s):
                    s.activation(scr[:, :], scr[:, :], AF.Ln, bias=1.0)
            with nc.Block() as blk:
                @blk.vector
                def _(v):
                    v.tensor_add(scr[:, :], scr[:, :], scr2[:, :])
            with nc.Block() as blk:
                @blk.vector
                def _(v):
                    v.tensor_scalar(scr[:, :], scr[:, :], nacol[:, i:i + 1], None, ALU.mult)
            with nc.Block() as blk:
                @blk.vector
                def _(v):
                    v.tensor_tensor_scan(gcf[i][:, :], resetm[:, :], scr[:, :], 0.0,
                                         ALU.mult, ALU.add)
            with nc.Block() as blk:
                @blk.gpsimd
                def _(g):
                    g.memset(gprev[i][:, 0:1], 0.0)
                @blk.vector
                def _(v):
                    v.tensor_tensor(gprev[i][:, 1:512], gcf[i][:, 0:511], resetm[:, 1:512], ALU.mult)
        with nc.Block() as blk:
            @blk.scalar
            def _(s):
                for i in range(2):
                    s.activation(gamc[:, i:32:2], gcf[i][:, 31:512:32], AF.Exp)

        for i in range(2):
            GC, GP = gcf[i], gprev[i]
            def bsub(out_, a_rep, b_full, w, off, rev=False):
                ap_rep = bass.AP(a_rep, off, [[512, 128], [w, 512 // w], [0, w]])
                if not rev:
                    return (_r3(out_[:, :], w), _r3(b_full[:, :], w), ap_rep)
                return (_r3(out_[:, :], w), ap_rep, _r3(b_full[:, :], w))
            for w, anm_k, anm_q in ((4, "aK4", "aQ4"), (8, "aK8", "aQ8"), (16, "aK16", "aQ16")):
                with nc.Block() as blk:
                    @blk.vector
                    def _(v):
                        o, a, b = bsub(scr, GP, GC, w, 0)
                        v.tensor_tensor(o, a, b, ALU.subtract)
                with nc.Block() as blk:
                    @blk.scalar
                    def _(s):
                        s.activation(scr2[:, :], scr[:, :], AF.Exp)
                with nc.Block() as blk:
                    @blk.vector
                    def _(v):
                        v.tensor_tensor(reb[anm_k][i][:, :], kb[i][:, :], scr2[:, :], ALU.mult)
                with nc.Block() as blk:
                    @blk.vector
                    def _(v):
                        v.tensor_tensor(reb[anm_q][i][:, :], qh[i][:, :], scr2[:, :], ALU.mult)
                if w == 4:
                    with nc.Block() as blk:
                        @blk.vector
                        def _(v):
                            v.tensor_scalar(scr[:, :], scr[:, :], -1.0, CAPA, ALU.mult, ALU.min)
                    with nc.Block() as blk:
                        @blk.scalar
                        def _(s):
                            s.activation(scr[:, :], scr[:, :], AF.Exp)
                    with nc.Block() as blk:
                        @blk.vector
                        def _(v):
                            v.tensor_tensor(reb["b4s"][i][:, :], kh[i][:, :], scr[:, :], ALU.mult)
            with nc.Block() as blk:
                @blk.scalar
                def _(s):
                    s.activation(scr2[:, :], GC[:, :], AF.Exp)
            with nc.Block() as blk:
                @blk.vector
                def _(v):
                    v.tensor_tensor(reb["aK32"][i][:, :], kb[i][:, :], scr2[:, :], ALU.mult)
            with nc.Block() as blk:
                @blk.vector
                def _(v):
                    v.tensor_tensor(reb["aQ32"][i][:, :], qh[i][:, :], scr2[:, :], ALU.mult)
            for w, bnm in ((4, "b4n"), (8, "b8n"), (16, "b16n")):
                with nc.Block() as blk:
                    @blk.vector
                    def _(v):
                        o, a, b = bsub(scr, GC, GC, w, w - 1, rev=True)
                        v.tensor_tensor(o, a, b, ALU.subtract)
                with nc.Block() as blk:
                    @blk.vector
                    def _(v):
                        v.tensor_scalar_min(scr[:, :], scr[:, :], CAPA)
                with nc.Block() as blk:
                    @blk.scalar
                    def _(s):
                        s.activation(scr[:, :], scr[:, :], AF.Exp)
                with nc.Block() as blk:
                    @blk.vector
                    def _(v):
                        v.tensor_tensor(reb[bnm][i][:, :], kh[i][:, :], scr[:, :], ALU.mult)
            with nc.Block() as blk:
                @blk.vector
                def _(v):
                    o, a, b = bsub(scr, GC, GC, 32, 31, rev=True)
                    v.tensor_tensor(o, a, b, ALU.subtract)
            with nc.Block() as blk:
                @blk.scalar
                def _(s):
                    s.activation(scr[:, :], scr[:, :], AF.Exp)
            with nc.Block() as blk:
                @blk.vector
                def _(v):
                    v.tensor_tensor(reb["btl"][i][:, :], kh[i][:, :], scr[:, :], ALU.mult)
        for q4 in range(4):
            tq = slice(128 * q4, 128 * (q4 + 1))
            with nc.Block() as blk:
                @blk.tensor
                def _(t):
                    t.matmul(P0[:, 0:256], fgout[64:128, tq], wfg[64:128, :], start=True, stop=True)
            with nc.Block() as blk:
                @blk.vector
                def _(v):
                    v.tensor_tensor(gatet[:, 256 * q4:256 * (q4 + 1)], P0[:, 0:256], bg2bc[:, :], ALU.add)
        with nc.Block() as blk:
            @blk.scalar
            def _(s):
                for q4 in range(4):
                    s.activation(gatet[:, 256 * q4:256 * (q4 + 1)],
                                 gatet[:, 256 * q4:256 * (q4 + 1)], AF.Sigmoid)

        lvls = (("b4s", "aK4", "aQ4"), ("b4n", "aK4", "aQ4"),
                ("b8n", "aK8", "aQ8"), ("b16n", "aK16", "aQ16"))
        for gi in range(4):
            with nc.Block() as blk:
                @blk.tensor
                def _(t):
                    for lv, (bnm, anmk, anmq) in enumerate(lvls):
                        for cj in range(4):
                            ccols = slice(128 * gi + 32 * cj, 128 * gi + 32 * (cj + 1))
                            for h in range(4):
                                par, grp = h % 2, h // 2
                                PX = PA if par == 0 else PB
                                hs64 = slice(64 * par, 64 * (par + 1))
                                cs_ = slice(64 * lv + 32 * grp, 64 * lv + 32 * (grp + 1))
                                t.matmul(PX[32 * cj:32 * (cj + 1), cs_],
                                         reb[bnm][grp][hs64, ccols], reb[anmk][grp][hs64, ccols],
                                         start=True, stop=True, tile_position=(64 * par, 32 * cj))
                                cs2 = slice(256 + 64 * lv + 32 * grp, 256 + 64 * lv + 32 * (grp + 1))
                                t.matmul(PX[32 * cj:32 * (cj + 1), cs2],
                                         reb[bnm][grp][hs64, ccols], reb[anmq][grp][hs64, ccols],
                                         start=True, stop=True, tile_position=(64 * par, 32 * cj))
            with nc.Block() as blk:
                @blk.tensor
                def _(t):
                    for lv, (bnm, anmk, anmq) in enumerate(lvls):
                        for cj in range(4):
                            ccols = slice(128 * gi + 32 * cj, 128 * gi + 32 * (cj + 1))
                            for h in range(4):
                                par, grp = h % 2, h // 2
                                PX = PS if par == 0 else PC[0]
                                hs64 = slice(64 * par, 64 * (par + 1))
                                cs_ = slice(64 * lv + 32 * grp, 64 * lv + 32 * (grp + 1))
                                t.matmul(PX[32 * cj:32 * (cj + 1), cs_],
                                         reb[anmk][grp][hs64, ccols], reb[bnm][grp][hs64, ccols],
                                         start=True, stop=True, tile_position=(64 * par, 32 * cj))
            for par in (0, 1):
                PX = PA if par == 0 else PB
                PXp = PS if par == 0 else PC[0]
                for base, M, dst_ in ((0, msel, LtX[par]), (256, msel, ScX[par])):
                    d4m = M[:, 0:64] if base == 0 else M[:, 64:128]
                    with nc.Block() as blk:
                        @blk.vector
                        def _(v):
                            v.select(scr[:, 0:64], d4m, PX[:, base:base + 64], zero64[:, :])
                    with nc.Block() as blk:
                        @blk.vector
                        def _(v):
                            v.select(scr[:, 0:64], M[:, 128:192], PX[:, base + 64:base + 128], scr[:, 0:64])
                    with nc.Block() as blk:
                        @blk.vector
                        def _(v):
                            v.select(scr[:, 0:64], M[:, 192:256], PX[:, base + 128:base + 192], scr[:, 0:64])
                    with nc.Block() as blk:
                        @blk.vector
                        def _(v):
                            v.select(dst_[:, :], M[:, 256:320], PX[:, base + 192:base + 256], scr[:, 0:64])
                with nc.Block() as blk:
                    @blk.vector
                    def _(v):
                        v.select(scr[:, 0:64], msel2[:, 0:64], PXp[:, 0:64], zero64[:, :])
                with nc.Block() as blk:
                    @blk.vector
                    def _(v):
                        v.select(scr[:, 0:64], msel2[:, 128:192], PXp[:, 64:128], scr[:, 0:64])
                with nc.Block() as blk:
                    @blk.vector
                    def _(v):
                        v.select(scr[:, 0:64], msel2[:, 192:256], PXp[:, 128:192], scr[:, 0:64])
                with nc.Block() as blk:
                    @blk.vector
                    def _(v):
                        v.select(LpX[par][:, :], msel2[:, 256:320], PXp[:, 192:256], scr[:, 0:64])
            for par in (0, 1):
                PX = PA if par == 0 else PB
                with nc.Block() as blk:
                    @blk.vector
                    def _(v):
                        v.tensor_copy(PtX[par][:, :], LtX[par][:, :])
                        v.tensor_copy(PpX[par][:, :], LpX[par][:, :])
                with nc.Block() as blk:
                    @blk.vector
                    def _(v):
                        v.tensor_tensor(GtX[par][:, :], i32g[:, :], LtX[par][:, :], ALU.subtract)
                for lv in range(4):
                    with nc.Block() as blk:
                        @blk.tensor
                        def _(t):
                            for cj in range(4):
                                rs = slice(32 * cj, 32 * (cj + 1))
                                for grp in range(2):
                                    cg = slice(32 * grp, 32 * (grp + 1))
                                    t.matmul(PX[rs, 32 * grp:32 * (grp + 1)],
                                             PtX[par][rs, cg], PpX[par][rs, cg],
                                             start=True, stop=True, tile_position=(32 * cj, 32 * cj))
                                    t.matmul(PX[rs, 64 + 32 * grp:96 + 32 * grp],
                                             PpX[par][rs, cg], PtX[par][rs, cg],
                                             start=True, stop=True, tile_position=(32 * cj, 32 * cj))
                    with nc.Block() as blk:
                        @blk.vector
                        def _(v):
                            v.tensor_copy(PpX[par][:, :], PX[:, 0:64])
                            v.tensor_copy(PtX[par][:, :], PX[:, 64:128])
                    with nc.Block() as blk:
                        @blk.tensor
                        def _(t):
                            for cj in range(4):
                                rs = slice(32 * cj, 32 * (cj + 1))
                                for grp in range(2):
                                    cg = slice(32 * grp, 32 * (grp + 1))
                                    t.matmul(PX[rs, 128 + 32 * grp:160 + 32 * grp],
                                             i32st[rs, :], GtX[par][rs, cg],
                                             start=True, stop=False, tile_position=(32 * cj, 32 * cj))
                                    t.matmul(PX[rs, 128 + 32 * grp:160 + 32 * grp],
                                             PpX[par][rs, cg], GtX[par][rs, cg],
                                             start=False, stop=True, tile_position=(32 * cj, 32 * cj))
                    with nc.Block() as blk:
                        @blk.vector
                        def _(v):
                            v.tensor_copy(GtX[par][:, :], PX[:, 128:192])
            with nc.Block() as blk:
                @blk.tensor
                def _(t):
                    for cj in range(4):
                        ccols = slice(128 * gi + 32 * cj, 128 * gi + 32 * (cj + 1))
                        for h in range(4):
                            par, grp = h % 2, h // 2
                            PX = PA if par == 0 else PB
                            hs64 = slice(64 * par, 64 * (par + 1))
                            i64sl = i64s[hs64, :]
                            t.matmul(PX[32 * cj:32 * (cj + 1), 128 * h:128 * h + 64],
                                     vb[grp][hs64, ccols], i64sl,
                                     start=True, stop=True, tile_position=(64 * par, 32 * cj))
                            t.matmul(PX[32 * cj:32 * (cj + 1), 128 * h + 64:128 * (h + 1)],
                                     reb["aK32"][grp][hs64, ccols], i64sl,
                                     start=True, stop=True, tile_position=(64 * par, 32 * cj))
            with nc.Block() as blk:
                @blk.vector
                def _(v):
                    v.tensor_copy(bass.AP(rhsb, 512 * gi, [[2048, 128], [256, 2], [1, 128]]),
                                  bass.AP(PA, 0, [[512, 128], [256, 2], [1, 128]]))
                    v.tensor_copy(bass.AP(rhsb, 512 * gi + 128, [[2048, 128], [256, 2], [1, 128]]),
                                  bass.AP(PB, 128, [[512, 128], [256, 2], [1, 128]]))
            with nc.Block() as blk:
                @blk.tensor
                def _(t):
                    for cj in range(4):
                        ccols = slice(128 * gi + 32 * cj, 128 * gi + 32 * (cj + 1))
                        for h in range(4):
                            par, grp = h % 2, h // 2
                            PX = PA if par == 0 else PB
                            hs64 = slice(64 * par, 64 * (par + 1))
                            t.matmul(PX[32 * cj:32 * (cj + 1), 64 * h:64 * (h + 1)],
                                     reb["btl"][grp][hs64, ccols], i64s[hs64, :],
                                     start=True, stop=True, tile_position=(64 * par, 32 * cj))
            with nc.Block() as blk:
                @blk.vector
                def _(v):
                    v.tensor_copy(bass.AP(btm, 256 * gi, [[1024, 128], [128, 2], [1, 64]]),
                                  bass.AP(PA, 0, [[512, 128], [128, 2], [1, 64]]))
                    v.tensor_copy(bass.AP(btm, 256 * gi + 64, [[1024, 128], [128, 2], [1, 64]]),
                                  bass.AP(PB, 64, [[512, 128], [128, 2], [1, 64]]))
            with nc.Block() as blk:
                @blk.tensor
                def _(t):
                    for cj in range(4):
                        rs = slice(32 * cj, 32 * (cj + 1))
                        for h in range(4):
                            par, grp = h % 2, h // 2
                            t.matmul(PS[rs, 128 * h:128 * (h + 1)],
                                     GtX[par][rs, 32 * grp:32 * (grp + 1)],
                                     rhsb[rs, 512 * gi + 128 * h:512 * gi + 128 * (h + 1)],
                                     start=True, stop=True, tile_position=(32 * cj, 32 * cj))
            with nc.Block() as blk:
                @blk.vector
                def _(v):
                    v.tensor_copy(wsol[:, 512 * gi:512 * (gi + 1)], PS[:, :])
            with nc.Block() as blk:
                @blk.tensor
                def _(t):
                    for cj in range(4):
                        rs = slice(32 * cj, 32 * (cj + 1))
                        for h in range(4):
                            par, grp = h % 2, h // 2
                            t.matmul(PS[rs, 64 * h:64 * (h + 1)],
                                     ScX[par][rs, 32 * grp:32 * (grp + 1)],
                                     wsol[rs, 512 * gi + 128 * h:512 * gi + 128 * h + 64],
                                     start=True, stop=True, tile_position=(32 * cj, 32 * cj))
            with nc.Block() as blk:
                @blk.vector
                def _(v):
                    v.tensor_copy(oloc[:, 256 * gi:256 * (gi + 1)], PS[:, 0:256])
            with nc.Block() as blk:
                @blk.tensor
                def _(t):
                    for cj in range(4):
                        rs = slice(32 * cj, 32 * (cj + 1))
                        for h in range(4):
                            par, grp = h % 2, h // 2
                            t.matmul(PC[cj][64 * par:64 * (par + 1), 32 * grp:32 * (grp + 1)],
                                     wsol[rs, 512 * gi + 128 * h + 64:512 * gi + 128 * (h + 1)],
                                     ScX[par][rs, 32 * grp:32 * (grp + 1)],
                                     start=True, stop=True, tile_position=(32 * cj, 64 * par))
                            t.matmul(PC[cj][64 * par:64 * (par + 1), 64 + 128 * grp:64 + 128 * (grp + 1)],
                                     btm[rs, 256 * gi + 64 * h:256 * gi + 64 * (h + 1)],
                                     wsol[rs, 512 * gi + 128 * h:512 * gi + 128 * (h + 1)],
                                     start=True, stop=True, tile_position=(32 * cj, 64 * par))
            with nc.Block() as blk:
                @blk.tensor
                def _(t):
                    for cj in range(4):
                        for h in range(4):
                            par, grp = h % 2, h // 2
                            t.matmul(PC[cj][64 * par:64 * (par + 1), 32 * grp:32 * (grp + 1)],
                                     ni64s[64 * par:64 * (par + 1), :],
                                     reb["aQ32"][grp][64 * par:64 * (par + 1),
                                                      128 * gi + 32 * cj:128 * gi + 32 * (cj + 1)],
                                     start=False, stop=True, tile_position=(64 * par, 64 * par))
            for cj in range(4):
                ch = 4 * gi + cj
                with nc.Block() as blk:
                    @blk.vector
                    def _(v):
                        v.tensor_scalar(omt[:, 64 * ch:64 * (ch + 1)], PC[cj][:, 0:64],
                                        -1.0, None, ALU.mult)
                        v.tensor_copy(bass.AP(psig, 128 * ch, [[2048, 128], [64, 2], [1, 64]]),
                                      bass.AP(PC[cj], 64, [[384, 128], [128, 2], [1, 64]]))
                with nc.Block() as blk:
                    @blk.vector
                    def _(v):
                        v.tensor_scalar(bass.AP(phir, 128 * ch, [[2048, 128], [64, 2], [1, 64]]),
                                        bass.AP(PC[cj], 128, [[384, 128], [128, 2], [1, 64]]),
                                        -1.0, None, ALU.mult)

        smA = ctx.sems["A_sm"]
        seA = ctx.sems["A_se"]
        with nc.Block() as blk:
            @blk.tensor
            def _(t):
                for ch in range(16):
                    G = 16 * tj + ch
                    cur = G % 2
                    if G > 0:
                        t.wait_ge(seA, G)
                    last = None
                    for h in range(4):
                        par, grp = h % 2, h // 2
                        hs64 = slice(64 * par, 64 * (par + 1))
                        ccs = slice(64 * grp, 64 * (grp + 1))
                        t.matmul(PS[hs64, ccs],
                                 phir[hs64, 128 * ch + 64 * grp:128 * ch + 64 * (grp + 1)],
                                 Sst[hs64, 128 * cur + 64 * grp:128 * cur + 64 * (grp + 1)],
                                 start=True, stop=False, tile_position=(64 * par, 64 * par))
                        last = t.matmul(PS[hs64, ccs], i64s[hs64, :],
                                        psig[hs64, 128 * ch + 64 * grp:128 * ch + 64 * (grp + 1)],
                                        start=False, stop=True, tile_position=(64 * par, 64 * par))
                    last.then_inc(smA, 1)
            @blk.vector
            def _(v):
                for ch in range(16):
                    G = 16 * tj + ch
                    cur, nxt = G % 2, 1 - (G % 2)
                    v.wait_ge(smA, G + 1)
                    v.tensor_copy(Shist[:, 128 * ch:128 * (ch + 1)], Sst[:, 128 * cur:128 * (cur + 1)])
                    for grp in range(2):
                        last = v.scalar_tensor_tensor(
                            Sst[:, 128 * nxt + 64 * grp:128 * nxt + 64 * (grp + 1)],
                            Sst[:, 128 * cur + 64 * grp:128 * cur + 64 * (grp + 1)],
                            gamc[:, 2 * ch + grp:2 * ch + grp + 1],
                            PS[:, 64 * grp:64 * (grp + 1)], ALU.mult, ALU.add)
                    last.then_inc(seA, 1)
        ctx.bump("A_sm", 16)
        ctx.bump("A_se", 16)

        for gi in range(4):
            with nc.Block() as blk:
                @blk.tensor
                def _(t):
                    for cj in range(4):
                        ch = 4 * gi + cj
                        for h in range(4):
                            par, grp = h % 2, h // 2
                            PX = PA if par == 0 else PB
                            hs64 = slice(64 * par, 64 * (par + 1))
                            t.matmul(PX[32 * cj:32 * (cj + 1), 64 * grp:64 * (grp + 1)],
                                     omt[hs64, 64 * ch + 32 * grp:64 * ch + 32 * (grp + 1)],
                                     Shist[hs64, 128 * ch + 64 * grp:128 * ch + 64 * (grp + 1)],
                                     start=True, stop=True, tile_position=(64 * par, 32 * cj))
            with nc.Block() as blk:
                @blk.vector
                def _(v):
                    v.tensor_tensor(bass.AP(oarr, 0, [[256, 128], [128, 2], [1, 64]]),
                                    bass.AP(oloc, 256 * gi, [[1024, 128], [128, 2], [1, 64]]),
                                    bass.AP(PA, 0, [[512, 128], [64, 2], [1, 64]]), ALU.add)
                    v.tensor_tensor(bass.AP(oarr, 64, [[256, 128], [128, 2], [1, 64]]),
                                    bass.AP(oloc, 256 * gi + 64, [[1024, 128], [128, 2], [1, 64]]),
                                    bass.AP(PB, 0, [[512, 128], [64, 2], [1, 64]]), ALU.add)
            with nc.Block() as blk:
                @blk.scalar
                def _(s):
                    s.activation(scr[:, 0:256], oarr[:, :], AF.Square)
            with nc.Block() as blk:
                @blk.vector
                def _(v):
                    v.reduce_sum(scr2[:, 0:4], _r3(scr[:, 0:256], 64), axis=AX.X)
            with nc.Block() as blk:
                @blk.vector
                def _(v):
                    v.tensor_scalar(scr2[:, 0:4], scr2[:, 0:4], 1.0 / 64.0, 1e-5, ALU.mult, ALU.add)
            with nc.Block() as blk:
                @blk.scalar
                def _(s):
                    s.activation(scr2[:, 4:8], scr2[:, 0:4], AF.Sqrt)
            with nc.Block() as blk:
                @blk.vector
                def _(v):
                    v.reciprocal(scr2[:, 4:8], scr2[:, 4:8])
            with nc.Block() as blk:
                @blk.vector
                def _(v):
                    v.tensor_tensor(_r3(oarr[:, :], 64), _r3(oarr[:, :], 64),
                                    bass.AP(scr2, 4, [[512, 128], [1, 4], [0, 64]]), ALU.mult)
            with nc.Block() as blk:
                @blk.vector
                def _(v):
                    v.tensor_mul(oarr[:, :], oarr[:, :], onwbc[:, :])
            with nc.Block() as blk:
                @blk.vector
                def _(v):
                    v.tensor_mul(oarr[:, :], oarr[:, :], gatet[:, 256 * gi:256 * (gi + 1)])
            with nc.Block() as blk:
                @blk.tensor
                def _(t):
                    t.matmul(PS[:, 0:128], oarr[:, 0:128], sb["c_i128f"][:, :], start=True, stop=True)
                    t.matmul(PS[:, 128:256], oarr[:, 128:256], sb["c_i128f"][:, :], start=True, stop=True)
            with nc.Block() as blk:
                @blk.vector
                def _(v):
                    v.tensor_copy(ofmt[:, 128 * gi:128 * (gi + 1)], PS[:, 0:128])
                    v.tensor_copy(ofmt[:, 512 + 128 * gi:512 + 128 * (gi + 1)], PS[:, 128:256])
        with nc.Block() as blk:
            @blk.sync
            def _(sy):
                sy.dma_start(out=oscr[0:128, tc], in_=ofmt[:, 0:512]).then_inc(st, 16)
                sy.dma_start(out=oscr[128:256, tc], in_=ofmt[:, 512:1024]).then_inc(st, 16)
                sy.wait_ge(st, ctx.bump("A_st", 32))

    _ps_stack.close()
    if do_collective:
        emit_stageA_epilogue(ctx, dst, TT)


def emit_stageA_epilogue(ctx, dst, TT):
    nc = ctx.nc
    sb = ctx.sb
    ld = ctx.sem("A_ld")
    st = ctx.sem("A_st")
    cc = ctx.sem("A_cc")
    P0 = sb["A_PSUMS"][0]
    agb = ctx.dram["agb"]
    with nc.Block() as blk:
        @blk.gpsimd
        def _(g):
            g.collective_compute(
                "AllGather", ALU.bypass,
                replica_groups=[[0, 1, 2, 3], [4, 5, 6, 7]],
                ins=[ctx.dram["oscr"].opt()], outs=[agb.opt()],
            ).then_inc(cc)
            g.wait_ge(cc, ctx.bump("A_cc", 1))
    woA = ctx.sbuf("A_wo", [128, 8 * 1024], BF16)
    og4 = ctx.sbuf("A_og4w", [128, 4 * 512], BF16)
    og = [ctx.sbuf(f"A_og{i}", [128, 512], BF16) for i in range(8)]
    winsel = ctx.sbuf("A_wsel", [128, 4], F32)
    hsres = ctx.sbuf("A_hsr", [128, 512], F32)
    yyA = [ctx.sbuf(f"A_yy{i}", [128, 512], F32) for i in range(8)]
    with nc.Block() as blk:
        @blk.gpsimd
        def _(g):
            g.dma_start(out=woA[:, :].rearrange("p (k c) -> p k c", k=8),
                        in_=ctx.dram["woA"].rearrange("(k p) c -> p k c", p=128)).then_inc(ld, 16)
        @blk.sync
        def _(sy):
            sy.dma_start(out=winsel[:, :], in_=ctx.dram["winsel"]).then_inc(ld, 16)
            sy.wait_ge(ld, ctx.bump("A_ld", 32))
    tpcn = TT // 4
    nt2 = tpcn // 512
    for n in range(nt2):
        for i in range(8):
            with nc.Block() as blk:
                @blk.sync
                def _(sy):
                    for q in range(4):
                        sy.dma_start(out=og4[:, 512 * q:512 * (q + 1)],
                                     in_=agb[128 * i:128 * (i + 1),
                                             tpcn * q + 512 * n:tpcn * q + 512 * (n + 1)]).then_inc(ld, 16)
                    sy.wait_ge(ld, ctx.bump("A_ld", 16 * 4))
            with nc.Block() as blk:
                @blk.vector
                def _(v):
                    v.tensor_scalar(og[i][:, :], og4[:, 0:512], winsel[:, 0:1], None, ALU.mult)
            for q in range(1, 4):
                with nc.Block() as blk:
                    @blk.vector
                    def _(v):
                        v.scalar_tensor_tensor(og[i][:, :], og4[:, 512 * q:512 * (q + 1)],
                                               winsel[:, q:q + 1], og[i][:, :], ALU.mult, ALU.add)
        for oc in range(8):
            with nc.Block() as blk:
                @blk.sync
                def _(sy):
                    sy.dma_start(out=hsres[:, :],
                                 in_=ctx.dram["hsres"][128 * oc:128 * (oc + 1),
                                                       512 * n:512 * (n + 1)]).then_inc(ld, 16)
                    sy.wait_ge(ld, ctx.bump("A_ld", 16))
                @blk.tensor
                def _(t):
                    for kc in range(8):
                        t.matmul(P0[:, :], woA[:, 1024 * kc + 128 * oc:1024 * kc + 128 * (oc + 1)],
                                 og[kc][:, :], start=(kc == 0), stop=(kc == 7))
            with nc.Block() as blk:
                @blk.vector
                def _(v):
                    v.tensor_add(yyA[oc][:, :], hsres[:, :], P0[:, :])
        emit_ln_fm(ctx, None, [yyA[i][:, :] for i in range(8)],
                   [yyA[i][:, :] for i in range(8)], 512)
        with nc.Block() as blk:
            @blk.sync
            def _(sy):
                for i in range(8):
                    sy.dma_start(out=ctx.dram[dst][128 * i:128 * (i + 1), 512 * n:512 * (n + 1)],
                                 in_=yyA[i][:, :]).then_inc(st, 16)
                sy.wait_ge(st, ctx.bump("A_st", 16 * 8))


def build_stageA_program(TT=T, debug=False, do_collective=True):
    nc = bass.Bass("TRN2", target_bir_lowering=False)
    stack = contextlib.ExitStack()
    ctx = Ctx(nc, stack)
    with stack:
        ctx.dram_in("hsT", [1024, TT], F32)
        ctx.dram_in("wA", [1024, 1024], F32)
        ctx.dram_in("wfg2", [128, 256], F32)
        ctx.dram_in("convw", [128, 24], F32)
        ctx.dram_in("convb", [128, 6], F32)
        ctx.dram_in("dtb2", [128, 2], F32)
        ctx.dram_in("alogc", [4, 1], F32)
        ctx.dram_in("bg2r", [1, 256], F32)
        ctx.dram_in("onw4", [1, 256], F32)
        ctx.dram_in("woA", [1024, 1024], F32)
        ctx.dram_in("hsres", [1024, TT // 4], F32)
        ctx.dram_in("winsel", [128, 4], F32)
        if debug:
            ctx.dram_out("oscr", [256, TT], BF16)
        else:
            ctx.dram_tmp("oscr", [256, TT], BF16)
        ctx.dram_tmp("agb", [1024, TT], BF16)
        ctx.dram_out("x0", [1024, TT // 4], F32)
        for name, arr in _const_arrays().items():
            ctx.dram_in(name, arr.shape, I32 if arr.dtype == np.int32 else F32)
        _preload_consts(ctx)
        emit_stageA(ctx, dst="x0", TT=TT, do_collective=do_collective)
    return nc


def prep_stageA_inputs(inp, core, TT=T):
    b, hq = core // 4, core % 4
    hcols = slice(256 * hq, 256 * (hq + 1))
    hs = inp["L0_attn_hidden_states"][b][:TT]
    wA = np.zeros((1024, 1024), np.float32)
    wA[:, 0:256] = inp["L0_attn_Wq"][:, hcols]
    wA[:, 256:512] = inp["L0_attn_Wk"][:, hcols]
    wA[:, 512:768] = inp["L0_attn_Wv"][:, hcols]
    wA[:, 768:832] = inp["L0_attn_Wf1"]
    wA[:, 832:896] = inp["L0_attn_Wg1"]
    wA[:, 896:900] = inp["L0_attn_Wb"][:, 4 * hq:4 * (hq + 1)]
    wfg2 = np.concatenate([inp["L0_attn_Wf2"][:, hcols], inp["L0_attn_Wg2"][:, hcols]], axis=0)
    cw = np.zeros((128, 24), np.float32)
    cb = np.zeros((128, 6), np.float32)
    for i in range(2):
        rows = slice(256 * hq + 128 * i, 256 * hq + 128 * (i + 1))
        cw[:, 12 * i + 0:12 * i + 4] = inp["L0_attn_qcw"][rows]
        cw[:, 12 * i + 4:12 * i + 8] = inp["L0_attn_kcw"][rows]
        cw[:, 12 * i + 8:12 * i + 12] = inp["L0_attn_vcw"][rows]
        cb[:, 3 * i + 0] = inp["L0_attn_qcb"][rows]
        cb[:, 3 * i + 1] = inp["L0_attn_kcb"][rows]
        cb[:, 3 * i + 2] = inp["L0_attn_vcb"][rows]
    win = TT // 4
    im = {
        "hsT": np.ascontiguousarray(hs.T),
        "wA": wA,
        "wfg2": np.ascontiguousarray(wfg2),
        "convw": cw,
        "convb": cb,
        "dtb2": np.ascontiguousarray(inp["L0_attn_dt_bias"][hcols].reshape(2, 128).T),
        "alogc": np.ascontiguousarray(inp["L0_attn_A_log"][4 * hq:4 * (hq + 1)][:, None]),
        "bg2r": np.ascontiguousarray(inp["L0_attn_bg2"][hcols][None, :]),
        "onw4": np.ascontiguousarray(np.tile(inp["L0_attn_o_norm_w"], 4)[None, :]),
        "woA": np.ascontiguousarray(inp["L0_attn_Wo"]),
        "hsres": np.ascontiguousarray(hs[win * hq:win * (hq + 1)].T),
        "winsel": np.ascontiguousarray(np.repeat((np.arange(4) == hq).astype(np.float32)[None, :], 128, 0)),
    }
    im.update(_const_arrays())
    return im


def _const_arrays():
    c = {}
    c["cd_ones128"] = np.ones((128, 1), np.float32)
    c["cd_ones1"] = np.ones((1, 128), np.float32)
    indqk = np.zeros((128, 128), np.float32)
    for i in range(8):
        indqk[0:64, 16 * i + 2 * i] = 1.0
        indqk[64:128, 16 * i + 2 * i + 1] = 1.0
    c["cd_indqk"] = indqk
    bc16 = np.zeros((16, 1024), np.float32)
    for i in range(8):
        bc16[2 * i, 128 * i:128 * i + 64] = 1.0
        bc16[2 * i + 1, 128 * i + 64:128 * (i + 1)] = 1.0
    c["cd_bc16"] = bc16
    hsel = np.zeros((16, 4), np.float32)
    for h in range(16):
        hsel[h, h % 4] = 1.0
    c["cd_hsel"] = hsel
    phm = np.zeros((16, 512), np.float32)
    t = np.arange(512)
    for h in range(16):
        phm[h, t % 4 == h % 4] = 1.0
    c["cd_phmask"] = phm
    c["cd_i128"] = np.eye(128, dtype=np.float32)
    non2 = np.zeros((128, 128), np.float32)
    non2[0, :] = -1.0
    non2[64, :] = -1.0
    c["cd_nones2"] = non2
    ind2 = np.zeros((128, 2), np.float32)
    ind2[0:64, 0] = 1.0
    ind2[64:128, 1] = 1.0
    c["cd_indh2"] = ind2
    c["cd_indh2T"] = ind2.T.copy()
    wm = np.zeros((128, 8 * 512), np.float32)
    kp = np.arange(128)[:, None]
    qfr = np.arange(512)[None, :]
    for kt in range(8):
        diff = qfr + 512 - 128 * kt - kp
        wm[:, 512 * kt:512 * (kt + 1)] = np.where((diff >= 0) & (diff <= 512), 0.0, -32768.0)
    c["cd_wmA"] = wm
    lo = np.zeros_like(wm)
    lo[64:128, :] = wm[0:64, :]
    c["cd_wmBlo"] = lo
    e32 = np.eye(32, dtype=np.float32)
    c["cd_i32g"] = np.tile(e32, (4, 2))
    c["cd_i32st"] = np.tile(e32, (4, 1))
    c["cd_i64st"] = np.tile(np.eye(64, dtype=np.float32), (2, 1))
    c["cd_ni64st"] = -np.tile(np.eye(64, dtype=np.float32), (2, 1))
    idx = np.arange(32)
    J, I = np.meshgrid(idx, idx, indexing="ij")
    b4 = idx // 4; b8 = idx // 8; b16 = idx // 16
    mD4s = ((b4[J] == b4[I]) & (J < I)).astype(np.float32)
    mD4i = ((b4[J] == b4[I]) & (J <= I)).astype(np.float32)
    mO4 = (b4[I] == b4[J] + 1).astype(np.float32)
    mO8 = ((b8[I] == b8[J] + 1) & (b4[I] >= b4[J] + 2)).astype(np.float32)
    mO16 = ((b16[I] == b16[J] + 1) & (b8[I] >= b8[J] + 2)).astype(np.float32)
    def tile42(m):
        return np.tile(m, (4, 2))
    c["cd_msel"] = np.concatenate([tile42(mD4s), tile42(mD4i), tile42(mO4),
                                   tile42(mO8), tile42(mO16)], axis=1).astype(np.int32)
    c["cd_msel2"] = np.concatenate([tile42(mD4s.T), tile42(mD4i.T), tile42(mO4.T),
                                    tile42(mO8.T), tile42(mO16.T)], axis=1).astype(np.int32)
    rst = np.ones((128, 512), np.float32)
    rst[:, ::32] = 0.0
    c["cd_reset"] = rst
    bc4 = np.zeros((4, 256), np.float32)
    for t_ in range(2):
        for p_ in range(128):
            bc4[2 * t_ + p_ // 64, 128 * t_ + p_] = 1.0
    c["cd_bc4"] = bc4
    c["cd_nbc4"] = -bc4
    return c


def _preload_consts(ctx):
    nc = ctx.nc
    ld = ctx.sem("c_ld")
    co = ctx.sbuf("c_ones128", [128, 1], F32)
    c1 = ctx.sbuf("c_ones1", [1, 128], F32)
    i128f = ctx.sbuf("c_i128f", [128, 128], F32)
    i128b = ctx.sbuf("c_i128b", [128, 128], BF16)
    nones2 = ctx.sbuf("c_nones2", [128, 128], BF16)
    onesb = ctx.sbuf("c_onesb", [128, 1], BF16)
    indh2 = ctx.sbuf("c_indh2", [128, 2], F32)
    indh2T = ctx.sbuf("c_indh2T", [2, 128], F32)
    with nc.Block() as blk:
        @blk.sync
        def _(sy):
            sy.dma_start(out=co[:, :], in_=ctx.dram["cd_ones128"]).then_inc(ld, 16)
            sy.dma_start(out=c1[:, :], in_=ctx.dram["cd_ones1"]).then_inc(ld, 16)
            sy.dma_start(out=i128f[:, :], in_=ctx.dram["cd_i128"]).then_inc(ld, 16)
            sy.dma_start(out=indh2[:, :], in_=ctx.dram["cd_indh2"]).then_inc(ld, 16)
            sy.dma_start(out=indh2T[:, :], in_=ctx.dram["cd_indh2T"]).then_inc(ld, 16)
        @blk.gpsimd
        def _(g):
            g.dma_start(out=i128b[:, :], in_=ctx.dram["cd_i128"]).then_inc(ld, 16)
            g.dma_start(out=nones2[:, :], in_=ctx.dram["cd_nones2"]).then_inc(ld, 16)
            g.dma_start(out=onesb[:, :], in_=ctx.dram["cd_ones128"]).then_inc(ld, 16)
            g.wait_ge(ld, ctx.bump("c_ld", 16 * 8))


def build_stageB_program():
    nc = bass.Bass("TRN2", target_bir_lowering=False)
    stack = contextlib.ExitStack()
    ctx = Ctx(nc, stack)
    with stack:
        ctx.dram_in("x0", [1024, TPC], F32)
        ctx.dram_in("wgT", [1024, 3072], I32)
        ctx.dram_in("wuT", [1024, 3072], I32)
        ctx.dram_in("wdT", [3072, 1024], I32)
        ctx.dram_out("x1", [1024, TPC], F32)
        for name, arr in _const_arrays().items():
            ctx.dram_in(name, arr.shape, I32 if arr.dtype == np.int32 else F32)
        _preload_consts(ctx)
        emit_stageB(ctx)
    return nc


def _emit_stageD_masks(ctx):
    nc = ctx.nc
    ctx.sbuf("D_wmA", [128, 8 * 512], BF16)
    ctx.sbuf("D_wmBlo", [128, 8 * 512], BF16)
    ctx.sbuf("D_pmA", [128, 4 * 512], BF16)
    ctx.sbuf("D_pmBlo", [128, 4 * 512], BF16)
    ctx.sbuf("D_hmask", [128, 1], F32)
    ld = ctx.sem("D_ld")
    with nc.Block() as blk:
        @blk.gpsimd
        def _(g):
            for nm in ("wmA", "wmBlo"):
                g.dma_start(out=ctx.sb[f"D_{nm}"][:, :], in_=ctx.dram[f"cd_{nm}"]).then_inc(ld, 16)
            for nm in ("pmA", "pmBlo"):
                g.dma_start(out=ctx.sb[f"D_{nm}"][:, :], in_=ctx.dram[nm]).then_inc(ld, 16)
        @blk.sync
        def _(sy):
            sy.dma_start(out=ctx.sb["D_hmask"][:, :], in_=ctx.dram["hmask"]).then_inc(ld, 16)
            sy.wait_ge(ld, ctx.bump("D_ld", 16 * 5))


def build_stageDa_program(l):
    nc = bass.Bass("TRN2", target_bir_lowering=False)
    stack = contextlib.ExitStack()
    ctx = Ctx(nc, stack)
    with stack:
        ctx.dram_in("xsrc", [1024, TPC], F32)
        ctx.dram_in("hl", [1024, HB], F32)
        ctx.dram_in("hmask", [128, 1], F32)
        ctx.dram_in("pmA", [128, 4 * 512], F32)
        ctx.dram_in("pmBlo", [128, 4 * 512], F32)
        ctx.dram_in(f"wqkvD{l}", [1024, 1536], F32)
        ctx.dram_in(f"woD{l}", [512, 1024], F32)
        ctx.dram_in(f"rotq{l}", [128, 128], F32)
        ctx.dram_in(f"rotk{l}", [128, 128], F32)
        for nm in ("csq", "ssq", "csk", "ssk"):
            ctx.dram_in(f"{nm}{l}", [128, NQK], F32)
        ctx.dram_out("xm", [1024, TPC], F32)
        for name, arr in _const_arrays().items():
            ctx.dram_in(name, arr.shape, I32 if arr.dtype == np.int32 else F32)
        _preload_consts(ctx)
        for nm in ("D_P0", "D_P1", "D_P2", "D_P3"):
            ctx.sb[nm] = ctx.psum(nm, [128, 512], F32)
        _emit_stageD_masks(ctx)
        emit_stageD_attn(ctx, l, "xsrc", "hl", "xm")
    return nc


def build_stageDf_program(l, final=False):
    nc = bass.Bass("TRN2", target_bir_lowering=False)
    stack = contextlib.ExitStack()
    ctx = Ctx(nc, stack)
    with stack:
        ctx.dram_in("xm", [1024, TPC], F32)
        ctx.dram_in("xmprevD", [128, 8], F32)
        ctx.dram_in("hmask", [128, 1], F32)
        ctx.dram_in(f"w1D{l}", [1024, 5632], F32)
        ctx.dram_in(f"w2D{l}", [2816, 1024], F32)
        ctx.dram_in(f"b1D{l}", [128, 44], F32)
        ctx.dram_in(f"b2D{l}", [128, 8], F32)
        ctx.dram_in(f"lngD{l}", [128, 8], F32)
        ctx.dram_in(f"lnbD{l}", [128, 8], F32)
        if final:
            ctx.dram_in("x2in", [1024, TPC], F32)
            ctx.dram_tmp("x3o", [1024, TPC], F32)
        ctx.dram_out("xout", [1024, TPC], F32)
        for name, arr in _const_arrays().items():
            ctx.dram_in(name, arr.shape, I32 if arr.dtype == np.int32 else F32)
        _preload_consts(ctx)
        ctx.sb["D_P0"] = ctx.psum("D_P0", [128, 512], F32)
        ctx.sbuf("D_hmask", [128, 1], F32)
        ld = ctx.sem("D_ld")
        with nc.Block() as blk:
            @blk.sync
            def _(sy):
                sy.dma_start(out=ctx.sb["D_hmask"][:, :], in_=ctx.dram["hmask"]).then_inc(ld, 16)
                sy.wait_ge(ld, ctx.bump("D_ld", 16))
        if final:
            emit_stageD_ffn(ctx, l, "xmprevD", "xm", "x3o")
            emit_final_ln(ctx, "x2in", "x3o", "xout")
        else:
            emit_stageD_ffn(ctx, l, "xmprevD", "xm", "xout")
    return nc


def _prep_B(inputs):
    return {
        "wgT": np.ascontiguousarray(inputs["L0_ffn_w_gate"].T),
        "wuT": np.ascontiguousarray(inputs["L0_ffn_w_up"].T),
        "wdT": np.ascontiguousarray(inputs["L0_ffn_w_down"].T),
    }


def _prep_C(inputs):
    return {
        "wC": np.ascontiguousarray(np.concatenate(
            [inputs["L1_attn_Wq"], inputs["L1_attn_Wk"], inputs["L1_attn_Wv"]], axis=1)),
        "bC": np.ascontiguousarray(np.concatenate(
            [inputs["L1_attn_bq"], inputs["L1_attn_bk"], inputs["L1_attn_bv"]]).reshape(24, 128).T),
        "woC": np.ascontiguousarray(inputs["L1_attn_Wo"]),
        "boC": np.ascontiguousarray(inputs["L1_attn_bo"].reshape(8, 128).T),
    }


def _prep_D_shared(inputs, l):
    w1 = inputs["L1_ffn_w1"][l]; b1 = inputs["L1_ffn_b1"][l]
    w2 = inputs["L1_ffn_w2"][l]; b2 = inputs["L1_ffn_b2"][l]
    w1a = np.zeros((1024, 2816), np.float32); w1g = np.zeros((1024, 2816), np.float32)
    w1a[:, 0:2730] = w1[:, 0:2730]; w1g[:, 0:2730] = w1[:, 2730:5460]
    b1a = np.zeros(2816, np.float32); b1g = np.zeros(2816, np.float32)
    b1a[0:2730] = b1[0:2730]; b1g[0:2730] = b1[2730:5460]
    w1p = np.zeros((1024, 5632), np.float32)
    b1p = np.zeros(5632, np.float32)
    for i in range(22):
        w1p[:, 256 * i:256 * i + 128] = w1a[:, 128 * i:128 * (i + 1)]
        w1p[:, 256 * i + 128:256 * (i + 1)] = w1g[:, 128 * i:128 * (i + 1)]
        b1p[256 * i:256 * i + 128] = b1a[128 * i:128 * (i + 1)]
        b1p[256 * i + 128:256 * (i + 1)] = b1g[128 * i:128 * (i + 1)]
    w2p = np.zeros((2816, 1024), np.float32)
    w2p[0:2730] = w2
    return {
        f"wqkvD{l}": np.ascontiguousarray(inputs["L1_ffn_wqkv"][l]),
        f"woD{l}": np.ascontiguousarray(inputs["L1_ffn_wo"][l]),
        f"w1D{l}": w1p,
        f"w2D{l}": w2p,
        f"b1D{l}": np.ascontiguousarray(b1p.reshape(44, 128).T),
        f"b2D{l}": np.ascontiguousarray(b2.reshape(8, 128).T),
        f"lngD{l}": np.ascontiguousarray(inputs["L1_ffn_ln_g"][l].reshape(8, 128).T),
        f"lnbD{l}": np.ascontiguousarray(inputs["L1_ffn_ln_b"][l].reshape(8, 128).T),
    }


def _prep_D_core(inputs, l, core):
    q = core % 4
    t0 = TPC * q
    DIM_HEAD = 64; WINDOW = 512
    tpos = np.arange(t0 - 512, t0 + TPC, dtype=np.float64)
    half_idx = np.arange(0, DIM_HEAD, 2, dtype=np.float64)
    inv_freq = 1.0 / 10000.0 ** (half_idx / DIM_HEAD)
    freqs = np.concatenate([tpos[:, None] * inv_freq[None, :]] * 2, axis=-1)
    base = (half_idx + 0.4 * DIM_HEAD) / (1.4 * DIM_HEAD)
    power = (tpos - T // 2) / float(WINDOW // 2)
    scale = np.concatenate([base[None, :] ** power[:, None]] * 2, axis=-1)
    cos, sin = np.cos(freqs), np.sin(freqs)
    qs = inputs["L1_ffn_q_scale"][l]; ks = inputs["L1_ffn_k_scale"][l]
    csq = (cos * scale * 8.0).T * qs[:, None]
    ssq = (sin * scale * 8.0).T
    csk = (cos / scale).T * ks[:, None]
    ssk = (sin / scale).T
    R = np.zeros((64, 64), np.float32)
    for d_ in range(32):
        R[d_, d_ + 32] = -1.0
        R[d_ + 32, d_] = 1.0
    def bd(m):
        z = np.zeros((128, 128), np.float32)
        z[0:64, 0:64] = m.T
        z[64:128, 64:128] = m.T
        return z
    return {
        f"rotq{l}": bd(R * qs[None, :]),
        f"rotk{l}": bd(R * ks[None, :]),
        f"csq{l}": np.ascontiguousarray(np.tile(csq, (2, 1)).astype(np.float32)),
        f"ssq{l}": np.ascontiguousarray(np.tile(ssq, (2, 1)).astype(np.float32)),
        f"csk{l}": np.ascontiguousarray(np.tile(csk, (2, 1)).astype(np.float32)),
        f"ssk{l}": np.ascontiguousarray(np.tile(ssk, (2, 1)).astype(np.float32)),
    }


def _halo_of(xfull_fm, core):
    q = core % 4
    t0 = TPC * q
    h = np.zeros((1024, HB), np.float32)
    if q > 0:
        h[:] = xfull_fm[:, t0 - HB:t0]
    return h


def _xmprev_of(xm_fm_prevcore, core):
    v = np.zeros((128, 8), np.float32)
    if core % 4 > 0:
        v[:] = xm_fm_prevcore[:, -1].reshape(8, 128).T
    return v


def _pads_of(core):
    pm = np.zeros((128, 4 * 512), np.float32)
    if core % 4 == 0:
        pm[:] = -32768.0
    pmlo = np.zeros_like(pm)
    pmlo[64:128] = pm[0:64]
    return pm, pmlo


def build_fused_program():
    nc = bass.Bass("TRN2", target_bir_lowering=False)
    stack = contextlib.ExitStack()
    ctx = Ctx(nc, stack)
    with stack:
        ctx.dram_in("hsT", [1024, T], F32)
        ctx.dram_in("wA", [1024, 1024], F32)
        ctx.dram_in("wfg2", [128, 256], F32)
        ctx.dram_in("convw", [128, 24], F32)
        ctx.dram_in("convb", [128, 6], F32)
        ctx.dram_in("dtb2", [128, 2], F32)
        ctx.dram_in("alogc", [4, 1], F32)
        ctx.dram_in("bg2r", [1, 256], F32)
        ctx.dram_in("onw4", [1, 256], F32)
        ctx.dram_in("woA", [1024, 1024], F32)
        ctx.dram_in("hsres", [1024, TPC], F32)
        ctx.dram_in("winsel", [128, 4], F32)
        ctx.dram_in("hselD", [128, 4], F32)
        ctx.dram_in("hmask", [128, 1], F32)
        ctx.dram_in("pmA", [128, 4 * 512], F32)
        ctx.dram_in("pmBlo", [128, 4 * 512], F32)
        ctx.dram_in("wgT", [1024, 3072], I32)
        ctx.dram_in("wuT", [1024, 3072], I32)
        ctx.dram_in("wdT", [3072, 1024], I32)
        ctx.dram_in("wC", [1024, 3072], F32)
        ctx.dram_in("bC", [128, 24], F32)
        ctx.dram_in("woC", [1024, 1024], F32)
        ctx.dram_in("boC", [128, 8], F32)
        for l in range(2):
            ctx.dram_in(f"wqkvD{l}", [1024, 1536], F32)
            ctx.dram_in(f"woD{l}", [512, 1024], F32)
            ctx.dram_in(f"w1D{l}", [1024, 5632], F32)
            ctx.dram_in(f"w2D{l}", [2816, 1024], F32)
            ctx.dram_in(f"b1D{l}", [128, 44], F32)
            ctx.dram_in(f"b2D{l}", [128, 8], F32)
            ctx.dram_in(f"lngD{l}", [128, 8], F32)
            ctx.dram_in(f"lnbD{l}", [128, 8], F32)
            ctx.dram_in(f"rotq{l}", [128, 128], F32)
            ctx.dram_in(f"rotk{l}", [128, 128], F32)
            for nm in ("csq", "ssq", "csk", "ssk"):
                ctx.dram_in(f"{nm}{l}", [128, NQK], F32)
        for name, arr in _const_arrays().items():
            ctx.dram_in(name, arr.shape, I32 if arr.dtype == np.int32 else F32)
        ctx.dram_tmp("oscr", [256, T], BF16)
        ctx.dram_tmp("agb", [1024, T], BF16)
        for nm in ("x0s", "x1s", "x2s", "xmscr", "x3s", "x4s"):
            ctx.dram_tmp(nm, [1024, TPC], F32)
        ctx.dram_tmp("h2pub", [1024, HB], F32)
        ctx.dram_tmp("h2ag", [4096, HB], F32)
        ctx.dram_tmp("h3pub", [1024, HB], F32)
        ctx.dram_tmp("h3ag", [4096, HB], F32)
        ctx.dram_tmp("xmpub0", [128, 8], F32)
        ctx.dram_tmp("xmag0", [512, 8], F32)
        ctx.dram_tmp("xmpub1", [128, 8], F32)
        ctx.dram_tmp("xmag1", [512, 8], F32)
        ctx.dram_out("out", [1024, TPC], F32)

        _preload_consts(ctx)
        ccs = ctx.sem("F_cc")
        ldf = ctx.sem("F_ld")

        with contextlib.ExitStack() as stA:
            emit_stageA(ctx.sub(stA), dst="x0s", TT=T, do_collective=True)
        with contextlib.ExitStack() as stB:
            emit_stageB(ctx.sub(stB), src="x0s", dst="x1s")
        with contextlib.ExitStack() as stC:
            emit_stageC(ctx.sub(stC), src="x1s", dst="x2s")
        with contextlib.ExitStack() as stD:
            cD = ctx.sub(stD)
            for nm in ("D_P0", "D_P1", "D_P2", "D_P3"):
                cD.sb[nm] = cD.psum(nm, [128, 512], F32)
            _emit_stageD_masks(cD)
            cD.sbuf("D_hsel", [128, 4], F32)
            cD.sbuf("D_xlast", [128, 8], F32)
            xmprev_sb = cD.sbuf("D_xmprev", [128, 8], F32)
            xmcand = cD.sbuf("D_xmcand", [128, 32], F32)
            with nc.Block() as blk:
                @blk.sync
                def _(sy):
                    sy.dma_start(out=cD.sb["D_hsel"][:, :], in_=ctx.dram["hselD"]).then_inc(ldf, 16)
                    sy.wait_ge(ldf, ctx.bump("F_ld", 16))
            for l in range(2):
                src = "x2s" if l == 0 else "x3s"
                hpub, hag = (f"h2pub", "h2ag") if l == 0 else ("h3pub", "h3ag")
                xmp, xma = (f"xmpub{l}", f"xmag{l}")
                with nc.Block() as blk:
                    @blk.sync
                    def _(sy):
                        sy.dma_start(out=ctx.dram[hpub], in_=ctx.dram[src][:, TPC - HB:TPC]).then_inc(ldf, 16)
                        sy.wait_ge(ldf, ctx.bump("F_ld", 16))
                with nc.Block() as blk:
                    @blk.gpsimd
                    def _(g):
                        g.collective_compute(
                            "AllGather", ALU.bypass,
                            replica_groups=[[0, 1, 2, 3], [4, 5, 6, 7]],
                            ins=[ctx.dram[hpub].opt()], outs=[ctx.dram[hag].opt()],
                        ).then_inc(ccs)
                        g.wait_ge(ccs, ctx.bump("F_cc", 1))
                with contextlib.ExitStack() as stDa:
                    emit_stageD_attn(cD.sub(stDa), l, src, None, "xmscr",
                                     halo_ag=hag, xmpub=xmp)
                with nc.Block() as blk:
                    @blk.gpsimd
                    def _(g):
                        g.collective_compute(
                            "AllGather", ALU.bypass,
                            replica_groups=[[0, 1, 2, 3], [4, 5, 6, 7]],
                            ins=[ctx.dram[xmp].opt()], outs=[ctx.dram[xma].opt()],
                        ).then_inc(ccs)
                        g.wait_ge(ccs, ctx.bump("F_cc", 1))
                with nc.Block() as blk:
                    @blk.sync
                    def _(sy):
                        for q in range(4):
                            sy.dma_start(out=xmcand[:, 8 * q:8 * (q + 1)],
                                         in_=ctx.dram[xma][128 * q:128 * (q + 1), :]).then_inc(ldf, 16)
                        sy.wait_ge(ldf, ctx.bump("F_ld", 16 * 4))
                with nc.Block() as blk:
                    @blk.vector
                    def _(v):
                        v.tensor_scalar(xmprev_sb[:, :], xmcand[:, 0:8], cD.sb["D_hsel"][:, 0:1], None, ALU.mult)
                for q in range(1, 4):
                    with nc.Block() as blk:
                        @blk.vector
                        def _(v):
                            v.scalar_tensor_tensor(xmprev_sb[:, :], xmcand[:, 8 * q:8 * (q + 1)],
                                                   cD.sb["D_hsel"][:, q:q + 1], xmprev_sb[:, :],
                                                   ALU.mult, ALU.add)
                with contextlib.ExitStack() as stDf:
                    emit_stageD_ffn(cD.sub(stDf), l, None, "xmscr",
                                    "x3s" if l == 0 else "x4s", xmprev_sb=xmprev_sb)
        with contextlib.ExitStack() as stF:
            emit_final_ln(ctx.sub(stF), "x2s", "x4s", "out")
    return nc


def prep_fused_inputs(inputs, core):
    im = prep_stageA_inputs(inputs, core)
    im.update(_prep_B(inputs))
    im.update(_prep_C(inputs))
    q = core % 4
    im["hmask"] = np.full((128, 1), 0.0 if q == 0 else 1.0, np.float32)
    hsel = np.zeros((128, 4), np.float32)
    if q > 0:
        hsel[:, q - 1] = 1.0
    im["hselD"] = hsel
    pm, pmlo = _pads_of(core)
    im["pmA"] = pm
    im["pmBlo"] = pmlo
    for l in range(2):
        im.update(_prep_D_shared(inputs, l))
        im.update(_prep_D_core(inputs, l, core))
    return im


def kernel(**inputs):
    cores = list(range(8))
    nc = build_fused_program()
    in_maps = [prep_fused_inputs(inputs, c) for c in cores]
    res = run_bass_kernel_spmd(nc, in_maps, core_ids=cores)
    outa = np.empty((2, T, D), np.float32)
    for c in cores:
        b, q = c // 4, c % 4
        outa[b, TPC * q:TPC * (q + 1)] = np.asarray(res.results[c]["out"]).T
    return outa


def kernel_multi(**inputs):
    consts = _const_arrays()
    cores = list(range(8))

    ncA = build_stageA_program()
    imsA = [prep_stageA_inputs(inputs, c) for c in cores]
    resA = run_bass_kernel_spmd(ncA, imsA, core_ids=cores)
    x0 = [np.asarray(resA.results[c]["x0"]) for c in cores]

    ncB = build_stageB_program()
    wB = _prep_B(inputs)
    imsB = [{**consts, **wB, "x0": x0[c]} for c in cores]
    resB = run_bass_kernel_spmd(ncB, imsB, core_ids=cores)
    x1 = [np.asarray(resB.results[c]["x1"]) for c in cores]

    ncC = build_stageC_program()
    wC = _prep_C(inputs)
    imsC = [{**consts, **wC, "x1": x1[c]} for c in cores]
    resC = run_bass_kernel_spmd(ncC, imsC, core_ids=cores)
    x2 = [np.asarray(resC.results[c]["x2"]) for c in cores]
    x2full = [np.concatenate(x2[0:4], axis=1), np.concatenate(x2[4:8], axis=1)]

    out = {}
    xcur = x2
    xcurfull = x2full
    for l in range(2):
        ncDa = build_stageDa_program(l)
        wDl = _prep_D_shared(inputs, l)
        ims = []
        for c in cores:
            pm, pmlo = _pads_of(c)
            ims.append({**consts,
                        "xsrc": xcur[c],
                        "hl": _halo_of(xcurfull[c // 4], c),
                        "hmask": np.full((128, 1), 0.0 if c % 4 == 0 else 1.0, np.float32),
                        "pmA": pm, "pmBlo": pmlo,
                        f"wqkvD{l}": wDl[f"wqkvD{l}"], f"woD{l}": wDl[f"woD{l}"],
                        **_prep_D_core(inputs, l, c)})
        resDa = run_bass_kernel_spmd(ncDa, ims, core_ids=cores)
        xm = [np.asarray(resDa.results[c]["xm"]) for c in cores]

        final = (l == 1)
        ncDf = build_stageDf_program(l, final=final)
        ims = []
        for c in cores:
            d = {**consts,
                 "xm": xm[c],
                 "xmprevD": _xmprev_of(xm[c - 1] if c % 4 > 0 else None, c),
                 "hmask": np.full((128, 1), 0.0 if c % 4 == 0 else 1.0, np.float32)}
            for k_ in (f"w1D{l}", f"w2D{l}", f"b1D{l}", f"b2D{l}", f"lngD{l}", f"lnbD{l}"):
                d[k_] = wDl[k_]
            if final:
                d["x2in"] = x2[c]
            ims.append(d)
        resDf = run_bass_kernel_spmd(ncDf, ims, core_ids=cores)
        xn = [np.asarray(resDf.results[c]["xout"]) for c in cores]
        xcur = xn
        xcurfull = [np.concatenate(xn[0:4], axis=1), np.concatenate(xn[4:8], axis=1)]

    outa = np.empty((2, T, D), np.float32)
    for c in cores:
        b, q = c // 4, c % 4
        outa[b, TPC * q:TPC * (q + 1)] = xcur[c].T
    return outa


# version 03
# speedup vs baseline: 1.1992x; 1.1992x over previous
import math
import contextlib
import numpy as np
import concourse.bass as bass
import concourse.mybir as mybir
from concourse.bass_utils import run_bass_kernel_spmd

SOFT_N = 10**9
STEPLOG = []
F32 = mybir.dt.float32
BF16 = mybir.dt.bfloat16
I32 = mybir.dt.int32
AF = mybir.ActivationFunctionType
ALU = mybir.AluOpType
AX = mybir.AxisListType

B, T, D = 2, 8192, 1024
_LN_UID = 0
NCORE = 8
TPC = T // 4
NT = 512
H, HK, HV = 16, 64, 64
HPC = 4


class _MockInst:
    __slots__ = ("ctx", "is_dma", "upds")

    def __init__(self, ctx, is_dma):
        self.ctx = ctx
        self.is_dma = is_dma
        self.upds = []

    def then_inc(self, sem, value=1, **k):
        ab = self.ctx._mockabs
        key = id(sem)
        ab[key] = ab.get(key, 0) + value
        self.upds.append((sem, ab[key]))
        return self


class _MockEng:
    def __init__(self, ctx):
        self.ctx = ctx
        self.last_real = None

    def __getattr__(self, nm):
        def f(*a, **k):
            mi = _MockInst(self.ctx, "dma" in nm)
            if not mi.is_dma:
                self.last_real = mi
            return mi
        return f


class _EngProxy:
    def __init__(self, eng):
        self._eng = eng
        self.last_real = None

    def __getattr__(self, nm):
        a = getattr(self._eng, nm)
        if callable(a):
            def wrap(*args, **kw):
                r = a(*args, **kw)
                if r is not None and hasattr(r, "then_inc") and "dma" not in nm:
                    self.last_real = r
                return r
            return wrap
        return a


class _RecBlock:
    ENGS = ("tensor", "vector", "scalar", "gpsimd", "sync")

    def __init__(self, ctx):
        self.ctx = ctx
        self.bodies = {}

    def _add(self, nm, f):
        assert nm not in self.bodies, f"two {nm} bodies in one block"
        import types as _t
        if f.__closure__:
            f = _t.FunctionType(f.__code__, f.__globals__, f.__name__,
                                f.__defaults__,
                                tuple(_t.CellType(c.cell_contents) for c in f.__closure__))
        self.bodies[nm] = f

    def tensor(self, f):
        self._add("tensor", f)

    def vector(self, f):
        self._add("vector", f)

    def scalar(self, f):
        self._add("scalar", f)

    def gpsimd(self, f):
        self._add("gpsimd", f)

    def sync(self, f):
        self._add("sync", f)

    def __enter__(self):
        return self

    def __exit__(self, et, ev, tb):
        if et is None:
            self.ctx.blockbuf.append(self.bodies)
            STEPLOG.append({nm: f.__code__.co_firstlineno for nm, f in self.bodies.items()})
        return False


class Ctx:
    def __init__(self, nc, stack):
        self.nc = nc
        self.stack = stack
        self.root_stack = stack
        self.dram = {}
        self.sb = {}
        self.sems = {}
        self.semct = {}
        self.blockbuf = []
        self._mockabs = {}
        self.stepstate = {"base": 0, "waited": {}, "lastw": {}}

    def block(self):
        return _RecBlock(self)

    def flush(self):
        allsteps = list(self.blockbuf)
        del self.blockbuf[:]
        if not allsteps:
            return
        nc = self.nc
        g0 = self.stepstate.setdefault("gctr", 0)
        nsoft = max(0, min(len(allsteps), SOFT_N - g0))
        self.stepstate["gctr"] = g0 + len(allsteps)
        steps = allsteps[:nsoft]
        hard = allsteps[nsoft:]
        if not steps:
            for hb in hard:
                self._emit_hard(hb)
            return
        stepsem = self.sem("STEPSEM")
        st = self.stepstate
        parts = [set(b.keys()) for b in steps]
        cum = [0]
        for p in parts:
            cum.append(cum[-1] + len(p))
        bumplogs = {}
        realinfo = {}
        for k, b in enumerate(steps):
            for nm in _RecBlock.ENGS:
                if nm in b:
                    st0 = self.stepstate
                    st0["_bumplog"] = []
                    mock = _MockEng(self)
                    b[nm](mock)
                    bumplogs[(k, nm)] = st0["_bumplog"]
                    lr = mock.last_real
                    realinfo[(k, nm)] = (lr is not None,
                                         lr.upds[-1] if lr is not None and lr.upds else None)
        self.stepstate["_bumplog"] = None
        with nc.Block() as blk:
            for nm in _RecBlock.ENGS:
                ks = [k for k, b in enumerate(steps) if nm in b]
                if not ks:
                    continue
                def run(eng, nm=nm, ks=ks):
                    px = _EngProxy(eng)
                    for k in ks:
                        need = st["base"] + cum[k]
                        if need > st["waited"].get(nm, 0):
                            eng.wait_ge(stepsem, need)
                            st["waited"][nm] = need
                        px.last_real = None
                        st["_bumpreplay"] = list(bumplogs[(k, nm)])
                        steps[k][nm](px)
                        assert not st["_bumpreplay"], "bump replay mismatch"
                        st["_bumpreplay"] = None
                        has_real, upd = realinfo[(k, nm)]
                        assert (px.last_real is not None) == has_real, (k, nm)
                        if px.last_real is None:
                            eng.wait_ge(stepsem, 0).then_inc(stepsem, 1)
                        elif upd is None:
                            px.last_real.then_inc(stepsem, 1)
                        else:
                            eng.wait_ge(upd[0], upd[1]).then_inc(stepsem, 1)
                getattr(blk, nm)(run)
        st["base"] += cum[-1]
        st["waited"] = {}
        for hb in hard:
            self._emit_hard(hb)

    def _emit_hard(self, bodies):
        nc = self.nc
        with nc.Block() as blk:
            for nm in _RecBlock.ENGS:
                if nm in bodies:
                    def run(eng, nm=nm):
                        bodies[nm](_EngProxy(eng))
                    getattr(blk, nm)(run)

    def dram_in(self, name, shape, dtype=F32):
        t = self.nc.dram_tensor(name, list(shape), dtype, kind="ExternalInput")
        self.dram[name] = t.ap()
        return self.dram[name]

    def dram_out(self, name, shape, dtype=F32):
        t = self.nc.dram_tensor(name, list(shape), dtype, kind="ExternalOutput")
        self.dram[name] = t.ap()
        return self.dram[name]

    def dram_tmp(self, name, shape, dtype=F32):
        t = self.nc.dram_tensor(name, list(shape), dtype)
        self.dram[name] = t.ap()
        return self.dram[name]

    def sbuf(self, name, shape, dtype=F32, side=None):
        h = self.stack.enter_context(self.nc.sbuf_tensor(name, list(shape), dtype, side=side))
        self.sb[name] = h
        return h

    def psum(self, name, shape, dtype=F32):
        h = self.stack.enter_context(self.nc.psum_tensor(name, list(shape), dtype))
        self.sb[name] = h
        return h

    def sem(self, name):
        if name not in self.sems:
            self.sems[name] = self.root_stack.enter_context(self.nc.semaphore(name))
            self.semct[name] = 0
        return self.sems[name]

    def sub(self, stack):
        c = Ctx.__new__(Ctx)
        c.__dict__.update(self.__dict__)
        c.stack = stack
        c.sb = {k: v for k, v in self.sb.items() if not k.startswith("ln_")}
        return c

    def bump(self, name, n):
        rp = self.stepstate.get("_bumpreplay")
        if rp is not None:
            return rp.pop(0)
        self.semct[name] += n
        v = self.semct[name]
        log = self.stepstate.get("_bumplog")
        if log is not None:
            log.append(v)
        return v

    def cur(self, name):
        return self.bump(name, 0)


def emit_ln_fm(ctx, blkfac, xtiles, outtiles, ncols, gain_col=None, bias_col=None,
               scratch_pref="ln", eps=1e-5, add_from=None):
    nc = ctx.nc
    global _LN_UID
    if "ln_sq" not in ctx.sb:
        _LN_UID += 1
        u = _LN_UID
        ctx.sb["ln_sq"] = ctx.stack.enter_context(nc.sbuf_tensor(f"ln_sq{u}", [128, NT], F32))
        ctx.sb["ln_rows"] = ctx.stack.enter_context(nc.sbuf_tensor(f"ln_rows{u}", [1, 4 * NT], F32))
        ctx.sb["ln_bc"] = ctx.stack.enter_context(nc.sbuf_tensor(f"ln_bc{u}", [128, 2 * NT], F32))
        ctx.sb["ln_pstat"] = ctx.stack.enter_context(nc.psum_tensor(f"ln_pstat{u}", [1, 1024], F32))
        ctx.sb["ln_pbc"] = ctx.stack.enter_context(nc.psum_tensor(f"ln_pbc{u}", [128, 1024], F32))
    sq = ctx.sb["ln_sq"]
    rows = ctx.sb["ln_rows"]
    bc = ctx.sb["ln_bc"]
    pstat = ctx.sb["ln_pstat"]
    assert ncols <= NT

    if add_from is not None:
        with ctx.block() as blk:
            @blk.vector
            def _(v):
                for i in range(8):
                    v.tensor_add(xtiles[i], xtiles[i], add_from[i])

    with ctx.block() as blk:
        @blk.tensor
        def _(t):
            for i in range(8):
                t.matmul(pstat[0:1, 0:ncols], ctx.sb["c_ones128"][:, 0:1], xtiles[i],
                         start=(i == 0), stop=(i == 7))
    for i in range(8):
        with ctx.block() as blk:
            @blk.scalar
            def _(s):
                s.activation(sq[:, 0:ncols], xtiles[i], AF.Square)
        with ctx.block() as blk:
            @blk.tensor
            def _(t):
                t.matmul(pstat[0:1, 512:512 + ncols], ctx.sb["c_ones128"][:, 0:1],
                         sq[:, 0:ncols], start=(i == 0), stop=(i == 7))
    with ctx.block() as blk:
        @blk.vector
        def _(v):
            v.tensor_scalar_mul(rows[0:1, 0:ncols], pstat[0:1, 0:ncols], 1.0 / D)
            v.tensor_scalar_mul(rows[0:1, NT:NT + ncols], pstat[0:1, 512:512 + ncols], 1.0 / D)
        @blk.scalar
        def _(s):
            pass
    with ctx.block() as blk:
        @blk.scalar
        def _(s):
            s.activation(rows[0:1, 2 * NT:2 * NT + ncols], rows[0:1, 0:ncols], AF.Square)
    with ctx.block() as blk:
        @blk.vector
        def _(v):
            v.tensor_sub(rows[0:1, NT:NT + ncols], rows[0:1, NT:NT + ncols],
                         rows[0:1, 2 * NT:2 * NT + ncols])
            v.tensor_scalar_add(rows[0:1, NT:NT + ncols], rows[0:1, NT:NT + ncols], eps)
    with ctx.block() as blk:
        @blk.scalar
        def _(s):
            s.activation(rows[0:1, 2 * NT:2 * NT + ncols], rows[0:1, NT:NT + ncols], AF.Sqrt)
    with ctx.block() as blk:
        @blk.vector
        def _(v):
            v.reciprocal(rows[0:1, 3 * NT:3 * NT + ncols], rows[0:1, 2 * NT:2 * NT + ncols])
    pbc = ctx.sb["ln_pbc"]
    with ctx.block() as blk:
        @blk.tensor
        def _(t):
            t.matmul(pbc[:, 0:ncols], ctx.sb["c_ones1"][0:1, :], rows[0:1, 0:ncols], start=True, stop=True)
            t.matmul(pbc[:, 512:512 + ncols], ctx.sb["c_ones1"][0:1, :], rows[0:1, 3 * NT:3 * NT + ncols],
                     start=True, stop=True)
    with ctx.block() as blk:
        @blk.vector
        def _(v):
            v.tensor_copy(bc[:, 0:ncols], pbc[:, 0:ncols])
            v.tensor_copy(bc[:, NT:NT + ncols], pbc[:, 512:512 + ncols])
    with ctx.block() as blk:
        @blk.vector
        def _(v):
            for i in range(8):
                v.tensor_sub(outtiles[i], xtiles[i], bc[:, 0:ncols])
        @blk.gpsimd
        def _(g):
            pass
    with ctx.block() as blk:
        @blk.vector
        def _(v):
            for i in range(8):
                v.tensor_mul(outtiles[i], outtiles[i], bc[:, NT:NT + ncols])
    if gain_col is not None:
        with ctx.block() as blk:
            @blk.vector
            def _(v):
                for i in range(8):
                    v.tensor_scalar(outtiles[i], outtiles[i], gain_col[i], None, ALU.mult)
    if bias_col is not None:
        with ctx.block() as blk:
            @blk.vector
            def _(v):
                for i in range(8):
                    v.tensor_scalar(outtiles[i], outtiles[i], bias_col[i], None, ALU.add)


def emit_stageB(ctx, src="x0", dst="x1"):
    nc = ctx.nc
    HID = 3072
    NOC = HID // 128
    TG = 1024
    xs = [ctx.sbuf(f"B_x{i}", [128, TG], F32) for i in range(8)]
    xb = [ctx.sbuf(f"B_xb{i}", [128, TG], BF16) for i in range(8)]
    hh = [ctx.sbuf(f"B_h{i}", [128, TG], BF16) for i in range(NOC)]
    yy = [ctx.sbuf(f"B_y{i}", [128, TG], F32) for i in range(8)]
    wbuf = ctx.sbuf("B_w", [128, 2048], BF16)
    dbuf = ctx.sbuf("B_wd", [128, 24 * 128], BF16)
    pg = ctx.psum("B_pg", [128, 512], F32)
    pu = ctx.psum("B_pu", [128, 512], F32)
    pd = ctx.psum("B_pd", [128, 512], F32)
    sg = ctx.sbuf("B_sg", [128, 512], BF16)
    ld = ctx.sem("B_ld")
    st = ctx.sem("B_st")

    wg = ctx.dram["wgT"]
    wu = ctx.dram["wuT"]
    wd = ctx.dram["wdT"]

    for g0 in range(TPC // TG):
        gcols = slice(TG * g0, TG * (g0 + 1))
        with ctx.block() as blk:
            @blk.sync
            def _(sy):
                for i in range(8):
                    sy.dma_start(out=xs[i][:, :], in_=ctx.dram[src][128 * i:128 * (i + 1), gcols]).then_inc(ld, 16)
                sy.wait_ge(ld, ctx.bump("B_ld", 16 * 8))
        with ctx.block() as blk:
            @blk.vector
            def _(v):
                for i in range(8):
                    v.tensor_copy(xb[i][:, :], xs[i][:, :])

        ntile = TG // 512
        for oc in range(NOC):
            with ctx.block() as blk:
                @blk.gpsimd
                def _(g):
                    g.dma_start(out=wbuf[:, 0:1024].rearrange("p (k c) -> p k c", k=8),
                                in_=wg[:, 128 * oc:128 * (oc + 1)].rearrange("(k p) c -> p k c", p=128)).then_inc(ld, 16)
                    g.dma_start(out=wbuf[:, 1024:2048].rearrange("p (k c) -> p k c", k=8),
                                in_=wu[:, 128 * oc:128 * (oc + 1)].rearrange("(k p) c -> p k c", p=128)).then_inc(ld, 16)
                    g.wait_ge(ld, ctx.bump("B_ld", 32))
            for n in range(ntile):
                cols = slice(512 * n, 512 * (n + 1))
                with ctx.block() as blk:
                    @blk.tensor
                    def _(t):
                        for kc in range(8):
                            t.matmul(pg[:, :], wbuf[:, 128 * kc:128 * (kc + 1)], xb[kc][:, cols],
                                     start=(kc == 0), stop=(kc == 7))
                        for kc in range(8):
                            t.matmul(pu[:, :], wbuf[:, 1024 + 128 * kc:1024 + 128 * (kc + 1)], xb[kc][:, cols],
                                     start=(kc == 0), stop=(kc == 7))
                with ctx.block() as blk:
                    @blk.scalar
                    def _(s):
                        s.activation(sg[:, :], pg[:, :], AF.Silu, scale=1.0 / 127.0)
                    @blk.vector
                    def _(v):
                        v.tensor_copy(hh[oc][:, cols], pu[:, :])
                with ctx.block() as blk:
                    @blk.vector
                    def _(v):
                        v.tensor_mul(hh[oc][:, cols], hh[oc][:, cols], sg[:, :])

        for oc in range(8):
            with ctx.block() as blk:
                @blk.gpsimd
                def _(g):
                    g.dma_start(out=dbuf[:, 0:24 * 128].rearrange("p (k c) -> p k c", k=24),
                                in_=wd[:, 128 * oc:128 * (oc + 1)].rearrange("(k p) c -> p k c", p=128)).then_inc(ld, 16)
                    g.wait_ge(ld, ctx.bump("B_ld", 16))
            for n in range(ntile):
                cols = slice(512 * n, 512 * (n + 1))
                with ctx.block() as blk:
                    @blk.tensor
                    def _(t):
                        for kc in range(24):
                            t.matmul(pd[:, :], dbuf[:, 128 * kc:128 * (kc + 1)], hh[kc][:, cols],
                                     start=(kc == 0), stop=(kc == 23))
                with ctx.block() as blk:
                    @blk.vector
                    def _(v):
                        v.scalar_tensor_tensor(yy[oc][:, cols], pd[:, :], 1.0 / (127.0 * 127.0),
                                               xs[oc][:, cols], ALU.mult, ALU.add)

        for n in range(ntile):
            cols = slice(512 * n, 512 * (n + 1))
            emit_ln_fm(ctx, None, [yy[i][:, cols] for i in range(8)],
                       [yy[i][:, cols] for i in range(8)], 512)
        with ctx.block() as blk:
            @blk.sync
            def _(sy):
                for i in range(8):
                    sy.dma_start(out=ctx.dram[dst][128 * i:128 * (i + 1), gcols], in_=yy[i][:, :]).then_inc(st, 16)
                sy.wait_ge(st, ctx.bump("B_st", 16 * 8))


def emit_stageC(ctx, src="x1", dst="x2", upto=99):
    nc = ctx.nc
    SEG = 512
    xs = [ctx.sbuf(f"C_x{i}", [128, SEG], F32) for i in range(8)]
    xb = [ctx.sbuf(f"C_xb{i}", [128, SEG], BF16) for i in range(8)]
    qkv = [ctx.sbuf(f"C_qkv{i}", [128, SEG], BF16) for i in range(24)]
    ob = [ctx.sbuf(f"C_o{i}", [128, SEG], BF16) for i in range(8)]
    yy = [ctx.sbuf(f"C_y{i}", [128, SEG], F32) for i in range(8)]
    wc = ctx.sbuf("C_w", [128, 8 * 3072], BF16)
    wo = ctx.sbuf("C_wo", [128, 8 * 1024], BF16)
    bc = ctx.sbuf("C_b", [128, 24], F32)
    bo = ctx.sbuf("C_bo", [128, 8], F32)
    qk = ctx.sbuf("C_qk", [128, SEG], BF16)
    sc = ctx.sbuf("C_sc", [16, SEG], F32)
    eb = ctx.sbuf("C_eb", [16, SEG], F32)
    srow = ctx.sbuf("C_srow", [16, 16], F32)
    pbc16 = ctx.sbuf("C_pb", [128, SEG], BF16)
    inds = ctx.sbuf("C_inds", [128, 128], BF16)
    bc16 = ctx.sbuf("C_bc16", [16, 1024], F32)
    hsel = ctx.sbuf("C_hsel", [16, 4], F32)
    phm = ctx.sbuf("C_phm", [16, SEG], F32)
    pqkv = ctx.psum("C_pqkv", [128, 512], F32)
    psc = ctx.psum("C_psc", [16, 512], F32)
    ppb = ctx.psum("C_ppb", [128, 512], F32)
    pwo = ctx.psum("C_pwo", [128, 512], F32)
    ld = ctx.sem("C_ld")
    st = ctx.sem("C_st")

    with ctx.block() as blk:
        @blk.gpsimd
        def _(g):
            g.dma_start(out=wc[:, :].rearrange("p (k c) -> p k c", k=8),
                        in_=ctx.dram["wC"].rearrange("(k p) c -> p k c", p=128)).then_inc(ld, 16)
            g.dma_start(out=wo[:, :].rearrange("p (k c) -> p k c", k=8),
                        in_=ctx.dram["woC"].rearrange("(k p) c -> p k c", p=128)).then_inc(ld, 16)
            g.dma_start(out=inds[:, :], in_=ctx.dram["cd_indqk"]).then_inc(ld, 16)
        @blk.sync
        def _(sy):
            sy.dma_start(out=bc[:, :], in_=ctx.dram["bC"]).then_inc(ld, 16)
            sy.dma_start(out=bo[:, :], in_=ctx.dram["boC"]).then_inc(ld, 16)
            sy.dma_start(out=bc16[:, :], in_=ctx.dram["cd_bc16"]).then_inc(ld, 16)
            sy.dma_start(out=hsel[:, :], in_=ctx.dram["cd_hsel"]).then_inc(ld, 16)
            sy.dma_start(out=phm[:, :], in_=ctx.dram["cd_phmask"]).then_inc(ld, 16)
            sy.wait_ge(ld, ctx.bump("C_ld", 16 * 8))

    if upto < 0:
        return
    for seg in range(TPC // SEG):
        if upto == 0 and seg > 0:
            break
        cols = slice(SEG * seg, SEG * (seg + 1))
        with ctx.block() as blk:
            @blk.sync
            def _(sy):
                for i in range(8):
                    sy.dma_start(out=xs[i][:, :], in_=ctx.dram[src][128 * i:128 * (i + 1), cols]).then_inc(ld, 16)
                sy.wait_ge(ld, ctx.bump("C_ld", 16 * 8))
        with ctx.block() as blk:
            @blk.vector
            def _(v):
                for i in range(8):
                    v.tensor_copy(xb[i][:, :], xs[i][:, :])
        if upto < 0.8:
            continue
        for oc in range(24 if upto >= 0.95 else 1):
            with ctx.block() as blk:
                @blk.tensor
                def _(t):
                    for kc in range(8):
                        t.matmul(pqkv[:, :], wc[:, 3072 * kc + 128 * oc:3072 * kc + 128 * (oc + 1)],
                                 xb[kc][:, :], start=(kc == 0), stop=(kc == 7))
            with ctx.block() as blk:
                @blk.scalar
                def _(s):
                    s.activation(qkv[oc][:, :], pqkv[:, :], AF.Identity, bias=bc[:, oc:oc + 1])
        if upto < 1:
            continue
        with ctx.block() as blk:
            @blk.vector
            def _(v):
                v.tensor_mul(qk[:, :], qkv[0][:, :], qkv[8][:, :])
            @blk.tensor
            def _(t):
                pass
        for i in range(8):
            with ctx.block() as blk:
                @blk.tensor
                def _(t):
                    t.matmul(psc[:, :], inds[:, 16 * i:16 * (i + 1)], qk[:, :],
                             start=(i == 0), stop=(i == 7))
            if i < 7:
                with ctx.block() as blk:
                    @blk.vector
                    def _(v):
                        v.tensor_mul(qk[:, :], qkv[i + 1][:, :], qkv[9 + i][:, :])
        with ctx.block() as blk:
            @blk.vector
            def _(v):
                v.tensor_copy(sc[:, :], psc[:, :])
        if upto < 2:
            continue
        with ctx.block() as blk:
            @blk.vector
            def _(v):
                for j in range(4):
                    v.reduce_max(srow[:, j:j + 1], sc[:, j::4], axis=AX.X)
        with ctx.block() as blk:
            @blk.vector
            def _(v):
                v.tensor_scalar_mul(srow[:, 4:8], srow[:, 0:4], -0.125)
        with ctx.block() as blk:
            @blk.scalar
            def _(s):
                for j in range(4):
                    s.activation(eb[:, j::4], sc[:, j::4], AF.Exp, bias=srow[:, 4 + j:5 + j],
                                 scale=0.125, accum_out=srow[:, 8 + j:9 + j])
        with ctx.block() as blk:
            @blk.vector
            def _(v):
                v.reciprocal(srow[:, 12:16], srow[:, 8:12])
        with ctx.block() as blk:
            @blk.vector
            def _(v):
                v.tensor_mul(srow[:, 0:4], srow[:, 12:16], hsel[:, :])
        with ctx.block() as blk:
            @blk.vector
            def _(v):
                v.reduce_sum(srow[:, 4:5], srow[:, 0:4], axis=AX.X)
        with ctx.block() as blk:
            @blk.vector
            def _(v):
                v.tensor_scalar(eb[:, :], eb[:, :], srow[:, 4:5], None, ALU.mult)
        with ctx.block() as blk:
            @blk.vector
            def _(v):
                v.tensor_mul(eb[:, :], eb[:, :], phm[:, :])
        if upto < 3:
            continue
        for i in range(8):
            with ctx.block() as blk:
                @blk.tensor
                def _(t):
                    t.matmul(ppb[:, :], bc16[:, 128 * i:128 * (i + 1)], eb[:, :], start=True, stop=True)
            with ctx.block() as blk:
                @blk.vector
                def _(v):
                    v.tensor_copy(pbc16[:, :], ppb[:, :])
            with ctx.block() as blk:
                @blk.vector
                def _(v):
                    v.tensor_mul(ob[i][:, :], qkv[16 + i][:, :], pbc16[:, :])
        if upto < 4:
            continue
        for oc in range(8):
            with ctx.block() as blk:
                @blk.tensor
                def _(t):
                    for kc in range(8):
                        t.matmul(pwo[:, :], wo[:, 1024 * kc + 128 * oc:1024 * kc + 128 * (oc + 1)],
                                 ob[kc][:, :], start=(kc == 0), stop=(kc == 7))
            with ctx.block() as blk:
                @blk.vector
                def _(v):
                    v.scalar_tensor_tensor(yy[oc][:, :], pwo[:, :], bo[:, oc:oc + 1], xs[oc][:, :],
                                           ALU.add, ALU.add)
        if upto >= 5:
            emit_ln_fm(ctx, None, [yy[i][:, :] for i in range(8)],
                       [yy[i][:, :] for i in range(8)], 512)
        if upto >= 4:
            with ctx.block() as blk:
                @blk.sync
                def _(sy):
                    for i in range(8):
                        sy.dma_start(out=ctx.dram[dst][128 * i:128 * (i + 1), cols], in_=yy[i][:, :]).then_inc(st, 16)
                    sy.wait_ge(st, ctx.bump("C_st", 16 * 8))


def build_stageC_program(upto=99):
    nc = bass.Bass("TRN2", target_bir_lowering=False)
    stack = contextlib.ExitStack()
    ctx = Ctx(nc, stack)
    with stack:
        ctx.dram_in("x1", [1024, TPC], F32)
        ctx.dram_in("wC", [1024, 3072], F32)
        ctx.dram_in("bC", [128, 24], F32)
        ctx.dram_in("woC", [1024, 1024], F32)
        ctx.dram_in("boC", [128, 8], F32)
        ctx.dram_out("x2", [1024, TPC], F32)
        for name, arr in _const_arrays().items():
            ctx.dram_in(name, arr.shape, I32 if arr.dtype == np.int32 else F32)
        _preload_consts(ctx)
        emit_stageC(ctx, upto=upto)
        ctx.flush()
    return nc


HB = 513
NTOK = HB + TPC
NQK = 2560


def emit_stageD_layer(ctx, l, src, halo, xmprev, xm_dram, dst):
    emit_stageD_attn(ctx, l, src, halo, xm_dram)
    emit_stageD_ffn(ctx, l, xmprev, xm_dram, dst)


def emit_stageD_attn(ctx, l, src, halo, xm_dram, halo_ag=None, xmpub=None):
    nc = ctx.nc
    ld = ctx.sem("D_ld")
    st = ctx.sem("D_st")
    sb = ctx.sb
    wmA, pmA = sb["D_wmA"], sb["D_pmA"]
    wmBlo, pmBlo = sb["D_wmBlo"], sb["D_pmBlo"]
    P0, P1, P2, P3 = sb["D_P0"], sb["D_P1"], sb["D_P2"], sb["D_P3"]
    i128b, i128f = sb["c_i128b"], sb["c_i128f"]
    nones2 = sb["c_nones2"]
    onesb = sb["c_onesb"]

    phA = contextlib.ExitStack()
    cA = Ctx.__new__(Ctx)
    cA.__dict__.update(ctx.__dict__)
    cA.stack = phA
    qf = [cA.sbuf(f"D{l}_qf{i}", [128, NQK], BF16, side="right") for i in range(4)]
    kf = [cA.sbuf(f"D{l}_kf{i}", [128, NQK], BF16, side="right") for i in range(4)]
    vtm = [cA.sbuf(f"D{l}_vtm{i}", [128, 512], BF16, side="right") for i in range(20)]

    with contextlib.ExitStack() as ph1:
        c1 = Ctx.__new__(Ctx)
        c1.__dict__.update(ctx.__dict__)
        c1.stack = ph1
        xb = [c1.sbuf(f"D{l}_xb{i}", [128, NTOK], BF16, side="right") for i in range(8)]
        wqb = c1.sbuf(f"D{l}_wqb", [128, 2 * 1024], BF16, side="right")

        if halo_ag is None:
            with ctx.block() as blk:
                @blk.gpsimd
                def _(g):
                    for i in range(8):
                        g.dma_start(out=xb[i][:, 0:HB], in_=ctx.dram[halo][128 * i:128 * (i + 1), :]).then_inc(ld, 16)
                        g.dma_start(out=xb[i][:, HB:NTOK], in_=ctx.dram[src][128 * i:128 * (i + 1), :]).then_inc(ld, 16)
                    g.wait_ge(ld, ctx.bump("D_ld", 16 * 16))
            with ctx.block() as blk:
                @blk.vector
                def _(v):
                    for i in range(8):
                        v.tensor_scalar(xb[i][:, 0:HB], xb[i][:, 0:HB], sb["D_hmask"][:, 0:1], None, ALU.mult)
        else:
            hcand = c1.sbuf(f"D{l}_hcand", [128, 4 * HB], BF16, side="right")
            for i in range(8):
                with ctx.block() as blk:
                    @blk.gpsimd
                    def _(g):
                        for q in range(4):
                            g.dma_start(out=hcand[:, HB * q:HB * (q + 1)],
                                        in_=ctx.dram[halo_ag][1024 * q + 128 * i:1024 * q + 128 * (i + 1), :]).then_inc(ld, 16)
                        g.dma_start(out=xb[i][:, HB:NTOK], in_=ctx.dram[src][128 * i:128 * (i + 1), :]).then_inc(ld, 16)
                        g.wait_ge(ld, ctx.bump("D_ld", 16 * 5))
                with ctx.block() as blk:
                    @blk.vector
                    def _(v):
                        v.tensor_scalar(xb[i][:, 0:HB], hcand[:, 0:HB], sb["D_hsel"][:, 0:1], None, ALU.mult)
                for q in range(1, 4):
                    with ctx.block() as blk:
                        @blk.vector
                        def _(v):
                            v.scalar_tensor_tensor(xb[i][:, 0:HB], hcand[:, HB * q:HB * (q + 1)],
                                                   sb["D_hsel"][:, q:q + 1], xb[i][:, 0:HB],
                                                   ALU.mult, ALU.add)
        hbv = [xb[i][:, 1:NTOK] if i < 4 else xb[i][:, 0:NTOK - 1] for i in range(8)]

        wqkv = ctx.dram[f"wqkvD{l}"]
        for oc in range(8):
            dstt = qf[oc] if oc < 4 else kf[oc - 4]
            with ctx.block() as blk:
                @blk.gpsimd
                def _(g):
                    g.dma_start(out=wqb[:, 0:1024].rearrange("p (k c) -> p k c", k=8),
                                in_=wqkv[:, 128 * oc:128 * (oc + 1)].rearrange("(k p) c -> p k c", p=128)).then_inc(ld, 16)
                    g.wait_ge(ld, ctx.bump("D_ld", 16))
            for n in range(5):
                cols = slice(512 * n, 512 * (n + 1))
                with ctx.block() as blk:
                    @blk.tensor
                    def _(t):
                        for kc in range(8):
                            t.matmul(P0[:, :], wqb[:, 128 * kc:128 * (kc + 1)], hbv[kc][:, cols],
                                     start=(kc == 0), stop=(kc == 7))
                with ctx.block() as blk:
                    @blk.vector
                    def _(v):
                        v.tensor_copy(dstt[:, cols], P0[:, :])
        with ctx.block() as blk:
            @blk.gpsimd
            def _(g):
                g.dma_start(out=wqb[:, 0:1024].rearrange("p (k c) -> p k c", k=8),
                            in_=wqkv[:, 1024:1152].rearrange("(k p) c -> p k c", p=128)).then_inc(ld, 16)
                g.dma_start(out=wqb[:, 1024:2048].rearrange("p (k c) -> p k c", k=8),
                            in_=wqkv[:, 1152:1280].rearrange("(k p) c -> p k c", p=128)).then_inc(ld, 16)
                g.wait_ge(ld, ctx.bump("D_ld", 32))
        for half in range(2):
            vc = slice(1024 + 256 * half, 1024 + 256 * (half + 1))
            with ctx.block() as blk:
                @blk.gpsimd
                def _(g):
                    g.dma_start(out=wqb[:, 0:2048].rearrange("p (k c) -> p k c", k=8),
                                in_=wqkv[:, vc].rearrange("(k p) c -> p k c", p=128)).then_inc(ld, 16)
                    g.wait_ge(ld, ctx.bump("D_ld", 16))
            for tt in range(20):
                tc = slice(128 * tt, 128 * (tt + 1))
                with ctx.block() as blk:
                    @blk.tensor
                    def _(t):
                        for kc in range(8):
                            t.matmul(P0[:, 0:256], hbv[kc][:, tc], wqb[:, 256 * kc:256 * (kc + 1)],
                                     start=(kc == 0), stop=(kc == 7))
                with ctx.block() as blk:
                    @blk.vector
                    def _(v):
                        v.tensor_copy(vtm[tt][:, 256 * half:256 * (half + 1)], P0[:, 0:256])

        ctx.flush()
    with contextlib.ExitStack() as ph2:
        c2 = Ctx.__new__(Ctx)
        c2.__dict__.update(ctx.__dict__)
        c2.stack = ph2
        cs = c2.sbuf(f"D{l}_cs", [128, NQK], BF16, side="right")
        ss = c2.sbuf(f"D{l}_ss", [128, NQK], BF16, side="right")
        scr = c2.sbuf(f"D{l}_scr", [128, 512], F32, side="right")
        qhb = c2.sbuf(f"D{l}_qhb", [128, 512], BF16, side="right")
        rot = c2.sbuf(f"D{l}_rot", [128, 256], BF16, side="right")
        t2b = c2.sbuf(f"D{l}_t2b", [128, 512], BF16, side="right")
        with ctx.block() as blk:
            @blk.gpsimd
            def _(g):
                g.dma_start(out=rot[:, 0:128], in_=ctx.dram[f"rotq{l}"]).then_inc(ld, 16)
                g.dma_start(out=rot[:, 128:256], in_=ctx.dram[f"rotk{l}"]).then_inc(ld, 16)
                g.wait_ge(ld, ctx.bump("D_ld", 32))
        for side in range(2):
            tiles = qf if side == 0 else kf
            rsl = rot[:, 0:128] if side == 0 else rot[:, 128:256]
            with ctx.block() as blk:
                @blk.gpsimd
                def _(g):
                    g.dma_start(out=cs[:, :], in_=ctx.dram[("csq" if side == 0 else "csk") + str(l)]).then_inc(ld, 16)
                    g.dma_start(out=ss[:, :], in_=ctx.dram[("ssq" if side == 0 else "ssk") + str(l)]).then_inc(ld, 16)
                    g.wait_ge(ld, ctx.bump("D_ld", 32))
            for i in range(4):
                for n in range(5):
                    cols = slice(512 * n, 512 * (n + 1))
                    with ctx.block() as blk:
                        @blk.vector
                        def _(v):
                            v.tensor_mul(scr[:, :], tiles[i][:, cols], tiles[i][:, cols])
                    with ctx.block() as blk:
                        @blk.tensor
                        def _(t):
                            t.matmul(P0[0:2, 0:512], sb["c_indh2"][:, 0:2], scr[:, :],
                                     start=True, stop=True)
                    with ctx.block() as blk:
                        @blk.scalar
                        def _(sc_):
                            sc_.activation(scr[0:2, :], P0[0:2, 0:512], AF.Sqrt)
                    with ctx.block() as blk:
                        @blk.vector
                        def _(v):
                            v.tensor_scalar_max(scr[0:2, :], scr[0:2, :], 1e-12)
                    with ctx.block() as blk:
                        @blk.vector
                        def _(v):
                            v.reciprocal(scr[0:2, :], scr[0:2, :])
                    with ctx.block() as blk:
                        @blk.tensor
                        def _(t):
                            t.matmul(P0[:, 0:512], sb["c_indh2T"][0:2, :], scr[0:2, :],
                                     start=True, stop=True)
                    with ctx.block() as blk:
                        @blk.vector
                        def _(v):
                            v.tensor_tensor(qhb[:, :], tiles[i][:, cols], P0[:, 0:512], ALU.mult)
                    with ctx.block() as blk:
                        @blk.tensor
                        def _(t):
                            t.matmul(P0[:, 0:512], rsl, qhb[:, :], start=True, stop=True)
                    with ctx.block() as blk:
                        @blk.vector
                        def _(v):
                            v.tensor_tensor(t2b[:, :], P0[:, 0:512], ss[:, cols], ALU.mult)
                    with ctx.block() as blk:
                        @blk.vector
                        def _(v):
                            v.tensor_tensor(tiles[i][:, cols], qhb[:, :], cs[:, cols], ALU.mult)
                    with ctx.block() as blk:
                        @blk.vector
                        def _(v):
                            v.tensor_add(tiles[i][:, cols], tiles[i][:, cols], t2b[:, :])

    if l == 0 and "dbgq" in ctx.dram:
        with ctx.block() as blk:
            @blk.sync
            def _(sy):
                for i in range(4):
                    sy.dma_start(out=ctx.dram["dbgq"][128 * i:128 * (i + 1), :], in_=qf[i][:, :]).then_inc(st, 16)
                    sy.dma_start(out=ctx.dram["dbgk"][128 * i:128 * (i + 1), :], in_=kf[i][:, :]).then_inc(st, 16)
                for i in range(20):
                    sy.dma_start(out=ctx.dram["dbgv"][128 * i:128 * (i + 1), :], in_=vtm[i][:, :]).then_inc(st, 16)
                sy.wait_ge(st, ctx.bump("D_st", 16 * 28))

        ctx.flush()
    ph3 = contextlib.ExitStack()
    c3a = Ctx.__new__(Ctx)
    c3a.__dict__.update(ctx.__dict__)
    c3a.stack = ph3
    wo = c3a.sbuf(f"D{l}_wo", [128, 4 * 1024], BF16, side="right")
    p2 = c3a.sbuf(f"D{l}_p2b", [128, 8 * 512], BF16, side="right")
    ofm = [c3a.sbuf(f"D{l}_ofm{i}", [128, 512], BF16, side="right") for i in range(4)]
    mrow = c3a.sbuf(f"D{l}_mrow", [128, 512], BF16, side="right")
    zrow = c3a.sbuf(f"D{l}_zrow", [1, 512], F32, side="right")
    rzbc = c3a.sbuf(f"D{l}_rzbc", [64, 512], F32, side="right")
    ctx.sb["D_mcol"] = c3a.sbuf(f"D{l}_mcol", [128, 2], F32, side="right")
    res1 = c3a.sbuf(f"D{l}_res1", [128, 512], F32, side="right")
    with ctx.block() as blk:
        @blk.gpsimd
        def _(g):
            g.dma_start(out=wo[:, :].rearrange("p (k c) -> p k c", k=4),
                        in_=ctx.dram[f"woD{l}"].rearrange("(k p) c -> p k c", p=128)).then_inc(ld, 16)
            g.wait_ge(ld, ctx.bump("D_ld", 16))
    for w in range(4):
        qoff = 512 * (w + 1)
        koff = 512 * w
        qcols = slice(qoff, qoff + 512)
        for h in range(8):
            til, hb0 = h // 2, 64 * (h % 2)
            hsl = slice(hb0, hb0 + 64)
            for qt in range(4):
                qc = slice(qoff + 128 * qt, qoff + 128 * (qt + 1))
                with ctx.block() as blk:
                    @blk.tensor
                    def _(t):
                        t.matmul(P1[:, :], qf[til][hsl, qc], kf[til][hsl, koff:koff + 512],
                                 start=True, stop=True)
                        t.matmul(P2[:, :], qf[til][hsl, qc], kf[til][hsl, koff + 512:koff + 1024],
                                 start=True, stop=True)
                with ctx.block() as blk:
                    @blk.vector
                    def _(v):
                        v.reduce_max(scrM(ctx)[:, 0:1], P1[:, :], axis=AX.X)
                        v.reduce_max(scrM(ctx)[:, 1:2], P2[:, :], axis=AX.X)
                with ctx.block() as blk:
                    @blk.vector
                    def _(v):
                        v.tensor_max(scrM(ctx)[:, 0:1], scrM(ctx)[:, 0:1], scrM(ctx)[:, 1:2])
                with ctx.block() as blk:
                    @blk.tensor
                    def _(t):
                        t.matmul(P3[0:1, 0:128], scrM(ctx)[:, 0:1], i128f[:, :], start=True, stop=True)
                with ctx.block() as blk:
                    @blk.vector
                    def _(v):
                        v.tensor_copy(mrow[0:1, 128 * qt:128 * (qt + 1)], P3[0:1, 0:128])
                        v.tensor_copy(mrow[64:65, 128 * qt:128 * (qt + 1)], P3[0:1, 0:128])
            for kt in range(8):
                kc = slice(koff + 128 * kt, koff + 128 * (kt + 1))
                pad = (w == 0 and kt < 4)
                with ctx.block() as blk:
                    @blk.tensor
                    def _(t):
                        if hb0 == 0:
                            t.matmul(P1[:, :], kf[til][hsl, kc], qf[til][hsl, qcols], start=True, stop=False)
                            t.matmul(P1[:, :], nones2[0:1, :], mrow[0:1, :], start=False, stop=False)
                            t.matmul(P1[:, :], i128b[:, :], wmA[:, 512 * kt:512 * (kt + 1)],
                                     start=False, stop=(not pad))
                            if pad:
                                t.matmul(P1[:, :], i128b[:, :], pmA[:, 512 * kt:512 * (kt + 1)],
                                         start=False, stop=True)
                        else:
                            t.matmul(P1[:, :], kf[til][hsl, kc], qf[til][hsl, qcols], start=True, stop=False)
                            t.matmul(P1[:, :], nones2[64:65, :], mrow[64:65, :], start=False, stop=False)
                            t.matmul(P1[0:64, :], i128b[64:128, 64:128], wmBlo[64:128, 512 * kt:512 * (kt + 1)],
                                     start=False, stop=False, tile_position=(64, 0))
                            t.matmul(P1[64:128, :], i128b[64:128, 64:128], wmA[64:128, 512 * kt:512 * (kt + 1)],
                                     start=False, stop=(not pad), tile_position=(64, 64))
                            if pad:
                                t.matmul(P1[0:64, :], i128b[64:128, 64:128], pmBlo[64:128, 512 * kt:512 * (kt + 1)],
                                         start=False, stop=False, tile_position=(64, 0))
                                t.matmul(P1[64:128, :], i128b[64:128, 64:128], pmA[64:128, 512 * kt:512 * (kt + 1)],
                                         start=False, stop=True, tile_position=(64, 64))
                with ctx.block() as blk:
                    @blk.scalar
                    def _(sc_):
                        sc_.activation(p2[:, 512 * kt:512 * (kt + 1)], P1[:, :], AF.Exp)
            with ctx.block() as blk:
                @blk.tensor
                def _(t):
                    for kt in range(8):
                        t.matmul(P3[0:1, 0:512], onesb[:, 0:1], p2[:, 512 * kt:512 * (kt + 1)],
                                 start=(kt == 0), stop=(kt == 7))
            with ctx.block() as blk:
                @blk.tensor
                def _(t):
                    for kt in range(8):
                        t.matmul(P2[0:64, :], vtm[4 * w + kt][:, 64 * h:64 * (h + 1)],
                                 p2[:, 512 * kt:512 * (kt + 1)], start=(kt == 0), stop=(kt == 7))
                @blk.vector
                def _(v):
                    v.tensor_copy(zrow[0:1, 0:512], P3[0:1, 0:512])
            with ctx.block() as blk:
                @blk.vector
                def _(v):
                    v.reciprocal(zrow[0:1, 0:512], zrow[0:1, 0:512])
            with ctx.block() as blk:
                @blk.tensor
                def _(t):
                    t.matmul(P0[0:64, :], sb["c_ones1"][0:1, 0:64], zrow[0:1, 0:512], start=True, stop=True)
            with ctx.block() as blk:
                @blk.vector
                def _(v):
                    v.tensor_copy(rzbc[0:64, :], P0[0:64, :])
            with ctx.block() as blk:
                @blk.vector
                def _(v):
                    v.tensor_tensor(ofm[til][hsl, :], P2[0:64, :], rzbc[0:64, :], ALU.mult)
        if l == 0 and "dbgo" in ctx.dram:
            with ctx.block() as blk:
                @blk.sync
                def _(sy):
                    for i in range(4):
                        sy.dma_start(out=ctx.dram["dbgo"][128 * i:128 * (i + 1), 512 * w:512 * (w + 1)],
                                     in_=ofm[i][:, :]).then_inc(st, 16)
                    sy.wait_ge(st, ctx.bump("D_st", 16 * 4))
        for oc in range(8):
            with ctx.block() as blk:
                @blk.sync
                def _(sy):
                    sy.dma_start(out=res1[:, :], in_=ctx.dram[src][128 * oc:128 * (oc + 1),
                                                                  512 * w:512 * (w + 1)]).then_inc(ld, 16)
                    sy.wait_ge(ld, ctx.bump("D_ld", 16))
                @blk.tensor
                def _(t):
                    for kc in range(4):
                        t.matmul(P0[:, :], wo[:, 1024 * kc + 128 * oc:1024 * kc + 128 * (oc + 1)],
                                 ofm[kc][:, :], start=(kc == 0), stop=(kc == 3))
            with ctx.block() as blk:
                @blk.vector
                def _(v):
                    v.tensor_add(res1[:, :], res1[:, :], P0[:, :])
                    if xmpub is not None and w == 3:
                        v.tensor_copy(sb["D_xlast"][:, oc:oc + 1], res1[:, 511:512])
            with ctx.block() as blk:
                @blk.sync
                def _(sy):
                    sy.dma_start(out=ctx.dram[xm_dram][128 * oc:128 * (oc + 1), 512 * w:512 * (w + 1)],
                                 in_=res1[:, :]).then_inc(st, 16)
                    sy.wait_ge(st, ctx.bump("D_st", 16))
        if xmpub is not None and w == 3:
            with ctx.block() as blk:
                @blk.sync
                def _(sy):
                    sy.dma_start(out=ctx.dram[xmpub], in_=sb["D_xlast"][:, :]).then_inc(st, 16)
                    sy.wait_ge(st, ctx.bump("D_st", 16))

    ctx.flush()
    ph3.close()
    phA.close()


def emit_stageD_ffn(ctx, l, xmprev, xm_dram, dst, xmprev_sb=None):
    nc = ctx.nc
    sb = ctx.sb
    ld = ctx.sem("D_ld")
    st = ctx.sem("D_st")
    P0 = sb["D_P0"]
    with contextlib.ExitStack() as phF:
        c3 = Ctx.__new__(Ctx)
        c3.__dict__.update(ctx.__dict__)
        c3.stack = phF
        TG = 512
        xm = [c3.sbuf(f"D{l}_fxm{i}", [128, TG + 1], F32, side="right") for i in range(8)]
        hblk = [c3.sbuf(f"D{l}_fh{i}", [128, TG], BF16, side="right") for i in range(44)]
        lnt = [c3.sbuf(f"D{l}_flt{i}", [128, TG], BF16, side="right") for i in range(8)]
        w1b = c3.sbuf(f"D{l}_fw1", [128, 1024], BF16, side="right")
        w2b = c3.sbuf(f"D{l}_fw2", [128, 22 * 128], BF16, side="right")
        bcol = c3.sbuf(f"D{l}_fbc", [128, 76], F32, side="right")
        x3t = [c3.sbuf(f"D{l}_fx3{i}", [128, 512], F32, side="right") for i in range(8)]
        with ctx.block() as blk:
            @blk.sync
            def _(sy):
                sy.dma_start(out=bcol[:, 0:44], in_=ctx.dram[f"b1D{l}"]).then_inc(ld, 16)
                sy.dma_start(out=bcol[:, 44:52], in_=ctx.dram[f"b2D{l}"]).then_inc(ld, 16)
                sy.dma_start(out=bcol[:, 52:60], in_=ctx.dram[f"lngD{l}"]).then_inc(ld, 16)
                sy.dma_start(out=bcol[:, 60:68], in_=ctx.dram[f"lnbD{l}"]).then_inc(ld, 16)
                if xmprev_sb is None:
                    sy.dma_start(out=bcol[:, 68:76], in_=ctx.dram[xmprev]).then_inc(ld, 16)
                    sy.wait_ge(ld, ctx.bump("D_ld", 16 * 5))
                else:
                    sy.wait_ge(ld, ctx.bump("D_ld", 16 * 4))
        if xmprev_sb is not None:
            with ctx.block() as blk:
                @blk.vector
                def _(v):
                    v.tensor_copy(bcol[:, 68:76], xmprev_sb[:, :])
        for g0 in range(TPC // TG):
            gcols = slice(TG * g0, TG * (g0 + 1))
            with ctx.block() as blk:
                @blk.sync
                def _(sy):
                    for i in range(8):
                        sy.dma_start(out=xm[i][:, 1:TG + 1],
                                     in_=ctx.dram[xm_dram][128 * i:128 * (i + 1), gcols]).then_inc(ld, 16)
                    sy.wait_ge(ld, ctx.bump("D_ld", 16 * 8))
            if g0 == 0:
                with ctx.block() as blk:
                    @blk.vector
                    def _(v):
                        for i in range(8):
                            v.tensor_scalar(xm[i][:, 0:1], bcol[:, 68 + i:69 + i],
                                            sb["D_hmask"][:, 0:1], None, ALU.mult)
            else:
                with ctx.block() as blk:
                    @blk.sync
                    def _(sy):
                        for i in range(8):
                            sy.dma_start(out=xm[i][:, 0:1],
                                         in_=ctx.dram[xm_dram][128 * i:128 * (i + 1),
                                                               TG * g0 - 1:TG * g0],
                                         allow_slow_non_contiguous=True).then_inc(ld, 16)
                        sy.wait_ge(ld, ctx.bump("D_ld", 16 * 8))
            hsv = [xm[i][:, 1:TG + 1] if i < 4 else xm[i][:, 0:TG] for i in range(8)]
            for n in range(TG // 512):
                cols = slice(512 * n, 512 * (n + 1))
                emit_ln_fm(ctx, None, [hsv[i][:, cols] for i in range(8)],
                           [lnt[i][:, cols] for i in range(8)], 512,
                           gain_col=[bcol[:, 52 + i:53 + i] for i in range(8)],
                           bias_col=[bcol[:, 60 + i:61 + i] for i in range(8)])
            w1 = ctx.dram[f"w1D{l}"]
            for oc in range(44):
                with ctx.block() as blk:
                    @blk.gpsimd
                    def _(g):
                        g.dma_start(out=w1b[:, :].rearrange("p (k c) -> p k c", k=8),
                                    in_=w1[:, 128 * oc:128 * (oc + 1)].rearrange("(k p) c -> p k c", p=128)).then_inc(ld, 16)
                        g.wait_ge(ld, ctx.bump("D_ld", 16))
                for n in range(TG // 512):
                    cols = slice(512 * n, 512 * (n + 1))
                    with ctx.block() as blk:
                        @blk.tensor
                        def _(t):
                            for kc in range(8):
                                t.matmul(P0[:, :], w1b[:, 128 * kc:128 * (kc + 1)], lnt[kc][:, cols],
                                         start=(kc == 0), stop=(kc == 7))
                    with ctx.block() as blk:
                        @blk.scalar
                        def _(sc_):
                            fn = AF.Identity if oc % 2 == 0 else AF.Gelu
                            sc_.activation(hblk[oc][:, cols], P0[:, :], fn, bias=bcol[:, oc:oc + 1])
            with ctx.block() as blk:
                @blk.vector
                def _(v):
                    for i in range(22):
                        v.tensor_mul(hblk[2 * i][:, :], hblk[2 * i][:, :], hblk[2 * i + 1][:, :])
            w2 = ctx.dram[f"w2D{l}"]
            for oc in range(8):
                with ctx.block() as blk:
                    @blk.gpsimd
                    def _(g):
                        g.dma_start(out=w2b[:, :].rearrange("p (k c) -> p k c", k=22),
                                    in_=w2[:, 128 * oc:128 * (oc + 1)].rearrange("(k p) c -> p k c", p=128)).then_inc(ld, 16)
                        g.wait_ge(ld, ctx.bump("D_ld", 16))
                for n in range(TG // 512):
                    cols = slice(512 * n, 512 * (n + 1))
                    with ctx.block() as blk:
                        @blk.tensor
                        def _(t):
                            for kc in range(22):
                                t.matmul(P0[:, :], w2b[:, 128 * kc:128 * (kc + 1)], hblk[2 * kc][:, cols],
                                         start=(kc == 0), stop=(kc == 21))
                    with ctx.block() as blk:
                        @blk.vector
                        def _(v):
                            v.scalar_tensor_tensor(x3t[oc][:, :], P0[:, :], bcol[:, 44 + oc:45 + oc],
                                                   xm[oc][:, 1 + 512 * n:1 + 512 * (n + 1)], ALU.add, ALU.add)
                    with ctx.block() as blk:
                        @blk.sync
                        def _(sy):
                            sy.dma_start(out=ctx.dram[dst][128 * oc:128 * (oc + 1),
                                                           TG * g0 + 512 * n:TG * g0 + 512 * (n + 1)],
                                         in_=x3t[oc][:, :]).then_inc(st, 16)
                            sy.wait_ge(st, ctx.bump("D_st", 16))
        ctx.flush()


def scrM(ctx):
    return ctx.sb["D_mcol"]


def emit_stageD(ctx, src="x2", dst="xd", nlayers=2):
    nc = ctx.nc
    ctx.sbuf("D_wmA", [128, 8 * 512], BF16)
    ctx.sbuf("D_wmBlo", [128, 8 * 512], BF16)
    ctx.sbuf("D_pmA", [128, 4 * 512], BF16)
    ctx.sbuf("D_pmBlo", [128, 4 * 512], BF16)
    ctx.sbuf("D_hmask", [128, 1], F32)
    ctx.sb["D_P0"] = ctx.psum("D_P0", [128, 512], F32)
    ctx.sb["D_P1"] = ctx.psum("D_P1", [128, 512], F32)
    ctx.sb["D_P2"] = ctx.psum("D_P2", [128, 512], F32)
    ctx.sb["D_P3"] = ctx.psum("D_P3", [128, 512], F32)
    ld = ctx.sem("D_ld")
    with ctx.block() as blk:
        @blk.gpsimd
        def _(g):
            for nm in ("wmA", "wmBlo"):
                g.dma_start(out=ctx.sb[f"D_{nm}"][:, :], in_=ctx.dram[f"cd_{nm}"]).then_inc(ld, 16)
            for nm in ("pmA", "pmBlo"):
                g.dma_start(out=ctx.sb[f"D_{nm}"][:, :], in_=ctx.dram[nm]).then_inc(ld, 16)
        @blk.sync
        def _(sy):
            sy.dma_start(out=ctx.sb["D_hmask"][:, :], in_=ctx.dram["hmask"]).then_inc(ld, 16)
            sy.wait_ge(ld, ctx.bump("D_ld", 16 * 5))
    emit_stageD_layer(ctx, 0, src, "h2", "xmprev0", "xmscr", "x3scr")
    if nlayers > 1:
        emit_stageD_layer(ctx, 1, "x3scr", "h3", "xmprev1", "xmscr", "x4scr")
    emit_final_ln(ctx, src, "x4scr", dst)


def emit_final_ln(ctx, src1, src2, dst):
    nc = ctx.nc
    ld = ctx.sem("D_ld")
    st = ctx.sem("D_st")
    yy = [ctx.sbuf(f"D_fin{i}", [128, 512], F32) for i in range(8)]
    y2 = [ctx.sbuf(f"D_fin2{i}", [128, 512], F32) for i in range(8)]
    for n in range(TPC // 512):
        cols = slice(512 * n, 512 * (n + 1))
        with ctx.block() as blk:
            @blk.sync
            def _(sy):
                for i in range(8):
                    sy.dma_start(out=yy[i][:, :], in_=ctx.dram[src1][128 * i:128 * (i + 1), cols]).then_inc(ld, 16)
                    sy.dma_start(out=y2[i][:, :], in_=ctx.dram[src2][128 * i:128 * (i + 1), cols]).then_inc(ld, 16)
                sy.wait_ge(ld, ctx.bump("D_ld", 16 * 16))
        with ctx.block() as blk:
            @blk.vector
            def _(v):
                for i in range(8):
                    v.tensor_add(yy[i][:, :], yy[i][:, :], y2[i][:, :])
        emit_ln_fm(ctx, None, [yy[i][:, :] for i in range(8)],
                   [yy[i][:, :] for i in range(8)], 512)
        with ctx.block() as blk:
            @blk.sync
            def _(sy):
                for i in range(8):
                    sy.dma_start(out=ctx.dram[dst][128 * i:128 * (i + 1), cols], in_=yy[i][:, :]).then_inc(st, 16)
                sy.wait_ge(st, ctx.bump("D_st", 16 * 8))


def build_stageD_program(debug=False, nlayers=2):
    nc = bass.Bass("TRN2", target_bir_lowering=False)
    stack = contextlib.ExitStack()
    ctx = Ctx(nc, stack)
    with stack:
        ctx.dram_in("x2", [1024, TPC], F32)
        ctx.dram_in("h2", [1024, HB], F32)
        ctx.dram_in("h3", [1024, HB], F32)
        ctx.dram_in("xmprev0", [128, 8], F32)
        ctx.dram_in("xmprev1", [128, 8], F32)
        ctx.dram_in("hmask", [128, 1], F32)
        ctx.dram_in("pmA", [128, 4 * 512], F32)
        ctx.dram_in("pmBlo", [128, 4 * 512], F32)
        for l in range(2):
            ctx.dram_in(f"wqkvD{l}", [1024, 1536], F32)
            ctx.dram_in(f"woD{l}", [512, 1024], F32)
            ctx.dram_in(f"w1D{l}", [1024, 5632], F32)
            ctx.dram_in(f"w2D{l}", [2816, 1024], F32)
            ctx.dram_in(f"b1D{l}", [128, 44], F32)
            ctx.dram_in(f"b2D{l}", [128, 8], F32)
            ctx.dram_in(f"lngD{l}", [128, 8], F32)
            ctx.dram_in(f"lnbD{l}", [128, 8], F32)
            ctx.dram_in(f"rotq{l}", [128, 128], F32)
            ctx.dram_in(f"rotk{l}", [128, 128], F32)
            for nm in ("csq", "ssq", "csk", "ssk"):
                ctx.dram_in(f"{nm}{l}", [128, NQK], F32)
        mk = ctx.dram_out if debug else ctx.dram_tmp
        if debug:
            import ml_dtypes
            ctx.dram_out("dbgq", [512, NQK], BF16)
            ctx.dram_out("dbgk", [512, NQK], BF16)
            ctx.dram_out("dbgv", [2560, 512], BF16)
            ctx.dram_out("dbgo", [512, TPC], BF16)
        mk("xmscr", [1024, TPC], F32)
        mk("x3scr", [1024, TPC], F32)
        mk("x4scr", [1024, TPC], F32)
        ctx.dram_out("xd", [1024, TPC], F32)
        for name, arr in _const_arrays().items():
            ctx.dram_in(name, arr.shape, I32 if arr.dtype == np.int32 else F32)
        _preload_consts(ctx)
        emit_stageD(ctx, nlayers=nlayers)
        ctx.flush()
    return nc


CC = 32
CAPA = 80.0


def _r3(ap, w):
    return ap.rearrange("p (a b) -> p a b", b=w)


def emit_stageA(ctx, dst="x0", TT=T, do_collective=True):
    nc = ctx.nc
    sb = ctx.sb
    ld = ctx.sem("A_ld")
    st = ctx.sem("A_st")
    ctx.sem("A_sm")
    ctx.sem("A_se")
    P0 = ctx.psum("A_P0", [128, 512], F32)
    _ps_stack = contextlib.ExitStack()
    _cps = Ctx.__new__(Ctx)
    _cps.__dict__.update(ctx.__dict__)
    _cps.stack = _ps_stack
    PA = _cps.psum("A_PA", [128, 512], F32)
    PB = _cps.psum("A_PB", [128, 512], F32)
    PS = _cps.psum("A_PS", [128, 512], F32)
    PC = [_cps.psum(f"A_PC{j}", [128, 384], F32) for j in range(4)]
    ctx.sb["A_PSUMS"] = (P0, PA, PB, PS, PC)

    wab = ctx.sbuf("A_wa", [128, 8 * 1024], BF16)
    wfg = ctx.sbuf("A_wfg", [128, 256], BF16)
    convw = ctx.sbuf("A_cw", [128, 24], F32)
    convb = ctx.sbuf("A_cb", [128, 6], F32)
    dtb = ctx.sbuf("A_dtb", [128, 2], F32)
    nacol = ctx.sbuf("A_na", [128, 2], F32)
    acol = ctx.sbuf("A_ac", [4, 2], F32)
    bg2bc = ctx.sbuf("A_bg2", [128, 256], F32)
    onwbc = ctx.sbuf("A_onw", [128, 256], F32)
    rbg2 = ctx.sbuf("A_r_bg2", [1, 256], F32)
    ronw = ctx.sbuf("A_r_onw", [1, 256], F32)
    i32g = ctx.sbuf("A_i32g", [128, 64], BF16)
    i32st = ctx.sbuf("A_i32s", [128, 32], BF16)
    i64s = ctx.sbuf("A_i64s", [128, 64], BF16)
    ni64s = ctx.sbuf("A_ni64s", [128, 64], BF16)
    msel = ctx.sbuf("A_msel", [128, 5 * 64], I32)
    msel2 = ctx.sbuf("A_msel2", [128, 5 * 64], I32)
    zero64 = ctx.sbuf("A_z64", [128, 64], F32)
    resetm = ctx.sbuf("A_rst", [128, 512], F32)
    bc4 = ctx.sbuf("A_bc4", [4, 256], F32)
    nbc4 = ctx.sbuf("A_nbc4", [4, 256], F32)

    Sst = ctx.sbuf("A_S", [128, 2 * 128], BF16)
    Shist = ctx.sbuf("A_Sh", [128, 16 * 128], BF16)

    hsbf = [ctx.sbuf(f"A_hs{i}", [128, 512], BF16) for i in range(8)]
    qraw = [ctx.sbuf(f"A_qr{i}", [128, 515], BF16) for i in range(2)]
    kraw = [ctx.sbuf(f"A_kr{i}", [128, 515], BF16) for i in range(2)]
    vraw = [ctx.sbuf(f"A_vr{i}", [128, 515], BF16) for i in range(2)]
    fgout = ctx.sbuf("A_fg", [128, 512], BF16)
    qh = [ctx.sbuf(f"A_qh{i}", [128, 512], BF16) for i in range(2)]
    kh = [ctx.sbuf(f"A_kh{i}", [128, 512], BF16) for i in range(2)]
    kb = [ctx.sbuf(f"A_kb{i}", [128, 512], BF16) for i in range(2)]
    vb = [ctx.sbuf(f"A_vb{i}", [128, 512], BF16) for i in range(2)]
    gcf = [ctx.sbuf(f"A_gc{i}", [128, 512], F32) for i in range(2)]
    gprev = [ctx.sbuf(f"A_gp{i}", [128, 512], F32) for i in range(2)]
    REB = ["aK4", "aQ4", "aK8", "aQ8", "aK16", "aQ16", "aK32", "aQ32",
           "b4s", "b4n", "b8n", "b16n", "btl"]
    reb = {nm: [ctx.sbuf(f"A_{nm}{i}", [128, 512], BF16) for i in range(2)] for nm in REB}
    scr = ctx.sbuf("A_scr", [128, 512], F32)
    scr2 = ctx.sbuf("A_scr2", [128, 512], F32)
    scrb = ctx.sbuf("A_scrb", [128, 512], F32)
    brow = ctx.sbuf("A_brow", [4, 512], F32)
    betabc = [ctx.sbuf(f"A_bbc{i}", [128, 512], BF16) for i in range(2)]

    LtX = {0: ctx.sbuf("A_LtA", [128, 64], BF16), 1: ctx.sbuf("A_LtB", [128, 64], BF16)}
    LpX = {0: ctx.sbuf("A_LpA", [128, 64], BF16), 1: ctx.sbuf("A_LpB", [128, 64], BF16)}
    ScX = {0: ctx.sbuf("A_ScA", [128, 64], BF16), 1: ctx.sbuf("A_ScB", [128, 64], BF16)}
    PtX = {0: ctx.sbuf("A_PtA", [128, 64], BF16), 1: ctx.sbuf("A_PtB", [128, 64], BF16)}
    PpX = {0: ctx.sbuf("A_PpA", [128, 64], BF16), 1: ctx.sbuf("A_PpB", [128, 64], BF16)}
    GtX = {0: ctx.sbuf("A_GtA", [128, 64], BF16), 1: ctx.sbuf("A_GtB", [128, 64], BF16)}
    rhsb = ctx.sbuf("A_rhs", [128, 4 * 512], BF16)
    wsol = ctx.sbuf("A_wsol", [128, 4 * 512], BF16)
    btm = ctx.sbuf("A_btm", [128, 4 * 256], BF16)
    oloc = ctx.sbuf("A_oloc", [128, 4 * 256], F32)
    omt = ctx.sbuf("A_omt", [128, 16 * 64], BF16)
    phir = ctx.sbuf("A_phr", [128, 16 * 128], BF16)
    psig = ctx.sbuf("A_psi", [128, 16 * 128], BF16)
    gamc = ctx.sbuf("A_gam", [128, 32], F32)
    oarr = ctx.sbuf("A_oar", [128, 256], F32)
    gatet = ctx.sbuf("A_gate", [128, 4 * 256], F32)
    ofmt = ctx.sbuf("A_ofmt", [128, 2 * 512], BF16)

    nt = TT // 512
    hsT = ctx.dram["hsT"]
    oscr = ctx.dram["oscr"]

    with ctx.block() as blk:
        @blk.gpsimd
        def _(g):
            g.dma_start(out=wab[:, :].rearrange("p (k c) -> p k c", k=8),
                        in_=ctx.dram["wA"].rearrange("(k p) c -> p k c", p=128)).then_inc(ld, 16)
            g.dma_start(out=wfg[:, :], in_=ctx.dram["wfg2"]).then_inc(ld, 16)
            g.dma_start(out=i32g[:, :], in_=ctx.dram["cd_i32g"]).then_inc(ld, 16)
            g.dma_start(out=i32st[:, :], in_=ctx.dram["cd_i32st"]).then_inc(ld, 16)
            g.dma_start(out=i64s[:, :], in_=ctx.dram["cd_i64st"]).then_inc(ld, 16)
            g.dma_start(out=ni64s[:, :], in_=ctx.dram["cd_ni64st"]).then_inc(ld, 16)
        @blk.sync
        def _(sy):
            sy.dma_start(out=msel[:, :], in_=ctx.dram["cd_msel"]).then_inc(ld, 16)
            sy.dma_start(out=msel2[:, :], in_=ctx.dram["cd_msel2"]).then_inc(ld, 16)
            sy.dma_start(out=convw[:, :], in_=ctx.dram["convw"]).then_inc(ld, 16)
            sy.dma_start(out=convb[:, :], in_=ctx.dram["convb"]).then_inc(ld, 16)
            sy.dma_start(out=dtb[:, :], in_=ctx.dram["dtb2"]).then_inc(ld, 16)
            sy.dma_start(out=acol[0:4, 0:1], in_=ctx.dram["alogc"]).then_inc(ld, 16)
            sy.dma_start(out=resetm[:, :], in_=ctx.dram["cd_reset"]).then_inc(ld, 16)
            sy.dma_start(out=bc4[:, :], in_=ctx.dram["cd_bc4"]).then_inc(ld, 16)
            sy.dma_start(out=nbc4[:, :], in_=ctx.dram["cd_nbc4"]).then_inc(ld, 16)
            sy.dma_start(out=rbg2[:, :], in_=ctx.dram["bg2r"]).then_inc(ld, 16)
            sy.dma_start(out=ronw[:, :], in_=ctx.dram["onw4"]).then_inc(ld, 16)
            sy.wait_ge(ld, ctx.bump("A_ld", 16 * 17))
    with ctx.block() as blk:
        @blk.scalar
        def _(s):
            s.activation(acol[0:4, 1:2], acol[0:4, 0:1], AF.Exp)
    with ctx.block() as blk:
        @blk.tensor
        def _(t):
            t.matmul(P0[:, 0:1], nbc4[0:4, 0:128], acol[0:4, 1:2], start=True, stop=True)
            t.matmul(P0[:, 1:2], nbc4[0:4, 128:256], acol[0:4, 1:2], start=True, stop=True)
            t.matmul(PA[:, 0:256], sb["c_ones1"][0:1, :], rbg2[0:1, :], start=True, stop=True)
            t.matmul(PB[:, 0:256], sb["c_ones1"][0:1, :], ronw[0:1, :], start=True, stop=True)
    with ctx.block() as blk:
        @blk.vector
        def _(v):
            v.tensor_copy(nacol[:, :], P0[:, 0:2])
            v.tensor_copy(bg2bc[:, :], PA[:, 0:256])
            v.tensor_copy(onwbc[:, :], PB[:, 0:256])
        @blk.gpsimd
        def _(g):
            for rr in (qraw, kraw, vraw):
                for i in range(2):
                    g.memset(rr[i][:, 0:3], 0.0)
            g.memset(Sst[:, :], 0.0)
            g.memset(zero64[:, :], 0.0)

    for tj in range(nt):
        tc = slice(512 * tj, 512 * (tj + 1))
        with ctx.block() as blk:
            @blk.gpsimd
            def _(g):
                for kc in range(8):
                    g.dma_start(out=hsbf[kc][:, :], in_=hsT[128 * kc:128 * (kc + 1), tc]).then_inc(ld, 16)
                g.wait_ge(ld, ctx.bump("A_ld", 16 * 8))
        for oc in range(8):
            with ctx.block() as blk:
                @blk.tensor
                def _(t):
                    for kc in range(8):
                        t.matmul(P0[:, :], wab[:, 1024 * kc + 128 * oc:1024 * kc + 128 * (oc + 1)],
                                 hsbf[kc][:, :], start=(kc == 0), stop=(kc == 7))
            with ctx.block() as blk:
                if oc < 6:
                    dstt = [qraw, qraw, kraw, kraw, vraw, vraw][oc][oc % 2]
                    @blk.vector
                    def _(v):
                        v.tensor_copy(dstt[:, 3:515], P0[:, :])
                elif oc == 6:
                    @blk.vector
                    def _(v):
                        v.tensor_copy(fgout[:, :], P0[:, :])
                else:
                    @blk.scalar
                    def _(s):
                        s.activation(brow[0:4, :], P0[0:4, :], AF.Sigmoid)
        for i in range(2):
            for raw, dstt, cwof, cbof in ((qraw[i], qh[i], 12 * i + 0, 3 * i + 0),
                                          (kraw[i], kh[i], 12 * i + 4, 3 * i + 1),
                                          (vraw[i], vb[i], 12 * i + 8, 3 * i + 2)):
                with ctx.block() as blk:
                    @blk.vector
                    def _(v):
                        v.tensor_scalar(scrb[:, :], raw[:, 0:512], convw[:, cwof + 0:cwof + 1], None, ALU.mult)
                for jj in range(1, 4):
                    with ctx.block() as blk:
                        @blk.vector
                        def _(v):
                            v.scalar_tensor_tensor(scrb[:, :], raw[:, jj:512 + jj],
                                                   convw[:, cwof + jj:cwof + jj + 1],
                                                   scrb[:, :], ALU.mult, ALU.add)
                with ctx.block() as blk:
                    @blk.scalar
                    def _(s):
                        s.activation(dstt[:, :], scrb[:, :], AF.Silu, bias=convb[:, cbof:cbof + 1])
                    @blk.vector
                    def _(v):
                        v.tensor_copy(raw[:, 0:3], raw[:, 512:515])
        with ctx.block() as blk:
            @blk.tensor
            def _(t):
                t.matmul(PA[:, 0:512], bc4[0:4, 0:128], brow[0:4, :], start=True, stop=True)
                t.matmul(PB[:, 0:512], bc4[0:4, 128:256], brow[0:4, :], start=True, stop=True)
        with ctx.block() as blk:
            @blk.vector
            def _(v):
                v.tensor_copy(betabc[0][:, :], PA[:, 0:512])
                v.tensor_copy(betabc[1][:, :], PB[:, 0:512])
        for i in range(2):
            with ctx.block() as blk:
                @blk.vector
                def _(v):
                    v.tensor_mul(vb[i][:, :], vb[i][:, :], betabc[i][:, :])
            for outt, qscale in ((qh[i], True), (kh[i], False)):
                with ctx.block() as blk:
                    @blk.vector
                    def _(v):
                        v.tensor_mul(scr[:, :], outt[:, :], outt[:, :])
                with ctx.block() as blk:
                    @blk.tensor
                    def _(t):
                        t.matmul(P0[0:2, :], sb["c_indh2"][:, 0:2], scr[:, :], start=True, stop=True)
                with ctx.block() as blk:
                    @blk.scalar
                    def _(s):
                        s.activation(scr[0:2, :], P0[0:2, :], AF.Sqrt)
                with ctx.block() as blk:
                    @blk.vector
                    def _(v):
                        if qscale:
                            v.tensor_scalar(scr[0:2, :], scr[0:2, :], 8.0, 8e-6, ALU.mult, ALU.add)
                        else:
                            v.tensor_scalar_add(scr[0:2, :], scr[0:2, :], 1e-6)
                with ctx.block() as blk:
                    @blk.vector
                    def _(v):
                        v.reciprocal(scr[0:2, :], scr[0:2, :])
                with ctx.block() as blk:
                    @blk.tensor
                    def _(t):
                        t.matmul(P0[:, :], sb["c_indh2T"][0:2, :], scr[0:2, :], start=True, stop=True)
                with ctx.block() as blk:
                    @blk.vector
                    def _(v):
                        v.tensor_tensor(outt[:, :], outt[:, :], P0[:, :], ALU.mult)
            with ctx.block() as blk:
                @blk.vector
                def _(v):
                    v.tensor_mul(kb[i][:, :], kh[i][:, :], betabc[i][:, :])
        with ctx.block() as blk:
            @blk.tensor
            def _(t):
                t.matmul(PA[:, 0:512], wfg[0:64, 0:128], fgout[0:64, :], start=True, stop=True)
                t.matmul(PB[:, 0:512], wfg[0:64, 128:256], fgout[0:64, :], start=True, stop=True)
        for i, PX in ((0, PA), (1, PB)):
            with ctx.block() as blk:
                @blk.vector
                def _(v):
                    v.tensor_scalar(scr[:, :], PX[:, 0:512], dtb[:, i:i + 1], None, ALU.add)
            with ctx.block() as blk:
                @blk.vector
                def _(v):
                    v.tensor_scalar(scr2[:, :], scr[:, :], 0.0, None, ALU.max)
            with ctx.block() as blk:
                @blk.vector
                def _(v):
                    v.scalar_tensor_tensor(scr[:, :], scr2[:, :], 2.0, scr[:, :],
                                           ALU.mult, ALU.subtract)
            with ctx.block() as blk:
                @blk.scalar
                def _(s):
                    s.activation(scr[:, :], scr[:, :], AF.Exp, scale=-1.0)
            with ctx.block() as blk:
                @blk.scalar
                def _(s):
                    s.activation(scr[:, :], scr[:, :], AF.Ln, bias=1.0)
            with ctx.block() as blk:
                @blk.vector
                def _(v):
                    v.tensor_add(scr[:, :], scr[:, :], scr2[:, :])
            with ctx.block() as blk:
                @blk.vector
                def _(v):
                    v.tensor_scalar(scr[:, :], scr[:, :], nacol[:, i:i + 1], None, ALU.mult)
            with ctx.block() as blk:
                @blk.vector
                def _(v):
                    v.tensor_tensor_scan(gcf[i][:, :], resetm[:, :], scr[:, :], 0.0,
                                         ALU.mult, ALU.add)
            with ctx.block() as blk:
                @blk.gpsimd
                def _(g):
                    g.memset(gprev[i][:, 0:1], 0.0)
                @blk.vector
                def _(v):
                    v.tensor_tensor(gprev[i][:, 1:512], gcf[i][:, 0:511], resetm[:, 1:512], ALU.mult)
        with ctx.block() as blk:
            @blk.scalar
            def _(s):
                for i in range(2):
                    s.activation(gamc[:, i:32:2], gcf[i][:, 31:512:32], AF.Exp)

        for i in range(2):
            GC, GP = gcf[i], gprev[i]
            def bsub(out_, a_rep, b_full, w, off, rev=False):
                ap_rep = bass.AP(a_rep, off, [[512, 128], [w, 512 // w], [0, w]])
                if not rev:
                    return (_r3(out_[:, :], w), _r3(b_full[:, :], w), ap_rep)
                return (_r3(out_[:, :], w), ap_rep, _r3(b_full[:, :], w))
            for w, anm_k, anm_q in ((4, "aK4", "aQ4"), (8, "aK8", "aQ8"), (16, "aK16", "aQ16")):
                with ctx.block() as blk:
                    @blk.vector
                    def _(v):
                        o, a, b = bsub(scr, GP, GC, w, 0)
                        v.tensor_tensor(o, a, b, ALU.subtract)
                with ctx.block() as blk:
                    @blk.scalar
                    def _(s):
                        s.activation(scr2[:, :], scr[:, :], AF.Exp)
                with ctx.block() as blk:
                    @blk.vector
                    def _(v):
                        v.tensor_tensor(reb[anm_k][i][:, :], kb[i][:, :], scr2[:, :], ALU.mult)
                with ctx.block() as blk:
                    @blk.vector
                    def _(v):
                        v.tensor_tensor(reb[anm_q][i][:, :], qh[i][:, :], scr2[:, :], ALU.mult)
                if w == 4:
                    with ctx.block() as blk:
                        @blk.vector
                        def _(v):
                            v.tensor_scalar(scr[:, :], scr[:, :], -1.0, CAPA, ALU.mult, ALU.min)
                    with ctx.block() as blk:
                        @blk.scalar
                        def _(s):
                            s.activation(scr[:, :], scr[:, :], AF.Exp)
                    with ctx.block() as blk:
                        @blk.vector
                        def _(v):
                            v.tensor_tensor(reb["b4s"][i][:, :], kh[i][:, :], scr[:, :], ALU.mult)
            with ctx.block() as blk:
                @blk.scalar
                def _(s):
                    s.activation(scr2[:, :], GC[:, :], AF.Exp)
            with ctx.block() as blk:
                @blk.vector
                def _(v):
                    v.tensor_tensor(reb["aK32"][i][:, :], kb[i][:, :], scr2[:, :], ALU.mult)
            with ctx.block() as blk:
                @blk.vector
                def _(v):
                    v.tensor_tensor(reb["aQ32"][i][:, :], qh[i][:, :], scr2[:, :], ALU.mult)
            for w, bnm in ((4, "b4n"), (8, "b8n"), (16, "b16n")):
                with ctx.block() as blk:
                    @blk.vector
                    def _(v):
                        o, a, b = bsub(scr, GC, GC, w, w - 1, rev=True)
                        v.tensor_tensor(o, a, b, ALU.subtract)
                with ctx.block() as blk:
                    @blk.vector
                    def _(v):
                        v.tensor_scalar_min(scr[:, :], scr[:, :], CAPA)
                with ctx.block() as blk:
                    @blk.scalar
                    def _(s):
                        s.activation(scr[:, :], scr[:, :], AF.Exp)
                with ctx.block() as blk:
                    @blk.vector
                    def _(v):
                        v.tensor_tensor(reb[bnm][i][:, :], kh[i][:, :], scr[:, :], ALU.mult)
            with ctx.block() as blk:
                @blk.vector
                def _(v):
                    o, a, b = bsub(scr, GC, GC, 32, 31, rev=True)
                    v.tensor_tensor(o, a, b, ALU.subtract)
            with ctx.block() as blk:
                @blk.scalar
                def _(s):
                    s.activation(scr[:, :], scr[:, :], AF.Exp)
            with ctx.block() as blk:
                @blk.vector
                def _(v):
                    v.tensor_tensor(reb["btl"][i][:, :], kh[i][:, :], scr[:, :], ALU.mult)
        for q4 in range(4):
            tq = slice(128 * q4, 128 * (q4 + 1))
            with ctx.block() as blk:
                @blk.tensor
                def _(t):
                    t.matmul(P0[:, 0:256], fgout[64:128, tq], wfg[64:128, :], start=True, stop=True)
            with ctx.block() as blk:
                @blk.vector
                def _(v):
                    v.tensor_tensor(gatet[:, 256 * q4:256 * (q4 + 1)], P0[:, 0:256], bg2bc[:, :], ALU.add)
        with ctx.block() as blk:
            @blk.scalar
            def _(s):
                for q4 in range(4):
                    s.activation(gatet[:, 256 * q4:256 * (q4 + 1)],
                                 gatet[:, 256 * q4:256 * (q4 + 1)], AF.Sigmoid)

        lvls = (("b4s", "aK4", "aQ4"), ("b4n", "aK4", "aQ4"),
                ("b8n", "aK8", "aQ8"), ("b16n", "aK16", "aQ16"))
        for gi in range(4):
            with ctx.block() as blk:
                @blk.tensor
                def _(t):
                    for lv, (bnm, anmk, anmq) in enumerate(lvls):
                        for cj in range(4):
                            ccols = slice(128 * gi + 32 * cj, 128 * gi + 32 * (cj + 1))
                            for h in range(4):
                                par, grp = h % 2, h // 2
                                PX = PA if par == 0 else PB
                                hs64 = slice(64 * par, 64 * (par + 1))
                                cs_ = slice(64 * lv + 32 * grp, 64 * lv + 32 * (grp + 1))
                                t.matmul(PX[32 * cj:32 * (cj + 1), cs_],
                                         reb[bnm][grp][hs64, ccols], reb[anmk][grp][hs64, ccols],
                                         start=True, stop=True, tile_position=(64 * par, 32 * cj))
                                cs2 = slice(256 + 64 * lv + 32 * grp, 256 + 64 * lv + 32 * (grp + 1))
                                t.matmul(PX[32 * cj:32 * (cj + 1), cs2],
                                         reb[bnm][grp][hs64, ccols], reb[anmq][grp][hs64, ccols],
                                         start=True, stop=True, tile_position=(64 * par, 32 * cj))
            with ctx.block() as blk:
                @blk.tensor
                def _(t):
                    for lv, (bnm, anmk, anmq) in enumerate(lvls):
                        for cj in range(4):
                            ccols = slice(128 * gi + 32 * cj, 128 * gi + 32 * (cj + 1))
                            for h in range(4):
                                par, grp = h % 2, h // 2
                                PX = PS if par == 0 else PC[0]
                                hs64 = slice(64 * par, 64 * (par + 1))
                                cs_ = slice(64 * lv + 32 * grp, 64 * lv + 32 * (grp + 1))
                                t.matmul(PX[32 * cj:32 * (cj + 1), cs_],
                                         reb[anmk][grp][hs64, ccols], reb[bnm][grp][hs64, ccols],
                                         start=True, stop=True, tile_position=(64 * par, 32 * cj))
            for par in (0, 1):
                PX = PA if par == 0 else PB
                PXp = PS if par == 0 else PC[0]
                for base, M, dst_ in ((0, msel, LtX[par]), (256, msel, ScX[par])):
                    d4m = M[:, 0:64] if base == 0 else M[:, 64:128]
                    with ctx.block() as blk:
                        @blk.vector
                        def _(v):
                            v.select(scr[:, 0:64], d4m, PX[:, base:base + 64], zero64[:, :])
                    with ctx.block() as blk:
                        @blk.vector
                        def _(v):
                            v.select(scr[:, 0:64], M[:, 128:192], PX[:, base + 64:base + 128], scr[:, 0:64])
                    with ctx.block() as blk:
                        @blk.vector
                        def _(v):
                            v.select(scr[:, 0:64], M[:, 192:256], PX[:, base + 128:base + 192], scr[:, 0:64])
                    with ctx.block() as blk:
                        @blk.vector
                        def _(v):
                            v.select(dst_[:, :], M[:, 256:320], PX[:, base + 192:base + 256], scr[:, 0:64])
                with ctx.block() as blk:
                    @blk.vector
                    def _(v):
                        v.select(scr[:, 0:64], msel2[:, 0:64], PXp[:, 0:64], zero64[:, :])
                with ctx.block() as blk:
                    @blk.vector
                    def _(v):
                        v.select(scr[:, 0:64], msel2[:, 128:192], PXp[:, 64:128], scr[:, 0:64])
                with ctx.block() as blk:
                    @blk.vector
                    def _(v):
                        v.select(scr[:, 0:64], msel2[:, 192:256], PXp[:, 128:192], scr[:, 0:64])
                with ctx.block() as blk:
                    @blk.vector
                    def _(v):
                        v.select(LpX[par][:, :], msel2[:, 256:320], PXp[:, 192:256], scr[:, 0:64])
            for par in (0, 1):
                PX = PA if par == 0 else PB
                with ctx.block() as blk:
                    @blk.vector
                    def _(v):
                        v.tensor_copy(PtX[par][:, :], LtX[par][:, :])
                        v.tensor_copy(PpX[par][:, :], LpX[par][:, :])
                with ctx.block() as blk:
                    @blk.vector
                    def _(v):
                        v.tensor_tensor(GtX[par][:, :], i32g[:, :], LtX[par][:, :], ALU.subtract)
                for lv in range(4):
                    with ctx.block() as blk:
                        @blk.tensor
                        def _(t):
                            for cj in range(4):
                                rs = slice(32 * cj, 32 * (cj + 1))
                                for grp in range(2):
                                    cg = slice(32 * grp, 32 * (grp + 1))
                                    t.matmul(PX[rs, 32 * grp:32 * (grp + 1)],
                                             PtX[par][rs, cg], PpX[par][rs, cg],
                                             start=True, stop=True, tile_position=(32 * cj, 32 * cj))
                                    t.matmul(PX[rs, 64 + 32 * grp:96 + 32 * grp],
                                             PpX[par][rs, cg], PtX[par][rs, cg],
                                             start=True, stop=True, tile_position=(32 * cj, 32 * cj))
                    with ctx.block() as blk:
                        @blk.vector
                        def _(v):
                            v.tensor_copy(PpX[par][:, :], PX[:, 0:64])
                            v.tensor_copy(PtX[par][:, :], PX[:, 64:128])
                    with ctx.block() as blk:
                        @blk.tensor
                        def _(t):
                            for cj in range(4):
                                rs = slice(32 * cj, 32 * (cj + 1))
                                for grp in range(2):
                                    cg = slice(32 * grp, 32 * (grp + 1))
                                    t.matmul(PX[rs, 128 + 32 * grp:160 + 32 * grp],
                                             i32st[rs, :], GtX[par][rs, cg],
                                             start=True, stop=False, tile_position=(32 * cj, 32 * cj))
                                    t.matmul(PX[rs, 128 + 32 * grp:160 + 32 * grp],
                                             PpX[par][rs, cg], GtX[par][rs, cg],
                                             start=False, stop=True, tile_position=(32 * cj, 32 * cj))
                    with ctx.block() as blk:
                        @blk.vector
                        def _(v):
                            v.tensor_copy(GtX[par][:, :], PX[:, 128:192])
            with ctx.block() as blk:
                @blk.tensor
                def _(t):
                    for cj in range(4):
                        ccols = slice(128 * gi + 32 * cj, 128 * gi + 32 * (cj + 1))
                        for h in range(4):
                            par, grp = h % 2, h // 2
                            PX = PA if par == 0 else PB
                            hs64 = slice(64 * par, 64 * (par + 1))
                            i64sl = i64s[hs64, :]
                            t.matmul(PX[32 * cj:32 * (cj + 1), 128 * h:128 * h + 64],
                                     vb[grp][hs64, ccols], i64sl,
                                     start=True, stop=True, tile_position=(64 * par, 32 * cj))
                            t.matmul(PX[32 * cj:32 * (cj + 1), 128 * h + 64:128 * (h + 1)],
                                     reb["aK32"][grp][hs64, ccols], i64sl,
                                     start=True, stop=True, tile_position=(64 * par, 32 * cj))
            with ctx.block() as blk:
                @blk.vector
                def _(v):
                    v.tensor_copy(bass.AP(rhsb, 512 * gi, [[2048, 128], [256, 2], [1, 128]]),
                                  bass.AP(PA, 0, [[512, 128], [256, 2], [1, 128]]))
                    v.tensor_copy(bass.AP(rhsb, 512 * gi + 128, [[2048, 128], [256, 2], [1, 128]]),
                                  bass.AP(PB, 128, [[512, 128], [256, 2], [1, 128]]))
            with ctx.block() as blk:
                @blk.tensor
                def _(t):
                    for cj in range(4):
                        ccols = slice(128 * gi + 32 * cj, 128 * gi + 32 * (cj + 1))
                        for h in range(4):
                            par, grp = h % 2, h // 2
                            PX = PA if par == 0 else PB
                            hs64 = slice(64 * par, 64 * (par + 1))
                            t.matmul(PX[32 * cj:32 * (cj + 1), 64 * h:64 * (h + 1)],
                                     reb["btl"][grp][hs64, ccols], i64s[hs64, :],
                                     start=True, stop=True, tile_position=(64 * par, 32 * cj))
            with ctx.block() as blk:
                @blk.vector
                def _(v):
                    v.tensor_copy(bass.AP(btm, 256 * gi, [[1024, 128], [128, 2], [1, 64]]),
                                  bass.AP(PA, 0, [[512, 128], [128, 2], [1, 64]]))
                    v.tensor_copy(bass.AP(btm, 256 * gi + 64, [[1024, 128], [128, 2], [1, 64]]),
                                  bass.AP(PB, 64, [[512, 128], [128, 2], [1, 64]]))
            with ctx.block() as blk:
                @blk.tensor
                def _(t):
                    for cj in range(4):
                        rs = slice(32 * cj, 32 * (cj + 1))
                        for h in range(4):
                            par, grp = h % 2, h // 2
                            t.matmul(PS[rs, 128 * h:128 * (h + 1)],
                                     GtX[par][rs, 32 * grp:32 * (grp + 1)],
                                     rhsb[rs, 512 * gi + 128 * h:512 * gi + 128 * (h + 1)],
                                     start=True, stop=True, tile_position=(32 * cj, 32 * cj))
            with ctx.block() as blk:
                @blk.vector
                def _(v):
                    v.tensor_copy(wsol[:, 512 * gi:512 * (gi + 1)], PS[:, :])
            with ctx.block() as blk:
                @blk.tensor
                def _(t):
                    for cj in range(4):
                        rs = slice(32 * cj, 32 * (cj + 1))
                        for h in range(4):
                            par, grp = h % 2, h // 2
                            t.matmul(PS[rs, 64 * h:64 * (h + 1)],
                                     ScX[par][rs, 32 * grp:32 * (grp + 1)],
                                     wsol[rs, 512 * gi + 128 * h:512 * gi + 128 * h + 64],
                                     start=True, stop=True, tile_position=(32 * cj, 32 * cj))
            with ctx.block() as blk:
                @blk.vector
                def _(v):
                    v.tensor_copy(oloc[:, 256 * gi:256 * (gi + 1)], PS[:, 0:256])
            with ctx.block() as blk:
                @blk.tensor
                def _(t):
                    for cj in range(4):
                        rs = slice(32 * cj, 32 * (cj + 1))
                        for h in range(4):
                            par, grp = h % 2, h // 2
                            t.matmul(PC[cj][64 * par:64 * (par + 1), 32 * grp:32 * (grp + 1)],
                                     wsol[rs, 512 * gi + 128 * h + 64:512 * gi + 128 * (h + 1)],
                                     ScX[par][rs, 32 * grp:32 * (grp + 1)],
                                     start=True, stop=True, tile_position=(32 * cj, 64 * par))
                            t.matmul(PC[cj][64 * par:64 * (par + 1), 64 + 128 * grp:64 + 128 * (grp + 1)],
                                     btm[rs, 256 * gi + 64 * h:256 * gi + 64 * (h + 1)],
                                     wsol[rs, 512 * gi + 128 * h:512 * gi + 128 * (h + 1)],
                                     start=True, stop=True, tile_position=(32 * cj, 64 * par))
            with ctx.block() as blk:
                @blk.tensor
                def _(t):
                    for cj in range(4):
                        for h in range(4):
                            par, grp = h % 2, h // 2
                            t.matmul(PC[cj][64 * par:64 * (par + 1), 32 * grp:32 * (grp + 1)],
                                     ni64s[64 * par:64 * (par + 1), :],
                                     reb["aQ32"][grp][64 * par:64 * (par + 1),
                                                      128 * gi + 32 * cj:128 * gi + 32 * (cj + 1)],
                                     start=False, stop=True, tile_position=(64 * par, 64 * par))
            for cj in range(4):
                ch = 4 * gi + cj
                with ctx.block() as blk:
                    @blk.vector
                    def _(v):
                        v.tensor_scalar(omt[:, 64 * ch:64 * (ch + 1)], PC[cj][:, 0:64],
                                        -1.0, None, ALU.mult)
                        v.tensor_copy(bass.AP(psig, 128 * ch, [[2048, 128], [64, 2], [1, 64]]),
                                      bass.AP(PC[cj], 64, [[384, 128], [128, 2], [1, 64]]))
                with ctx.block() as blk:
                    @blk.vector
                    def _(v):
                        v.tensor_scalar(bass.AP(phir, 128 * ch, [[2048, 128], [64, 2], [1, 64]]),
                                        bass.AP(PC[cj], 128, [[384, 128], [128, 2], [1, 64]]),
                                        -1.0, None, ALU.mult)

        smA = ctx.sems["A_sm"]
        seA = ctx.sems["A_se"]
        with ctx.block() as blk:
            @blk.tensor
            def _(t):
                for ch in range(16):
                    G = 16 * tj + ch
                    cur = G % 2
                    if G > 0:
                        t.wait_ge(seA, G)
                    last = None
                    for h in range(4):
                        par, grp = h % 2, h // 2
                        hs64 = slice(64 * par, 64 * (par + 1))
                        ccs = slice(64 * grp, 64 * (grp + 1))
                        t.matmul(PS[hs64, ccs],
                                 phir[hs64, 128 * ch + 64 * grp:128 * ch + 64 * (grp + 1)],
                                 Sst[hs64, 128 * cur + 64 * grp:128 * cur + 64 * (grp + 1)],
                                 start=True, stop=False, tile_position=(64 * par, 64 * par))
                        last = t.matmul(PS[hs64, ccs], i64s[hs64, :],
                                        psig[hs64, 128 * ch + 64 * grp:128 * ch + 64 * (grp + 1)],
                                        start=False, stop=True, tile_position=(64 * par, 64 * par))
                    last.then_inc(smA, 1)
            @blk.vector
            def _(v):
                for ch in range(16):
                    G = 16 * tj + ch
                    cur, nxt = G % 2, 1 - (G % 2)
                    v.wait_ge(smA, G + 1)
                    v.tensor_copy(Shist[:, 128 * ch:128 * (ch + 1)], Sst[:, 128 * cur:128 * (cur + 1)])
                    for grp in range(2):
                        last = v.scalar_tensor_tensor(
                            Sst[:, 128 * nxt + 64 * grp:128 * nxt + 64 * (grp + 1)],
                            Sst[:, 128 * cur + 64 * grp:128 * cur + 64 * (grp + 1)],
                            gamc[:, 2 * ch + grp:2 * ch + grp + 1],
                            PS[:, 64 * grp:64 * (grp + 1)], ALU.mult, ALU.add)
                    last.then_inc(seA, 1)
        ctx.bump("A_sm", 16)
        ctx.bump("A_se", 16)

        for gi in range(4):
            with ctx.block() as blk:
                @blk.tensor
                def _(t):
                    for cj in range(4):
                        ch = 4 * gi + cj
                        for h in range(4):
                            par, grp = h % 2, h // 2
                            PX = PA if par == 0 else PB
                            hs64 = slice(64 * par, 64 * (par + 1))
                            t.matmul(PX[32 * cj:32 * (cj + 1), 64 * grp:64 * (grp + 1)],
                                     omt[hs64, 64 * ch + 32 * grp:64 * ch + 32 * (grp + 1)],
                                     Shist[hs64, 128 * ch + 64 * grp:128 * ch + 64 * (grp + 1)],
                                     start=True, stop=True, tile_position=(64 * par, 32 * cj))
            with ctx.block() as blk:
                @blk.vector
                def _(v):
                    v.tensor_tensor(bass.AP(oarr, 0, [[256, 128], [128, 2], [1, 64]]),
                                    bass.AP(oloc, 256 * gi, [[1024, 128], [128, 2], [1, 64]]),
                                    bass.AP(PA, 0, [[512, 128], [64, 2], [1, 64]]), ALU.add)
                    v.tensor_tensor(bass.AP(oarr, 64, [[256, 128], [128, 2], [1, 64]]),
                                    bass.AP(oloc, 256 * gi + 64, [[1024, 128], [128, 2], [1, 64]]),
                                    bass.AP(PB, 0, [[512, 128], [64, 2], [1, 64]]), ALU.add)
            with ctx.block() as blk:
                @blk.scalar
                def _(s):
                    s.activation(scr[:, 0:256], oarr[:, :], AF.Square)
            with ctx.block() as blk:
                @blk.vector
                def _(v):
                    v.reduce_sum(scr2[:, 0:4], _r3(scr[:, 0:256], 64), axis=AX.X)
            with ctx.block() as blk:
                @blk.vector
                def _(v):
                    v.tensor_scalar(scr2[:, 0:4], scr2[:, 0:4], 1.0 / 64.0, 1e-5, ALU.mult, ALU.add)
            with ctx.block() as blk:
                @blk.scalar
                def _(s):
                    s.activation(scr2[:, 4:8], scr2[:, 0:4], AF.Sqrt)
            with ctx.block() as blk:
                @blk.vector
                def _(v):
                    v.reciprocal(scr2[:, 4:8], scr2[:, 4:8])
            with ctx.block() as blk:
                @blk.vector
                def _(v):
                    v.tensor_tensor(_r3(oarr[:, :], 64), _r3(oarr[:, :], 64),
                                    bass.AP(scr2, 4, [[512, 128], [1, 4], [0, 64]]), ALU.mult)
            with ctx.block() as blk:
                @blk.vector
                def _(v):
                    v.tensor_mul(oarr[:, :], oarr[:, :], onwbc[:, :])
            with ctx.block() as blk:
                @blk.vector
                def _(v):
                    v.tensor_mul(oarr[:, :], oarr[:, :], gatet[:, 256 * gi:256 * (gi + 1)])
            with ctx.block() as blk:
                @blk.tensor
                def _(t):
                    t.matmul(PS[:, 0:128], oarr[:, 0:128], sb["c_i128f"][:, :], start=True, stop=True)
                    t.matmul(PS[:, 128:256], oarr[:, 128:256], sb["c_i128f"][:, :], start=True, stop=True)
            with ctx.block() as blk:
                @blk.vector
                def _(v):
                    v.tensor_copy(ofmt[:, 128 * gi:128 * (gi + 1)], PS[:, 0:128])
                    v.tensor_copy(ofmt[:, 512 + 128 * gi:512 + 128 * (gi + 1)], PS[:, 128:256])
        with ctx.block() as blk:
            @blk.sync
            def _(sy):
                sy.dma_start(out=oscr[0:128, tc], in_=ofmt[:, 0:512]).then_inc(st, 16)
                sy.dma_start(out=oscr[128:256, tc], in_=ofmt[:, 512:1024]).then_inc(st, 16)
                sy.wait_ge(st, ctx.bump("A_st", 32))

    ctx.flush()
    _ps_stack.close()
    if do_collective:
        emit_stageA_epilogue(ctx, dst, TT)


def emit_stageA_epilogue(ctx, dst, TT):
    nc = ctx.nc
    sb = ctx.sb
    ld = ctx.sem("A_ld")
    st = ctx.sem("A_st")
    cc = ctx.sem("A_cc")
    P0 = sb["A_PSUMS"][0]
    agb = ctx.dram["agb"]
    with ctx.block() as blk:
        @blk.gpsimd
        def _(g):
            g.collective_compute(
                "AllGather", ALU.bypass,
                replica_groups=[[0, 1, 2, 3], [4, 5, 6, 7]],
                ins=[ctx.dram["oscr"].opt()], outs=[agb.opt()],
            ).then_inc(cc)
            g.wait_ge(cc, ctx.bump("A_cc", 1))
    woA = ctx.sbuf("A_wo", [128, 8 * 1024], BF16)
    og4 = ctx.sbuf("A_og4w", [128, 4 * 512], BF16)
    og = [ctx.sbuf(f"A_og{i}", [128, 512], BF16) for i in range(8)]
    winsel = ctx.sbuf("A_wsel", [128, 4], F32)
    hsres = ctx.sbuf("A_hsr", [128, 512], F32)
    yyA = [ctx.sbuf(f"A_yy{i}", [128, 512], F32) for i in range(8)]
    with ctx.block() as blk:
        @blk.gpsimd
        def _(g):
            g.dma_start(out=woA[:, :].rearrange("p (k c) -> p k c", k=8),
                        in_=ctx.dram["woA"].rearrange("(k p) c -> p k c", p=128)).then_inc(ld, 16)
        @blk.sync
        def _(sy):
            sy.dma_start(out=winsel[:, :], in_=ctx.dram["winsel"]).then_inc(ld, 16)
            sy.wait_ge(ld, ctx.bump("A_ld", 32))
    tpcn = TT // 4
    nt2 = tpcn // 512
    for n in range(nt2):
        for i in range(8):
            with ctx.block() as blk:
                @blk.sync
                def _(sy):
                    for q in range(4):
                        sy.dma_start(out=og4[:, 512 * q:512 * (q + 1)],
                                     in_=agb[128 * i:128 * (i + 1),
                                             tpcn * q + 512 * n:tpcn * q + 512 * (n + 1)]).then_inc(ld, 16)
                    sy.wait_ge(ld, ctx.bump("A_ld", 16 * 4))
            with ctx.block() as blk:
                @blk.vector
                def _(v):
                    v.tensor_scalar(og[i][:, :], og4[:, 0:512], winsel[:, 0:1], None, ALU.mult)
            for q in range(1, 4):
                with ctx.block() as blk:
                    @blk.vector
                    def _(v):
                        v.scalar_tensor_tensor(og[i][:, :], og4[:, 512 * q:512 * (q + 1)],
                                               winsel[:, q:q + 1], og[i][:, :], ALU.mult, ALU.add)
        for oc in range(8):
            with ctx.block() as blk:
                @blk.sync
                def _(sy):
                    sy.dma_start(out=hsres[:, :],
                                 in_=ctx.dram["hsres"][128 * oc:128 * (oc + 1),
                                                       512 * n:512 * (n + 1)]).then_inc(ld, 16)
                    sy.wait_ge(ld, ctx.bump("A_ld", 16))
                @blk.tensor
                def _(t):
                    for kc in range(8):
                        t.matmul(P0[:, :], woA[:, 1024 * kc + 128 * oc:1024 * kc + 128 * (oc + 1)],
                                 og[kc][:, :], start=(kc == 0), stop=(kc == 7))
            with ctx.block() as blk:
                @blk.vector
                def _(v):
                    v.tensor_add(yyA[oc][:, :], hsres[:, :], P0[:, :])
        emit_ln_fm(ctx, None, [yyA[i][:, :] for i in range(8)],
                   [yyA[i][:, :] for i in range(8)], 512)
        with ctx.block() as blk:
            @blk.sync
            def _(sy):
                for i in range(8):
                    sy.dma_start(out=ctx.dram[dst][128 * i:128 * (i + 1), 512 * n:512 * (n + 1)],
                                 in_=yyA[i][:, :]).then_inc(st, 16)
                sy.wait_ge(st, ctx.bump("A_st", 16 * 8))


def build_stageA_program(TT=T, debug=False, do_collective=True):
    nc = bass.Bass("TRN2", target_bir_lowering=False)
    stack = contextlib.ExitStack()
    ctx = Ctx(nc, stack)
    with stack:
        ctx.dram_in("hsT", [1024, TT], F32)
        ctx.dram_in("wA", [1024, 1024], F32)
        ctx.dram_in("wfg2", [128, 256], F32)
        ctx.dram_in("convw", [128, 24], F32)
        ctx.dram_in("convb", [128, 6], F32)
        ctx.dram_in("dtb2", [128, 2], F32)
        ctx.dram_in("alogc", [4, 1], F32)
        ctx.dram_in("bg2r", [1, 256], F32)
        ctx.dram_in("onw4", [1, 256], F32)
        ctx.dram_in("woA", [1024, 1024], F32)
        ctx.dram_in("hsres", [1024, TT // 4], F32)
        ctx.dram_in("winsel", [128, 4], F32)
        if debug:
            ctx.dram_out("oscr", [256, TT], BF16)
        else:
            ctx.dram_tmp("oscr", [256, TT], BF16)
        ctx.dram_tmp("agb", [1024, TT], BF16)
        ctx.dram_out("x0", [1024, TT // 4], F32)
        for name, arr in _const_arrays().items():
            ctx.dram_in(name, arr.shape, I32 if arr.dtype == np.int32 else F32)
        _preload_consts(ctx)
        emit_stageA(ctx, dst="x0", TT=TT, do_collective=do_collective)
        ctx.flush()
    return nc


def prep_stageA_inputs(inp, core, TT=T):
    b, hq = core // 4, core % 4
    hcols = slice(256 * hq, 256 * (hq + 1))
    hs = inp["L0_attn_hidden_states"][b][:TT]
    wA = np.zeros((1024, 1024), np.float32)
    wA[:, 0:256] = inp["L0_attn_Wq"][:, hcols]
    wA[:, 256:512] = inp["L0_attn_Wk"][:, hcols]
    wA[:, 512:768] = inp["L0_attn_Wv"][:, hcols]
    wA[:, 768:832] = inp["L0_attn_Wf1"]
    wA[:, 832:896] = inp["L0_attn_Wg1"]
    wA[:, 896:900] = inp["L0_attn_Wb"][:, 4 * hq:4 * (hq + 1)]
    wfg2 = np.concatenate([inp["L0_attn_Wf2"][:, hcols], inp["L0_attn_Wg2"][:, hcols]], axis=0)
    cw = np.zeros((128, 24), np.float32)
    cb = np.zeros((128, 6), np.float32)
    for i in range(2):
        rows = slice(256 * hq + 128 * i, 256 * hq + 128 * (i + 1))
        cw[:, 12 * i + 0:12 * i + 4] = inp["L0_attn_qcw"][rows]
        cw[:, 12 * i + 4:12 * i + 8] = inp["L0_attn_kcw"][rows]
        cw[:, 12 * i + 8:12 * i + 12] = inp["L0_attn_vcw"][rows]
        cb[:, 3 * i + 0] = inp["L0_attn_qcb"][rows]
        cb[:, 3 * i + 1] = inp["L0_attn_kcb"][rows]
        cb[:, 3 * i + 2] = inp["L0_attn_vcb"][rows]
    win = TT // 4
    im = {
        "hsT": np.ascontiguousarray(hs.T),
        "wA": wA,
        "wfg2": np.ascontiguousarray(wfg2),
        "convw": cw,
        "convb": cb,
        "dtb2": np.ascontiguousarray(inp["L0_attn_dt_bias"][hcols].reshape(2, 128).T),
        "alogc": np.ascontiguousarray(inp["L0_attn_A_log"][4 * hq:4 * (hq + 1)][:, None]),
        "bg2r": np.ascontiguousarray(inp["L0_attn_bg2"][hcols][None, :]),
        "onw4": np.ascontiguousarray(np.tile(inp["L0_attn_o_norm_w"], 4)[None, :]),
        "woA": np.ascontiguousarray(inp["L0_attn_Wo"]),
        "hsres": np.ascontiguousarray(hs[win * hq:win * (hq + 1)].T),
        "winsel": np.ascontiguousarray(np.repeat((np.arange(4) == hq).astype(np.float32)[None, :], 128, 0)),
    }
    im.update(_const_arrays())
    return im


def _const_arrays():
    c = {}
    c["cd_ones128"] = np.ones((128, 1), np.float32)
    c["cd_ones1"] = np.ones((1, 128), np.float32)
    indqk = np.zeros((128, 128), np.float32)
    for i in range(8):
        indqk[0:64, 16 * i + 2 * i] = 1.0
        indqk[64:128, 16 * i + 2 * i + 1] = 1.0
    c["cd_indqk"] = indqk
    bc16 = np.zeros((16, 1024), np.float32)
    for i in range(8):
        bc16[2 * i, 128 * i:128 * i + 64] = 1.0
        bc16[2 * i + 1, 128 * i + 64:128 * (i + 1)] = 1.0
    c["cd_bc16"] = bc16
    hsel = np.zeros((16, 4), np.float32)
    for h in range(16):
        hsel[h, h % 4] = 1.0
    c["cd_hsel"] = hsel
    phm = np.zeros((16, 512), np.float32)
    t = np.arange(512)
    for h in range(16):
        phm[h, t % 4 == h % 4] = 1.0
    c["cd_phmask"] = phm
    c["cd_i128"] = np.eye(128, dtype=np.float32)
    non2 = np.zeros((128, 128), np.float32)
    non2[0, :] = -1.0
    non2[64, :] = -1.0
    c["cd_nones2"] = non2
    ind2 = np.zeros((128, 2), np.float32)
    ind2[0:64, 0] = 1.0
    ind2[64:128, 1] = 1.0
    c["cd_indh2"] = ind2
    c["cd_indh2T"] = ind2.T.copy()
    wm = np.zeros((128, 8 * 512), np.float32)
    kp = np.arange(128)[:, None]
    qfr = np.arange(512)[None, :]
    for kt in range(8):
        diff = qfr + 512 - 128 * kt - kp
        wm[:, 512 * kt:512 * (kt + 1)] = np.where((diff >= 0) & (diff <= 512), 0.0, -32768.0)
    c["cd_wmA"] = wm
    lo = np.zeros_like(wm)
    lo[64:128, :] = wm[0:64, :]
    c["cd_wmBlo"] = lo
    e32 = np.eye(32, dtype=np.float32)
    c["cd_i32g"] = np.tile(e32, (4, 2))
    c["cd_i32st"] = np.tile(e32, (4, 1))
    c["cd_i64st"] = np.tile(np.eye(64, dtype=np.float32), (2, 1))
    c["cd_ni64st"] = -np.tile(np.eye(64, dtype=np.float32), (2, 1))
    idx = np.arange(32)
    J, I = np.meshgrid(idx, idx, indexing="ij")
    b4 = idx // 4; b8 = idx // 8; b16 = idx // 16
    mD4s = ((b4[J] == b4[I]) & (J < I)).astype(np.float32)
    mD4i = ((b4[J] == b4[I]) & (J <= I)).astype(np.float32)
    mO4 = (b4[I] == b4[J] + 1).astype(np.float32)
    mO8 = ((b8[I] == b8[J] + 1) & (b4[I] >= b4[J] + 2)).astype(np.float32)
    mO16 = ((b16[I] == b16[J] + 1) & (b8[I] >= b8[J] + 2)).astype(np.float32)
    def tile42(m):
        return np.tile(m, (4, 2))
    c["cd_msel"] = np.concatenate([tile42(mD4s), tile42(mD4i), tile42(mO4),
                                   tile42(mO8), tile42(mO16)], axis=1).astype(np.int32)
    c["cd_msel2"] = np.concatenate([tile42(mD4s.T), tile42(mD4i.T), tile42(mO4.T),
                                    tile42(mO8.T), tile42(mO16.T)], axis=1).astype(np.int32)
    rst = np.ones((128, 512), np.float32)
    rst[:, ::32] = 0.0
    c["cd_reset"] = rst
    bc4 = np.zeros((4, 256), np.float32)
    for t_ in range(2):
        for p_ in range(128):
            bc4[2 * t_ + p_ // 64, 128 * t_ + p_] = 1.0
    c["cd_bc4"] = bc4
    c["cd_nbc4"] = -bc4
    return c


def _preload_consts(ctx):
    nc = ctx.nc
    ld = ctx.sem("c_ld")
    co = ctx.sbuf("c_ones128", [128, 1], F32)
    c1 = ctx.sbuf("c_ones1", [1, 128], F32)
    i128f = ctx.sbuf("c_i128f", [128, 128], F32)
    i128b = ctx.sbuf("c_i128b", [128, 128], BF16)
    nones2 = ctx.sbuf("c_nones2", [128, 128], BF16)
    onesb = ctx.sbuf("c_onesb", [128, 1], BF16)
    indh2 = ctx.sbuf("c_indh2", [128, 2], F32)
    indh2T = ctx.sbuf("c_indh2T", [2, 128], F32)
    with ctx.block() as blk:
        @blk.sync
        def _(sy):
            sy.dma_start(out=co[:, :], in_=ctx.dram["cd_ones128"]).then_inc(ld, 16)
            sy.dma_start(out=c1[:, :], in_=ctx.dram["cd_ones1"]).then_inc(ld, 16)
            sy.dma_start(out=i128f[:, :], in_=ctx.dram["cd_i128"]).then_inc(ld, 16)
            sy.dma_start(out=indh2[:, :], in_=ctx.dram["cd_indh2"]).then_inc(ld, 16)
            sy.dma_start(out=indh2T[:, :], in_=ctx.dram["cd_indh2T"]).then_inc(ld, 16)
        @blk.gpsimd
        def _(g):
            g.dma_start(out=i128b[:, :], in_=ctx.dram["cd_i128"]).then_inc(ld, 16)
            g.dma_start(out=nones2[:, :], in_=ctx.dram["cd_nones2"]).then_inc(ld, 16)
            g.dma_start(out=onesb[:, :], in_=ctx.dram["cd_ones128"]).then_inc(ld, 16)
            g.wait_ge(ld, ctx.bump("c_ld", 16 * 8))


def build_stageB_program():
    nc = bass.Bass("TRN2", target_bir_lowering=False)
    stack = contextlib.ExitStack()
    ctx = Ctx(nc, stack)
    with stack:
        ctx.dram_in("x0", [1024, TPC], F32)
        ctx.dram_in("wgT", [1024, 3072], I32)
        ctx.dram_in("wuT", [1024, 3072], I32)
        ctx.dram_in("wdT", [3072, 1024], I32)
        ctx.dram_out("x1", [1024, TPC], F32)
        for name, arr in _const_arrays().items():
            ctx.dram_in(name, arr.shape, I32 if arr.dtype == np.int32 else F32)
        _preload_consts(ctx)
        emit_stageB(ctx)
        ctx.flush()
    return nc


def _emit_stageD_masks(ctx):
    nc = ctx.nc
    ctx.sbuf("D_wmA", [128, 8 * 512], BF16)
    ctx.sbuf("D_wmBlo", [128, 8 * 512], BF16)
    ctx.sbuf("D_pmA", [128, 4 * 512], BF16)
    ctx.sbuf("D_pmBlo", [128, 4 * 512], BF16)
    ctx.sbuf("D_hmask", [128, 1], F32)
    ld = ctx.sem("D_ld")
    with ctx.block() as blk:
        @blk.gpsimd
        def _(g):
            for nm in ("wmA", "wmBlo"):
                g.dma_start(out=ctx.sb[f"D_{nm}"][:, :], in_=ctx.dram[f"cd_{nm}"]).then_inc(ld, 16)
            for nm in ("pmA", "pmBlo"):
                g.dma_start(out=ctx.sb[f"D_{nm}"][:, :], in_=ctx.dram[nm]).then_inc(ld, 16)
        @blk.sync
        def _(sy):
            sy.dma_start(out=ctx.sb["D_hmask"][:, :], in_=ctx.dram["hmask"]).then_inc(ld, 16)
            sy.wait_ge(ld, ctx.bump("D_ld", 16 * 5))


def build_stageDa_program(l):
    nc = bass.Bass("TRN2", target_bir_lowering=False)
    stack = contextlib.ExitStack()
    ctx = Ctx(nc, stack)
    with stack:
        ctx.dram_in("xsrc", [1024, TPC], F32)
        ctx.dram_in("hl", [1024, HB], F32)
        ctx.dram_in("hmask", [128, 1], F32)
        ctx.dram_in("pmA", [128, 4 * 512], F32)
        ctx.dram_in("pmBlo", [128, 4 * 512], F32)
        ctx.dram_in(f"wqkvD{l}", [1024, 1536], F32)
        ctx.dram_in(f"woD{l}", [512, 1024], F32)
        ctx.dram_in(f"rotq{l}", [128, 128], F32)
        ctx.dram_in(f"rotk{l}", [128, 128], F32)
        for nm in ("csq", "ssq", "csk", "ssk"):
            ctx.dram_in(f"{nm}{l}", [128, NQK], F32)
        ctx.dram_out("xm", [1024, TPC], F32)
        for name, arr in _const_arrays().items():
            ctx.dram_in(name, arr.shape, I32 if arr.dtype == np.int32 else F32)
        _preload_consts(ctx)
        for nm in ("D_P0", "D_P1", "D_P2", "D_P3"):
            ctx.sb[nm] = ctx.psum(nm, [128, 512], F32)
        _emit_stageD_masks(ctx)
        emit_stageD_attn(ctx, l, "xsrc", "hl", "xm")
        ctx.flush()
    return nc


def build_stageDf_program(l, final=False):
    nc = bass.Bass("TRN2", target_bir_lowering=False)
    stack = contextlib.ExitStack()
    ctx = Ctx(nc, stack)
    with stack:
        ctx.dram_in("xm", [1024, TPC], F32)
        ctx.dram_in("xmprevD", [128, 8], F32)
        ctx.dram_in("hmask", [128, 1], F32)
        ctx.dram_in(f"w1D{l}", [1024, 5632], F32)
        ctx.dram_in(f"w2D{l}", [2816, 1024], F32)
        ctx.dram_in(f"b1D{l}", [128, 44], F32)
        ctx.dram_in(f"b2D{l}", [128, 8], F32)
        ctx.dram_in(f"lngD{l}", [128, 8], F32)
        ctx.dram_in(f"lnbD{l}", [128, 8], F32)
        if final:
            ctx.dram_in("x2in", [1024, TPC], F32)
            ctx.dram_tmp("x3o", [1024, TPC], F32)
        ctx.dram_out("xout", [1024, TPC], F32)
        for name, arr in _const_arrays().items():
            ctx.dram_in(name, arr.shape, I32 if arr.dtype == np.int32 else F32)
        _preload_consts(ctx)
        ctx.sb["D_P0"] = ctx.psum("D_P0", [128, 512], F32)
        ctx.sbuf("D_hmask", [128, 1], F32)
        ld = ctx.sem("D_ld")
        with ctx.block() as blk:
            @blk.sync
            def _(sy):
                sy.dma_start(out=ctx.sb["D_hmask"][:, :], in_=ctx.dram["hmask"]).then_inc(ld, 16)
                sy.wait_ge(ld, ctx.bump("D_ld", 16))
        if final:
            emit_stageD_ffn(ctx, l, "xmprevD", "xm", "x3o")
            emit_final_ln(ctx, "x2in", "x3o", "xout")
        else:
            emit_stageD_ffn(ctx, l, "xmprevD", "xm", "xout")
        ctx.flush()
    return nc


def _prep_B(inputs):
    return {
        "wgT": np.ascontiguousarray(inputs["L0_ffn_w_gate"].T),
        "wuT": np.ascontiguousarray(inputs["L0_ffn_w_up"].T),
        "wdT": np.ascontiguousarray(inputs["L0_ffn_w_down"].T),
    }


def _prep_C(inputs):
    return {
        "wC": np.ascontiguousarray(np.concatenate(
            [inputs["L1_attn_Wq"], inputs["L1_attn_Wk"], inputs["L1_attn_Wv"]], axis=1)),
        "bC": np.ascontiguousarray(np.concatenate(
            [inputs["L1_attn_bq"], inputs["L1_attn_bk"], inputs["L1_attn_bv"]]).reshape(24, 128).T),
        "woC": np.ascontiguousarray(inputs["L1_attn_Wo"]),
        "boC": np.ascontiguousarray(inputs["L1_attn_bo"].reshape(8, 128).T),
    }


def _prep_D_shared(inputs, l):
    w1 = inputs["L1_ffn_w1"][l]; b1 = inputs["L1_ffn_b1"][l]
    w2 = inputs["L1_ffn_w2"][l]; b2 = inputs["L1_ffn_b2"][l]
    w1a = np.zeros((1024, 2816), np.float32); w1g = np.zeros((1024, 2816), np.float32)
    w1a[:, 0:2730] = w1[:, 0:2730]; w1g[:, 0:2730] = w1[:, 2730:5460]
    b1a = np.zeros(2816, np.float32); b1g = np.zeros(2816, np.float32)
    b1a[0:2730] = b1[0:2730]; b1g[0:2730] = b1[2730:5460]
    w1p = np.zeros((1024, 5632), np.float32)
    b1p = np.zeros(5632, np.float32)
    for i in range(22):
        w1p[:, 256 * i:256 * i + 128] = w1a[:, 128 * i:128 * (i + 1)]
        w1p[:, 256 * i + 128:256 * (i + 1)] = w1g[:, 128 * i:128 * (i + 1)]
        b1p[256 * i:256 * i + 128] = b1a[128 * i:128 * (i + 1)]
        b1p[256 * i + 128:256 * (i + 1)] = b1g[128 * i:128 * (i + 1)]
    w2p = np.zeros((2816, 1024), np.float32)
    w2p[0:2730] = w2
    return {
        f"wqkvD{l}": np.ascontiguousarray(inputs["L1_ffn_wqkv"][l]),
        f"woD{l}": np.ascontiguousarray(inputs["L1_ffn_wo"][l]),
        f"w1D{l}": w1p,
        f"w2D{l}": w2p,
        f"b1D{l}": np.ascontiguousarray(b1p.reshape(44, 128).T),
        f"b2D{l}": np.ascontiguousarray(b2.reshape(8, 128).T),
        f"lngD{l}": np.ascontiguousarray(inputs["L1_ffn_ln_g"][l].reshape(8, 128).T),
        f"lnbD{l}": np.ascontiguousarray(inputs["L1_ffn_ln_b"][l].reshape(8, 128).T),
    }


def _prep_D_core(inputs, l, core):
    q = core % 4
    t0 = TPC * q
    DIM_HEAD = 64; WINDOW = 512
    tpos = np.arange(t0 - 512, t0 + TPC, dtype=np.float64)
    half_idx = np.arange(0, DIM_HEAD, 2, dtype=np.float64)
    inv_freq = 1.0 / 10000.0 ** (half_idx / DIM_HEAD)
    freqs = np.concatenate([tpos[:, None] * inv_freq[None, :]] * 2, axis=-1)
    base = (half_idx + 0.4 * DIM_HEAD) / (1.4 * DIM_HEAD)
    power = (tpos - T // 2) / float(WINDOW // 2)
    scale = np.concatenate([base[None, :] ** power[:, None]] * 2, axis=-1)
    cos, sin = np.cos(freqs), np.sin(freqs)
    qs = inputs["L1_ffn_q_scale"][l]; ks = inputs["L1_ffn_k_scale"][l]
    csq = (cos * scale * 8.0).T * qs[:, None]
    ssq = (sin * scale * 8.0).T
    csk = (cos / scale).T * ks[:, None]
    ssk = (sin / scale).T
    R = np.zeros((64, 64), np.float32)
    for d_ in range(32):
        R[d_, d_ + 32] = -1.0
        R[d_ + 32, d_] = 1.0
    def bd(m):
        z = np.zeros((128, 128), np.float32)
        z[0:64, 0:64] = m.T
        z[64:128, 64:128] = m.T
        return z
    return {
        f"rotq{l}": bd(R * qs[None, :]),
        f"rotk{l}": bd(R * ks[None, :]),
        f"csq{l}": np.ascontiguousarray(np.tile(csq, (2, 1)).astype(np.float32)),
        f"ssq{l}": np.ascontiguousarray(np.tile(ssq, (2, 1)).astype(np.float32)),
        f"csk{l}": np.ascontiguousarray(np.tile(csk, (2, 1)).astype(np.float32)),
        f"ssk{l}": np.ascontiguousarray(np.tile(ssk, (2, 1)).astype(np.float32)),
    }


def _halo_of(xfull_fm, core):
    q = core % 4
    t0 = TPC * q
    h = np.zeros((1024, HB), np.float32)
    if q > 0:
        h[:] = xfull_fm[:, t0 - HB:t0]
    return h


def _xmprev_of(xm_fm_prevcore, core):
    v = np.zeros((128, 8), np.float32)
    if core % 4 > 0:
        v[:] = xm_fm_prevcore[:, -1].reshape(8, 128).T
    return v


def _pads_of(core):
    pm = np.zeros((128, 4 * 512), np.float32)
    if core % 4 == 0:
        pm[:] = -32768.0
    pmlo = np.zeros_like(pm)
    pmlo[64:128] = pm[0:64]
    return pm, pmlo


def build_fused_program():
    nc = bass.Bass("TRN2", target_bir_lowering=False)
    stack = contextlib.ExitStack()
    ctx = Ctx(nc, stack)
    with stack:
        ctx.dram_in("hsT", [1024, T], F32)
        ctx.dram_in("wA", [1024, 1024], F32)
        ctx.dram_in("wfg2", [128, 256], F32)
        ctx.dram_in("convw", [128, 24], F32)
        ctx.dram_in("convb", [128, 6], F32)
        ctx.dram_in("dtb2", [128, 2], F32)
        ctx.dram_in("alogc", [4, 1], F32)
        ctx.dram_in("bg2r", [1, 256], F32)
        ctx.dram_in("onw4", [1, 256], F32)
        ctx.dram_in("woA", [1024, 1024], F32)
        ctx.dram_in("hsres", [1024, TPC], F32)
        ctx.dram_in("winsel", [128, 4], F32)
        ctx.dram_in("hselD", [128, 4], F32)
        ctx.dram_in("hmask", [128, 1], F32)
        ctx.dram_in("pmA", [128, 4 * 512], F32)
        ctx.dram_in("pmBlo", [128, 4 * 512], F32)
        ctx.dram_in("wgT", [1024, 3072], I32)
        ctx.dram_in("wuT", [1024, 3072], I32)
        ctx.dram_in("wdT", [3072, 1024], I32)
        ctx.dram_in("wC", [1024, 3072], F32)
        ctx.dram_in("bC", [128, 24], F32)
        ctx.dram_in("woC", [1024, 1024], F32)
        ctx.dram_in("boC", [128, 8], F32)
        for l in range(2):
            ctx.dram_in(f"wqkvD{l}", [1024, 1536], F32)
            ctx.dram_in(f"woD{l}", [512, 1024], F32)
            ctx.dram_in(f"w1D{l}", [1024, 5632], F32)
            ctx.dram_in(f"w2D{l}", [2816, 1024], F32)
            ctx.dram_in(f"b1D{l}", [128, 44], F32)
            ctx.dram_in(f"b2D{l}", [128, 8], F32)
            ctx.dram_in(f"lngD{l}", [128, 8], F32)
            ctx.dram_in(f"lnbD{l}", [128, 8], F32)
            ctx.dram_in(f"rotq{l}", [128, 128], F32)
            ctx.dram_in(f"rotk{l}", [128, 128], F32)
            for nm in ("csq", "ssq", "csk", "ssk"):
                ctx.dram_in(f"{nm}{l}", [128, NQK], F32)
        for name, arr in _const_arrays().items():
            ctx.dram_in(name, arr.shape, I32 if arr.dtype == np.int32 else F32)
        ctx.dram_tmp("oscr", [256, T], BF16)
        ctx.dram_tmp("agb", [1024, T], BF16)
        for nm in ("x0s", "x1s", "x2s", "xmscr", "x3s", "x4s"):
            ctx.dram_tmp(nm, [1024, TPC], F32)
        ctx.dram_tmp("h2pub", [1024, HB], F32)
        ctx.dram_tmp("h2ag", [4096, HB], F32)
        ctx.dram_tmp("h3pub", [1024, HB], F32)
        ctx.dram_tmp("h3ag", [4096, HB], F32)
        ctx.dram_tmp("xmpub0", [128, 8], F32)
        ctx.dram_tmp("xmag0", [512, 8], F32)
        ctx.dram_tmp("xmpub1", [128, 8], F32)
        ctx.dram_tmp("xmag1", [512, 8], F32)
        ctx.dram_out("out", [1024, TPC], F32)

        _preload_consts(ctx)
        ccs = ctx.sem("F_cc")
        ldf = ctx.sem("F_ld")

        with contextlib.ExitStack() as stA:
            emit_stageA(ctx.sub(stA), dst="x0s", TT=T, do_collective=True)
            ctx.flush()
        with contextlib.ExitStack() as stB:
            emit_stageB(ctx.sub(stB), src="x0s", dst="x1s")
            ctx.flush()
        with contextlib.ExitStack() as stC:
            emit_stageC(ctx.sub(stC), src="x1s", dst="x2s")
            ctx.flush()
        with contextlib.ExitStack() as stD:
            cD = ctx.sub(stD)
            for nm in ("D_P0", "D_P1", "D_P2", "D_P3"):
                cD.sb[nm] = cD.psum(nm, [128, 512], F32)
            _emit_stageD_masks(cD)
            cD.sbuf("D_hsel", [128, 4], F32)
            cD.sbuf("D_xlast", [128, 8], F32)
            xmprev_sb = cD.sbuf("D_xmprev", [128, 8], F32)
            xmcand = cD.sbuf("D_xmcand", [128, 32], F32)
            with ctx.block() as blk:
                @blk.sync
                def _(sy):
                    sy.dma_start(out=cD.sb["D_hsel"][:, :], in_=ctx.dram["hselD"]).then_inc(ldf, 16)
                    sy.wait_ge(ldf, ctx.bump("F_ld", 16))
            for l in range(2):
                src = "x2s" if l == 0 else "x3s"
                hpub, hag = (f"h2pub", "h2ag") if l == 0 else ("h3pub", "h3ag")
                xmp, xma = (f"xmpub{l}", f"xmag{l}")
                with ctx.block() as blk:
                    @blk.sync
                    def _(sy):
                        sy.dma_start(out=ctx.dram[hpub], in_=ctx.dram[src][:, TPC - HB:TPC]).then_inc(ldf, 16)
                        sy.wait_ge(ldf, ctx.bump("F_ld", 16))
                with ctx.block() as blk:
                    @blk.gpsimd
                    def _(g):
                        g.collective_compute(
                            "AllGather", ALU.bypass,
                            replica_groups=[[0, 1, 2, 3], [4, 5, 6, 7]],
                            ins=[ctx.dram[hpub].opt()], outs=[ctx.dram[hag].opt()],
                        ).then_inc(ccs)
                        g.wait_ge(ccs, ctx.bump("F_cc", 1))
                with contextlib.ExitStack() as stDa:
                    emit_stageD_attn(cD.sub(stDa), l, src, None, "xmscr",
                                     halo_ag=hag, xmpub=xmp)
                    ctx.flush()
                with ctx.block() as blk:
                    @blk.gpsimd
                    def _(g):
                        g.collective_compute(
                            "AllGather", ALU.bypass,
                            replica_groups=[[0, 1, 2, 3], [4, 5, 6, 7]],
                            ins=[ctx.dram[xmp].opt()], outs=[ctx.dram[xma].opt()],
                        ).then_inc(ccs)
                        g.wait_ge(ccs, ctx.bump("F_cc", 1))
                with ctx.block() as blk:
                    @blk.sync
                    def _(sy):
                        for q in range(4):
                            sy.dma_start(out=xmcand[:, 8 * q:8 * (q + 1)],
                                         in_=ctx.dram[xma][128 * q:128 * (q + 1), :]).then_inc(ldf, 16)
                        sy.wait_ge(ldf, ctx.bump("F_ld", 16 * 4))
                with ctx.block() as blk:
                    @blk.vector
                    def _(v):
                        v.tensor_scalar(xmprev_sb[:, :], xmcand[:, 0:8], cD.sb["D_hsel"][:, 0:1], None, ALU.mult)
                for q in range(1, 4):
                    with ctx.block() as blk:
                        @blk.vector
                        def _(v):
                            v.scalar_tensor_tensor(xmprev_sb[:, :], xmcand[:, 8 * q:8 * (q + 1)],
                                                   cD.sb["D_hsel"][:, q:q + 1], xmprev_sb[:, :],
                                                   ALU.mult, ALU.add)
                with contextlib.ExitStack() as stDf:
                    emit_stageD_ffn(cD.sub(stDf), l, None, "xmscr",
                                    "x3s" if l == 0 else "x4s", xmprev_sb=xmprev_sb)
                    ctx.flush()
        with contextlib.ExitStack() as stF:
            emit_final_ln(ctx.sub(stF), "x2s", "x4s", "out")
            ctx.flush()
    return nc


def prep_fused_inputs(inputs, core):
    im = prep_stageA_inputs(inputs, core)
    im.update(_prep_B(inputs))
    im.update(_prep_C(inputs))
    q = core % 4
    im["hmask"] = np.full((128, 1), 0.0 if q == 0 else 1.0, np.float32)
    hsel = np.zeros((128, 4), np.float32)
    if q > 0:
        hsel[:, q - 1] = 1.0
    im["hselD"] = hsel
    pm, pmlo = _pads_of(core)
    im["pmA"] = pm
    im["pmBlo"] = pmlo
    for l in range(2):
        im.update(_prep_D_shared(inputs, l))
        im.update(_prep_D_core(inputs, l, core))
    return im


def kernel(**inputs):
    cores = list(range(8))
    nc = build_fused_program()
    in_maps = [prep_fused_inputs(inputs, c) for c in cores]
    res = run_bass_kernel_spmd(nc, in_maps, core_ids=cores)
    outa = np.empty((2, T, D), np.float32)
    for c in cores:
        b, q = c // 4, c % 4
        outa[b, TPC * q:TPC * (q + 1)] = np.asarray(res.results[c]["out"]).T
    return outa


def kernel_multi(**inputs):
    consts = _const_arrays()
    cores = list(range(8))

    ncA = build_stageA_program()
    imsA = [prep_stageA_inputs(inputs, c) for c in cores]
    resA = run_bass_kernel_spmd(ncA, imsA, core_ids=cores)
    x0 = [np.asarray(resA.results[c]["x0"]) for c in cores]

    ncB = build_stageB_program()
    wB = _prep_B(inputs)
    imsB = [{**consts, **wB, "x0": x0[c]} for c in cores]
    resB = run_bass_kernel_spmd(ncB, imsB, core_ids=cores)
    x1 = [np.asarray(resB.results[c]["x1"]) for c in cores]

    ncC = build_stageC_program()
    wC = _prep_C(inputs)
    imsC = [{**consts, **wC, "x1": x1[c]} for c in cores]
    resC = run_bass_kernel_spmd(ncC, imsC, core_ids=cores)
    x2 = [np.asarray(resC.results[c]["x2"]) for c in cores]
    x2full = [np.concatenate(x2[0:4], axis=1), np.concatenate(x2[4:8], axis=1)]

    out = {}
    xcur = x2
    xcurfull = x2full
    for l in range(2):
        ncDa = build_stageDa_program(l)
        wDl = _prep_D_shared(inputs, l)
        ims = []
        for c in cores:
            pm, pmlo = _pads_of(c)
            ims.append({**consts,
                        "xsrc": xcur[c],
                        "hl": _halo_of(xcurfull[c // 4], c),
                        "hmask": np.full((128, 1), 0.0 if c % 4 == 0 else 1.0, np.float32),
                        "pmA": pm, "pmBlo": pmlo,
                        f"wqkvD{l}": wDl[f"wqkvD{l}"], f"woD{l}": wDl[f"woD{l}"],
                        **_prep_D_core(inputs, l, c)})
        resDa = run_bass_kernel_spmd(ncDa, ims, core_ids=cores)
        xm = [np.asarray(resDa.results[c]["xm"]) for c in cores]

        final = (l == 1)
        ncDf = build_stageDf_program(l, final=final)
        ims = []
        for c in cores:
            d = {**consts,
                 "xm": xm[c],
                 "xmprevD": _xmprev_of(xm[c - 1] if c % 4 > 0 else None, c),
                 "hmask": np.full((128, 1), 0.0 if c % 4 == 0 else 1.0, np.float32)}
            for k_ in (f"w1D{l}", f"w2D{l}", f"b1D{l}", f"b2D{l}", f"lngD{l}", f"lnbD{l}"):
                d[k_] = wDl[k_]
            if final:
                d["x2in"] = x2[c]
            ims.append(d)
        resDf = run_bass_kernel_spmd(ncDf, ims, core_ids=cores)
        xn = [np.asarray(resDf.results[c]["xout"]) for c in cores]
        xcur = xn
        xcurfull = [np.concatenate(xn[0:4], axis=1), np.concatenate(xn[4:8], axis=1)]

    outa = np.empty((2, T, D), np.float32)
    for c in cores:
        b, q = c // 4, c % 4
        outa[b, TPC * q:TPC * (q + 1)] = xcur[c].T
    return outa


# version 04 (final)
# speedup vs baseline: 1.2711x; 1.0600x over previous
import math
import contextlib
import numpy as np
import concourse.bass as bass
import concourse.mybir as mybir
from concourse.bass_utils import run_bass_kernel_spmd

SOFT_N = 10**9
STEPLOG = []
STEPLOG2 = []
DEPMODE = None
DEPFLUSHN = 1
DEPFROM = 0
DEPSTEPN = 10**9
BODYIO = {}
FENCES = set()
DEPS = None
ENGS5 = ("tensor", "vector", "scalar", "gpsimd", "sync")


def _inst_io(binst):
    rd = set(); wr = set()
    try:
        for o in binst.ins:
            try: rd.add(str(o.memsetref))
            except Exception: pass
    except Exception: pass
    try:
        for o in binst.outs:
            try: wr.add(str(o.memsetref))
            except Exception: pass
    except Exception: pass
    return rd, wr


def compute_deps():
    global DEPS
    import bisect as _bis
    lastw = {}; readers = {}; semwaiters = {}; semwg = {}
    cnt = {e: 0 for e in ENGS5}
    floor = {e: 0 for e in ENGS5}
    deps = {}
    fl = sorted(FENCES); fi = 0
    for (g, e) in sorted(BODYIO.keys(), key=lambda k: (k[0], ENGS5.index(k[1]))):
        while fi < len(fl) and fl[fi] <= g:
            floor = dict(cnt)
            fi += 1
        rd, wr, incs, waits, hasreal = BODYIO[(g, e)]
        dep = dict(floor)
        def add(t):
            if t is None: return
            e2, o2, g2, hr2, incs2 = t
            if g2 != g and o2 > dep.get(e2, 0):
                dep[e2] = o2
            if not hr2:
                for sm in incs2:
                    ws = semwaiters.get(sm)
                    if not ws: continue
                    j = _bis.bisect_left(semwg[sm], g2)
                    if j < len(ws):
                        we, wo, wg = ws[j]
                        if wg != g and wo > dep.get(we, 0): dep[we] = wo
        for n in rd: add(lastw.get(n))
        for n in wr:
            add(lastw.get(n))
            for t in readers.get(n, []): add(t)
        for sm in incs:
            for (we, wo, wg) in semwaiters.get(sm, []):
                if wg != g and wo > dep.get(we, 0): dep[we] = wo
        o = cnt[e] + 1; cnt[e] = o
        me = (e, o, g, hasreal, incs)
        for n in wr:
            lastw[n] = me; readers[n] = []
        for n in rd: readers.setdefault(n, []).append(me)
        for sm in waits:
            semwaiters.setdefault(sm, []).append((e, o, g))
            semwg.setdefault(sm, []).append(g)
        deps[(g, e)] = {k: v for k, v in dep.items() if v > 0}
    DEPS = deps

F32 = mybir.dt.float32
BF16 = mybir.dt.bfloat16
I32 = mybir.dt.int32
AF = mybir.ActivationFunctionType
ALU = mybir.AluOpType
AX = mybir.AxisListType

B, T, D = 2, 8192, 1024
_LN_UID = 0
NCORE = 8
TPC = T // 4
NT = 512
H, HK, HV = 16, 64, 64
HPC = 4


class _MockInst:
    __slots__ = ("ctx", "is_dma", "upds")

    def __init__(self, ctx, is_dma):
        self.ctx = ctx
        self.is_dma = is_dma
        self.upds = []

    def then_inc(self, sem, value=1, **k):
        ab = self.ctx._mockabs
        key = id(sem)
        ab[key] = ab.get(key, 0) + value
        self.upds.append((sem, ab[key]))
        return self


class _MockEng:
    def __init__(self, ctx):
        self.ctx = ctx
        self.last_real = None

    def __getattr__(self, nm):
        def f(*a, **k):
            mi = _MockInst(self.ctx, "dma" in nm)
            if not mi.is_dma:
                self.last_real = mi
            return mi
        return f


class _EngProxy:
    def __init__(self, eng):
        self._eng = eng
        self.last_real = None
        self.insts = []
        self.waited = set()

    def __getattr__(self, nm):
        a = getattr(self._eng, nm)
        if callable(a):
            def wrap(*args, **kw):
                r = a(*args, **kw)
                if r is not None and hasattr(r, "then_inc"):
                    if "dma" not in nm:
                        self.last_real = r
                    if DEPMODE == "record":
                        self.insts.append(r)
                        if nm == "wait_ge" and args:
                            try:
                                self.waited.add(args[0].num)
                            except Exception:
                                pass
                return r
            return wrap
        return a


class _RecBlock:
    ENGS = ("tensor", "vector", "scalar", "gpsimd", "sync")

    def __init__(self, ctx):
        self.ctx = ctx
        self.bodies = {}

    def _add(self, nm, f):
        assert nm not in self.bodies, f"two {nm} bodies in one block"
        import types as _t
        if f.__closure__:
            f = _t.FunctionType(f.__code__, f.__globals__, f.__name__,
                                f.__defaults__,
                                tuple(_t.CellType(c.cell_contents) for c in f.__closure__))
        self.bodies[nm] = f

    def tensor(self, f):
        self._add("tensor", f)

    def vector(self, f):
        self._add("vector", f)

    def scalar(self, f):
        self._add("scalar", f)

    def gpsimd(self, f):
        self._add("gpsimd", f)

    def sync(self, f):
        self._add("sync", f)

    def __enter__(self):
        return self

    def __exit__(self, et, ev, tb):
        if et is None:
            self.ctx.blockbuf.append(self.bodies)
            STEPLOG.append({nm: f.__code__.co_firstlineno for nm, f in self.bodies.items()})
        return False


class Ctx:
    def __init__(self, nc, stack):
        self.nc = nc
        self.stack = stack
        self.root_stack = stack
        self.dram = {}
        self.sb = {}
        self.sems = {}
        self.semct = {}
        self.blockbuf = []
        self._mockabs = {}
        self.stepstate = {"base": 0, "waited": {}, "lastw": {}, "dw": {}}

    def block(self):
        return _RecBlock(self)

    def flush(self):
        allsteps = list(self.blockbuf)
        del self.blockbuf[:]
        if not allsteps:
            return
        nc = self.nc
        g0 = self.stepstate.setdefault("gctr", 0)
        nsoft = max(0, min(len(allsteps), SOFT_N - g0))
        self.stepstate["gctr"] = g0 + len(allsteps)
        steps = allsteps[:nsoft]
        hard = allsteps[nsoft:]
        if not steps:
            for hb in hard:
                self._emit_hard(hb)
            return
        stepsem = self.sem("STEPSEM")
        st = self.stepstate
        parts = [set(b.keys()) for b in steps]
        cum = [0]
        for p in parts:
            cum.append(cum[-1] + len(p))
        bumplogs = {}
        realinfo = {}
        for k, b in enumerate(steps):
            for nm in _RecBlock.ENGS:
                if nm in b:
                    st0 = self.stepstate
                    st0["_bumplog"] = []
                    mock = _MockEng(self)
                    b[nm](mock)
                    bumplogs[(k, nm)] = st0["_bumplog"]
                    lr = mock.last_real
                    realinfo[(k, nm)] = (lr is not None,
                                         lr.upds[-1] if lr is not None and lr.upds else None)
        self.stepstate["_bumplog"] = None
        fidx = st.get("flushidx", 0)
        st["flushidx"] = fidx + 1
        depuse = (DEPMODE == "use" and DEPFROM <= fidx < DEPFLUSHN)
        esems = {e: self.sem(f"ESEM_{e}") for e in _RecBlock.ENGS} if depuse else None
        eord0 = st.setdefault("eord", {e: 0 for e in _RecBlock.ENGS}) if depuse else None
        myords = {}
        precnt = {}
        if depuse:
            ec = dict(eord0)
            for k, b in enumerate(steps):
                precnt[k] = dict(ec)
                for nm in _RecBlock.ENGS:
                    if nm in b:
                        ec[nm] += 1
                        myords[(k, nm)] = ec[nm]
            st["eord"] = ec
        with nc.Block() as blk:
            for nm in _RecBlock.ENGS:
                ks = [k for k, b in enumerate(steps) if nm in b]
                if not ks:
                    continue
                def run(eng, nm=nm, ks=ks):
                    px = _EngProxy(eng)
                    for k in ks:
                        if depuse:
                            gk = g0 + k
                            myo = myords[(k, nm)]
                            if gk >= DEPSTEPN:
                                dd = {e2: v for e2, v in precnt[k].items() if v > 0}
                            else:
                                dd = DEPS[(gk, nm)]
                            for e2, o2 in sorted(dd.items()):
                                key = (nm, e2)
                                if o2 > st["dw"].get(key, 0):
                                    eng.wait_ge(esems[e2], o2)
                                    st["dw"][key] = o2
                        else:
                            need = st["base"] + cum[k]
                            if need > st["waited"].get(nm, 0):
                                eng.wait_ge(stepsem, need)
                                st["waited"][nm] = need
                        px.last_real = None
                        px.insts = []
                        px.waited = set()
                        st["_bumpreplay"] = list(bumplogs[(k, nm)])
                        steps[k][nm](px)
                        assert not st["_bumpreplay"], "bump replay mismatch"
                        st["_bumpreplay"] = None
                        if DEPMODE == "record":
                            rd = set(); wr = set(); incs = set()
                            for r in px.insts:
                                r1, w1 = _inst_io(r.ins)
                                rd |= r1; wr |= w1
                                si = r.ins.sync_info
                                if si is not None:
                                    for u in si.on_update:
                                        incs.add(u.id)
                            _w = set(px.waited)
                            _hr, _upd = realinfo[(k, nm)]
                            if _upd is not None:
                                _w.add(_upd[0].num)
                            BODYIO[(g0 + k, nm)] = (rd, wr, incs, _w, _hr)
                        has_real, upd = realinfo[(k, nm)]
                        assert (px.last_real is not None) == has_real, (k, nm)
                        csem = esems[nm] if depuse else stepsem
                        if px.last_real is None:
                            tgt = eng.wait_ge(csem, 0).then_inc(csem, 1)
                        elif upd is None:
                            tgt = px.last_real.then_inc(csem, 1)
                        else:
                            tgt = eng.wait_ge(upd[0], upd[1]).then_inc(csem, 1)
                        if depuse:
                            _si2 = tgt.ins.sync_info
                            if _si2 is None or not any(u.id == csem.num for u in _si2.on_update):
                                print("MISSING INC", g0 + k, nm, type(tgt.ins).__name__)
                        STEPLOG2.append((g0 + k, nm, getattr(csem, "name", "?")))
                getattr(blk, nm)(run)
        if not depuse:
            st["base"] += cum[-1]
        st["waited"] = {}
        st["dw"] = {}
        for hb in hard:
            self._emit_hard(hb)

    def _emit_hard(self, bodies):
        nc = self.nc
        with nc.Block() as blk:
            for nm in _RecBlock.ENGS:
                if nm in bodies:
                    def run(eng, nm=nm):
                        bodies[nm](_EngProxy(eng))
                    getattr(blk, nm)(run)

    def dram_in(self, name, shape, dtype=F32):
        t = self.nc.dram_tensor(name, list(shape), dtype, kind="ExternalInput")
        self.dram[name] = t.ap()
        return self.dram[name]

    def dram_out(self, name, shape, dtype=F32):
        t = self.nc.dram_tensor(name, list(shape), dtype, kind="ExternalOutput")
        self.dram[name] = t.ap()
        return self.dram[name]

    def dram_tmp(self, name, shape, dtype=F32):
        t = self.nc.dram_tensor(name, list(shape), dtype)
        self.dram[name] = t.ap()
        return self.dram[name]

    def _fence(self):
        if DEPMODE == "record":
            FENCES.add(self.stepstate.get("gctr", 0) + len(self.blockbuf))

    def sbuf(self, name, shape, dtype=F32, side=None):
        self._fence()
        h = self.stack.enter_context(self.nc.sbuf_tensor(name, list(shape), dtype, side=side))
        self.sb[name] = h
        return h

    def psum(self, name, shape, dtype=F32):
        self._fence()
        h = self.stack.enter_context(self.nc.psum_tensor(name, list(shape), dtype))
        self.sb[name] = h
        return h

    def sem(self, name):
        if name not in self.sems:
            self.sems[name] = self.root_stack.enter_context(self.nc.semaphore(name))
            self.semct[name] = 0
        return self.sems[name]

    def sub(self, stack):
        c = Ctx.__new__(Ctx)
        c.__dict__.update(self.__dict__)
        c.stack = stack
        c.sb = {k: v for k, v in self.sb.items() if not k.startswith("ln_")}
        return c

    def bump(self, name, n):
        rp = self.stepstate.get("_bumpreplay")
        if rp is not None:
            return rp.pop(0)
        self.semct[name] += n
        v = self.semct[name]
        log = self.stepstate.get("_bumplog")
        if log is not None:
            log.append(v)
        return v

    def cur(self, name):
        return self.bump(name, 0)


def emit_ln_fm(ctx, blkfac, xtiles, outtiles, ncols, gain_col=None, bias_col=None,
               scratch_pref="ln", eps=1e-5, add_from=None):
    nc = ctx.nc
    global _LN_UID
    if "ln_sq" not in ctx.sb:
        _LN_UID += 1
        u = _LN_UID
        ctx.sb["ln_sq"] = ctx.stack.enter_context(nc.sbuf_tensor(f"ln_sq{u}", [128, NT], F32))
        ctx.sb["ln_rows"] = ctx.stack.enter_context(nc.sbuf_tensor(f"ln_rows{u}", [1, 4 * NT], F32))
        ctx.sb["ln_bc"] = ctx.stack.enter_context(nc.sbuf_tensor(f"ln_bc{u}", [128, 2 * NT], F32))
        ctx.sb["ln_pstat"] = ctx.stack.enter_context(nc.psum_tensor(f"ln_pstat{u}", [1, 1024], F32))
        ctx.sb["ln_pbc"] = ctx.stack.enter_context(nc.psum_tensor(f"ln_pbc{u}", [128, 1024], F32))
    sq = ctx.sb["ln_sq"]
    rows = ctx.sb["ln_rows"]
    bc = ctx.sb["ln_bc"]
    pstat = ctx.sb["ln_pstat"]
    assert ncols <= NT

    if add_from is not None:
        with ctx.block() as blk:
            @blk.vector
            def _(v):
                for i in range(8):
                    v.tensor_add(xtiles[i], xtiles[i], add_from[i])

    with ctx.block() as blk:
        @blk.tensor
        def _(t):
            for i in range(8):
                t.matmul(pstat[0:1, 0:ncols], ctx.sb["c_ones128"][:, 0:1], xtiles[i],
                         start=(i == 0), stop=(i == 7))
    for i in range(8):
        with ctx.block() as blk:
            @blk.scalar
            def _(s):
                s.activation(sq[:, 0:ncols], xtiles[i], AF.Square)
        with ctx.block() as blk:
            @blk.tensor
            def _(t):
                t.matmul(pstat[0:1, 512:512 + ncols], ctx.sb["c_ones128"][:, 0:1],
                         sq[:, 0:ncols], start=(i == 0), stop=(i == 7))
    with ctx.block() as blk:
        @blk.vector
        def _(v):
            v.tensor_scalar_mul(rows[0:1, 0:ncols], pstat[0:1, 0:ncols], 1.0 / D)
            v.tensor_scalar_mul(rows[0:1, NT:NT + ncols], pstat[0:1, 512:512 + ncols], 1.0 / D)
        @blk.scalar
        def _(s):
            pass
    with ctx.block() as blk:
        @blk.scalar
        def _(s):
            s.activation(rows[0:1, 2 * NT:2 * NT + ncols], rows[0:1, 0:ncols], AF.Square)
    with ctx.block() as blk:
        @blk.vector
        def _(v):
            v.tensor_sub(rows[0:1, NT:NT + ncols], rows[0:1, NT:NT + ncols],
                         rows[0:1, 2 * NT:2 * NT + ncols])
            v.tensor_scalar_add(rows[0:1, NT:NT + ncols], rows[0:1, NT:NT + ncols], eps)
    with ctx.block() as blk:
        @blk.scalar
        def _(s):
            s.activation(rows[0:1, 2 * NT:2 * NT + ncols], rows[0:1, NT:NT + ncols], AF.Sqrt)
    with ctx.block() as blk:
        @blk.vector
        def _(v):
            v.reciprocal(rows[0:1, 3 * NT:3 * NT + ncols], rows[0:1, 2 * NT:2 * NT + ncols])
    pbc = ctx.sb["ln_pbc"]
    with ctx.block() as blk:
        @blk.tensor
        def _(t):
            t.matmul(pbc[:, 0:ncols], ctx.sb["c_ones1"][0:1, :], rows[0:1, 0:ncols], start=True, stop=True)
            t.matmul(pbc[:, 512:512 + ncols], ctx.sb["c_ones1"][0:1, :], rows[0:1, 3 * NT:3 * NT + ncols],
                     start=True, stop=True)
    with ctx.block() as blk:
        @blk.vector
        def _(v):
            v.tensor_copy(bc[:, 0:ncols], pbc[:, 0:ncols])
            v.tensor_copy(bc[:, NT:NT + ncols], pbc[:, 512:512 + ncols])
    with ctx.block() as blk:
        @blk.vector
        def _(v):
            for i in range(8):
                v.tensor_sub(outtiles[i], xtiles[i], bc[:, 0:ncols])
        @blk.gpsimd
        def _(g):
            pass
    with ctx.block() as blk:
        @blk.vector
        def _(v):
            for i in range(8):
                v.tensor_mul(outtiles[i], outtiles[i], bc[:, NT:NT + ncols])
    if gain_col is not None:
        with ctx.block() as blk:
            @blk.vector
            def _(v):
                for i in range(8):
                    v.tensor_scalar(outtiles[i], outtiles[i], gain_col[i], None, ALU.mult)
    if bias_col is not None:
        with ctx.block() as blk:
            @blk.vector
            def _(v):
                for i in range(8):
                    v.tensor_scalar(outtiles[i], outtiles[i], bias_col[i], None, ALU.add)


def emit_stageB(ctx, src="x0", dst="x1"):
    nc = ctx.nc
    HID = 3072
    NOC = HID // 128
    TG = 1024
    xs = [ctx.sbuf(f"B_x{i}", [128, TG], F32) for i in range(8)]
    xb = [ctx.sbuf(f"B_xb{i}", [128, TG], BF16) for i in range(8)]
    hh = [ctx.sbuf(f"B_h{i}", [128, TG], BF16) for i in range(NOC)]
    yy = [ctx.sbuf(f"B_y{i}", [128, TG], F32) for i in range(8)]
    wbuf = ctx.sbuf("B_w", [128, 2048], BF16)
    dbuf = ctx.sbuf("B_wd", [128, 24 * 128], BF16)
    pg = ctx.psum("B_pg", [128, 512], F32)
    pu = ctx.psum("B_pu", [128, 512], F32)
    pd = ctx.psum("B_pd", [128, 512], F32)
    sg = ctx.sbuf("B_sg", [128, 512], BF16)
    ld = ctx.sem("B_ld")
    st = ctx.sem("B_st")

    wg = ctx.dram["wgT"]
    wu = ctx.dram["wuT"]
    wd = ctx.dram["wdT"]

    for g0 in range(TPC // TG):
        gcols = slice(TG * g0, TG * (g0 + 1))
        with ctx.block() as blk:
            @blk.sync
            def _(sy):
                for i in range(8):
                    sy.dma_start(out=xs[i][:, :], in_=ctx.dram[src][128 * i:128 * (i + 1), gcols]).then_inc(ld, 16)
                sy.wait_ge(ld, ctx.bump("B_ld", 16 * 8))
        with ctx.block() as blk:
            @blk.vector
            def _(v):
                for i in range(8):
                    v.tensor_copy(xb[i][:, :], xs[i][:, :])

        ntile = TG // 512
        for oc in range(NOC):
            with ctx.block() as blk:
                @blk.gpsimd
                def _(g):
                    g.dma_start(out=wbuf[:, 0:1024].rearrange("p (k c) -> p k c", k=8),
                                in_=wg[:, 128 * oc:128 * (oc + 1)].rearrange("(k p) c -> p k c", p=128)).then_inc(ld, 16)
                    g.dma_start(out=wbuf[:, 1024:2048].rearrange("p (k c) -> p k c", k=8),
                                in_=wu[:, 128 * oc:128 * (oc + 1)].rearrange("(k p) c -> p k c", p=128)).then_inc(ld, 16)
                    g.wait_ge(ld, ctx.bump("B_ld", 32))
            for n in range(ntile):
                cols = slice(512 * n, 512 * (n + 1))
                with ctx.block() as blk:
                    @blk.tensor
                    def _(t):
                        for kc in range(8):
                            t.matmul(pg[:, :], wbuf[:, 128 * kc:128 * (kc + 1)], xb[kc][:, cols],
                                     start=(kc == 0), stop=(kc == 7))
                        for kc in range(8):
                            t.matmul(pu[:, :], wbuf[:, 1024 + 128 * kc:1024 + 128 * (kc + 1)], xb[kc][:, cols],
                                     start=(kc == 0), stop=(kc == 7))
                with ctx.block() as blk:
                    @blk.scalar
                    def _(s):
                        s.activation(sg[:, :], pg[:, :], AF.Silu, scale=1.0 / 127.0)
                    @blk.vector
                    def _(v):
                        v.tensor_copy(hh[oc][:, cols], pu[:, :])
                with ctx.block() as blk:
                    @blk.vector
                    def _(v):
                        v.tensor_mul(hh[oc][:, cols], hh[oc][:, cols], sg[:, :])

        for oc in range(8):
            with ctx.block() as blk:
                @blk.gpsimd
                def _(g):
                    g.dma_start(out=dbuf[:, 0:24 * 128].rearrange("p (k c) -> p k c", k=24),
                                in_=wd[:, 128 * oc:128 * (oc + 1)].rearrange("(k p) c -> p k c", p=128)).then_inc(ld, 16)
                    g.wait_ge(ld, ctx.bump("B_ld", 16))
            for n in range(ntile):
                cols = slice(512 * n, 512 * (n + 1))
                with ctx.block() as blk:
                    @blk.tensor
                    def _(t):
                        for kc in range(24):
                            t.matmul(pd[:, :], dbuf[:, 128 * kc:128 * (kc + 1)], hh[kc][:, cols],
                                     start=(kc == 0), stop=(kc == 23))
                with ctx.block() as blk:
                    @blk.vector
                    def _(v):
                        v.scalar_tensor_tensor(yy[oc][:, cols], pd[:, :], 1.0 / (127.0 * 127.0),
                                               xs[oc][:, cols], ALU.mult, ALU.add)

        for n in range(ntile):
            cols = slice(512 * n, 512 * (n + 1))
            emit_ln_fm(ctx, None, [yy[i][:, cols] for i in range(8)],
                       [yy[i][:, cols] for i in range(8)], 512)
        with ctx.block() as blk:
            @blk.sync
            def _(sy):
                for i in range(8):
                    sy.dma_start(out=ctx.dram[dst][128 * i:128 * (i + 1), gcols], in_=yy[i][:, :]).then_inc(st, 16)
                sy.wait_ge(st, ctx.bump("B_st", 16 * 8))


def emit_stageC(ctx, src="x1", dst="x2", upto=99):
    nc = ctx.nc
    SEG = 512
    xs = [ctx.sbuf(f"C_x{i}", [128, SEG], F32) for i in range(8)]
    xb = [ctx.sbuf(f"C_xb{i}", [128, SEG], BF16) for i in range(8)]
    qkv = [ctx.sbuf(f"C_qkv{i}", [128, SEG], BF16) for i in range(24)]
    ob = [ctx.sbuf(f"C_o{i}", [128, SEG], BF16) for i in range(8)]
    yy = [ctx.sbuf(f"C_y{i}", [128, SEG], F32) for i in range(8)]
    wc = ctx.sbuf("C_w", [128, 8 * 3072], BF16)
    wo = ctx.sbuf("C_wo", [128, 8 * 1024], BF16)
    bc = ctx.sbuf("C_b", [128, 24], F32)
    bo = ctx.sbuf("C_bo", [128, 8], F32)
    qk = ctx.sbuf("C_qk", [128, SEG], BF16)
    sc = ctx.sbuf("C_sc", [16, SEG], F32)
    eb = ctx.sbuf("C_eb", [16, SEG], F32)
    srow = ctx.sbuf("C_srow", [16, 16], F32)
    pbc16 = ctx.sbuf("C_pb", [128, SEG], BF16)
    inds = ctx.sbuf("C_inds", [128, 128], BF16)
    bc16 = ctx.sbuf("C_bc16", [16, 1024], F32)
    hsel = ctx.sbuf("C_hsel", [16, 4], F32)
    phm = ctx.sbuf("C_phm", [16, SEG], F32)
    pqkv = ctx.psum("C_pqkv", [128, 512], F32)
    psc = ctx.psum("C_psc", [16, 512], F32)
    ppb = ctx.psum("C_ppb", [128, 512], F32)
    pwo = ctx.psum("C_pwo", [128, 512], F32)
    ld = ctx.sem("C_ld")
    st = ctx.sem("C_st")

    with ctx.block() as blk:
        @blk.gpsimd
        def _(g):
            g.dma_start(out=wc[:, :].rearrange("p (k c) -> p k c", k=8),
                        in_=ctx.dram["wC"].rearrange("(k p) c -> p k c", p=128)).then_inc(ld, 16)
            g.dma_start(out=wo[:, :].rearrange("p (k c) -> p k c", k=8),
                        in_=ctx.dram["woC"].rearrange("(k p) c -> p k c", p=128)).then_inc(ld, 16)
            g.dma_start(out=inds[:, :], in_=ctx.dram["cd_indqk"]).then_inc(ld, 16)
        @blk.sync
        def _(sy):
            sy.dma_start(out=bc[:, :], in_=ctx.dram["bC"]).then_inc(ld, 16)
            sy.dma_start(out=bo[:, :], in_=ctx.dram["boC"]).then_inc(ld, 16)
            sy.dma_start(out=bc16[:, :], in_=ctx.dram["cd_bc16"]).then_inc(ld, 16)
            sy.dma_start(out=hsel[:, :], in_=ctx.dram["cd_hsel"]).then_inc(ld, 16)
            sy.dma_start(out=phm[:, :], in_=ctx.dram["cd_phmask"]).then_inc(ld, 16)
            sy.wait_ge(ld, ctx.bump("C_ld", 16 * 8))

    if upto < 0:
        return
    for seg in range(TPC // SEG):
        if upto == 0 and seg > 0:
            break
        cols = slice(SEG * seg, SEG * (seg + 1))
        with ctx.block() as blk:
            @blk.sync
            def _(sy):
                for i in range(8):
                    sy.dma_start(out=xs[i][:, :], in_=ctx.dram[src][128 * i:128 * (i + 1), cols]).then_inc(ld, 16)
                sy.wait_ge(ld, ctx.bump("C_ld", 16 * 8))
        with ctx.block() as blk:
            @blk.vector
            def _(v):
                for i in range(8):
                    v.tensor_copy(xb[i][:, :], xs[i][:, :])
        if upto < 0.8:
            continue
        for oc in range(24 if upto >= 0.95 else 1):
            with ctx.block() as blk:
                @blk.tensor
                def _(t):
                    for kc in range(8):
                        t.matmul(pqkv[:, :], wc[:, 3072 * kc + 128 * oc:3072 * kc + 128 * (oc + 1)],
                                 xb[kc][:, :], start=(kc == 0), stop=(kc == 7))
            with ctx.block() as blk:
                @blk.scalar
                def _(s):
                    s.activation(qkv[oc][:, :], pqkv[:, :], AF.Identity, bias=bc[:, oc:oc + 1])
        if upto < 1:
            continue
        with ctx.block() as blk:
            @blk.vector
            def _(v):
                v.tensor_mul(qk[:, :], qkv[0][:, :], qkv[8][:, :])
            @blk.tensor
            def _(t):
                pass
        for i in range(8):
            with ctx.block() as blk:
                @blk.tensor
                def _(t):
                    t.matmul(psc[:, :], inds[:, 16 * i:16 * (i + 1)], qk[:, :],
                             start=(i == 0), stop=(i == 7))
            if i < 7:
                with ctx.block() as blk:
                    @blk.vector
                    def _(v):
                        v.tensor_mul(qk[:, :], qkv[i + 1][:, :], qkv[9 + i][:, :])
        with ctx.block() as blk:
            @blk.vector
            def _(v):
                v.tensor_copy(sc[:, :], psc[:, :])
        if upto < 2:
            continue
        with ctx.block() as blk:
            @blk.vector
            def _(v):
                for j in range(4):
                    v.reduce_max(srow[:, j:j + 1], sc[:, j::4], axis=AX.X)
        with ctx.block() as blk:
            @blk.vector
            def _(v):
                v.tensor_scalar_mul(srow[:, 4:8], srow[:, 0:4], -0.125)
        with ctx.block() as blk:
            @blk.scalar
            def _(s):
                for j in range(4):
                    s.activation(eb[:, j::4], sc[:, j::4], AF.Exp, bias=srow[:, 4 + j:5 + j],
                                 scale=0.125, accum_out=srow[:, 8 + j:9 + j])
        with ctx.block() as blk:
            @blk.vector
            def _(v):
                v.reciprocal(srow[:, 12:16], srow[:, 8:12])
        with ctx.block() as blk:
            @blk.vector
            def _(v):
                v.tensor_mul(srow[:, 0:4], srow[:, 12:16], hsel[:, :])
        with ctx.block() as blk:
            @blk.vector
            def _(v):
                v.reduce_sum(srow[:, 4:5], srow[:, 0:4], axis=AX.X)
        with ctx.block() as blk:
            @blk.vector
            def _(v):
                v.tensor_scalar(eb[:, :], eb[:, :], srow[:, 4:5], None, ALU.mult)
        with ctx.block() as blk:
            @blk.vector
            def _(v):
                v.tensor_mul(eb[:, :], eb[:, :], phm[:, :])
        if upto < 3:
            continue
        for i in range(8):
            with ctx.block() as blk:
                @blk.tensor
                def _(t):
                    t.matmul(ppb[:, :], bc16[:, 128 * i:128 * (i + 1)], eb[:, :], start=True, stop=True)
            with ctx.block() as blk:
                @blk.vector
                def _(v):
                    v.tensor_copy(pbc16[:, :], ppb[:, :])
            with ctx.block() as blk:
                @blk.vector
                def _(v):
                    v.tensor_mul(ob[i][:, :], qkv[16 + i][:, :], pbc16[:, :])
        if upto < 4:
            continue
        for oc in range(8):
            with ctx.block() as blk:
                @blk.tensor
                def _(t):
                    for kc in range(8):
                        t.matmul(pwo[:, :], wo[:, 1024 * kc + 128 * oc:1024 * kc + 128 * (oc + 1)],
                                 ob[kc][:, :], start=(kc == 0), stop=(kc == 7))
            with ctx.block() as blk:
                @blk.vector
                def _(v):
                    v.scalar_tensor_tensor(yy[oc][:, :], pwo[:, :], bo[:, oc:oc + 1], xs[oc][:, :],
                                           ALU.add, ALU.add)
        if upto >= 5:
            emit_ln_fm(ctx, None, [yy[i][:, :] for i in range(8)],
                       [yy[i][:, :] for i in range(8)], 512)
        if upto >= 4:
            with ctx.block() as blk:
                @blk.sync
                def _(sy):
                    for i in range(8):
                        sy.dma_start(out=ctx.dram[dst][128 * i:128 * (i + 1), cols], in_=yy[i][:, :]).then_inc(st, 16)
                    sy.wait_ge(st, ctx.bump("C_st", 16 * 8))


def build_stageC_program(upto=99):
    nc = bass.Bass("TRN2", target_bir_lowering=False)
    stack = contextlib.ExitStack()
    ctx = Ctx(nc, stack)
    with stack:
        ctx.dram_in("x1", [1024, TPC], F32)
        ctx.dram_in("wC", [1024, 3072], F32)
        ctx.dram_in("bC", [128, 24], F32)
        ctx.dram_in("woC", [1024, 1024], F32)
        ctx.dram_in("boC", [128, 8], F32)
        ctx.dram_out("x2", [1024, TPC], F32)
        for name, arr in _const_arrays().items():
            ctx.dram_in(name, arr.shape, I32 if arr.dtype == np.int32 else F32)
        _preload_consts(ctx)
        emit_stageC(ctx, upto=upto)
        ctx.flush()
    return nc


HB = 513
NTOK = HB + TPC
NQK = 2560


def emit_stageD_layer(ctx, l, src, halo, xmprev, xm_dram, dst):
    emit_stageD_attn(ctx, l, src, halo, xm_dram)
    emit_stageD_ffn(ctx, l, xmprev, xm_dram, dst)


def emit_stageD_attn(ctx, l, src, halo, xm_dram, halo_ag=None, xmpub=None):
    nc = ctx.nc
    ld = ctx.sem("D_ld")
    st = ctx.sem("D_st")
    sb = ctx.sb
    wmA, pmA = sb["D_wmA"], sb["D_pmA"]
    wmBlo, pmBlo = sb["D_wmBlo"], sb["D_pmBlo"]
    P0, P1, P2, P3 = sb["D_P0"], sb["D_P1"], sb["D_P2"], sb["D_P3"]
    i128b, i128f = sb["c_i128b"], sb["c_i128f"]
    nones2 = sb["c_nones2"]
    onesb = sb["c_onesb"]

    phA = contextlib.ExitStack()
    cA = Ctx.__new__(Ctx)
    cA.__dict__.update(ctx.__dict__)
    cA.stack = phA
    qf = [cA.sbuf(f"D{l}_qf{i}", [128, NQK], BF16, side="right") for i in range(4)]
    kf = [cA.sbuf(f"D{l}_kf{i}", [128, NQK], BF16, side="right") for i in range(4)]
    vtm = [cA.sbuf(f"D{l}_vtm{i}", [128, 512], BF16, side="right") for i in range(20)]

    with contextlib.ExitStack() as ph1:
        c1 = Ctx.__new__(Ctx)
        c1.__dict__.update(ctx.__dict__)
        c1.stack = ph1
        xb = [c1.sbuf(f"D{l}_xb{i}", [128, NTOK], BF16, side="right") for i in range(8)]
        wqb = c1.sbuf(f"D{l}_wqb", [128, 2 * 1024], BF16, side="right")

        if halo_ag is None:
            with ctx.block() as blk:
                @blk.gpsimd
                def _(g):
                    for i in range(8):
                        g.dma_start(out=xb[i][:, 0:HB], in_=ctx.dram[halo][128 * i:128 * (i + 1), :]).then_inc(ld, 16)
                        g.dma_start(out=xb[i][:, HB:NTOK], in_=ctx.dram[src][128 * i:128 * (i + 1), :]).then_inc(ld, 16)
                    g.wait_ge(ld, ctx.bump("D_ld", 16 * 16))
            with ctx.block() as blk:
                @blk.vector
                def _(v):
                    for i in range(8):
                        v.tensor_scalar(xb[i][:, 0:HB], xb[i][:, 0:HB], sb["D_hmask"][:, 0:1], None, ALU.mult)
        else:
            hcand = c1.sbuf(f"D{l}_hcand", [128, 4 * HB], BF16, side="right")
            for i in range(8):
                with ctx.block() as blk:
                    @blk.gpsimd
                    def _(g):
                        for q in range(4):
                            g.dma_start(out=hcand[:, HB * q:HB * (q + 1)],
                                        in_=ctx.dram[halo_ag][1024 * q + 128 * i:1024 * q + 128 * (i + 1), :]).then_inc(ld, 16)
                        g.dma_start(out=xb[i][:, HB:NTOK], in_=ctx.dram[src][128 * i:128 * (i + 1), :]).then_inc(ld, 16)
                        g.wait_ge(ld, ctx.bump("D_ld", 16 * 5))
                with ctx.block() as blk:
                    @blk.vector
                    def _(v):
                        v.tensor_scalar(xb[i][:, 0:HB], hcand[:, 0:HB], sb["D_hsel"][:, 0:1], None, ALU.mult)
                for q in range(1, 4):
                    with ctx.block() as blk:
                        @blk.vector
                        def _(v):
                            v.scalar_tensor_tensor(xb[i][:, 0:HB], hcand[:, HB * q:HB * (q + 1)],
                                                   sb["D_hsel"][:, q:q + 1], xb[i][:, 0:HB],
                                                   ALU.mult, ALU.add)
        hbv = [xb[i][:, 1:NTOK] if i < 4 else xb[i][:, 0:NTOK - 1] for i in range(8)]

        wqkv = ctx.dram[f"wqkvD{l}"]
        for oc in range(8):
            dstt = qf[oc] if oc < 4 else kf[oc - 4]
            with ctx.block() as blk:
                @blk.gpsimd
                def _(g):
                    g.dma_start(out=wqb[:, 0:1024].rearrange("p (k c) -> p k c", k=8),
                                in_=wqkv[:, 128 * oc:128 * (oc + 1)].rearrange("(k p) c -> p k c", p=128)).then_inc(ld, 16)
                    g.wait_ge(ld, ctx.bump("D_ld", 16))
            for n in range(5):
                cols = slice(512 * n, 512 * (n + 1))
                with ctx.block() as blk:
                    @blk.tensor
                    def _(t):
                        for kc in range(8):
                            t.matmul(P0[:, :], wqb[:, 128 * kc:128 * (kc + 1)], hbv[kc][:, cols],
                                     start=(kc == 0), stop=(kc == 7))
                with ctx.block() as blk:
                    @blk.vector
                    def _(v):
                        v.tensor_copy(dstt[:, cols], P0[:, :])
        with ctx.block() as blk:
            @blk.gpsimd
            def _(g):
                g.dma_start(out=wqb[:, 0:1024].rearrange("p (k c) -> p k c", k=8),
                            in_=wqkv[:, 1024:1152].rearrange("(k p) c -> p k c", p=128)).then_inc(ld, 16)
                g.dma_start(out=wqb[:, 1024:2048].rearrange("p (k c) -> p k c", k=8),
                            in_=wqkv[:, 1152:1280].rearrange("(k p) c -> p k c", p=128)).then_inc(ld, 16)
                g.wait_ge(ld, ctx.bump("D_ld", 32))
        for half in range(2):
            vc = slice(1024 + 256 * half, 1024 + 256 * (half + 1))
            with ctx.block() as blk:
                @blk.gpsimd
                def _(g):
                    g.dma_start(out=wqb[:, 0:2048].rearrange("p (k c) -> p k c", k=8),
                                in_=wqkv[:, vc].rearrange("(k p) c -> p k c", p=128)).then_inc(ld, 16)
                    g.wait_ge(ld, ctx.bump("D_ld", 16))
            for tt in range(20):
                tc = slice(128 * tt, 128 * (tt + 1))
                with ctx.block() as blk:
                    @blk.tensor
                    def _(t):
                        for kc in range(8):
                            t.matmul(P0[:, 0:256], hbv[kc][:, tc], wqb[:, 256 * kc:256 * (kc + 1)],
                                     start=(kc == 0), stop=(kc == 7))
                with ctx.block() as blk:
                    @blk.vector
                    def _(v):
                        v.tensor_copy(vtm[tt][:, 256 * half:256 * (half + 1)], P0[:, 0:256])

        ctx.flush()
    with contextlib.ExitStack() as ph2:
        c2 = Ctx.__new__(Ctx)
        c2.__dict__.update(ctx.__dict__)
        c2.stack = ph2
        cs = c2.sbuf(f"D{l}_cs", [128, NQK], BF16, side="right")
        ss = c2.sbuf(f"D{l}_ss", [128, NQK], BF16, side="right")
        scr = c2.sbuf(f"D{l}_scr", [128, 512], F32, side="right")
        qhb = c2.sbuf(f"D{l}_qhb", [128, 512], BF16, side="right")
        rot = c2.sbuf(f"D{l}_rot", [128, 256], BF16, side="right")
        t2b = c2.sbuf(f"D{l}_t2b", [128, 512], BF16, side="right")
        with ctx.block() as blk:
            @blk.gpsimd
            def _(g):
                g.dma_start(out=rot[:, 0:128], in_=ctx.dram[f"rotq{l}"]).then_inc(ld, 16)
                g.dma_start(out=rot[:, 128:256], in_=ctx.dram[f"rotk{l}"]).then_inc(ld, 16)
                g.wait_ge(ld, ctx.bump("D_ld", 32))
        for side in range(2):
            tiles = qf if side == 0 else kf
            rsl = rot[:, 0:128] if side == 0 else rot[:, 128:256]
            with ctx.block() as blk:
                @blk.gpsimd
                def _(g):
                    g.dma_start(out=cs[:, :], in_=ctx.dram[("csq" if side == 0 else "csk") + str(l)]).then_inc(ld, 16)
                    g.dma_start(out=ss[:, :], in_=ctx.dram[("ssq" if side == 0 else "ssk") + str(l)]).then_inc(ld, 16)
                    g.wait_ge(ld, ctx.bump("D_ld", 32))
            for i in range(4):
                for n in range(5):
                    cols = slice(512 * n, 512 * (n + 1))
                    with ctx.block() as blk:
                        @blk.vector
                        def _(v):
                            v.tensor_mul(scr[:, :], tiles[i][:, cols], tiles[i][:, cols])
                    with ctx.block() as blk:
                        @blk.tensor
                        def _(t):
                            t.matmul(P0[0:2, 0:512], sb["c_indh2"][:, 0:2], scr[:, :],
                                     start=True, stop=True)
                    with ctx.block() as blk:
                        @blk.scalar
                        def _(sc_):
                            sc_.activation(scr[0:2, :], P0[0:2, 0:512], AF.Sqrt)
                    with ctx.block() as blk:
                        @blk.vector
                        def _(v):
                            v.tensor_scalar_max(scr[0:2, :], scr[0:2, :], 1e-12)
                    with ctx.block() as blk:
                        @blk.vector
                        def _(v):
                            v.reciprocal(scr[0:2, :], scr[0:2, :])
                    with ctx.block() as blk:
                        @blk.tensor
                        def _(t):
                            t.matmul(P0[:, 0:512], sb["c_indh2T"][0:2, :], scr[0:2, :],
                                     start=True, stop=True)
                    with ctx.block() as blk:
                        @blk.vector
                        def _(v):
                            v.tensor_tensor(qhb[:, :], tiles[i][:, cols], P0[:, 0:512], ALU.mult)
                    with ctx.block() as blk:
                        @blk.tensor
                        def _(t):
                            t.matmul(P0[:, 0:512], rsl, qhb[:, :], start=True, stop=True)
                    with ctx.block() as blk:
                        @blk.vector
                        def _(v):
                            v.tensor_tensor(t2b[:, :], P0[:, 0:512], ss[:, cols], ALU.mult)
                    with ctx.block() as blk:
                        @blk.vector
                        def _(v):
                            v.tensor_tensor(tiles[i][:, cols], qhb[:, :], cs[:, cols], ALU.mult)
                    with ctx.block() as blk:
                        @blk.vector
                        def _(v):
                            v.tensor_add(tiles[i][:, cols], tiles[i][:, cols], t2b[:, :])

    if l == 0 and "dbgq" in ctx.dram:
        with ctx.block() as blk:
            @blk.sync
            def _(sy):
                for i in range(4):
                    sy.dma_start(out=ctx.dram["dbgq"][128 * i:128 * (i + 1), :], in_=qf[i][:, :]).then_inc(st, 16)
                    sy.dma_start(out=ctx.dram["dbgk"][128 * i:128 * (i + 1), :], in_=kf[i][:, :]).then_inc(st, 16)
                for i in range(20):
                    sy.dma_start(out=ctx.dram["dbgv"][128 * i:128 * (i + 1), :], in_=vtm[i][:, :]).then_inc(st, 16)
                sy.wait_ge(st, ctx.bump("D_st", 16 * 28))

        ctx.flush()
    ph3 = contextlib.ExitStack()
    c3a = Ctx.__new__(Ctx)
    c3a.__dict__.update(ctx.__dict__)
    c3a.stack = ph3
    wo = c3a.sbuf(f"D{l}_wo", [128, 4 * 1024], BF16, side="right")
    p2 = c3a.sbuf(f"D{l}_p2b", [128, 8 * 512], BF16, side="right")
    ofm = [c3a.sbuf(f"D{l}_ofm{i}", [128, 512], BF16, side="right") for i in range(4)]
    mrow = c3a.sbuf(f"D{l}_mrow", [128, 512], BF16, side="right")
    zrow = c3a.sbuf(f"D{l}_zrow", [1, 512], F32, side="right")
    rzbc = c3a.sbuf(f"D{l}_rzbc", [64, 512], F32, side="right")
    ctx.sb["D_mcol"] = c3a.sbuf(f"D{l}_mcol", [128, 2], F32, side="right")
    res1 = c3a.sbuf(f"D{l}_res1", [128, 512], F32, side="right")
    with ctx.block() as blk:
        @blk.gpsimd
        def _(g):
            g.dma_start(out=wo[:, :].rearrange("p (k c) -> p k c", k=4),
                        in_=ctx.dram[f"woD{l}"].rearrange("(k p) c -> p k c", p=128)).then_inc(ld, 16)
            g.wait_ge(ld, ctx.bump("D_ld", 16))
    for w in range(4):
        qoff = 512 * (w + 1)
        koff = 512 * w
        qcols = slice(qoff, qoff + 512)
        for h in range(8):
            til, hb0 = h // 2, 64 * (h % 2)
            hsl = slice(hb0, hb0 + 64)
            for qt in range(4):
                qc = slice(qoff + 128 * qt, qoff + 128 * (qt + 1))
                with ctx.block() as blk:
                    @blk.tensor
                    def _(t):
                        t.matmul(P1[:, :], qf[til][hsl, qc], kf[til][hsl, koff:koff + 512],
                                 start=True, stop=True)
                        t.matmul(P2[:, :], qf[til][hsl, qc], kf[til][hsl, koff + 512:koff + 1024],
                                 start=True, stop=True)
                with ctx.block() as blk:
                    @blk.vector
                    def _(v):
                        v.reduce_max(scrM(ctx)[:, 0:1], P1[:, :], axis=AX.X)
                        v.reduce_max(scrM(ctx)[:, 1:2], P2[:, :], axis=AX.X)
                with ctx.block() as blk:
                    @blk.vector
                    def _(v):
                        v.tensor_max(scrM(ctx)[:, 0:1], scrM(ctx)[:, 0:1], scrM(ctx)[:, 1:2])
                with ctx.block() as blk:
                    @blk.tensor
                    def _(t):
                        t.matmul(P3[0:1, 0:128], scrM(ctx)[:, 0:1], i128f[:, :], start=True, stop=True)
                with ctx.block() as blk:
                    @blk.vector
                    def _(v):
                        v.tensor_copy(mrow[0:1, 128 * qt:128 * (qt + 1)], P3[0:1, 0:128])
                        v.tensor_copy(mrow[64:65, 128 * qt:128 * (qt + 1)], P3[0:1, 0:128])
            for kt in range(8):
                kc = slice(koff + 128 * kt, koff + 128 * (kt + 1))
                pad = (w == 0 and kt < 4)
                with ctx.block() as blk:
                    @blk.tensor
                    def _(t):
                        if hb0 == 0:
                            t.matmul(P1[:, :], kf[til][hsl, kc], qf[til][hsl, qcols], start=True, stop=False)
                            t.matmul(P1[:, :], nones2[0:1, :], mrow[0:1, :], start=False, stop=False)
                            t.matmul(P1[:, :], i128b[:, :], wmA[:, 512 * kt:512 * (kt + 1)],
                                     start=False, stop=(not pad))
                            if pad:
                                t.matmul(P1[:, :], i128b[:, :], pmA[:, 512 * kt:512 * (kt + 1)],
                                         start=False, stop=True)
                        else:
                            t.matmul(P1[:, :], kf[til][hsl, kc], qf[til][hsl, qcols], start=True, stop=False)
                            t.matmul(P1[:, :], nones2[64:65, :], mrow[64:65, :], start=False, stop=False)
                            t.matmul(P1[0:64, :], i128b[64:128, 64:128], wmBlo[64:128, 512 * kt:512 * (kt + 1)],
                                     start=False, stop=False, tile_position=(64, 0))
                            t.matmul(P1[64:128, :], i128b[64:128, 64:128], wmA[64:128, 512 * kt:512 * (kt + 1)],
                                     start=False, stop=(not pad), tile_position=(64, 64))
                            if pad:
                                t.matmul(P1[0:64, :], i128b[64:128, 64:128], pmBlo[64:128, 512 * kt:512 * (kt + 1)],
                                         start=False, stop=False, tile_position=(64, 0))
                                t.matmul(P1[64:128, :], i128b[64:128, 64:128], pmA[64:128, 512 * kt:512 * (kt + 1)],
                                         start=False, stop=True, tile_position=(64, 64))
                with ctx.block() as blk:
                    @blk.scalar
                    def _(sc_):
                        sc_.activation(p2[:, 512 * kt:512 * (kt + 1)], P1[:, :], AF.Exp)
            with ctx.block() as blk:
                @blk.tensor
                def _(t):
                    for kt in range(8):
                        t.matmul(P3[0:1, 0:512], onesb[:, 0:1], p2[:, 512 * kt:512 * (kt + 1)],
                                 start=(kt == 0), stop=(kt == 7))
            with ctx.block() as blk:
                @blk.tensor
                def _(t):
                    for kt in range(8):
                        t.matmul(P2[0:64, :], vtm[4 * w + kt][:, 64 * h:64 * (h + 1)],
                                 p2[:, 512 * kt:512 * (kt + 1)], start=(kt == 0), stop=(kt == 7))
                @blk.vector
                def _(v):
                    v.tensor_copy(zrow[0:1, 0:512], P3[0:1, 0:512])
            with ctx.block() as blk:
                @blk.vector
                def _(v):
                    v.reciprocal(zrow[0:1, 0:512], zrow[0:1, 0:512])
            with ctx.block() as blk:
                @blk.tensor
                def _(t):
                    t.matmul(P0[0:64, :], sb["c_ones1"][0:1, 0:64], zrow[0:1, 0:512], start=True, stop=True)
            with ctx.block() as blk:
                @blk.vector
                def _(v):
                    v.tensor_copy(rzbc[0:64, :], P0[0:64, :])
            with ctx.block() as blk:
                @blk.vector
                def _(v):
                    v.tensor_tensor(ofm[til][hsl, :], P2[0:64, :], rzbc[0:64, :], ALU.mult)
        if l == 0 and "dbgo" in ctx.dram:
            with ctx.block() as blk:
                @blk.sync
                def _(sy):
                    for i in range(4):
                        sy.dma_start(out=ctx.dram["dbgo"][128 * i:128 * (i + 1), 512 * w:512 * (w + 1)],
                                     in_=ofm[i][:, :]).then_inc(st, 16)
                    sy.wait_ge(st, ctx.bump("D_st", 16 * 4))
        for oc in range(8):
            with ctx.block() as blk:
                @blk.sync
                def _(sy):
                    sy.dma_start(out=res1[:, :], in_=ctx.dram[src][128 * oc:128 * (oc + 1),
                                                                  512 * w:512 * (w + 1)]).then_inc(ld, 16)
                    sy.wait_ge(ld, ctx.bump("D_ld", 16))
                @blk.tensor
                def _(t):
                    for kc in range(4):
                        t.matmul(P0[:, :], wo[:, 1024 * kc + 128 * oc:1024 * kc + 128 * (oc + 1)],
                                 ofm[kc][:, :], start=(kc == 0), stop=(kc == 3))
            with ctx.block() as blk:
                @blk.vector
                def _(v):
                    v.tensor_add(res1[:, :], res1[:, :], P0[:, :])
                    if xmpub is not None and w == 3:
                        v.tensor_copy(sb["D_xlast"][:, oc:oc + 1], res1[:, 511:512])
            with ctx.block() as blk:
                @blk.sync
                def _(sy):
                    sy.dma_start(out=ctx.dram[xm_dram][128 * oc:128 * (oc + 1), 512 * w:512 * (w + 1)],
                                 in_=res1[:, :]).then_inc(st, 16)
                    sy.wait_ge(st, ctx.bump("D_st", 16))
        if xmpub is not None and w == 3:
            with ctx.block() as blk:
                @blk.sync
                def _(sy):
                    sy.dma_start(out=ctx.dram[xmpub], in_=sb["D_xlast"][:, :]).then_inc(st, 16)
                    sy.wait_ge(st, ctx.bump("D_st", 16))

    ctx.flush()
    ph3.close()
    phA.close()


def emit_stageD_ffn(ctx, l, xmprev, xm_dram, dst, xmprev_sb=None):
    nc = ctx.nc
    sb = ctx.sb
    ld = ctx.sem("D_ld")
    st = ctx.sem("D_st")
    P0 = sb["D_P0"]
    with contextlib.ExitStack() as phF:
        c3 = Ctx.__new__(Ctx)
        c3.__dict__.update(ctx.__dict__)
        c3.stack = phF
        TG = 512
        xm = [c3.sbuf(f"D{l}_fxm{i}", [128, TG + 1], F32, side="right") for i in range(8)]
        hblk = [c3.sbuf(f"D{l}_fh{i}", [128, TG], BF16, side="right") for i in range(44)]
        lnt = [c3.sbuf(f"D{l}_flt{i}", [128, TG], BF16, side="right") for i in range(8)]
        w1b = c3.sbuf(f"D{l}_fw1", [128, 1024], BF16, side="right")
        w2b = c3.sbuf(f"D{l}_fw2", [128, 22 * 128], BF16, side="right")
        bcol = c3.sbuf(f"D{l}_fbc", [128, 76], F32, side="right")
        x3t = [c3.sbuf(f"D{l}_fx3{i}", [128, 512], F32, side="right") for i in range(8)]
        with ctx.block() as blk:
            @blk.sync
            def _(sy):
                sy.dma_start(out=bcol[:, 0:44], in_=ctx.dram[f"b1D{l}"]).then_inc(ld, 16)
                sy.dma_start(out=bcol[:, 44:52], in_=ctx.dram[f"b2D{l}"]).then_inc(ld, 16)
                sy.dma_start(out=bcol[:, 52:60], in_=ctx.dram[f"lngD{l}"]).then_inc(ld, 16)
                sy.dma_start(out=bcol[:, 60:68], in_=ctx.dram[f"lnbD{l}"]).then_inc(ld, 16)
                if xmprev_sb is None:
                    sy.dma_start(out=bcol[:, 68:76], in_=ctx.dram[xmprev]).then_inc(ld, 16)
                    sy.wait_ge(ld, ctx.bump("D_ld", 16 * 5))
                else:
                    sy.wait_ge(ld, ctx.bump("D_ld", 16 * 4))
        if xmprev_sb is not None:
            with ctx.block() as blk:
                @blk.vector
                def _(v):
                    v.tensor_copy(bcol[:, 68:76], xmprev_sb[:, :])
        for g0 in range(TPC // TG):
            gcols = slice(TG * g0, TG * (g0 + 1))
            with ctx.block() as blk:
                @blk.sync
                def _(sy):
                    for i in range(8):
                        sy.dma_start(out=xm[i][:, 1:TG + 1],
                                     in_=ctx.dram[xm_dram][128 * i:128 * (i + 1), gcols]).then_inc(ld, 16)
                    sy.wait_ge(ld, ctx.bump("D_ld", 16 * 8))
            if g0 == 0:
                with ctx.block() as blk:
                    @blk.vector
                    def _(v):
                        for i in range(8):
                            v.tensor_scalar(xm[i][:, 0:1], bcol[:, 68 + i:69 + i],
                                            sb["D_hmask"][:, 0:1], None, ALU.mult)
            else:
                with ctx.block() as blk:
                    @blk.sync
                    def _(sy):
                        for i in range(8):
                            sy.dma_start(out=xm[i][:, 0:1],
                                         in_=ctx.dram[xm_dram][128 * i:128 * (i + 1),
                                                               TG * g0 - 1:TG * g0],
                                         allow_slow_non_contiguous=True).then_inc(ld, 16)
                        sy.wait_ge(ld, ctx.bump("D_ld", 16 * 8))
            hsv = [xm[i][:, 1:TG + 1] if i < 4 else xm[i][:, 0:TG] for i in range(8)]
            for n in range(TG // 512):
                cols = slice(512 * n, 512 * (n + 1))
                emit_ln_fm(ctx, None, [hsv[i][:, cols] for i in range(8)],
                           [lnt[i][:, cols] for i in range(8)], 512,
                           gain_col=[bcol[:, 52 + i:53 + i] for i in range(8)],
                           bias_col=[bcol[:, 60 + i:61 + i] for i in range(8)])
            w1 = ctx.dram[f"w1D{l}"]
            for oc in range(44):
                with ctx.block() as blk:
                    @blk.gpsimd
                    def _(g):
                        g.dma_start(out=w1b[:, :].rearrange("p (k c) -> p k c", k=8),
                                    in_=w1[:, 128 * oc:128 * (oc + 1)].rearrange("(k p) c -> p k c", p=128)).then_inc(ld, 16)
                        g.wait_ge(ld, ctx.bump("D_ld", 16))
                for n in range(TG // 512):
                    cols = slice(512 * n, 512 * (n + 1))
                    with ctx.block() as blk:
                        @blk.tensor
                        def _(t):
                            for kc in range(8):
                                t.matmul(P0[:, :], w1b[:, 128 * kc:128 * (kc + 1)], lnt[kc][:, cols],
                                         start=(kc == 0), stop=(kc == 7))
                    with ctx.block() as blk:
                        @blk.scalar
                        def _(sc_):
                            fn = AF.Identity if oc % 2 == 0 else AF.Gelu
                            sc_.activation(hblk[oc][:, cols], P0[:, :], fn, bias=bcol[:, oc:oc + 1])
            with ctx.block() as blk:
                @blk.vector
                def _(v):
                    for i in range(22):
                        v.tensor_mul(hblk[2 * i][:, :], hblk[2 * i][:, :], hblk[2 * i + 1][:, :])
            w2 = ctx.dram[f"w2D{l}"]
            for oc in range(8):
                with ctx.block() as blk:
                    @blk.gpsimd
                    def _(g):
                        g.dma_start(out=w2b[:, :].rearrange("p (k c) -> p k c", k=22),
                                    in_=w2[:, 128 * oc:128 * (oc + 1)].rearrange("(k p) c -> p k c", p=128)).then_inc(ld, 16)
                        g.wait_ge(ld, ctx.bump("D_ld", 16))
                for n in range(TG // 512):
                    cols = slice(512 * n, 512 * (n + 1))
                    with ctx.block() as blk:
                        @blk.tensor
                        def _(t):
                            for kc in range(22):
                                t.matmul(P0[:, :], w2b[:, 128 * kc:128 * (kc + 1)], hblk[2 * kc][:, cols],
                                         start=(kc == 0), stop=(kc == 21))
                    with ctx.block() as blk:
                        @blk.vector
                        def _(v):
                            v.scalar_tensor_tensor(x3t[oc][:, :], P0[:, :], bcol[:, 44 + oc:45 + oc],
                                                   xm[oc][:, 1 + 512 * n:1 + 512 * (n + 1)], ALU.add, ALU.add)
                    with ctx.block() as blk:
                        @blk.sync
                        def _(sy):
                            sy.dma_start(out=ctx.dram[dst][128 * oc:128 * (oc + 1),
                                                           TG * g0 + 512 * n:TG * g0 + 512 * (n + 1)],
                                         in_=x3t[oc][:, :]).then_inc(st, 16)
                            sy.wait_ge(st, ctx.bump("D_st", 16))
        ctx.flush()


def scrM(ctx):
    return ctx.sb["D_mcol"]


def emit_stageD(ctx, src="x2", dst="xd", nlayers=2):
    nc = ctx.nc
    ctx.sbuf("D_wmA", [128, 8 * 512], BF16)
    ctx.sbuf("D_wmBlo", [128, 8 * 512], BF16)
    ctx.sbuf("D_pmA", [128, 4 * 512], BF16)
    ctx.sbuf("D_pmBlo", [128, 4 * 512], BF16)
    ctx.sbuf("D_hmask", [128, 1], F32)
    ctx.sb["D_P0"] = ctx.psum("D_P0", [128, 512], F32)
    ctx.sb["D_P1"] = ctx.psum("D_P1", [128, 512], F32)
    ctx.sb["D_P2"] = ctx.psum("D_P2", [128, 512], F32)
    ctx.sb["D_P3"] = ctx.psum("D_P3", [128, 512], F32)
    ld = ctx.sem("D_ld")
    with ctx.block() as blk:
        @blk.gpsimd
        def _(g):
            for nm in ("wmA", "wmBlo"):
                g.dma_start(out=ctx.sb[f"D_{nm}"][:, :], in_=ctx.dram[f"cd_{nm}"]).then_inc(ld, 16)
            for nm in ("pmA", "pmBlo"):
                g.dma_start(out=ctx.sb[f"D_{nm}"][:, :], in_=ctx.dram[nm]).then_inc(ld, 16)
        @blk.sync
        def _(sy):
            sy.dma_start(out=ctx.sb["D_hmask"][:, :], in_=ctx.dram["hmask"]).then_inc(ld, 16)
            sy.wait_ge(ld, ctx.bump("D_ld", 16 * 5))
    emit_stageD_layer(ctx, 0, src, "h2", "xmprev0", "xmscr", "x3scr")
    if nlayers > 1:
        emit_stageD_layer(ctx, 1, "x3scr", "h3", "xmprev1", "xmscr", "x4scr")
    emit_final_ln(ctx, src, "x4scr", dst)


def emit_final_ln(ctx, src1, src2, dst):
    nc = ctx.nc
    ld = ctx.sem("D_ld")
    st = ctx.sem("D_st")
    yy = [ctx.sbuf(f"D_fin{i}", [128, 512], F32) for i in range(8)]
    y2 = [ctx.sbuf(f"D_fin2{i}", [128, 512], F32) for i in range(8)]
    for n in range(TPC // 512):
        cols = slice(512 * n, 512 * (n + 1))
        with ctx.block() as blk:
            @blk.sync
            def _(sy):
                for i in range(8):
                    sy.dma_start(out=yy[i][:, :], in_=ctx.dram[src1][128 * i:128 * (i + 1), cols]).then_inc(ld, 16)
                    sy.dma_start(out=y2[i][:, :], in_=ctx.dram[src2][128 * i:128 * (i + 1), cols]).then_inc(ld, 16)
                sy.wait_ge(ld, ctx.bump("D_ld", 16 * 16))
        with ctx.block() as blk:
            @blk.vector
            def _(v):
                for i in range(8):
                    v.tensor_add(yy[i][:, :], yy[i][:, :], y2[i][:, :])
        emit_ln_fm(ctx, None, [yy[i][:, :] for i in range(8)],
                   [yy[i][:, :] for i in range(8)], 512)
        with ctx.block() as blk:
            @blk.sync
            def _(sy):
                for i in range(8):
                    sy.dma_start(out=ctx.dram[dst][128 * i:128 * (i + 1), cols], in_=yy[i][:, :]).then_inc(st, 16)
                sy.wait_ge(st, ctx.bump("D_st", 16 * 8))


def build_stageD_program(debug=False, nlayers=2):
    nc = bass.Bass("TRN2", target_bir_lowering=False)
    stack = contextlib.ExitStack()
    ctx = Ctx(nc, stack)
    with stack:
        ctx.dram_in("x2", [1024, TPC], F32)
        ctx.dram_in("h2", [1024, HB], F32)
        ctx.dram_in("h3", [1024, HB], F32)
        ctx.dram_in("xmprev0", [128, 8], F32)
        ctx.dram_in("xmprev1", [128, 8], F32)
        ctx.dram_in("hmask", [128, 1], F32)
        ctx.dram_in("pmA", [128, 4 * 512], F32)
        ctx.dram_in("pmBlo", [128, 4 * 512], F32)
        for l in range(2):
            ctx.dram_in(f"wqkvD{l}", [1024, 1536], F32)
            ctx.dram_in(f"woD{l}", [512, 1024], F32)
            ctx.dram_in(f"w1D{l}", [1024, 5632], F32)
            ctx.dram_in(f"w2D{l}", [2816, 1024], F32)
            ctx.dram_in(f"b1D{l}", [128, 44], F32)
            ctx.dram_in(f"b2D{l}", [128, 8], F32)
            ctx.dram_in(f"lngD{l}", [128, 8], F32)
            ctx.dram_in(f"lnbD{l}", [128, 8], F32)
            ctx.dram_in(f"rotq{l}", [128, 128], F32)
            ctx.dram_in(f"rotk{l}", [128, 128], F32)
            for nm in ("csq", "ssq", "csk", "ssk"):
                ctx.dram_in(f"{nm}{l}", [128, NQK], F32)
        mk = ctx.dram_out if debug else ctx.dram_tmp
        if debug:
            import ml_dtypes
            ctx.dram_out("dbgq", [512, NQK], BF16)
            ctx.dram_out("dbgk", [512, NQK], BF16)
            ctx.dram_out("dbgv", [2560, 512], BF16)
            ctx.dram_out("dbgo", [512, TPC], BF16)
        mk("xmscr", [1024, TPC], F32)
        mk("x3scr", [1024, TPC], F32)
        mk("x4scr", [1024, TPC], F32)
        ctx.dram_out("xd", [1024, TPC], F32)
        for name, arr in _const_arrays().items():
            ctx.dram_in(name, arr.shape, I32 if arr.dtype == np.int32 else F32)
        _preload_consts(ctx)
        emit_stageD(ctx, nlayers=nlayers)
        ctx.flush()
    return nc


CC = 32
CAPA = 80.0


def _r3(ap, w):
    return ap.rearrange("p (a b) -> p a b", b=w)


def emit_stageA(ctx, dst="x0", TT=T, do_collective=True):
    nc = ctx.nc
    sb = ctx.sb
    ld = ctx.sem("A_ld")
    st = ctx.sem("A_st")
    ctx.sem("A_sm")
    ctx.sem("A_se")
    P0 = ctx.psum("A_P0", [128, 512], F32)
    _ps_stack = contextlib.ExitStack()
    _cps = Ctx.__new__(Ctx)
    _cps.__dict__.update(ctx.__dict__)
    _cps.stack = _ps_stack
    PA = _cps.psum("A_PA", [128, 512], F32)
    PB = _cps.psum("A_PB", [128, 512], F32)
    PS = _cps.psum("A_PS", [128, 512], F32)
    PC = [_cps.psum(f"A_PC{j}", [128, 384], F32) for j in range(4)]
    ctx.sb["A_PSUMS"] = (P0, PA, PB, PS, PC)

    wab = ctx.sbuf("A_wa", [128, 8 * 1024], BF16)
    wfg = ctx.sbuf("A_wfg", [128, 256], BF16)
    convw = ctx.sbuf("A_cw", [128, 24], F32)
    convb = ctx.sbuf("A_cb", [128, 6], F32)
    dtb = ctx.sbuf("A_dtb", [128, 2], F32)
    nacol = ctx.sbuf("A_na", [128, 2], F32)
    acol = ctx.sbuf("A_ac", [4, 2], F32)
    bg2bc = ctx.sbuf("A_bg2", [128, 256], F32)
    onwbc = ctx.sbuf("A_onw", [128, 256], F32)
    rbg2 = ctx.sbuf("A_r_bg2", [1, 256], F32)
    ronw = ctx.sbuf("A_r_onw", [1, 256], F32)
    i32g = ctx.sbuf("A_i32g", [128, 64], BF16)
    i32st = ctx.sbuf("A_i32s", [128, 32], BF16)
    i64s = ctx.sbuf("A_i64s", [128, 64], BF16)
    ni64s = ctx.sbuf("A_ni64s", [128, 64], BF16)
    msel = ctx.sbuf("A_msel", [128, 5 * 64], I32)
    msel2 = ctx.sbuf("A_msel2", [128, 5 * 64], I32)
    zero64 = ctx.sbuf("A_z64", [128, 64], F32)
    resetm = ctx.sbuf("A_rst", [128, 512], F32)
    bc4 = ctx.sbuf("A_bc4", [4, 256], F32)
    nbc4 = ctx.sbuf("A_nbc4", [4, 256], F32)

    Sst = ctx.sbuf("A_S", [128, 2 * 128], BF16)
    Shist = ctx.sbuf("A_Sh", [128, 16 * 128], BF16)

    hsbf = [ctx.sbuf(f"A_hs{i}", [128, 512], BF16) for i in range(8)]
    qraw = [ctx.sbuf(f"A_qr{i}", [128, 515], BF16) for i in range(2)]
    kraw = [ctx.sbuf(f"A_kr{i}", [128, 515], BF16) for i in range(2)]
    vraw = [ctx.sbuf(f"A_vr{i}", [128, 515], BF16) for i in range(2)]
    fgout = ctx.sbuf("A_fg", [128, 512], BF16)
    qh = [ctx.sbuf(f"A_qh{i}", [128, 512], BF16) for i in range(2)]
    kh = [ctx.sbuf(f"A_kh{i}", [128, 512], BF16) for i in range(2)]
    kb = [ctx.sbuf(f"A_kb{i}", [128, 512], BF16) for i in range(2)]
    vb = [ctx.sbuf(f"A_vb{i}", [128, 512], BF16) for i in range(2)]
    gcf = [ctx.sbuf(f"A_gc{i}", [128, 512], F32) for i in range(2)]
    gprev = [ctx.sbuf(f"A_gp{i}", [128, 512], F32) for i in range(2)]
    REB = ["aK4", "aQ4", "aK8", "aQ8", "aK16", "aQ16", "aK32", "aQ32",
           "b4s", "b4n", "b8n", "b16n", "btl"]
    reb = {nm: [ctx.sbuf(f"A_{nm}{i}", [128, 512], BF16) for i in range(2)] for nm in REB}
    scr = ctx.sbuf("A_scr", [128, 512], F32)
    scr2 = ctx.sbuf("A_scr2", [128, 512], F32)
    scrb = ctx.sbuf("A_scrb", [128, 512], F32)
    brow = ctx.sbuf("A_brow", [4, 512], F32)
    betabc = [ctx.sbuf(f"A_bbc{i}", [128, 512], BF16) for i in range(2)]

    LtX = {0: ctx.sbuf("A_LtA", [128, 64], BF16), 1: ctx.sbuf("A_LtB", [128, 64], BF16)}
    LpX = {0: ctx.sbuf("A_LpA", [128, 64], BF16), 1: ctx.sbuf("A_LpB", [128, 64], BF16)}
    ScX = {0: ctx.sbuf("A_ScA", [128, 64], BF16), 1: ctx.sbuf("A_ScB", [128, 64], BF16)}
    PtX = {0: ctx.sbuf("A_PtA", [128, 64], BF16), 1: ctx.sbuf("A_PtB", [128, 64], BF16)}
    PpX = {0: ctx.sbuf("A_PpA", [128, 64], BF16), 1: ctx.sbuf("A_PpB", [128, 64], BF16)}
    GtX = {0: ctx.sbuf("A_GtA", [128, 64], BF16), 1: ctx.sbuf("A_GtB", [128, 64], BF16)}
    rhsb = ctx.sbuf("A_rhs", [128, 4 * 512], BF16)
    wsol = ctx.sbuf("A_wsol", [128, 4 * 512], BF16)
    btm = ctx.sbuf("A_btm", [128, 4 * 256], BF16)
    oloc = ctx.sbuf("A_oloc", [128, 4 * 256], F32)
    omt = ctx.sbuf("A_omt", [128, 16 * 64], BF16)
    phir = ctx.sbuf("A_phr", [128, 16 * 128], BF16)
    psig = ctx.sbuf("A_psi", [128, 16 * 128], BF16)
    gamc = ctx.sbuf("A_gam", [128, 32], F32)
    oarr = ctx.sbuf("A_oar", [128, 256], F32)
    gatet = ctx.sbuf("A_gate", [128, 4 * 256], F32)
    ofmt = ctx.sbuf("A_ofmt", [128, 2 * 512], BF16)

    nt = TT // 512
    hsT = ctx.dram["hsT"]
    oscr = ctx.dram["oscr"]

    with ctx.block() as blk:
        @blk.gpsimd
        def _(g):
            g.dma_start(out=wab[:, :].rearrange("p (k c) -> p k c", k=8),
                        in_=ctx.dram["wA"].rearrange("(k p) c -> p k c", p=128)).then_inc(ld, 16)
            g.dma_start(out=wfg[:, :], in_=ctx.dram["wfg2"]).then_inc(ld, 16)
            g.dma_start(out=i32g[:, :], in_=ctx.dram["cd_i32g"]).then_inc(ld, 16)
            g.dma_start(out=i32st[:, :], in_=ctx.dram["cd_i32st"]).then_inc(ld, 16)
            g.dma_start(out=i64s[:, :], in_=ctx.dram["cd_i64st"]).then_inc(ld, 16)
            g.dma_start(out=ni64s[:, :], in_=ctx.dram["cd_ni64st"]).then_inc(ld, 16)
        @blk.sync
        def _(sy):
            sy.dma_start(out=msel[:, :], in_=ctx.dram["cd_msel"]).then_inc(ld, 16)
            sy.dma_start(out=msel2[:, :], in_=ctx.dram["cd_msel2"]).then_inc(ld, 16)
            sy.dma_start(out=convw[:, :], in_=ctx.dram["convw"]).then_inc(ld, 16)
            sy.dma_start(out=convb[:, :], in_=ctx.dram["convb"]).then_inc(ld, 16)
            sy.dma_start(out=dtb[:, :], in_=ctx.dram["dtb2"]).then_inc(ld, 16)
            sy.dma_start(out=acol[0:4, 0:1], in_=ctx.dram["alogc"]).then_inc(ld, 16)
            sy.dma_start(out=resetm[:, :], in_=ctx.dram["cd_reset"]).then_inc(ld, 16)
            sy.dma_start(out=bc4[:, :], in_=ctx.dram["cd_bc4"]).then_inc(ld, 16)
            sy.dma_start(out=nbc4[:, :], in_=ctx.dram["cd_nbc4"]).then_inc(ld, 16)
            sy.dma_start(out=rbg2[:, :], in_=ctx.dram["bg2r"]).then_inc(ld, 16)
            sy.dma_start(out=ronw[:, :], in_=ctx.dram["onw4"]).then_inc(ld, 16)
            sy.wait_ge(ld, ctx.bump("A_ld", 16 * 17))
    with ctx.block() as blk:
        @blk.scalar
        def _(s):
            s.activation(acol[0:4, 1:2], acol[0:4, 0:1], AF.Exp)
    with ctx.block() as blk:
        @blk.tensor
        def _(t):
            t.matmul(P0[:, 0:1], nbc4[0:4, 0:128], acol[0:4, 1:2], start=True, stop=True)
            t.matmul(P0[:, 1:2], nbc4[0:4, 128:256], acol[0:4, 1:2], start=True, stop=True)
            t.matmul(PA[:, 0:256], sb["c_ones1"][0:1, :], rbg2[0:1, :], start=True, stop=True)
            t.matmul(PB[:, 0:256], sb["c_ones1"][0:1, :], ronw[0:1, :], start=True, stop=True)
    with ctx.block() as blk:
        @blk.vector
        def _(v):
            v.tensor_copy(nacol[:, :], P0[:, 0:2])
            v.tensor_copy(bg2bc[:, :], PA[:, 0:256])
            v.tensor_copy(onwbc[:, :], PB[:, 0:256])
        @blk.gpsimd
        def _(g):
            for rr in (qraw, kraw, vraw):
                for i in range(2):
                    g.memset(rr[i][:, 0:3], 0.0)
            g.memset(Sst[:, :], 0.0)
            g.memset(zero64[:, :], 0.0)

    for tj in range(nt):
        tc = slice(512 * tj, 512 * (tj + 1))
        with ctx.block() as blk:
            @blk.gpsimd
            def _(g):
                for kc in range(8):
                    g.dma_start(out=hsbf[kc][:, :], in_=hsT[128 * kc:128 * (kc + 1), tc]).then_inc(ld, 16)
                g.wait_ge(ld, ctx.bump("A_ld", 16 * 8))
        for oc in range(8):
            with ctx.block() as blk:
                @blk.tensor
                def _(t):
                    for kc in range(8):
                        t.matmul(P0[:, :], wab[:, 1024 * kc + 128 * oc:1024 * kc + 128 * (oc + 1)],
                                 hsbf[kc][:, :], start=(kc == 0), stop=(kc == 7))
            with ctx.block() as blk:
                if oc < 6:
                    dstt = [qraw, qraw, kraw, kraw, vraw, vraw][oc][oc % 2]
                    @blk.vector
                    def _(v):
                        v.tensor_copy(dstt[:, 3:515], P0[:, :])
                elif oc == 6:
                    @blk.vector
                    def _(v):
                        v.tensor_copy(fgout[:, :], P0[:, :])
                else:
                    @blk.scalar
                    def _(s):
                        s.activation(brow[0:4, :], P0[0:4, :], AF.Sigmoid)
        for i in range(2):
            for raw, dstt, cwof, cbof in ((qraw[i], qh[i], 12 * i + 0, 3 * i + 0),
                                          (kraw[i], kh[i], 12 * i + 4, 3 * i + 1),
                                          (vraw[i], vb[i], 12 * i + 8, 3 * i + 2)):
                with ctx.block() as blk:
                    @blk.vector
                    def _(v):
                        v.tensor_scalar(scrb[:, :], raw[:, 0:512], convw[:, cwof + 0:cwof + 1], None, ALU.mult)
                for jj in range(1, 4):
                    with ctx.block() as blk:
                        @blk.vector
                        def _(v):
                            v.scalar_tensor_tensor(scrb[:, :], raw[:, jj:512 + jj],
                                                   convw[:, cwof + jj:cwof + jj + 1],
                                                   scrb[:, :], ALU.mult, ALU.add)
                with ctx.block() as blk:
                    @blk.scalar
                    def _(s):
                        s.activation(dstt[:, :], scrb[:, :], AF.Silu, bias=convb[:, cbof:cbof + 1])
                    @blk.vector
                    def _(v):
                        v.tensor_copy(raw[:, 0:3], raw[:, 512:515])
        with ctx.block() as blk:
            @blk.tensor
            def _(t):
                t.matmul(PA[:, 0:512], bc4[0:4, 0:128], brow[0:4, :], start=True, stop=True)
                t.matmul(PB[:, 0:512], bc4[0:4, 128:256], brow[0:4, :], start=True, stop=True)
        with ctx.block() as blk:
            @blk.vector
            def _(v):
                v.tensor_copy(betabc[0][:, :], PA[:, 0:512])
                v.tensor_copy(betabc[1][:, :], PB[:, 0:512])
        for i in range(2):
            with ctx.block() as blk:
                @blk.vector
                def _(v):
                    v.tensor_mul(vb[i][:, :], vb[i][:, :], betabc[i][:, :])
            for outt, qscale in ((qh[i], True), (kh[i], False)):
                with ctx.block() as blk:
                    @blk.vector
                    def _(v):
                        v.tensor_mul(scr[:, :], outt[:, :], outt[:, :])
                with ctx.block() as blk:
                    @blk.tensor
                    def _(t):
                        t.matmul(P0[0:2, :], sb["c_indh2"][:, 0:2], scr[:, :], start=True, stop=True)
                with ctx.block() as blk:
                    @blk.scalar
                    def _(s):
                        s.activation(scr[0:2, :], P0[0:2, :], AF.Sqrt)
                with ctx.block() as blk:
                    @blk.vector
                    def _(v):
                        if qscale:
                            v.tensor_scalar(scr[0:2, :], scr[0:2, :], 8.0, 8e-6, ALU.mult, ALU.add)
                        else:
                            v.tensor_scalar_add(scr[0:2, :], scr[0:2, :], 1e-6)
                with ctx.block() as blk:
                    @blk.vector
                    def _(v):
                        v.reciprocal(scr[0:2, :], scr[0:2, :])
                with ctx.block() as blk:
                    @blk.tensor
                    def _(t):
                        t.matmul(P0[:, :], sb["c_indh2T"][0:2, :], scr[0:2, :], start=True, stop=True)
                with ctx.block() as blk:
                    @blk.vector
                    def _(v):
                        v.tensor_tensor(outt[:, :], outt[:, :], P0[:, :], ALU.mult)
            with ctx.block() as blk:
                @blk.vector
                def _(v):
                    v.tensor_mul(kb[i][:, :], kh[i][:, :], betabc[i][:, :])
        with ctx.block() as blk:
            @blk.tensor
            def _(t):
                t.matmul(PA[:, 0:512], wfg[0:64, 0:128], fgout[0:64, :], start=True, stop=True)
                t.matmul(PB[:, 0:512], wfg[0:64, 128:256], fgout[0:64, :], start=True, stop=True)
        for i, PX in ((0, PA), (1, PB)):
            with ctx.block() as blk:
                @blk.vector
                def _(v):
                    v.tensor_scalar(scr[:, :], PX[:, 0:512], dtb[:, i:i + 1], None, ALU.add)
            with ctx.block() as blk:
                @blk.vector
                def _(v):
                    v.tensor_scalar(scr2[:, :], scr[:, :], 0.0, None, ALU.max)
            with ctx.block() as blk:
                @blk.vector
                def _(v):
                    v.scalar_tensor_tensor(scr[:, :], scr2[:, :], 2.0, scr[:, :],
                                           ALU.mult, ALU.subtract)
            with ctx.block() as blk:
                @blk.scalar
                def _(s):
                    s.activation(scr[:, :], scr[:, :], AF.Exp, scale=-1.0)
            with ctx.block() as blk:
                @blk.scalar
                def _(s):
                    s.activation(scr[:, :], scr[:, :], AF.Ln, bias=1.0)
            with ctx.block() as blk:
                @blk.vector
                def _(v):
                    v.tensor_add(scr[:, :], scr[:, :], scr2[:, :])
            with ctx.block() as blk:
                @blk.vector
                def _(v):
                    v.tensor_scalar(scr[:, :], scr[:, :], nacol[:, i:i + 1], None, ALU.mult)
            with ctx.block() as blk:
                @blk.vector
                def _(v):
                    v.tensor_tensor_scan(gcf[i][:, :], resetm[:, :], scr[:, :], 0.0,
                                         ALU.mult, ALU.add)
            with ctx.block() as blk:
                @blk.gpsimd
                def _(g):
                    g.memset(gprev[i][:, 0:1], 0.0)
                @blk.vector
                def _(v):
                    v.tensor_tensor(gprev[i][:, 1:512], gcf[i][:, 0:511], resetm[:, 1:512], ALU.mult)
        with ctx.block() as blk:
            @blk.scalar
            def _(s):
                for i in range(2):
                    s.activation(gamc[:, i:32:2], gcf[i][:, 31:512:32], AF.Exp)

        for i in range(2):
            GC, GP = gcf[i], gprev[i]
            def bsub(out_, a_rep, b_full, w, off, rev=False):
                ap_rep = bass.AP(a_rep, off, [[512, 128], [w, 512 // w], [0, w]])
                if not rev:
                    return (_r3(out_[:, :], w), _r3(b_full[:, :], w), ap_rep)
                return (_r3(out_[:, :], w), ap_rep, _r3(b_full[:, :], w))
            for w, anm_k, anm_q in ((4, "aK4", "aQ4"), (8, "aK8", "aQ8"), (16, "aK16", "aQ16")):
                with ctx.block() as blk:
                    @blk.vector
                    def _(v):
                        o, a, b = bsub(scr, GP, GC, w, 0)
                        v.tensor_tensor(o, a, b, ALU.subtract)
                with ctx.block() as blk:
                    @blk.scalar
                    def _(s):
                        s.activation(scr2[:, :], scr[:, :], AF.Exp)
                with ctx.block() as blk:
                    @blk.vector
                    def _(v):
                        v.tensor_tensor(reb[anm_k][i][:, :], kb[i][:, :], scr2[:, :], ALU.mult)
                with ctx.block() as blk:
                    @blk.vector
                    def _(v):
                        v.tensor_tensor(reb[anm_q][i][:, :], qh[i][:, :], scr2[:, :], ALU.mult)
                if w == 4:
                    with ctx.block() as blk:
                        @blk.vector
                        def _(v):
                            v.tensor_scalar(scr[:, :], scr[:, :], -1.0, CAPA, ALU.mult, ALU.min)
                    with ctx.block() as blk:
                        @blk.scalar
                        def _(s):
                            s.activation(scr[:, :], scr[:, :], AF.Exp)
                    with ctx.block() as blk:
                        @blk.vector
                        def _(v):
                            v.tensor_tensor(reb["b4s"][i][:, :], kh[i][:, :], scr[:, :], ALU.mult)
            with ctx.block() as blk:
                @blk.scalar
                def _(s):
                    s.activation(scr2[:, :], GC[:, :], AF.Exp)
            with ctx.block() as blk:
                @blk.vector
                def _(v):
                    v.tensor_tensor(reb["aK32"][i][:, :], kb[i][:, :], scr2[:, :], ALU.mult)
            with ctx.block() as blk:
                @blk.vector
                def _(v):
                    v.tensor_tensor(reb["aQ32"][i][:, :], qh[i][:, :], scr2[:, :], ALU.mult)
            for w, bnm in ((4, "b4n"), (8, "b8n"), (16, "b16n")):
                with ctx.block() as blk:
                    @blk.vector
                    def _(v):
                        o, a, b = bsub(scr, GC, GC, w, w - 1, rev=True)
                        v.tensor_tensor(o, a, b, ALU.subtract)
                with ctx.block() as blk:
                    @blk.vector
                    def _(v):
                        v.tensor_scalar_min(scr[:, :], scr[:, :], CAPA)
                with ctx.block() as blk:
                    @blk.scalar
                    def _(s):
                        s.activation(scr[:, :], scr[:, :], AF.Exp)
                with ctx.block() as blk:
                    @blk.vector
                    def _(v):
                        v.tensor_tensor(reb[bnm][i][:, :], kh[i][:, :], scr[:, :], ALU.mult)
            with ctx.block() as blk:
                @blk.vector
                def _(v):
                    o, a, b = bsub(scr, GC, GC, 32, 31, rev=True)
                    v.tensor_tensor(o, a, b, ALU.subtract)
            with ctx.block() as blk:
                @blk.scalar
                def _(s):
                    s.activation(scr[:, :], scr[:, :], AF.Exp)
            with ctx.block() as blk:
                @blk.vector
                def _(v):
                    v.tensor_tensor(reb["btl"][i][:, :], kh[i][:, :], scr[:, :], ALU.mult)
        for q4 in range(4):
            tq = slice(128 * q4, 128 * (q4 + 1))
            with ctx.block() as blk:
                @blk.tensor
                def _(t):
                    t.matmul(P0[:, 0:256], fgout[64:128, tq], wfg[64:128, :], start=True, stop=True)
            with ctx.block() as blk:
                @blk.vector
                def _(v):
                    v.tensor_tensor(gatet[:, 256 * q4:256 * (q4 + 1)], P0[:, 0:256], bg2bc[:, :], ALU.add)
        with ctx.block() as blk:
            @blk.scalar
            def _(s):
                for q4 in range(4):
                    s.activation(gatet[:, 256 * q4:256 * (q4 + 1)],
                                 gatet[:, 256 * q4:256 * (q4 + 1)], AF.Sigmoid)

        lvls = (("b4s", "aK4", "aQ4"), ("b4n", "aK4", "aQ4"),
                ("b8n", "aK8", "aQ8"), ("b16n", "aK16", "aQ16"))
        for gi in range(4):
            with ctx.block() as blk:
                @blk.tensor
                def _(t):
                    for lv, (bnm, anmk, anmq) in enumerate(lvls):
                        for cj in range(4):
                            ccols = slice(128 * gi + 32 * cj, 128 * gi + 32 * (cj + 1))
                            for h in range(4):
                                par, grp = h % 2, h // 2
                                PX = PA if par == 0 else PB
                                hs64 = slice(64 * par, 64 * (par + 1))
                                cs_ = slice(64 * lv + 32 * grp, 64 * lv + 32 * (grp + 1))
                                t.matmul(PX[32 * cj:32 * (cj + 1), cs_],
                                         reb[bnm][grp][hs64, ccols], reb[anmk][grp][hs64, ccols],
                                         start=True, stop=True, tile_position=(64 * par, 32 * cj))
                                cs2 = slice(256 + 64 * lv + 32 * grp, 256 + 64 * lv + 32 * (grp + 1))
                                t.matmul(PX[32 * cj:32 * (cj + 1), cs2],
                                         reb[bnm][grp][hs64, ccols], reb[anmq][grp][hs64, ccols],
                                         start=True, stop=True, tile_position=(64 * par, 32 * cj))
            with ctx.block() as blk:
                @blk.tensor
                def _(t):
                    for lv, (bnm, anmk, anmq) in enumerate(lvls):
                        for cj in range(4):
                            ccols = slice(128 * gi + 32 * cj, 128 * gi + 32 * (cj + 1))
                            for h in range(4):
                                par, grp = h % 2, h // 2
                                PX = PS if par == 0 else PC[0]
                                hs64 = slice(64 * par, 64 * (par + 1))
                                cs_ = slice(64 * lv + 32 * grp, 64 * lv + 32 * (grp + 1))
                                t.matmul(PX[32 * cj:32 * (cj + 1), cs_],
                                         reb[anmk][grp][hs64, ccols], reb[bnm][grp][hs64, ccols],
                                         start=True, stop=True, tile_position=(64 * par, 32 * cj))
            for par in (0, 1):
                PX = PA if par == 0 else PB
                PXp = PS if par == 0 else PC[0]
                for base, M, dst_ in ((0, msel, LtX[par]), (256, msel, ScX[par])):
                    d4m = M[:, 0:64] if base == 0 else M[:, 64:128]
                    with ctx.block() as blk:
                        @blk.vector
                        def _(v):
                            v.select(scr[:, 0:64], d4m, PX[:, base:base + 64], zero64[:, :])
                    with ctx.block() as blk:
                        @blk.vector
                        def _(v):
                            v.select(scr[:, 0:64], M[:, 128:192], PX[:, base + 64:base + 128], scr[:, 0:64])
                    with ctx.block() as blk:
                        @blk.vector
                        def _(v):
                            v.select(scr[:, 0:64], M[:, 192:256], PX[:, base + 128:base + 192], scr[:, 0:64])
                    with ctx.block() as blk:
                        @blk.vector
                        def _(v):
                            v.select(dst_[:, :], M[:, 256:320], PX[:, base + 192:base + 256], scr[:, 0:64])
                with ctx.block() as blk:
                    @blk.vector
                    def _(v):
                        v.select(scr[:, 0:64], msel2[:, 0:64], PXp[:, 0:64], zero64[:, :])
                with ctx.block() as blk:
                    @blk.vector
                    def _(v):
                        v.select(scr[:, 0:64], msel2[:, 128:192], PXp[:, 64:128], scr[:, 0:64])
                with ctx.block() as blk:
                    @blk.vector
                    def _(v):
                        v.select(scr[:, 0:64], msel2[:, 192:256], PXp[:, 128:192], scr[:, 0:64])
                with ctx.block() as blk:
                    @blk.vector
                    def _(v):
                        v.select(LpX[par][:, :], msel2[:, 256:320], PXp[:, 192:256], scr[:, 0:64])
            for par in (0, 1):
                PX = PA if par == 0 else PB
                with ctx.block() as blk:
                    @blk.vector
                    def _(v):
                        v.tensor_copy(PtX[par][:, :], LtX[par][:, :])
                        v.tensor_copy(PpX[par][:, :], LpX[par][:, :])
                with ctx.block() as blk:
                    @blk.vector
                    def _(v):
                        v.tensor_tensor(GtX[par][:, :], i32g[:, :], LtX[par][:, :], ALU.subtract)
                for lv in range(4):
                    with ctx.block() as blk:
                        @blk.tensor
                        def _(t):
                            for cj in range(4):
                                rs = slice(32 * cj, 32 * (cj + 1))
                                for grp in range(2):
                                    cg = slice(32 * grp, 32 * (grp + 1))
                                    t.matmul(PX[rs, 32 * grp:32 * (grp + 1)],
                                             PtX[par][rs, cg], PpX[par][rs, cg],
                                             start=True, stop=True, tile_position=(32 * cj, 32 * cj))
                                    t.matmul(PX[rs, 64 + 32 * grp:96 + 32 * grp],
                                             PpX[par][rs, cg], PtX[par][rs, cg],
                                             start=True, stop=True, tile_position=(32 * cj, 32 * cj))
                    with ctx.block() as blk:
                        @blk.vector
                        def _(v):
                            v.tensor_copy(PpX[par][:, :], PX[:, 0:64])
                            v.tensor_copy(PtX[par][:, :], PX[:, 64:128])
                    with ctx.block() as blk:
                        @blk.tensor
                        def _(t):
                            for cj in range(4):
                                rs = slice(32 * cj, 32 * (cj + 1))
                                for grp in range(2):
                                    cg = slice(32 * grp, 32 * (grp + 1))
                                    t.matmul(PX[rs, 128 + 32 * grp:160 + 32 * grp],
                                             i32st[rs, :], GtX[par][rs, cg],
                                             start=True, stop=False, tile_position=(32 * cj, 32 * cj))
                                    t.matmul(PX[rs, 128 + 32 * grp:160 + 32 * grp],
                                             PpX[par][rs, cg], GtX[par][rs, cg],
                                             start=False, stop=True, tile_position=(32 * cj, 32 * cj))
                    with ctx.block() as blk:
                        @blk.vector
                        def _(v):
                            v.tensor_copy(GtX[par][:, :], PX[:, 128:192])
            with ctx.block() as blk:
                @blk.tensor
                def _(t):
                    for cj in range(4):
                        ccols = slice(128 * gi + 32 * cj, 128 * gi + 32 * (cj + 1))
                        for h in range(4):
                            par, grp = h % 2, h // 2
                            PX = PA if par == 0 else PB
                            hs64 = slice(64 * par, 64 * (par + 1))
                            i64sl = i64s[hs64, :]
                            t.matmul(PX[32 * cj:32 * (cj + 1), 128 * h:128 * h + 64],
                                     vb[grp][hs64, ccols], i64sl,
                                     start=True, stop=True, tile_position=(64 * par, 32 * cj))
                            t.matmul(PX[32 * cj:32 * (cj + 1), 128 * h + 64:128 * (h + 1)],
                                     reb["aK32"][grp][hs64, ccols], i64sl,
                                     start=True, stop=True, tile_position=(64 * par, 32 * cj))
            with ctx.block() as blk:
                @blk.vector
                def _(v):
                    v.tensor_copy(bass.AP(rhsb, 512 * gi, [[2048, 128], [256, 2], [1, 128]]),
                                  bass.AP(PA, 0, [[512, 128], [256, 2], [1, 128]]))
                    v.tensor_copy(bass.AP(rhsb, 512 * gi + 128, [[2048, 128], [256, 2], [1, 128]]),
                                  bass.AP(PB, 128, [[512, 128], [256, 2], [1, 128]]))
            with ctx.block() as blk:
                @blk.tensor
                def _(t):
                    for cj in range(4):
                        ccols = slice(128 * gi + 32 * cj, 128 * gi + 32 * (cj + 1))
                        for h in range(4):
                            par, grp = h % 2, h // 2
                            PX = PA if par == 0 else PB
                            hs64 = slice(64 * par, 64 * (par + 1))
                            t.matmul(PX[32 * cj:32 * (cj + 1), 64 * h:64 * (h + 1)],
                                     reb["btl"][grp][hs64, ccols], i64s[hs64, :],
                                     start=True, stop=True, tile_position=(64 * par, 32 * cj))
            with ctx.block() as blk:
                @blk.vector
                def _(v):
                    v.tensor_copy(bass.AP(btm, 256 * gi, [[1024, 128], [128, 2], [1, 64]]),
                                  bass.AP(PA, 0, [[512, 128], [128, 2], [1, 64]]))
                    v.tensor_copy(bass.AP(btm, 256 * gi + 64, [[1024, 128], [128, 2], [1, 64]]),
                                  bass.AP(PB, 64, [[512, 128], [128, 2], [1, 64]]))
            with ctx.block() as blk:
                @blk.tensor
                def _(t):
                    for cj in range(4):
                        rs = slice(32 * cj, 32 * (cj + 1))
                        for h in range(4):
                            par, grp = h % 2, h // 2
                            t.matmul(PS[rs, 128 * h:128 * (h + 1)],
                                     GtX[par][rs, 32 * grp:32 * (grp + 1)],
                                     rhsb[rs, 512 * gi + 128 * h:512 * gi + 128 * (h + 1)],
                                     start=True, stop=True, tile_position=(32 * cj, 32 * cj))
            with ctx.block() as blk:
                @blk.vector
                def _(v):
                    v.tensor_copy(wsol[:, 512 * gi:512 * (gi + 1)], PS[:, :])
            with ctx.block() as blk:
                @blk.tensor
                def _(t):
                    for cj in range(4):
                        rs = slice(32 * cj, 32 * (cj + 1))
                        for h in range(4):
                            par, grp = h % 2, h // 2
                            t.matmul(PS[rs, 64 * h:64 * (h + 1)],
                                     ScX[par][rs, 32 * grp:32 * (grp + 1)],
                                     wsol[rs, 512 * gi + 128 * h:512 * gi + 128 * h + 64],
                                     start=True, stop=True, tile_position=(32 * cj, 32 * cj))
            with ctx.block() as blk:
                @blk.vector
                def _(v):
                    v.tensor_copy(oloc[:, 256 * gi:256 * (gi + 1)], PS[:, 0:256])
            with ctx.block() as blk:
                @blk.tensor
                def _(t):
                    for cj in range(4):
                        rs = slice(32 * cj, 32 * (cj + 1))
                        for h in range(4):
                            par, grp = h % 2, h // 2
                            t.matmul(PC[cj][64 * par:64 * (par + 1), 32 * grp:32 * (grp + 1)],
                                     wsol[rs, 512 * gi + 128 * h + 64:512 * gi + 128 * (h + 1)],
                                     ScX[par][rs, 32 * grp:32 * (grp + 1)],
                                     start=True, stop=True, tile_position=(32 * cj, 64 * par))
                            t.matmul(PC[cj][64 * par:64 * (par + 1), 64 + 128 * grp:64 + 128 * (grp + 1)],
                                     btm[rs, 256 * gi + 64 * h:256 * gi + 64 * (h + 1)],
                                     wsol[rs, 512 * gi + 128 * h:512 * gi + 128 * (h + 1)],
                                     start=True, stop=True, tile_position=(32 * cj, 64 * par))
            with ctx.block() as blk:
                @blk.tensor
                def _(t):
                    for cj in range(4):
                        for h in range(4):
                            par, grp = h % 2, h // 2
                            t.matmul(PC[cj][64 * par:64 * (par + 1), 32 * grp:32 * (grp + 1)],
                                     ni64s[64 * par:64 * (par + 1), :],
                                     reb["aQ32"][grp][64 * par:64 * (par + 1),
                                                      128 * gi + 32 * cj:128 * gi + 32 * (cj + 1)],
                                     start=False, stop=True, tile_position=(64 * par, 64 * par))
            for cj in range(4):
                ch = 4 * gi + cj
                with ctx.block() as blk:
                    @blk.vector
                    def _(v):
                        v.tensor_scalar(omt[:, 64 * ch:64 * (ch + 1)], PC[cj][:, 0:64],
                                        -1.0, None, ALU.mult)
                        v.tensor_copy(bass.AP(psig, 128 * ch, [[2048, 128], [64, 2], [1, 64]]),
                                      bass.AP(PC[cj], 64, [[384, 128], [128, 2], [1, 64]]))
                with ctx.block() as blk:
                    @blk.vector
                    def _(v):
                        v.tensor_scalar(bass.AP(phir, 128 * ch, [[2048, 128], [64, 2], [1, 64]]),
                                        bass.AP(PC[cj], 128, [[384, 128], [128, 2], [1, 64]]),
                                        -1.0, None, ALU.mult)

        smA = ctx.sems["A_sm"]
        seA = ctx.sems["A_se"]
        with ctx.block() as blk:
            @blk.tensor
            def _(t):
                for ch in range(16):
                    G = 16 * tj + ch
                    cur = G % 2
                    if G > 0:
                        t.wait_ge(seA, G)
                    last = None
                    for h in range(4):
                        par, grp = h % 2, h // 2
                        hs64 = slice(64 * par, 64 * (par + 1))
                        ccs = slice(64 * grp, 64 * (grp + 1))
                        t.matmul(PS[hs64, ccs],
                                 phir[hs64, 128 * ch + 64 * grp:128 * ch + 64 * (grp + 1)],
                                 Sst[hs64, 128 * cur + 64 * grp:128 * cur + 64 * (grp + 1)],
                                 start=True, stop=False, tile_position=(64 * par, 64 * par))
                        last = t.matmul(PS[hs64, ccs], i64s[hs64, :],
                                        psig[hs64, 128 * ch + 64 * grp:128 * ch + 64 * (grp + 1)],
                                        start=False, stop=True, tile_position=(64 * par, 64 * par))
                    last.then_inc(smA, 1)
            @blk.vector
            def _(v):
                for ch in range(16):
                    G = 16 * tj + ch
                    cur, nxt = G % 2, 1 - (G % 2)
                    v.wait_ge(smA, G + 1)
                    v.tensor_copy(Shist[:, 128 * ch:128 * (ch + 1)], Sst[:, 128 * cur:128 * (cur + 1)])
                    for grp in range(2):
                        last = v.scalar_tensor_tensor(
                            Sst[:, 128 * nxt + 64 * grp:128 * nxt + 64 * (grp + 1)],
                            Sst[:, 128 * cur + 64 * grp:128 * cur + 64 * (grp + 1)],
                            gamc[:, 2 * ch + grp:2 * ch + grp + 1],
                            PS[:, 64 * grp:64 * (grp + 1)], ALU.mult, ALU.add)
                    last.then_inc(seA, 1)
        ctx.bump("A_sm", 16)
        ctx.bump("A_se", 16)

        for gi in range(4):
            with ctx.block() as blk:
                @blk.tensor
                def _(t):
                    for cj in range(4):
                        ch = 4 * gi + cj
                        for h in range(4):
                            par, grp = h % 2, h // 2
                            PX = PA if par == 0 else PB
                            hs64 = slice(64 * par, 64 * (par + 1))
                            t.matmul(PX[32 * cj:32 * (cj + 1), 64 * grp:64 * (grp + 1)],
                                     omt[hs64, 64 * ch + 32 * grp:64 * ch + 32 * (grp + 1)],
                                     Shist[hs64, 128 * ch + 64 * grp:128 * ch + 64 * (grp + 1)],
                                     start=True, stop=True, tile_position=(64 * par, 32 * cj))
            with ctx.block() as blk:
                @blk.vector
                def _(v):
                    v.tensor_tensor(bass.AP(oarr, 0, [[256, 128], [128, 2], [1, 64]]),
                                    bass.AP(oloc, 256 * gi, [[1024, 128], [128, 2], [1, 64]]),
                                    bass.AP(PA, 0, [[512, 128], [64, 2], [1, 64]]), ALU.add)
                    v.tensor_tensor(bass.AP(oarr, 64, [[256, 128], [128, 2], [1, 64]]),
                                    bass.AP(oloc, 256 * gi + 64, [[1024, 128], [128, 2], [1, 64]]),
                                    bass.AP(PB, 0, [[512, 128], [64, 2], [1, 64]]), ALU.add)
            with ctx.block() as blk:
                @blk.scalar
                def _(s):
                    s.activation(scr[:, 0:256], oarr[:, :], AF.Square)
            with ctx.block() as blk:
                @blk.vector
                def _(v):
                    v.reduce_sum(scr2[:, 0:4], _r3(scr[:, 0:256], 64), axis=AX.X)
            with ctx.block() as blk:
                @blk.vector
                def _(v):
                    v.tensor_scalar(scr2[:, 0:4], scr2[:, 0:4], 1.0 / 64.0, 1e-5, ALU.mult, ALU.add)
            with ctx.block() as blk:
                @blk.scalar
                def _(s):
                    s.activation(scr2[:, 4:8], scr2[:, 0:4], AF.Sqrt)
            with ctx.block() as blk:
                @blk.vector
                def _(v):
                    v.reciprocal(scr2[:, 4:8], scr2[:, 4:8])
            with ctx.block() as blk:
                @blk.vector
                def _(v):
                    v.tensor_tensor(_r3(oarr[:, :], 64), _r3(oarr[:, :], 64),
                                    bass.AP(scr2, 4, [[512, 128], [1, 4], [0, 64]]), ALU.mult)
            with ctx.block() as blk:
                @blk.vector
                def _(v):
                    v.tensor_mul(oarr[:, :], oarr[:, :], onwbc[:, :])
            with ctx.block() as blk:
                @blk.vector
                def _(v):
                    v.tensor_mul(oarr[:, :], oarr[:, :], gatet[:, 256 * gi:256 * (gi + 1)])
            with ctx.block() as blk:
                @blk.tensor
                def _(t):
                    t.matmul(PS[:, 0:128], oarr[:, 0:128], sb["c_i128f"][:, :], start=True, stop=True)
                    t.matmul(PS[:, 128:256], oarr[:, 128:256], sb["c_i128f"][:, :], start=True, stop=True)
            with ctx.block() as blk:
                @blk.vector
                def _(v):
                    v.tensor_copy(ofmt[:, 128 * gi:128 * (gi + 1)], PS[:, 0:128])
                    v.tensor_copy(ofmt[:, 512 + 128 * gi:512 + 128 * (gi + 1)], PS[:, 128:256])
        with ctx.block() as blk:
            @blk.sync
            def _(sy):
                sy.dma_start(out=oscr[0:128, tc], in_=ofmt[:, 0:512]).then_inc(st, 16)
                sy.dma_start(out=oscr[128:256, tc], in_=ofmt[:, 512:1024]).then_inc(st, 16)
                sy.wait_ge(st, ctx.bump("A_st", 32))

    ctx.flush()
    _ps_stack.close()
    if do_collective:
        emit_stageA_epilogue(ctx, dst, TT)


def emit_stageA_epilogue(ctx, dst, TT):
    nc = ctx.nc
    sb = ctx.sb
    ld = ctx.sem("A_ld")
    st = ctx.sem("A_st")
    cc = ctx.sem("A_cc")
    P0 = sb["A_PSUMS"][0]
    agb = ctx.dram["agb"]
    with ctx.block() as blk:
        @blk.gpsimd
        def _(g):
            g.collective_compute(
                "AllGather", ALU.bypass,
                replica_groups=[[0, 1, 2, 3], [4, 5, 6, 7]],
                ins=[ctx.dram["oscr"].opt()], outs=[agb.opt()],
            ).then_inc(cc)
            g.wait_ge(cc, ctx.bump("A_cc", 1))
    woA = ctx.sbuf("A_wo", [128, 8 * 1024], BF16)
    og4 = ctx.sbuf("A_og4w", [128, 4 * 512], BF16)
    og = [ctx.sbuf(f"A_og{i}", [128, 512], BF16) for i in range(8)]
    winsel = ctx.sbuf("A_wsel", [128, 4], F32)
    hsres = ctx.sbuf("A_hsr", [128, 512], F32)
    yyA = [ctx.sbuf(f"A_yy{i}", [128, 512], F32) for i in range(8)]
    with ctx.block() as blk:
        @blk.gpsimd
        def _(g):
            g.dma_start(out=woA[:, :].rearrange("p (k c) -> p k c", k=8),
                        in_=ctx.dram["woA"].rearrange("(k p) c -> p k c", p=128)).then_inc(ld, 16)
        @blk.sync
        def _(sy):
            sy.dma_start(out=winsel[:, :], in_=ctx.dram["winsel"]).then_inc(ld, 16)
            sy.wait_ge(ld, ctx.bump("A_ld", 32))
    tpcn = TT // 4
    nt2 = tpcn // 512
    for n in range(nt2):
        for i in range(8):
            with ctx.block() as blk:
                @blk.sync
                def _(sy):
                    for q in range(4):
                        sy.dma_start(out=og4[:, 512 * q:512 * (q + 1)],
                                     in_=agb[128 * i:128 * (i + 1),
                                             tpcn * q + 512 * n:tpcn * q + 512 * (n + 1)]).then_inc(ld, 16)
                    sy.wait_ge(ld, ctx.bump("A_ld", 16 * 4))
            with ctx.block() as blk:
                @blk.vector
                def _(v):
                    v.tensor_scalar(og[i][:, :], og4[:, 0:512], winsel[:, 0:1], None, ALU.mult)
            for q in range(1, 4):
                with ctx.block() as blk:
                    @blk.vector
                    def _(v):
                        v.scalar_tensor_tensor(og[i][:, :], og4[:, 512 * q:512 * (q + 1)],
                                               winsel[:, q:q + 1], og[i][:, :], ALU.mult, ALU.add)
        for oc in range(8):
            with ctx.block() as blk:
                @blk.sync
                def _(sy):
                    sy.dma_start(out=hsres[:, :],
                                 in_=ctx.dram["hsres"][128 * oc:128 * (oc + 1),
                                                       512 * n:512 * (n + 1)]).then_inc(ld, 16)
                    sy.wait_ge(ld, ctx.bump("A_ld", 16))
                @blk.tensor
                def _(t):
                    for kc in range(8):
                        t.matmul(P0[:, :], woA[:, 1024 * kc + 128 * oc:1024 * kc + 128 * (oc + 1)],
                                 og[kc][:, :], start=(kc == 0), stop=(kc == 7))
            with ctx.block() as blk:
                @blk.vector
                def _(v):
                    v.tensor_add(yyA[oc][:, :], hsres[:, :], P0[:, :])
        emit_ln_fm(ctx, None, [yyA[i][:, :] for i in range(8)],
                   [yyA[i][:, :] for i in range(8)], 512)
        with ctx.block() as blk:
            @blk.sync
            def _(sy):
                for i in range(8):
                    sy.dma_start(out=ctx.dram[dst][128 * i:128 * (i + 1), 512 * n:512 * (n + 1)],
                                 in_=yyA[i][:, :]).then_inc(st, 16)
                sy.wait_ge(st, ctx.bump("A_st", 16 * 8))


def build_stageA_program(TT=T, debug=False, do_collective=True):
    nc = bass.Bass("TRN2", target_bir_lowering=False)
    stack = contextlib.ExitStack()
    ctx = Ctx(nc, stack)
    with stack:
        ctx.dram_in("hsT", [1024, TT], F32)
        ctx.dram_in("wA", [1024, 1024], F32)
        ctx.dram_in("wfg2", [128, 256], F32)
        ctx.dram_in("convw", [128, 24], F32)
        ctx.dram_in("convb", [128, 6], F32)
        ctx.dram_in("dtb2", [128, 2], F32)
        ctx.dram_in("alogc", [4, 1], F32)
        ctx.dram_in("bg2r", [1, 256], F32)
        ctx.dram_in("onw4", [1, 256], F32)
        ctx.dram_in("woA", [1024, 1024], F32)
        ctx.dram_in("hsres", [1024, TT // 4], F32)
        ctx.dram_in("winsel", [128, 4], F32)
        if debug:
            ctx.dram_out("oscr", [256, TT], BF16)
        else:
            ctx.dram_tmp("oscr", [256, TT], BF16)
        ctx.dram_tmp("agb", [1024, TT], BF16)
        ctx.dram_out("x0", [1024, TT // 4], F32)
        for name, arr in _const_arrays().items():
            ctx.dram_in(name, arr.shape, I32 if arr.dtype == np.int32 else F32)
        _preload_consts(ctx)
        emit_stageA(ctx, dst="x0", TT=TT, do_collective=do_collective)
        ctx.flush()
    return nc


def prep_stageA_inputs(inp, core, TT=T):
    b, hq = core // 4, core % 4
    hcols = slice(256 * hq, 256 * (hq + 1))
    hs = inp["L0_attn_hidden_states"][b][:TT]
    wA = np.zeros((1024, 1024), np.float32)
    wA[:, 0:256] = inp["L0_attn_Wq"][:, hcols]
    wA[:, 256:512] = inp["L0_attn_Wk"][:, hcols]
    wA[:, 512:768] = inp["L0_attn_Wv"][:, hcols]
    wA[:, 768:832] = inp["L0_attn_Wf1"]
    wA[:, 832:896] = inp["L0_attn_Wg1"]
    wA[:, 896:900] = inp["L0_attn_Wb"][:, 4 * hq:4 * (hq + 1)]
    wfg2 = np.concatenate([inp["L0_attn_Wf2"][:, hcols], inp["L0_attn_Wg2"][:, hcols]], axis=0)
    cw = np.zeros((128, 24), np.float32)
    cb = np.zeros((128, 6), np.float32)
    for i in range(2):
        rows = slice(256 * hq + 128 * i, 256 * hq + 128 * (i + 1))
        cw[:, 12 * i + 0:12 * i + 4] = inp["L0_attn_qcw"][rows]
        cw[:, 12 * i + 4:12 * i + 8] = inp["L0_attn_kcw"][rows]
        cw[:, 12 * i + 8:12 * i + 12] = inp["L0_attn_vcw"][rows]
        cb[:, 3 * i + 0] = inp["L0_attn_qcb"][rows]
        cb[:, 3 * i + 1] = inp["L0_attn_kcb"][rows]
        cb[:, 3 * i + 2] = inp["L0_attn_vcb"][rows]
    win = TT // 4
    im = {
        "hsT": np.ascontiguousarray(hs.T),
        "wA": wA,
        "wfg2": np.ascontiguousarray(wfg2),
        "convw": cw,
        "convb": cb,
        "dtb2": np.ascontiguousarray(inp["L0_attn_dt_bias"][hcols].reshape(2, 128).T),
        "alogc": np.ascontiguousarray(inp["L0_attn_A_log"][4 * hq:4 * (hq + 1)][:, None]),
        "bg2r": np.ascontiguousarray(inp["L0_attn_bg2"][hcols][None, :]),
        "onw4": np.ascontiguousarray(np.tile(inp["L0_attn_o_norm_w"], 4)[None, :]),
        "woA": np.ascontiguousarray(inp["L0_attn_Wo"]),
        "hsres": np.ascontiguousarray(hs[win * hq:win * (hq + 1)].T),
        "winsel": np.ascontiguousarray(np.repeat((np.arange(4) == hq).astype(np.float32)[None, :], 128, 0)),
    }
    im.update(_const_arrays())
    return im


def _const_arrays():
    c = {}
    c["cd_ones128"] = np.ones((128, 1), np.float32)
    c["cd_ones1"] = np.ones((1, 128), np.float32)
    indqk = np.zeros((128, 128), np.float32)
    for i in range(8):
        indqk[0:64, 16 * i + 2 * i] = 1.0
        indqk[64:128, 16 * i + 2 * i + 1] = 1.0
    c["cd_indqk"] = indqk
    bc16 = np.zeros((16, 1024), np.float32)
    for i in range(8):
        bc16[2 * i, 128 * i:128 * i + 64] = 1.0
        bc16[2 * i + 1, 128 * i + 64:128 * (i + 1)] = 1.0
    c["cd_bc16"] = bc16
    hsel = np.zeros((16, 4), np.float32)
    for h in range(16):
        hsel[h, h % 4] = 1.0
    c["cd_hsel"] = hsel
    phm = np.zeros((16, 512), np.float32)
    t = np.arange(512)
    for h in range(16):
        phm[h, t % 4 == h % 4] = 1.0
    c["cd_phmask"] = phm
    c["cd_i128"] = np.eye(128, dtype=np.float32)
    non2 = np.zeros((128, 128), np.float32)
    non2[0, :] = -1.0
    non2[64, :] = -1.0
    c["cd_nones2"] = non2
    ind2 = np.zeros((128, 2), np.float32)
    ind2[0:64, 0] = 1.0
    ind2[64:128, 1] = 1.0
    c["cd_indh2"] = ind2
    c["cd_indh2T"] = ind2.T.copy()
    wm = np.zeros((128, 8 * 512), np.float32)
    kp = np.arange(128)[:, None]
    qfr = np.arange(512)[None, :]
    for kt in range(8):
        diff = qfr + 512 - 128 * kt - kp
        wm[:, 512 * kt:512 * (kt + 1)] = np.where((diff >= 0) & (diff <= 512), 0.0, -32768.0)
    c["cd_wmA"] = wm
    lo = np.zeros_like(wm)
    lo[64:128, :] = wm[0:64, :]
    c["cd_wmBlo"] = lo
    e32 = np.eye(32, dtype=np.float32)
    c["cd_i32g"] = np.tile(e32, (4, 2))
    c["cd_i32st"] = np.tile(e32, (4, 1))
    c["cd_i64st"] = np.tile(np.eye(64, dtype=np.float32), (2, 1))
    c["cd_ni64st"] = -np.tile(np.eye(64, dtype=np.float32), (2, 1))
    idx = np.arange(32)
    J, I = np.meshgrid(idx, idx, indexing="ij")
    b4 = idx // 4; b8 = idx // 8; b16 = idx // 16
    mD4s = ((b4[J] == b4[I]) & (J < I)).astype(np.float32)
    mD4i = ((b4[J] == b4[I]) & (J <= I)).astype(np.float32)
    mO4 = (b4[I] == b4[J] + 1).astype(np.float32)
    mO8 = ((b8[I] == b8[J] + 1) & (b4[I] >= b4[J] + 2)).astype(np.float32)
    mO16 = ((b16[I] == b16[J] + 1) & (b8[I] >= b8[J] + 2)).astype(np.float32)
    def tile42(m):
        return np.tile(m, (4, 2))
    c["cd_msel"] = np.concatenate([tile42(mD4s), tile42(mD4i), tile42(mO4),
                                   tile42(mO8), tile42(mO16)], axis=1).astype(np.int32)
    c["cd_msel2"] = np.concatenate([tile42(mD4s.T), tile42(mD4i.T), tile42(mO4.T),
                                    tile42(mO8.T), tile42(mO16.T)], axis=1).astype(np.int32)
    rst = np.ones((128, 512), np.float32)
    rst[:, ::32] = 0.0
    c["cd_reset"] = rst
    bc4 = np.zeros((4, 256), np.float32)
    for t_ in range(2):
        for p_ in range(128):
            bc4[2 * t_ + p_ // 64, 128 * t_ + p_] = 1.0
    c["cd_bc4"] = bc4
    c["cd_nbc4"] = -bc4
    return c


def _preload_consts(ctx):
    nc = ctx.nc
    ld = ctx.sem("c_ld")
    co = ctx.sbuf("c_ones128", [128, 1], F32)
    c1 = ctx.sbuf("c_ones1", [1, 128], F32)
    i128f = ctx.sbuf("c_i128f", [128, 128], F32)
    i128b = ctx.sbuf("c_i128b", [128, 128], BF16)
    nones2 = ctx.sbuf("c_nones2", [128, 128], BF16)
    onesb = ctx.sbuf("c_onesb", [128, 1], BF16)
    indh2 = ctx.sbuf("c_indh2", [128, 2], F32)
    indh2T = ctx.sbuf("c_indh2T", [2, 128], F32)
    with ctx.block() as blk:
        @blk.sync
        def _(sy):
            sy.dma_start(out=co[:, :], in_=ctx.dram["cd_ones128"]).then_inc(ld, 16)
            sy.dma_start(out=c1[:, :], in_=ctx.dram["cd_ones1"]).then_inc(ld, 16)
            sy.dma_start(out=i128f[:, :], in_=ctx.dram["cd_i128"]).then_inc(ld, 16)
            sy.dma_start(out=indh2[:, :], in_=ctx.dram["cd_indh2"]).then_inc(ld, 16)
            sy.dma_start(out=indh2T[:, :], in_=ctx.dram["cd_indh2T"]).then_inc(ld, 16)
        @blk.gpsimd
        def _(g):
            g.dma_start(out=i128b[:, :], in_=ctx.dram["cd_i128"]).then_inc(ld, 16)
            g.dma_start(out=nones2[:, :], in_=ctx.dram["cd_nones2"]).then_inc(ld, 16)
            g.dma_start(out=onesb[:, :], in_=ctx.dram["cd_ones128"]).then_inc(ld, 16)
            g.wait_ge(ld, ctx.bump("c_ld", 16 * 8))


def build_stageB_program():
    nc = bass.Bass("TRN2", target_bir_lowering=False)
    stack = contextlib.ExitStack()
    ctx = Ctx(nc, stack)
    with stack:
        ctx.dram_in("x0", [1024, TPC], F32)
        ctx.dram_in("wgT", [1024, 3072], I32)
        ctx.dram_in("wuT", [1024, 3072], I32)
        ctx.dram_in("wdT", [3072, 1024], I32)
        ctx.dram_out("x1", [1024, TPC], F32)
        for name, arr in _const_arrays().items():
            ctx.dram_in(name, arr.shape, I32 if arr.dtype == np.int32 else F32)
        _preload_consts(ctx)
        emit_stageB(ctx)
        ctx.flush()
    return nc


def _emit_stageD_masks(ctx):
    nc = ctx.nc
    ctx.sbuf("D_wmA", [128, 8 * 512], BF16)
    ctx.sbuf("D_wmBlo", [128, 8 * 512], BF16)
    ctx.sbuf("D_pmA", [128, 4 * 512], BF16)
    ctx.sbuf("D_pmBlo", [128, 4 * 512], BF16)
    ctx.sbuf("D_hmask", [128, 1], F32)
    ld = ctx.sem("D_ld")
    with ctx.block() as blk:
        @blk.gpsimd
        def _(g):
            for nm in ("wmA", "wmBlo"):
                g.dma_start(out=ctx.sb[f"D_{nm}"][:, :], in_=ctx.dram[f"cd_{nm}"]).then_inc(ld, 16)
            for nm in ("pmA", "pmBlo"):
                g.dma_start(out=ctx.sb[f"D_{nm}"][:, :], in_=ctx.dram[nm]).then_inc(ld, 16)
        @blk.sync
        def _(sy):
            sy.dma_start(out=ctx.sb["D_hmask"][:, :], in_=ctx.dram["hmask"]).then_inc(ld, 16)
            sy.wait_ge(ld, ctx.bump("D_ld", 16 * 5))


def build_stageDa_program(l):
    nc = bass.Bass("TRN2", target_bir_lowering=False)
    stack = contextlib.ExitStack()
    ctx = Ctx(nc, stack)
    with stack:
        ctx.dram_in("xsrc", [1024, TPC], F32)
        ctx.dram_in("hl", [1024, HB], F32)
        ctx.dram_in("hmask", [128, 1], F32)
        ctx.dram_in("pmA", [128, 4 * 512], F32)
        ctx.dram_in("pmBlo", [128, 4 * 512], F32)
        ctx.dram_in(f"wqkvD{l}", [1024, 1536], F32)
        ctx.dram_in(f"woD{l}", [512, 1024], F32)
        ctx.dram_in(f"rotq{l}", [128, 128], F32)
        ctx.dram_in(f"rotk{l}", [128, 128], F32)
        for nm in ("csq", "ssq", "csk", "ssk"):
            ctx.dram_in(f"{nm}{l}", [128, NQK], F32)
        ctx.dram_out("xm", [1024, TPC], F32)
        for name, arr in _const_arrays().items():
            ctx.dram_in(name, arr.shape, I32 if arr.dtype == np.int32 else F32)
        _preload_consts(ctx)
        for nm in ("D_P0", "D_P1", "D_P2", "D_P3"):
            ctx.sb[nm] = ctx.psum(nm, [128, 512], F32)
        _emit_stageD_masks(ctx)
        emit_stageD_attn(ctx, l, "xsrc", "hl", "xm")
        ctx.flush()
    return nc


def build_stageDf_program(l, final=False):
    nc = bass.Bass("TRN2", target_bir_lowering=False)
    stack = contextlib.ExitStack()
    ctx = Ctx(nc, stack)
    with stack:
        ctx.dram_in("xm", [1024, TPC], F32)
        ctx.dram_in("xmprevD", [128, 8], F32)
        ctx.dram_in("hmask", [128, 1], F32)
        ctx.dram_in(f"w1D{l}", [1024, 5632], F32)
        ctx.dram_in(f"w2D{l}", [2816, 1024], F32)
        ctx.dram_in(f"b1D{l}", [128, 44], F32)
        ctx.dram_in(f"b2D{l}", [128, 8], F32)
        ctx.dram_in(f"lngD{l}", [128, 8], F32)
        ctx.dram_in(f"lnbD{l}", [128, 8], F32)
        if final:
            ctx.dram_in("x2in", [1024, TPC], F32)
            ctx.dram_tmp("x3o", [1024, TPC], F32)
        ctx.dram_out("xout", [1024, TPC], F32)
        for name, arr in _const_arrays().items():
            ctx.dram_in(name, arr.shape, I32 if arr.dtype == np.int32 else F32)
        _preload_consts(ctx)
        ctx.sb["D_P0"] = ctx.psum("D_P0", [128, 512], F32)
        ctx.sbuf("D_hmask", [128, 1], F32)
        ld = ctx.sem("D_ld")
        with ctx.block() as blk:
            @blk.sync
            def _(sy):
                sy.dma_start(out=ctx.sb["D_hmask"][:, :], in_=ctx.dram["hmask"]).then_inc(ld, 16)
                sy.wait_ge(ld, ctx.bump("D_ld", 16))
        if final:
            emit_stageD_ffn(ctx, l, "xmprevD", "xm", "x3o")
            emit_final_ln(ctx, "x2in", "x3o", "xout")
        else:
            emit_stageD_ffn(ctx, l, "xmprevD", "xm", "xout")
        ctx.flush()
    return nc


def _prep_B(inputs):
    return {
        "wgT": np.ascontiguousarray(inputs["L0_ffn_w_gate"].T),
        "wuT": np.ascontiguousarray(inputs["L0_ffn_w_up"].T),
        "wdT": np.ascontiguousarray(inputs["L0_ffn_w_down"].T),
    }


def _prep_C(inputs):
    return {
        "wC": np.ascontiguousarray(np.concatenate(
            [inputs["L1_attn_Wq"], inputs["L1_attn_Wk"], inputs["L1_attn_Wv"]], axis=1)),
        "bC": np.ascontiguousarray(np.concatenate(
            [inputs["L1_attn_bq"], inputs["L1_attn_bk"], inputs["L1_attn_bv"]]).reshape(24, 128).T),
        "woC": np.ascontiguousarray(inputs["L1_attn_Wo"]),
        "boC": np.ascontiguousarray(inputs["L1_attn_bo"].reshape(8, 128).T),
    }


def _prep_D_shared(inputs, l):
    w1 = inputs["L1_ffn_w1"][l]; b1 = inputs["L1_ffn_b1"][l]
    w2 = inputs["L1_ffn_w2"][l]; b2 = inputs["L1_ffn_b2"][l]
    w1a = np.zeros((1024, 2816), np.float32); w1g = np.zeros((1024, 2816), np.float32)
    w1a[:, 0:2730] = w1[:, 0:2730]; w1g[:, 0:2730] = w1[:, 2730:5460]
    b1a = np.zeros(2816, np.float32); b1g = np.zeros(2816, np.float32)
    b1a[0:2730] = b1[0:2730]; b1g[0:2730] = b1[2730:5460]
    w1p = np.zeros((1024, 5632), np.float32)
    b1p = np.zeros(5632, np.float32)
    for i in range(22):
        w1p[:, 256 * i:256 * i + 128] = w1a[:, 128 * i:128 * (i + 1)]
        w1p[:, 256 * i + 128:256 * (i + 1)] = w1g[:, 128 * i:128 * (i + 1)]
        b1p[256 * i:256 * i + 128] = b1a[128 * i:128 * (i + 1)]
        b1p[256 * i + 128:256 * (i + 1)] = b1g[128 * i:128 * (i + 1)]
    w2p = np.zeros((2816, 1024), np.float32)
    w2p[0:2730] = w2
    return {
        f"wqkvD{l}": np.ascontiguousarray(inputs["L1_ffn_wqkv"][l]),
        f"woD{l}": np.ascontiguousarray(inputs["L1_ffn_wo"][l]),
        f"w1D{l}": w1p,
        f"w2D{l}": w2p,
        f"b1D{l}": np.ascontiguousarray(b1p.reshape(44, 128).T),
        f"b2D{l}": np.ascontiguousarray(b2.reshape(8, 128).T),
        f"lngD{l}": np.ascontiguousarray(inputs["L1_ffn_ln_g"][l].reshape(8, 128).T),
        f"lnbD{l}": np.ascontiguousarray(inputs["L1_ffn_ln_b"][l].reshape(8, 128).T),
    }


def _prep_D_core(inputs, l, core):
    q = core % 4
    t0 = TPC * q
    DIM_HEAD = 64; WINDOW = 512
    tpos = np.arange(t0 - 512, t0 + TPC, dtype=np.float64)
    half_idx = np.arange(0, DIM_HEAD, 2, dtype=np.float64)
    inv_freq = 1.0 / 10000.0 ** (half_idx / DIM_HEAD)
    freqs = np.concatenate([tpos[:, None] * inv_freq[None, :]] * 2, axis=-1)
    base = (half_idx + 0.4 * DIM_HEAD) / (1.4 * DIM_HEAD)
    power = (tpos - T // 2) / float(WINDOW // 2)
    scale = np.concatenate([base[None, :] ** power[:, None]] * 2, axis=-1)
    cos, sin = np.cos(freqs), np.sin(freqs)
    qs = inputs["L1_ffn_q_scale"][l]; ks = inputs["L1_ffn_k_scale"][l]
    csq = (cos * scale * 8.0).T * qs[:, None]
    ssq = (sin * scale * 8.0).T
    csk = (cos / scale).T * ks[:, None]
    ssk = (sin / scale).T
    R = np.zeros((64, 64), np.float32)
    for d_ in range(32):
        R[d_, d_ + 32] = -1.0
        R[d_ + 32, d_] = 1.0
    def bd(m):
        z = np.zeros((128, 128), np.float32)
        z[0:64, 0:64] = m.T
        z[64:128, 64:128] = m.T
        return z
    return {
        f"rotq{l}": bd(R * qs[None, :]),
        f"rotk{l}": bd(R * ks[None, :]),
        f"csq{l}": np.ascontiguousarray(np.tile(csq, (2, 1)).astype(np.float32)),
        f"ssq{l}": np.ascontiguousarray(np.tile(ssq, (2, 1)).astype(np.float32)),
        f"csk{l}": np.ascontiguousarray(np.tile(csk, (2, 1)).astype(np.float32)),
        f"ssk{l}": np.ascontiguousarray(np.tile(ssk, (2, 1)).astype(np.float32)),
    }


def _halo_of(xfull_fm, core):
    q = core % 4
    t0 = TPC * q
    h = np.zeros((1024, HB), np.float32)
    if q > 0:
        h[:] = xfull_fm[:, t0 - HB:t0]
    return h


def _xmprev_of(xm_fm_prevcore, core):
    v = np.zeros((128, 8), np.float32)
    if core % 4 > 0:
        v[:] = xm_fm_prevcore[:, -1].reshape(8, 128).T
    return v


def _pads_of(core):
    pm = np.zeros((128, 4 * 512), np.float32)
    if core % 4 == 0:
        pm[:] = -32768.0
    pmlo = np.zeros_like(pm)
    pmlo[64:128] = pm[0:64]
    return pm, pmlo


def build_fused_program():
    nc = bass.Bass("TRN2", target_bir_lowering=False)
    stack = contextlib.ExitStack()
    ctx = Ctx(nc, stack)
    with stack:
        ctx.dram_in("hsT", [1024, T], F32)
        ctx.dram_in("wA", [1024, 1024], F32)
        ctx.dram_in("wfg2", [128, 256], F32)
        ctx.dram_in("convw", [128, 24], F32)
        ctx.dram_in("convb", [128, 6], F32)
        ctx.dram_in("dtb2", [128, 2], F32)
        ctx.dram_in("alogc", [4, 1], F32)
        ctx.dram_in("bg2r", [1, 256], F32)
        ctx.dram_in("onw4", [1, 256], F32)
        ctx.dram_in("woA", [1024, 1024], F32)
        ctx.dram_in("hsres", [1024, TPC], F32)
        ctx.dram_in("winsel", [128, 4], F32)
        ctx.dram_in("hselD", [128, 4], F32)
        ctx.dram_in("hmask", [128, 1], F32)
        ctx.dram_in("pmA", [128, 4 * 512], F32)
        ctx.dram_in("pmBlo", [128, 4 * 512], F32)
        ctx.dram_in("wgT", [1024, 3072], I32)
        ctx.dram_in("wuT", [1024, 3072], I32)
        ctx.dram_in("wdT", [3072, 1024], I32)
        ctx.dram_in("wC", [1024, 3072], F32)
        ctx.dram_in("bC", [128, 24], F32)
        ctx.dram_in("woC", [1024, 1024], F32)
        ctx.dram_in("boC", [128, 8], F32)
        for l in range(2):
            ctx.dram_in(f"wqkvD{l}", [1024, 1536], F32)
            ctx.dram_in(f"woD{l}", [512, 1024], F32)
            ctx.dram_in(f"w1D{l}", [1024, 5632], F32)
            ctx.dram_in(f"w2D{l}", [2816, 1024], F32)
            ctx.dram_in(f"b1D{l}", [128, 44], F32)
            ctx.dram_in(f"b2D{l}", [128, 8], F32)
            ctx.dram_in(f"lngD{l}", [128, 8], F32)
            ctx.dram_in(f"lnbD{l}", [128, 8], F32)
            ctx.dram_in(f"rotq{l}", [128, 128], F32)
            ctx.dram_in(f"rotk{l}", [128, 128], F32)
            for nm in ("csq", "ssq", "csk", "ssk"):
                ctx.dram_in(f"{nm}{l}", [128, NQK], F32)
        for name, arr in _const_arrays().items():
            ctx.dram_in(name, arr.shape, I32 if arr.dtype == np.int32 else F32)
        ctx.dram_tmp("oscr", [256, T], BF16)
        ctx.dram_tmp("agb", [1024, T], BF16)
        for nm in ("x0s", "x1s", "x2s", "xmscr", "x3s", "x4s"):
            ctx.dram_tmp(nm, [1024, TPC], F32)
        ctx.dram_tmp("h2pub", [1024, HB], F32)
        ctx.dram_tmp("h2ag", [4096, HB], F32)
        ctx.dram_tmp("h3pub", [1024, HB], F32)
        ctx.dram_tmp("h3ag", [4096, HB], F32)
        ctx.dram_tmp("xmpub0", [128, 8], F32)
        ctx.dram_tmp("xmag0", [512, 8], F32)
        ctx.dram_tmp("xmpub1", [128, 8], F32)
        ctx.dram_tmp("xmag1", [512, 8], F32)
        ctx.dram_out("out", [1024, TPC], F32)

        _preload_consts(ctx)
        ccs = ctx.sem("F_cc")
        ldf = ctx.sem("F_ld")

        with contextlib.ExitStack() as stA:
            emit_stageA(ctx.sub(stA), dst="x0s", TT=T, do_collective=True)
            ctx.flush()
        with contextlib.ExitStack() as stB:
            emit_stageB(ctx.sub(stB), src="x0s", dst="x1s")
            ctx.flush()
        with contextlib.ExitStack() as stC:
            emit_stageC(ctx.sub(stC), src="x1s", dst="x2s")
            ctx.flush()
        with contextlib.ExitStack() as stD:
            cD = ctx.sub(stD)
            for nm in ("D_P0", "D_P1", "D_P2", "D_P3"):
                cD.sb[nm] = cD.psum(nm, [128, 512], F32)
            _emit_stageD_masks(cD)
            cD.sbuf("D_hsel", [128, 4], F32)
            cD.sbuf("D_xlast", [128, 8], F32)
            xmprev_sb = cD.sbuf("D_xmprev", [128, 8], F32)
            xmcand = cD.sbuf("D_xmcand", [128, 32], F32)
            with ctx.block() as blk:
                @blk.sync
                def _(sy):
                    sy.dma_start(out=cD.sb["D_hsel"][:, :], in_=ctx.dram["hselD"]).then_inc(ldf, 16)
                    sy.wait_ge(ldf, ctx.bump("F_ld", 16))
            for l in range(2):
                src = "x2s" if l == 0 else "x3s"
                hpub, hag = (f"h2pub", "h2ag") if l == 0 else ("h3pub", "h3ag")
                xmp, xma = (f"xmpub{l}", f"xmag{l}")
                with ctx.block() as blk:
                    @blk.sync
                    def _(sy):
                        sy.dma_start(out=ctx.dram[hpub], in_=ctx.dram[src][:, TPC - HB:TPC]).then_inc(ldf, 16)
                        sy.wait_ge(ldf, ctx.bump("F_ld", 16))
                with ctx.block() as blk:
                    @blk.gpsimd
                    def _(g):
                        g.collective_compute(
                            "AllGather", ALU.bypass,
                            replica_groups=[[0, 1, 2, 3], [4, 5, 6, 7]],
                            ins=[ctx.dram[hpub].opt()], outs=[ctx.dram[hag].opt()],
                        ).then_inc(ccs)
                        g.wait_ge(ccs, ctx.bump("F_cc", 1))
                with contextlib.ExitStack() as stDa:
                    emit_stageD_attn(cD.sub(stDa), l, src, None, "xmscr",
                                     halo_ag=hag, xmpub=xmp)
                    ctx.flush()
                with ctx.block() as blk:
                    @blk.gpsimd
                    def _(g):
                        g.collective_compute(
                            "AllGather", ALU.bypass,
                            replica_groups=[[0, 1, 2, 3], [4, 5, 6, 7]],
                            ins=[ctx.dram[xmp].opt()], outs=[ctx.dram[xma].opt()],
                        ).then_inc(ccs)
                        g.wait_ge(ccs, ctx.bump("F_cc", 1))
                with ctx.block() as blk:
                    @blk.sync
                    def _(sy):
                        for q in range(4):
                            sy.dma_start(out=xmcand[:, 8 * q:8 * (q + 1)],
                                         in_=ctx.dram[xma][128 * q:128 * (q + 1), :]).then_inc(ldf, 16)
                        sy.wait_ge(ldf, ctx.bump("F_ld", 16 * 4))
                with ctx.block() as blk:
                    @blk.vector
                    def _(v):
                        v.tensor_scalar(xmprev_sb[:, :], xmcand[:, 0:8], cD.sb["D_hsel"][:, 0:1], None, ALU.mult)
                for q in range(1, 4):
                    with ctx.block() as blk:
                        @blk.vector
                        def _(v):
                            v.scalar_tensor_tensor(xmprev_sb[:, :], xmcand[:, 8 * q:8 * (q + 1)],
                                                   cD.sb["D_hsel"][:, q:q + 1], xmprev_sb[:, :],
                                                   ALU.mult, ALU.add)
                with contextlib.ExitStack() as stDf:
                    emit_stageD_ffn(cD.sub(stDf), l, None, "xmscr",
                                    "x3s" if l == 0 else "x4s", xmprev_sb=xmprev_sb)
                    ctx.flush()
        with contextlib.ExitStack() as stF:
            emit_final_ln(ctx.sub(stF), "x2s", "x4s", "out")
            ctx.flush()
    return nc


def prep_fused_inputs(inputs, core):
    im = prep_stageA_inputs(inputs, core)
    im.update(_prep_B(inputs))
    im.update(_prep_C(inputs))
    q = core % 4
    im["hmask"] = np.full((128, 1), 0.0 if q == 0 else 1.0, np.float32)
    hsel = np.zeros((128, 4), np.float32)
    if q > 0:
        hsel[:, q - 1] = 1.0
    im["hselD"] = hsel
    pm, pmlo = _pads_of(core)
    im["pmA"] = pm
    im["pmBlo"] = pmlo
    for l in range(2):
        im.update(_prep_D_shared(inputs, l))
        im.update(_prep_D_core(inputs, l, core))
    return im


USE_DEPS = True


def kernel(**inputs):
    global DEPMODE, BODYIO, FENCES, DEPS
    cores = list(range(8))
    if USE_DEPS:
        DEPMODE = "record"
        BODYIO = {}
        FENCES = set()
        build_fused_program()
        compute_deps()
        DEPMODE = "use"
        nc = build_fused_program()
        DEPMODE = None
    else:
        DEPMODE = None
        nc = build_fused_program()
    in_maps = [prep_fused_inputs(inputs, c) for c in cores]
    res = run_bass_kernel_spmd(nc, in_maps, core_ids=cores)
    outa = np.empty((2, T, D), np.float32)
    for c in cores:
        b, q = c // 4, c % 4
        outa[b, TPC * q:TPC * (q + 1)] = np.asarray(res.results[c]["out"]).T
    return outa


def kernel_multi(**inputs):
    consts = _const_arrays()
    cores = list(range(8))

    ncA = build_stageA_program()
    imsA = [prep_stageA_inputs(inputs, c) for c in cores]
    resA = run_bass_kernel_spmd(ncA, imsA, core_ids=cores)
    x0 = [np.asarray(resA.results[c]["x0"]) for c in cores]

    ncB = build_stageB_program()
    wB = _prep_B(inputs)
    imsB = [{**consts, **wB, "x0": x0[c]} for c in cores]
    resB = run_bass_kernel_spmd(ncB, imsB, core_ids=cores)
    x1 = [np.asarray(resB.results[c]["x1"]) for c in cores]

    ncC = build_stageC_program()
    wC = _prep_C(inputs)
    imsC = [{**consts, **wC, "x1": x1[c]} for c in cores]
    resC = run_bass_kernel_spmd(ncC, imsC, core_ids=cores)
    x2 = [np.asarray(resC.results[c]["x2"]) for c in cores]
    x2full = [np.concatenate(x2[0:4], axis=1), np.concatenate(x2[4:8], axis=1)]

    out = {}
    xcur = x2
    xcurfull = x2full
    for l in range(2):
        ncDa = build_stageDa_program(l)
        wDl = _prep_D_shared(inputs, l)
        ims = []
        for c in cores:
            pm, pmlo = _pads_of(c)
            ims.append({**consts,
                        "xsrc": xcur[c],
                        "hl": _halo_of(xcurfull[c // 4], c),
                        "hmask": np.full((128, 1), 0.0 if c % 4 == 0 else 1.0, np.float32),
                        "pmA": pm, "pmBlo": pmlo,
                        f"wqkvD{l}": wDl[f"wqkvD{l}"], f"woD{l}": wDl[f"woD{l}"],
                        **_prep_D_core(inputs, l, c)})
        resDa = run_bass_kernel_spmd(ncDa, ims, core_ids=cores)
        xm = [np.asarray(resDa.results[c]["xm"]) for c in cores]

        final = (l == 1)
        ncDf = build_stageDf_program(l, final=final)
        ims = []
        for c in cores:
            d = {**consts,
                 "xm": xm[c],
                 "xmprevD": _xmprev_of(xm[c - 1] if c % 4 > 0 else None, c),
                 "hmask": np.full((128, 1), 0.0 if c % 4 == 0 else 1.0, np.float32)}
            for k_ in (f"w1D{l}", f"w2D{l}", f"b1D{l}", f"b2D{l}", f"lngD{l}", f"lnbD{l}"):
                d[k_] = wDl[k_]
            if final:
                d["x2in"] = x2[c]
            ims.append(d)
        resDf = run_bass_kernel_spmd(ncDf, ims, core_ids=cores)
        xn = [np.asarray(resDf.results[c]["xout"]) for c in cores]
        xcur = xn
        xcurfull = [np.concatenate(xn[0:4], axis=1), np.concatenate(xn[4:8], axis=1)]

    outa = np.empty((2, T, D), np.float32)
    for c in cores:
        b, q = c // 4, c % 4
        outa[b, TPC * q:TPC * (q + 1)] = xcur[c].T
    return outa
